# version 07
# baseline (speedup 1.0000x reference)
import contextlib

import numpy as np
import concourse.bass as bass
import concourse.mybir as mybir
from concourse.bass_utils import run_bass_kernel_spmd

F32 = mybir.dt.float32
BF16 = mybir.dt.bfloat16
ALU = mybir.AluOpType
AF = mybir.ActivationFunctionType
AX = mybir.AxisListType

NCORES = 8
B, S, D, I = 16, 2048, 1024, 3072
TOK = (B // NCORES) * S
TD = 512
NT = TOK // TD
DC = D // 128
ICH = I // 128
QMAX = 32767.0
C_LO = 1.5 * 2.0 ** 23
C_DELTA = 1.5 * 2.0 ** 52 - 1.5 * 2.0 ** 23

ENGS = ("sync", "gpsimd", "tensor", "scalar", "vector")

NAPPLY = 72
NA = NT * 2 * ICH
NB = NT * (ICH // 4)
NJ = ICH // 4


def _ap_span(ap):
    try:
        dims = list(ap.ap)
        off = ap.offset
        span = 1
        for st, cnt in dims:
            span += abs(st) * (cnt - 1)
        refined = None
        if dims:
            pstr, pcnt = dims[0]
            rest = 1
            for st, cnt in dims[1:]:
                rest += abs(st) * (cnt - 1)
            if pstr >= rest and pstr > 0:
                p_lo = off // pstr
                c_lo = off % pstr
                refined = (pstr, p_lo, p_lo + pcnt, c_lo, c_lo + rest)
        return (ap.tensor.name, off, off + span, refined)
    except AttributeError:
        return None


def _overlap(k, lst):
    for (nm, lo, hi, ref) in lst:
        if nm != k[0]:
            continue
        if k[2] <= lo or hi <= k[1]:
            continue
        if ref is not None and k[3] is not None and ref[0] == k[3][0]:
            if (k[3][2] <= ref[1] or ref[2] <= k[3][1]
                    or k[3][4] <= ref[3] or ref[4] <= k[3][3]):
                continue
        return True
    return False


class FE:

    def __init__(self, eng, state):
        self._e = eng
        self._w = state["w"]
        self._r = state["r"]

    def __getattr__(self, name):
        return getattr(self._e, name)

    def drain(self, *a, **k):
        self._w.clear()
        self._r.clear()
        return self._e.drain(*a, **k)

    def _pre(self, outs, ins):
        need = False
        for ap in list(outs) + list(ins):
            if ap is None or isinstance(ap, (int, float)):
                continue
            k = _ap_span(ap)
            if k is not None and _overlap(k, self._w):
                need = True
                break
        if not need:
            for ap in outs:
                if ap is None or isinstance(ap, (int, float)):
                    continue
                k = _ap_span(ap)
                if k is not None and _overlap(k, self._r):
                    need = True
                    break
        if need:
            self._e.drain()
            self._w.clear()
            self._r.clear()

    def _post(self, outs, ins):
        for ap in outs:
            if ap is None or isinstance(ap, (int, float)):
                continue
            k = _ap_span(ap)
            if k is not None:
                self._w.append(k)
        for ap in ins:
            if ap is None or isinstance(ap, (int, float)):
                continue
            k = _ap_span(ap)
            if k is not None:
                self._r.append(k)

    def _op(self, outs, ins, fn):
        self._pre(outs, ins)
        r = fn()
        self._post(outs, ins)
        return r

    def memset(self, out, v):
        return self._op([out], [], lambda: self._e.memset(out, v))

    def tensor_scalar(self, out, in0, s1, s2, op0, op1=None):
        if op1 is not None:
            return self._op([out], [in0, s1, s2],
                            lambda: self._e.tensor_scalar(out, in0, s1, s2, op0, op1))
        return self._op([out], [in0, s1],
                        lambda: self._e.tensor_scalar(out, in0, s1, s2, op0))

    def tensor_tensor(self, out, a, b, op):
        return self._op([out], [a, b], lambda: self._e.tensor_tensor(out, a, b, op))

    def scalar_tensor_tensor(self, out, in0, sc, in1, op0, op1):
        return self._op([out], [in0, sc, in1],
                        lambda: self._e.scalar_tensor_tensor(out, in0, sc, in1, op0, op1))

    def tensor_tensor_scan(self, out, d0, d1, init, op0, op1):
        return self._op([out], [d0, d1, init],
                        lambda: self._e.tensor_tensor_scan(out, d0, d1, init, op0, op1))

    def reduce_max(self, out, in_, axis, apply_absolute_value=False):
        return self._op([out], [in_],
                        lambda: self._e.reduce_max(out, in_, axis=axis,
                                                   apply_absolute_value=apply_absolute_value))

    def reciprocal(self, out, in_):
        return self._op([out], [in_], lambda: self._e.reciprocal(out, in_))

    def activation(self, out, in_, func, bias=None, scale=None):
        kw = {}
        if bias is not None:
            kw["bias"] = bias
        if scale is not None:
            kw["scale"] = scale
        return self._op([out], [in_, bias, scale],
                        lambda: self._e.activation(out, in_, func, **kw))

    def affine_select(self, out, in_, compare_op, fill, base, pattern,
                      channel_multiplier):
        return self._op([out], [in_],
                        lambda: self._e.affine_select(
                            out=out, in_=in_, compare_op=compare_op, fill=fill,
                            base=base, pattern=pattern,
                            channel_multiplier=channel_multiplier))

    def tensor_copy(self, out, in_):
        return self._op([out], [in_], lambda: self._e.tensor_copy(out, in_))

    def dma_start(self, out, in_, **kw):
        self._pre([], [in_])
        return self._e.dma_start(out=out, in_=in_, **kw)


def build(stop_after="full", debug=False):
    nc = bass.Bass("TRN2", target_bir_lowering=False)
    dbg = "ExternalOutput" if debug else "Internal"

    x_fm = nc.dram_tensor("x_fm", [DC, 128, TOK], F32, kind="ExternalInput").ap()
    wg_t = nc.dram_tensor("wg_t", [DC, 128, D], F32, kind="ExternalInput").ap()
    wv_t = nc.dram_tensor("wv_t", [DC, 128, D], F32, kind="ExternalInput").ap()
    wd_t = nc.dram_tensor("wd_t", [DC, 128, D], F32, kind="ExternalInput").ap()
    bgh = nc.dram_tensor("bgh", [128, DC], F32, kind="ExternalInput").ap()
    bvh = nc.dram_tensor("bvh", [128, DC], F32, kind="ExternalInput").ap()
    bdh = nc.dram_tensor("bdh", [128, DC], F32, kind="ExternalInput").ap()
    wu_n = nc.dram_tensor("wu_n", [ICH, 128, D], F32, kind="ExternalInput").ap()
    wq_n = nc.dram_tensor("wq_n", [ICH, 128, D], F32, kind="ExternalInput").ap()
    wdn_n = nc.dram_tensor("wdn_n", [DC, 128, I], F32, kind="ExternalInput").ap()

    out_fm = nc.dram_tensor("out_fm", [DC, 128, TOK], F32, kind="ExternalOutput").ap()

    sp_x1 = nc.dram_tensor("sp_x1", [DC, 128, TOK], F32, kind=dbg).ap()
    sp_xq = nc.dram_tensor("sp_xq", [DC, 128, TOK], BF16, kind=dbg).ap()
    sp_up = nc.dram_tensor("sp_up", [ICH, 128, TOK], F32, kind=dbg).ap()
    sp_gt = nc.dram_tensor("sp_gt", [ICH, 128, TOK], F32, kind=dbg).ap()
    sp_p1 = nc.dram_tensor("sp_p1", [ICH, 128, TOK], F32, kind=dbg).ap()
    sp_p2 = nc.dram_tensor("sp_p2", [ICH, 128, TOK], F32, kind=dbg).ap()
    sp_wu = nc.dram_tensor("sp_wu", [ICH, 128, D], BF16, kind=dbg).ap()
    sp_wq = nc.dram_tensor("sp_wq", [ICH, 128, D], BF16, kind=dbg).ap()
    sp_wd = nc.dram_tensor("sp_wd", [DC, 128, I], BF16, kind=dbg).ap()
    ccb = []
    for bi in range(4):
        ci = nc.dram_tensor(f"cc_in{bi}", [128, 2], F32).ap()
        co = nc.dram_tensor(f"cc_out{bi}", [128, 2], F32).ap()
        ccb.append((ci, co))
    gdbg = nc.dram_tensor("gdbg", [128, 16], F32, kind=dbg).ap()

    fences = {k: {"w": [], "r": []} for k in ENGS}
    es = contextlib.ExitStack()
    with es:
        sbuf = lambda name, shape, dt=F32: es.enter_context(nc.sbuf_tensor(name, shape, dt))
        sem = lambda name: es.enter_context(nc.semaphore(name))

        cst = sbuf("cst", [128, 16])
        ident = sbuf("ident", [128, 128])
        ones_bf = sbuf("ones_bf", [128, 1], BF16)
        ones_r = sbuf("ones_r", [1, 128])
        ones_c = sbuf("ones_c", [128, 1])
        selm = sbuf("selm", [2, 256])
        biases = sbuf("biases", [128, 3 * DC])
        scl = sbuf("scl", [128, 16])
        gmaxs = sbuf("gmaxs", [128, 8])
        mx2 = sbuf("mx2", [128, 2])
        gsc = sbuf("gsc", [2, 1])
        stx = sbuf("stx", [1, 1024])
        abv = sbuf("abv", [1, 1024])
        carry = sbuf("carry", [128, DC])

        pb = [es.enter_context(nc.psum_tensor(f"pb{i}", [128, 512], F32))
              for i in range(8)]

        A0 = 40928
        LIM = 229344

        def at(name, shape, dt, off):
            end = off + shape[1] * mybir.dt.size(dt)
            assert end <= LIM, (name, end)
            return nc.alloc_sbuf_tensor_at(name, shape, dt, offset=off)

        wgm = at("wgm", [128, DC * D], BF16, A0)
        wvm = at("wvm", [128, DC * D], BF16, A0 + 16384)
        wdm = at("wdm", [128, DC * D], BF16, A0 + 32768)
        xf = [at(f"xf{i}", [128, DC * TD], F32, A0 + 49152 + 16384 * i) for i in range(2)]
        xbm = [at(f"xbm{i}", [128, DC * TD], BF16, A0 + 81920 + 8192 * i) for i in range(2)]
        wk_xg = [at(f"wkxg{i}", [128, TD], F32, A0 + 98304 + 2048 * i) for i in range(2)]
        wk_tv = [at(f"wktv{i}", [128, TD], F32, A0 + 102400 + 2048 * i) for i in range(2)]
        wk_sa = [at(f"wksa{i}", [128, TD], F32, A0 + 106496 + 2048 * i) for i in range(2)]
        wk_xs = at("wkxs", [128, TD], F32, A0 + 110592)
        wk_av = at("wkav", [128, TD], F32, A0 + 112640)
        hbuf = at("hbuf", [128, 2 * TD], F32, A0 + 114688)
        rbuf = at("rbuf", [128, DC * TD], F32, A0 + 118784)
        sqt8 = at("sqt8", [128, DC * TD], F32, A0 + 135168)
        rbuf2 = at("rbuf2", [128, DC * TD], F32, A0 + 151552)
        rbufs = [rbuf, rbuf2]
        mxc = at("mxc", [128, 8], F32, A0 + 167936)
        DQ0 = A0 + 168000
        dq_nat = [at(f"dqn{i}", [128, D], F32, DQ0 + 4096 * i) for i in range(3)]
        dq_wb = [at(f"dqwb{i}", [128, D], BF16, DQ0 + 12288 + 2048 * i) for i in range(2)]
        dq_sc = at("dqsc", [128, 128], F32, DQ0 + 16384)

        wu_sb = at("wu_sb", [128, DC * I], BF16, A0)
        wq_sb = at("wq_sb", [128, DC * I], BF16, A0 + 49152)
        xqa = [at(f"xqa{i}", [128, DC * TD], BF16, A0 + 98304 + 8192 * i) for i in range(2)]
        gbuf = [at(f"gbuf{i}", [128, 2048], F32, A0 + 114688 + 8192 * i) for i in range(2)]
        astat = at("astat", [128, 96], F32, A0 + 131072)

        ld1 = [at(f"ld1_{i}", [128, 2048], F32, A0 + 8192 * i) for i in range(2)]
        ld2 = [at(f"ld2_{i}", [128, 2048], F32, A0 + 16384 + 8192 * i) for i in range(2)]
        tb1 = [at(f"tb1_{i}", [128, 2048], F32, A0 + 32768 + 8192 * i) for i in range(2)]
        tb2 = [at(f"tb2_{i}", [128, 2048], F32, A0 + 49152 + 8192 * i) for i in range(2)]
        tb3 = [at(f"tb3_{i}", [128, 2048], F32, A0 + 65536 + 8192 * i) for i in range(2)]
        tb4 = [at(f"tb4_{i}", [128, 2048], F32, A0 + 81920 + 8192 * i) for i in range(2)]
        tb5 = [at(f"tb5_{i}", [128, 2048], F32, A0 + 98304 + 8192 * i) for i in range(2)]
        bstat = at("bstat", [128, 48], F32, A0 + 114688)

        wd_sb = at("wd_sb", [128, ICH * D], BF16, A0)
        otile = [at(f"otile{i}", [128, ICH * TD], BF16, A0 + 49152 + 24576 * i) for i in range(2)]
        p2ld = [at(f"p2ld{i}", [128, 2048], F32, A0 + 98304 + 8192 * i) for i in range(2)]
        t5b = at("t5b", [128, 2048], F32, A0 + 114688)
        x1d = at("x1d", [128, DC * TD], F32, A0 + 122880)
        r2b = at("r2b", [128, DC * TD], F32, A0 + 139264)
        r2b2 = at("r2b2", [128, DC * TD], F32, A0 + 155648)
        r2bs = [r2b, r2b2]
        sqt28 = at("sqt28", [128, DC * TD], F32, A0 + 172032)

        S = {n: sem(n) for n in (
            "sWM sWC sMM sEP sRS sCB sSTe sST sSE sVC sSQ sAB sBC sXL "
            "sSPX sPB sXQV sANL sAPC "
            "sWL2 sWQL sWDL sAMM sAEV sARD sBS sBR sBV sCS sCV "
            "sDS2 sDV2 sDX1 sDMM sDEV sDCB sDST sDSTe sDSE2 sDVC sDSQ "
            "sDAB sDBC sDXL sOUT sSU".split())}
        for pn in ("sLX sLB sXQL sXQA sAWW sXQS sASP sBL sBSP "
                   "sCL sCSP sDL".split()):
            S[pn] = [sem(pn + "0"), sem(pn + "1")]
        for bi in range(4):
            S[f"bPRE{bi}"] = sem(f"bPRE{bi}")
            S[f"bDMA{bi}"] = sem(f"bDMA{bi}")
            S[f"bCC{bi}"] = sem(f"bCC{bi}")

        C_ap = cst[:, 1:2]
        negC_ap = cst[:, 2:3]
        EPS_ap = cst[0:1, 5:6]

        apply_steps = []
        for c in range(ICH):
            apply_steps.append((wu_n[c][:, :], sp_wu[c][:, :]))
        for c in range(ICH):
            apply_steps.append((wq_n[c][:, :], sp_wq[c][:, :]))
        for hc in range(DC):
            for j in range(3):
                apply_steps.append((wdn_n[hc][:, j * D:(j + 1) * D],
                                    sp_wd[hc][:, j * D:(j + 1) * D]))
        assert len(apply_steps) == NAPPLY
        APT = 6

        def applies_for_tile(t):
            return range(APT * t, APT * (t + 1))

        def x_src(t):
            return x_fm.rearrange("k p n -> p k n")[:, :, t * TD:(t + 1) * TD]

        def pbar(e, k):
            e.drain()
            if hasattr(e._e, "engine_nop"):
                e._e.engine_nop().then_inc(S["sPB"], 1)
            else:
                e._e.nop().then_inc(S["sPB"], 1)
            e._e.wait_ge(S["sPB"], 5 * k)
            e._w.clear()
            e._r.clear()


        def ph_setup(eng, e):
            if eng == "gpsimd":
                e.memset(ident[:, :], 0.0)
                e.affine_select(out=ident[:, :], in_=ident[:, :],
                                compare_op=ALU.not_equal, fill=1.0, base=0,
                                pattern=[[-1, 128]], channel_multiplier=1)
                e.memset(ones_bf[:, :], 1.0)
                e.memset(ones_c[:, :], 1.0)
                e.memset(ones_r[:, :], 1.0)
                e.memset(selm[0:2, :], 1.0)
                e.affine_select(out=selm[0:2, :], in_=selm[0:2, :],
                                compare_op=ALU.is_ge, fill=0.0, base=0,
                                pattern=[[1, 256]], channel_multiplier=-128)
                e.affine_select(out=selm[0:2, :], in_=selm[0:2, :],
                                compare_op=ALU.is_ge, fill=0.0, base=127,
                                pattern=[[-1, 256]], channel_multiplier=128)
                e._e.engine_nop().then_inc(S["sSU"], 1)
                for dst, srcw in ((wgm, wg_t), (wvm, wv_t), (wdm, wd_t)):
                    e.dma_start(out=dst.ap().rearrange("p (k n) -> p k n", k=DC),
                                in_=srcw.rearrange("k p n -> p k n")).then_inc(S["sWC"], 16)
            elif eng == "vector":
                e.memset(cst[:, 3:4], 1.0)
                e.memset(cst[:, 5:6], 1e-5)
                e.tensor_scalar(cst[:, 4:5], cst[:, 3:4], 2.0 ** -30, 1.0,
                                ALU.add, ALU.subtract)
                e.tensor_scalar(cst[:, 0:1], cst[:, 4:5], 2.0 ** 30, 1.0,
                                ALU.mult, ALU.mult)
                e.tensor_scalar(cst[:, 1:2], cst[:, 0:1], C_DELTA, C_LO,
                                ALU.mult, ALU.add)
                e.tensor_scalar(cst[:, 2:3], cst[:, 1:2], -1.0, 0.0, ALU.mult, ALU.add)
            elif eng == "sync":
                e.dma_start(out=biases[:, 0:DC], in_=bgh[:, :]).then_inc(S["sWM"], 16)
                e.dma_start(out=biases[:, DC:2 * DC], in_=bvh[:, :]).then_inc(S["sWM"], 16)
                e.dma_start(out=biases[:, 2 * DC:3 * DC], in_=bdh[:, :]).then_inc(S["sWM"], 16)

        def ph_gapply_store(e, s):
            e.wait_ge(S["sAPC"], s + 1)
            _, pdst = apply_steps[s]
            e.dma_start(out=pdst,
                        in_=dq_wb[s % 2][:, :]).then_inc(S["sAWW"][s % 2], 16)

        def ph_gapply(e, s, do_store=True):
            src, dst = apply_steps[s]
            if s >= 1 and do_store:
                ph_gapply_store(e, s - 1)
            elif s >= 3:
                e.wait_ge(S["sAPC"], s - 2)
            e.dma_start(out=dq_nat[s % 3][:, :], in_=src).then_inc(S["sANL"], 16)

        def ph_vapply(e, s):
            nb = dq_nat[s % 3]
            wb = dq_wb[s % 2]
            c0 = (s % 2) * 64
            sa = dq_sc[:, c0:c0 + 32]
            ra = dq_sc[:, c0 + 32:c0 + 64]
            base = dq_sc[:, :]
            sa_b = bass.AP(base.tensor, base.offset + c0, [[128, 128], [1, 32], [0, 32]])
            ra_b = bass.AP(base.tensor, base.offset + c0 + 32,
                           [[128, 128], [1, 32], [0, 32]])
            e.wait_ge(S["sANL"], 16 * (s + 1))
            if s >= 2:
                e.wait_ge(S["sAWW"][s % 2], 16 * ((s - 2) // 2 + 1))
            nb3 = nb.ap().rearrange("p (b x) -> p b x", x=32)
            e.reduce_max(sa, nb3, axis=AX.X, apply_absolute_value=True)
            e.tensor_scalar(sa, sa, 1.0 / 7.0, 1e-12, ALU.mult, ALU.max)
            e.reciprocal(ra, sa)
            e.tensor_tensor(nb3, nb3, ra_b, ALU.mult)
            e.tensor_scalar(nb[:, :], nb[:, :], C_ap, C_ap, ALU.add, ALU.subtract)
            e.tensor_tensor(wb.ap().rearrange("p (b x) -> p b x", x=32), nb3,
                            sa_b, ALU.mult).then_inc(S["sAPC"], 1)
        def ph_mixer_tile(eng, e, t):
            if eng == "sync":
                if t == 0:
                    for tt in (0, 1):
                        e.dma_start(out=xf[tt].ap().rearrange("p (k n) -> p k n", k=DC),
                                    in_=x_src(tt)).then_inc(S["sLX"][tt % 2], 16)
                if t >= 1:
                    e.wait_ge(S["sXL"], t)
                    e.dma_start(
                        out=sp_x1.rearrange("k p n -> p k n")[:, :,
                                                              (t - 1) * TD:t * TD],
                        in_=xf[(t - 1) % 2].ap().rearrange("p (k n) -> p k n", k=DC)
                    ).then_inc(S["sSPX"], 16)
                    if t + 1 < NT:
                        e.wait_ge(S["sSPX"], 16 * t)
                        e.dma_start(out=xf[(t + 1) % 2].ap().rearrange("p (k n) -> p k n", k=DC),
                                    in_=x_src(t + 1)).then_inc(S["sLX"][(t + 1) % 2], 16)
            elif eng == "gpsimd":
                if t == 0:
                    for tt in (0, 1):
                        e.dma_start(out=xbm[tt].ap().rearrange("p (k n) -> p k n", k=DC),
                                    in_=x_src(tt)).then_inc(S["sLB"][tt % 2], 16)
                for s in applies_for_tile(t):
                    ph_gapply(e, s)
                if t + 2 < NT:
                    e.wait_ge(S["sMM"], 8 * (t + 1))
                    e.dma_start(out=xbm[t % 2].ap().rearrange("p (k n) -> p k n", k=DC),
                                in_=x_src(t + 2)).then_inc(S["sLB"][t % 2], 16)
                if t == NT - 1:
                    ph_gapply_store(e, APT * NT - 1)
            elif eng == "tensor":
                if t == 0:
                    e.wait_ge(S["sWM"], 48)
                    e.wait_ge(S["sWC"], 48)
                e.wait_ge(S["sLB"][t % 2], 16 * (t // 2 + 1))
                e.wait_ge(S["sLX"][t % 2], 16 * (t // 2 + 1))

                def stats(ts):
                    if ts >= 2:
                        e.wait_ge(S["sVC"], ts)
                    for ei in range(DC):
                        e.wait_ge(S["sCB"], 8 * ts + ei + 1)
                        if ei < DC - 1:
                            e.matmul(pb[3][0:1, :], ones_c[:, :],
                                     rbufs[ts % 2][:, ei * TD:(ei + 1) * TD],
                                     start=(ei == 0), stop=False)
                            e.matmul(pb[4][0:1, :], ones_c[:, :], sqt8[:, ei * TD:(ei + 1) * TD],
                                     start=(ei == 0), stop=False).then_inc(S["sSTe"], 1)
                        else:
                            e.matmul(pb[4][0:1, :], ones_c[:, :], sqt8[:, ei * TD:(ei + 1) * TD],
                                     start=False, stop=True).then_inc(S["sSTe"], 1)
                            e.matmul(pb[3][0:1, :], ones_c[:, :],
                                     rbufs[ts % 2][:, ei * TD:(ei + 1) * TD],
                                     start=False, stop=True).then_inc(S["sST"], 1)

                def bcast(ts):
                    e.wait_ge(S["sAB"], ts + 1)
                    if ts >= 1:
                        e.wait_ge(S["sXL"], ts)
                    e.matmul(pb[4][:, :], ones_r[:, :], abv[0:1, 512:1024],
                             start=True, stop=True)
                    e.matmul(pb[5][:, :], ones_r[:, :], stx[0:1, 0:512],
                             start=True, stop=True).then_inc(S["sBC"], 1)

                if t >= 1:
                    stats(t - 1)
                for ei in range(DC):
                    if 8 * t + ei >= 1:
                        e.wait_ge(S["sEP"], 8 * t + ei)
                    last = None
                    for wi, wsb in enumerate((wgm, wvm, wdm)):
                        for k in range(DC):
                            last = e.matmul(
                                pb[wi][:, :],
                                wsb[:, k * D + 128 * ei: k * D + 128 * ei + 128],
                                xbm[t % 2][:, k * TD:(k + 1) * TD],
                                start=(k == 0), stop=(k == DC - 1))
                    last.then_inc(S["sMM"], 1)
                    if ei == 3 and t >= 1:
                        bcast(t - 1)
                if t == NT - 1:
                    stats(t)
                    bcast(t)
            elif eng == "scalar":
                def sq_m(k):
                    e.wait_ge(S["sRS"], 8 * t + k + 1)
                    if t >= 1:
                        e.wait_ge(S["sSTe"], 8 * (t - 1) + k + 1)
                    e.activation(sqt8[:, k * TD:(k + 1) * TD],
                                 rbufs[t % 2][:, k * TD:(k + 1) * TD],
                                 AF.Square).then_inc(S["sCB"], 1)

                if t >= 1:
                    e.wait_ge(S["sVC"], t)
                    e.activation(stx[0:1, 0:512], stx[0:1, 512:1024], AF.Sqrt,
                                 bias=EPS_ap).then_inc(S["sSQ"], 1)
                for ei in range(DC):
                    e.wait_ge(S["sMM"], 8 * t + ei + 1)
                    if ei >= 2:
                        e.wait_ge(S["sRS"], max(8 * t + 1, 8 * t + ei - 2))
                    elif t >= 1:
                        e.wait_ge(S["sRS"], 8 * t - 2 + ei)
                    e.activation(wk_xg[ei % 2][:, :], pb[0][:, :], AF.Sigmoid,
                                 bias=biases[:, ei:ei + 1])
                    e.activation(wk_tv[ei % 2][:, :], pb[1][:, :], AF.Tanh,
                                 bias=biases[:, DC + ei:DC + ei + 1])
                    e.activation(wk_sa[ei % 2][:, :], pb[2][:, :], AF.Sigmoid,
                                 bias=biases[:, 2 * DC + ei:2 * DC + ei + 1]
                                 ).then_inc(S["sEP"], 1)
                    if ei >= 1:
                        sq_m(ei - 1)
                sq_m(DC - 1)
                if t == NT - 1:
                    e.wait_ge(S["sVC"], t + 1)
                    e.activation(stx[0:1, 0:512], stx[0:1, 512:1024], AF.Sqrt,
                                 bias=EPS_ap).then_inc(S["sSQ"], 1)
            elif eng == "vector":
                def ln_post(ts):
                    e.wait_ge(S["sST"], ts + 1)
                    e.tensor_copy(stx[0:1, 0:512], pb[3][0:1, :])
                    e.tensor_copy(stx[0:1, 512:1024], pb[4][0:1, :])
                    e.tensor_scalar(abv[0:1, 0:512], stx[0:1, 0:512],
                                    1.0 / 1024.0, 0.0, ALU.mult, ALU.add)
                    e.tensor_scalar(abv[0:1, 512:1024], stx[0:1, 512:1024],
                                    1.0 / 1024.0, 0.0, ALU.mult, ALU.add)
                    e.tensor_tensor(stx[0:1, 0:512], abv[0:1, 0:512],
                                    abv[0:1, 0:512], ALU.mult)
                    e.tensor_tensor(stx[0:1, 512:1024], abv[0:1, 512:1024],
                                    stx[0:1, 0:512], ALU.subtract).then_inc(S["sVC"], 1)
                    e._w.clear()
                    e._r.clear()
                    e.drain()

                def ab_post(ts):
                    e.wait_ge(S["sSQ"], ts + 1)
                    e.reciprocal(abv[0:1, 512:1024], stx[0:1, 0:512])
                    e.scalar_tensor_tensor(stx[0:1, 0:512], abv[0:1, 0:512], -1.0,
                                           abv[0:1, 512:1024], ALU.mult,
                                           ALU.mult).then_inc(S["sAB"], 1)

                def affine(ts):
                    e.wait_ge(S["sBC"], ts + 1)
                    if ts >= 1:
                        e.wait_ge(S["sSPX"], 16 * ts)
                    for ei in range(DC):
                        e.tensor_tensor(xf[ts % 2][:, ei * TD:(ei + 1) * TD],
                                        rbufs[ts % 2][:, ei * TD:(ei + 1) * TD],
                                        pb[4][:, :], ALU.mult)
                    for ei in range(DC):
                        e.tensor_tensor(xf[ts % 2][:, ei * TD:(ei + 1) * TD],
                                        xf[ts % 2][:, ei * TD:(ei + 1) * TD],
                                        pb[5][:, :], ALU.add)
                    e.reduce_max(mxc[:, ts:ts + 1], xf[ts % 2][:, :], axis=AX.X,
                                 apply_absolute_value=True).then_inc(S["sXL"], 1)

                if t >= 1:
                    ln_post(t - 1)
                rb = rbufs[t % 2]
                for ei in range(DC):
                    e.wait_ge(S["sEP"], 8 * t + ei + 1)
                    sl = (ei % 2) * TD
                    e.tensor_tensor(wk_xs[:, :], wk_xg[ei % 2][:, :],
                                    wk_tv[ei % 2][:, :], ALU.mult)
                    e.tensor_scalar(wk_av[:, :], wk_sa[ei % 2][:, :],
                                    0.998, 0.001, ALU.mult, ALU.add)
                    init = 0.0 if t % 4 == 0 else carry[:, ei:ei + 1]
                    e.tensor_tensor_scan(hbuf[:, sl:sl + TD], wk_av[:, :], wk_xs[:, :],
                                         init, ALU.mult, ALU.add)
                    if ei >= 1:
                        pl = ((ei - 1) % 2) * TD
                        e.tensor_tensor(rb[:, (ei - 1) * TD:ei * TD],
                                        xf[t % 2][:, (ei - 1) * TD:ei * TD],
                                        hbuf[:, pl:pl + TD], ALU.add).then_inc(S["sRS"], 1)
                    e.tensor_scalar(carry[:, ei:ei + 1], hbuf[:, sl + TD - 1:sl + TD],
                                    1.0, 0.0, ALU.mult, ALU.add)
                    if ei == 1 and t >= 1:
                        ab_post(t - 1)
                    if ei == 3 and t >= 1:
                        affine(t - 1)
                    if ei < APT:
                        ph_vapply(e, APT * t + ei)
                pl = ((DC - 1) % 2) * TD
                e.tensor_tensor(rb[:, (DC - 1) * TD:DC * TD],
                                xf[t % 2][:, (DC - 1) * TD:DC * TD],
                                hbuf[:, pl:pl + TD], ALU.add).then_inc(S["sRS"], 1)
                if t == NT - 1:
                    ln_post(t)
                    ab_post(t)
                    affine(t)

        def ph_mixer_tail(eng, e):
            if eng == "sync":
                e.wait_ge(S["sXL"], NT)
                e.dma_start(
                    out=sp_x1.rearrange("k p n -> p k n")[:, :, (NT - 1) * TD:NT * TD],
                    in_=xf[(NT - 1) % 2].ap().rearrange("p (k n) -> p k n", k=DC)
                ).then_inc(S["sSPX"], 16)
        def ph_gmax_barrier(eng, e, bi, k):
            cin, cout = ccb[bi]
            bP, bD, bC = S[f"bPRE{bi}"], S[f"bDMA{bi}"], S[f"bCC{bi}"]
            if eng == "sync":
                e.wait_ge(bP, 1)
                e.dma_start(out=cin[:, :], in_=mx2[:, :]).then_inc(bD, 16)
                e.wait_ge(bC, 1)
                e.dma_start(out=mx2[:, :], in_=cout[:, :]).then_inc(bD, 16)
            elif eng == "gpsimd":
                e.wait_ge(bD, 16)
                e.collective_compute(
                    "AllReduce", ALU.max, replica_groups=[list(range(NCORES))],
                    ins=[cin[:, :]], outs=[cout[:, :]]).then_inc(bC, 1)
            elif eng == "tensor":
                e.wait_ge(bD, 32)
                e.transpose(pb[7][0:2, 0:128], mx2[:, :], ident[:, :]).then_inc(bP, 1)
                e.wait_ge(bP, 3)
                m = None
                for j in range(k):
                    m = e.matmul(pb[6][:, j:j + 1], selm[0:2, 128 * j:128 * j + 128],
                                 gsc[0:2, 0:1], start=True, stop=True)
                m.then_inc(bP, 1)
            elif eng == "scalar":
                e.wait_ge(bP, 4)
                e.activation(gmaxs[:, 2 * bi:2 * bi + k], pb[6][:, 0:k],
                             AF.Copy).then_inc(bP, 1)
            elif eng == "vector":
                e.wait_ge(bP, 2)
                e.reduce_max(gsc[0:2, 0:1], pb[7][0:2, 0:128], axis=AX.X).then_inc(bP, 1)
                e.wait_ge(bP, 5)

        def ph_bar1_fill(eng, e):
            if eng == "vector":
                e.wait_ge(S["sXL"], NT)
                e.reduce_max(mx2[:, 0:1], mxc[:, 0:NT], axis=AX.X)
                e.reduce_max(mx2[:, 1:2], mxc[:, 0:NT], axis=AX.X).then_inc(S["bPRE0"], 1)

        def ph_bar1_post(eng, e):
            if eng == "vector":
                e.tensor_scalar(scl[:, 0:1], gmaxs[:, 0:1], 1.0 / QMAX, 1e-12,
                                ALU.mult, ALU.max)
                e.reciprocal(scl[:, 1:2], scl[:, 0:1]).then_inc(S["bPRE0"], 1)

        def ph_xq(eng, e):
            for t in range(NT):
                if eng == "sync":
                    if t == 0:
                        e.wait_ge(S["sSPX"], 16 * NT)
                    if t >= 2:
                        e.wait_ge(S["sXQV"], t - 1)
                    e.dma_start(
                        out=xf[t % 2].ap().rearrange("p (k n) -> p k n", k=DC),
                        in_=sp_x1.rearrange("k p n -> p k n")[:, :, t * TD:(t + 1) * TD]
                    ).then_inc(S["sXQL"][t % 2], 16)
                elif eng == "vector":
                    e.wait_ge(S["sXQL"][t % 2], 16 * (t // 2 + 1))
                    if t == 0:
                        e.wait_ge(S["bPRE0"], 6)
                    if t >= 2:
                        e.wait_ge(S["sXQS"][t % 2], 16 * ((t - 2) // 2 + 1))
                    if t in (2, 3):
                        e.wait_ge(S["sXQA"][t % 2], 16)
                    e.tensor_scalar(sqt8[:, :], xf[t % 2][:, :], scl[:, 1:2], C_ap,
                                    ALU.mult, ALU.add)
                    e.tensor_scalar(xbm[t % 2][:, :], sqt8[:, :], C_ap, scl[:, 0:1],
                                    ALU.subtract, ALU.mult).then_inc(S["sXQV"], 1)
                if eng == "sync":
                    e.wait_ge(S["sXQV"], t + 1)
                    e.dma_start(
                        out=sp_xq.rearrange("k p n -> p k n")[:, :, t * TD:(t + 1) * TD],
                        in_=xbm[t % 2].ap().rearrange("p (k n) -> p k n", k=DC)
                    ).then_inc(S["sXQS"][t % 2], 16)
                    if t <= 1:
                        e.dma_start(out=xqa[t][:, :],
                                    in_=xbm[t][:, :]).then_inc(S["sXQA"][t % 2], 16)

        aorder = [(0, 0), (1, 0), (0, 1), (1, 1)]
        for t in range(2, NT):
            aorder.append((t, 0))
            aorder.append((t, 1))

        def ph_A_wu_loads(eng, e):
            if eng == "sync":
                e.wait_ge(S["sMM"], 8 * NT)
                e.wait_ge(S["sAWW"][0], 16 * 12)
                e.wait_ge(S["sAWW"][1], 16 * 12)
                flat = sp_wu.rearrange("c p n -> (c p) n")
                for k in range(DC):
                    e.dma_start(out=wu_sb[:, k * I:(k + 1) * I],
                                in_=flat[:, 128 * k:128 * (k + 1)],
                                transpose=True).then_inc(S["sWL2"], 16)

        def ph_A(eng, e):
            if eng == "sync":
                e.wait_ge(S["sXQS"][0], 16 * (NT // 2))
                e.wait_ge(S["sXQS"][1], 16 * (NT // 2))
                e.wait_ge(S["sAWW"][0], 16 * 24)
                e.wait_ge(S["sAWW"][1], 16 * 24)
                flat = sp_wq.rearrange("c p n -> (c p) n")
                for k in range(DC):
                    e.dma_start(out=wq_sb[:, k * I:(k + 1) * I],
                                in_=flat[:, 128 * k:128 * (k + 1)],
                                transpose=True).then_inc(S["sWQL"], 16)
            for oi, (t, w) in enumerate(aorder):
                _sc = nc.named_scope(f"pa{oi}")
                _sc.__enter__()
                wsb = wu_sb if w == 0 else wq_sb
                spw = sp_up if w == 0 else sp_gt
                if eng == "tensor":
                    if oi == 0:
                        e.wait_ge(S["sWL2"], 16 * 8)
                        e.wait_ge(S["sEP"], 8 * NT)
                        e.wait_ge(S["sVC"], NT)
                    if oi == 2:
                        e.wait_ge(S["sWQL"], 16 * 8)
                    if w == 0:
                        e.wait_ge(S["sXQA"][t % 2], 16 * (t // 2 + 1))
                    for i in range(ICH):
                        n = oi * ICH + i
                        if n >= 4:
                            e.wait_ge(S["sAEV"], n - 3)
                        m = None
                        for k in range(DC):
                            m = e.matmul(
                                pb[n % 4][:, :],
                                wsb[:, k * I + 128 * i: k * I + 128 * i + 128],
                                xqa[t % 2][:, k * TD:(k + 1) * TD],
                                start=(k == 0), stop=(k == DC - 1))
                        m.then_inc(S["sAMM"], 1)
                elif eng == "scalar":
                    for i in range(ICH):
                        n = oi * ICH + i
                        g = n // 4
                        if n == 0:
                            e.wait_ge(S["sST"], NT)
                            e.wait_ge(S["sXL"], NT)
                        if g >= 2:
                            e.wait_ge(S["sASP"][g % 2], 16 * ((g - 2) // 2 + 1))
                            e.wait_ge(S["sARD"], g - 1)
                        e.wait_ge(S["sAMM"], n + 1)
                        e.activation(gbuf[g % 2][:, 512 * (i % 4):512 * (i % 4) + 512],
                                     pb[n % 4][:, :], AF.Copy).then_inc(S["sAEV"], 1)
                elif eng == "vector":
                    for i in range(3, ICH, 4):
                        n = oi * ICH + i
                        g = n // 4
                        col = w * 48 + t * 6 + i // 4
                        e.wait_ge(S["sAEV"], n + 1)
                        e.reduce_max(astat[:, col:col + 1], gbuf[g % 2][:, :],
                                     axis=AX.X, apply_absolute_value=True
                                     ).then_inc(S["sARD"], 1)
                elif eng == "sync":
                    for i in range(3, ICH, 4):
                        n = oi * ICH + i
                        g = n // 4
                        e.wait_ge(S["sAEV"], n + 1)
                        e.dma_start(
                            out=spw.rearrange("k p n -> p k n")[:, i - 3:i + 1,
                                                                t * TD:(t + 1) * TD],
                            in_=gbuf[g % 2].ap().rearrange("p (c n) -> p c n", c=4)
                        ).then_inc(S["sASP"][g % 2], 16)
                    if t + 2 < NT and (w == 1 if t >= 2 else oi == 3):
                        for tl in ((t + 2,) if t >= 2 else (2, 3)):
                            e.wait_ge(S["sAMM"], (aorder.index((tl - 2, 1)) + 1) * ICH)
                            e.dma_start(
                                out=xqa[tl % 2].ap().rearrange("p (k n) -> p k n", k=DC),
                                in_=sp_xq.rearrange("k p n -> p k n")[:, :,
                                                                      tl * TD:(tl + 1) * TD]
                            ).then_inc(S["sXQA"][tl % 2], 16)
                _sc.__exit__(None, None, None)

        def ph_bar2_fill(eng, e):
            if eng == "vector":
                e.wait_ge(S["sARD"], NA // 4)
                e.reduce_max(mx2[:, 0:1], astat[:, 0:48], axis=AX.X)
                e.reduce_max(mx2[:, 1:2], astat[:, 48:96], axis=AX.X).then_inc(S["bPRE1"], 1)

        def ph_bar2_post(eng, e):
            if eng == "vector":
                e.tensor_scalar(scl[:, 4:5], gmaxs[:, 2:3], 1.0 / QMAX, 1e-12,
                                ALU.mult, ALU.max)
                e.tensor_scalar(scl[:, 2:3], gmaxs[:, 3:4], 1.0 / QMAX, 1e-12,
                                ALU.mult, ALU.max)
                e.reciprocal(scl[:, 5:6], scl[:, 4:5])
                e.reciprocal(scl[:, 3:4], scl[:, 2:3]).then_inc(S["bPRE1"], 1)

        def ph_B(eng, e):
            def vtail(n):
                e.wait_ge(S["sBS"], 2 * (n + 1))
                if n >= 2:
                    e.wait_ge(S["sBSP"][n % 2], 16 * ((n - 2) // 2 + 1))
                e.tensor_scalar(tb5[n % 2][:, :], tb4[n % 2][:, :], C_ap, 65535.0,
                                ALU.subtract, ALU.min)
                e.tensor_tensor(tb5[n % 2][:, :], tb2[n % 2][:, :],
                                tb5[n % 2][:, :], ALU.mult)
                e.reduce_max(bstat[:, n:n + 1], tb5[n % 2][:, :], axis=AX.X,
                             apply_absolute_value=True).then_inc(S["sBV"], 3)

            for n in range(NB):
                t, j = divmod(n, ICH // 4)
                if eng == "sync":
                    if n == 0:
                        e.wait_ge(S["sASP"][0], 16 * (NA // 8))
                        e.wait_ge(S["sASP"][1], 16 * (NA // 8))
                    if n >= 2:
                        e.wait_ge(S["sBR"], n - 1)
                    e.dma_start(
                        out=ld1[n % 2].ap().rearrange("p (c x) -> p c x", c=4),
                        in_=sp_gt.rearrange("k p n -> p k n")[:, 4 * j:4 * j + 4,
                                                              t * TD:(t + 1) * TD]
                    ).then_inc(S["sBL"][n % 2], 16)
                elif eng == "vector":
                    e.wait_ge(S["sBL"][n % 2], 16 * (n // 2 + 1))
                    if n == 0:
                        e.wait_ge(S["bPRE1"], 6)
                    if n >= 2:
                        e.wait_ge(S["sBS"], 2 * (n - 1))
                    e.tensor_scalar(tb1[n % 2][:, :], ld1[n % 2][:, :],
                                    scl[:, 3:4], C_ap, ALU.mult, ALU.add)
                    e.tensor_scalar(tb2[n % 2][:, :], tb1[n % 2][:, :], C_ap, 1.0,
                                    ALU.subtract, ALU.mult).then_inc(S["sBR"], 1)
                    if n >= 1:
                        vtail(n - 1)
                    if n == NB - 1:
                        vtail(n)
                elif eng == "scalar":
                    e.wait_ge(S["sBR"], n + 1)
                    if n >= 2:
                        e.wait_ge(S["sBV"], 3 * (n - 1))
                    e.activation(tb3[n % 2][:, :], tb2[n % 2][:, :], AF.Sigmoid,
                                 scale=scl[:, 2:3])
                    e.activation(tb4[n % 2][:, :], tb3[n % 2][:, :], AF.Identity,
                                 bias=C_ap, scale=65536.0).then_inc(S["sBS"], 2)
                elif eng == "gpsimd":
                    e.wait_ge(S["sBV"], 3 * (n + 1))
                    e.dma_start(
                        out=sp_p1.rearrange("k p n -> p k n")[:, 4 * j:4 * j + 4,
                                                              t * TD:(t + 1) * TD],
                        in_=tb5[n % 2].ap().rearrange("p (c x) -> p c x", c=4)
                    ).then_inc(S["sBSP"][n % 2], 16)

        def ph_bar3_fill(eng, e):
            if eng == "vector":
                e.wait_ge(S["sBV"], 3 * NB)
                e.reduce_max(mx2[:, 0:1], bstat[:, 0:NB], axis=AX.X)
                e.reduce_max(mx2[:, 1:2], bstat[:, 0:NB], axis=AX.X).then_inc(S["bPRE2"], 1)

        def ph_bar3_post(eng, e):
            if eng == "vector":
                e.reciprocal(scl[:, 7:8], gmaxs[:, 4:5])
                e.tensor_scalar(scl[:, 6:7], scl[:, 7:8], QMAX, 0.0,
                                ALU.mult, ALU.add).then_inc(S["bPRE2"], 1)

        def ph_C(eng, e):
            if eng == "gpsimd":
                ph_gapply(e, APT * NT, do_store=False)
            for n in range(NB):
                t, j = divmod(n, ICH // 4)
                if eng == "sync":
                    if n == 0:
                        e.wait_ge(S["sBSP"][0], 16 * (NB // 2))
                        e.wait_ge(S["sBSP"][1], 16 * (NB // 2))
                    if n >= 2:
                        e.wait_ge(S["sCS"], 2 * (n - 1))
                    e.dma_start(
                        out=ld1[n % 2].ap().rearrange("p (c x) -> p c x", c=4),
                        in_=sp_p1.rearrange("k p n -> p k n")[:, 4 * j:4 * j + 4,
                                                              t * TD:(t + 1) * TD]
                    ).then_inc(S["sCL"][n % 2], 16)
                    e.dma_start(
                        out=ld2[n % 2].ap().rearrange("p (c x) -> p c x", c=4),
                        in_=sp_up.rearrange("k p n -> p k n")[:, 4 * j:4 * j + 4,
                                                              t * TD:(t + 1) * TD]
                    ).then_inc(S["sCL"][n % 2], 16)
                elif eng == "scalar":
                    e.wait_ge(S["sCL"][n % 2], 32 * (n // 2 + 1))
                    if n == 0:
                        e.wait_ge(S["bPRE2"], 6)
                    if n >= 2:
                        e.wait_ge(S["sCV"], 2 * (n - 1))
                    e.activation(tb1[n % 2][:, :], ld1[n % 2][:, :], AF.Identity,
                                 bias=C_ap, scale=scl[:, 6:7])
                    e.activation(tb2[n % 2][:, :], tb1[n % 2][:, :], AF.Identity,
                                 bias=negC_ap)
                    e.activation(tb3[n % 2][:, :], ld2[n % 2][:, :], AF.Identity,
                                 bias=C_ap, scale=scl[:, 5:6]).then_inc(S["sCS"], 2)
                elif eng == "vector":
                    e.wait_ge(S["sCS"], 2 * (n + 1))
                    if n >= 2:
                        e.wait_ge(S["sCSP"][n % 2], 16 * ((n - 2) // 2 + 1))
                    e.scalar_tensor_tensor(tb4[n % 2][:, :], tb3[n % 2][:, :], negC_ap,
                                           tb2[n % 2][:, :], ALU.add, ALU.mult)
                    e.reduce_max(bstat[:, n:n + 1], tb4[n % 2][:, :], axis=AX.X,
                                 apply_absolute_value=True).then_inc(S["sCV"], 2)
                    if n % 2 == 0:
                        ph_vapply(e, APT * NT + n // 2)
                elif eng == "gpsimd":
                    e.wait_ge(S["sCV"], 2 * (n + 1))
                    e.dma_start(
                        out=sp_p2.rearrange("k p n -> p k n")[:, 4 * j:4 * j + 4,
                                                              t * TD:(t + 1) * TD],
                        in_=tb4[n % 2].ap().rearrange("p (c x) -> p c x", c=4)
                    ).then_inc(S["sCSP"][n % 2], 16)
                    if n % 2 == 1 and n < NB - 2:
                        ph_gapply(e, APT * NT + (n + 1) // 2)
            if eng == "gpsimd":
                ph_gapply_store(e, NAPPLY - 1)

        def ph_bar4_fill(eng, e):
            if eng == "vector":
                e.wait_ge(S["sCV"], 2 * NB)
                e.reduce_max(mx2[:, 0:1], bstat[:, 0:NB], axis=AX.X)
                e.reduce_max(mx2[:, 1:2], bstat[:, 0:NB], axis=AX.X).then_inc(S["bPRE3"], 1)

        def ph_bar4_post(eng, e):
            if eng == "vector":
                e.reciprocal(scl[:, 10:11], gmaxs[:, 6:7])
                e.tensor_scalar(scl[:, 11:12], gmaxs[:, 4:5], 1.0 / (QMAX * 65536.0),
                                0.0, ALU.mult, ALU.add)
                e.tensor_scalar(scl[:, 8:9], scl[:, 10:11], QMAX, 0.0, ALU.mult, ALU.add)
                e.tensor_tensor(scl[:, 12:13], scl[:, 11:12], scl[:, 2:3], ALU.mult)
                e.tensor_tensor(scl[:, 13:14], scl[:, 12:13], scl[:, 4:5], ALU.mult)
                e.tensor_tensor(scl[:, 14:15], scl[:, 13:14], gmaxs[:, 6:7], ALU.mult)
                e.tensor_scalar(scl[:, 9:10], scl[:, 14:15], 1.0 / QMAX, 0.0,
                                ALU.mult, ALU.add).then_inc(S["bPRE3"], 1)

        def ph_D(eng, e):
            def ln_post2(ts):
                e.wait_ge(S["sDST"], ts + 1)
                e.tensor_copy(stx[0:1, 0:512], pb[3][0:1, :])
                e.tensor_copy(stx[0:1, 512:1024], pb[4][0:1, :])
                e.tensor_scalar(abv[0:1, 0:512], stx[0:1, 0:512],
                                1.0 / 1024.0, 0.0, ALU.mult, ALU.add)
                e.tensor_scalar(abv[0:1, 512:1024], stx[0:1, 512:1024],
                                1.0 / 1024.0, 0.0, ALU.mult, ALU.add)
                e.tensor_tensor(stx[0:1, 0:512], abv[0:1, 0:512],
                                abv[0:1, 0:512], ALU.mult)
                e.tensor_tensor(stx[0:1, 512:1024], abv[0:1, 512:1024],
                                stx[0:1, 0:512], ALU.subtract).then_inc(S["sDVC"], 1)
                e.drain()

            def ab_post2(ts):
                e.wait_ge(S["sDSQ"], ts + 1)
                e.reciprocal(abv[0:1, 512:1024], stx[0:1, 0:512])
                e.scalar_tensor_tensor(stx[0:1, 0:512], abv[0:1, 0:512], -1.0,
                                       abv[0:1, 512:1024], ALU.mult,
                                       ALU.mult).then_inc(S["sDAB"], 1)

            def affine2(ts):
                e.wait_ge(S["sDBC"], ts + 1)
                e.wait_ge(S["sOUT"], 16 * ts)
                srcb = r2bs[ts % 2]
                for ei in range(DC):
                    e.tensor_tensor(sqt28[:, ei * TD:(ei + 1) * TD],
                                    srcb[:, ei * TD:(ei + 1) * TD],
                                    pb[4][:, :], ALU.mult)
                for ei in range(DC):
                    e.tensor_tensor(sqt28[:, ei * TD:(ei + 1) * TD],
                                    sqt28[:, ei * TD:(ei + 1) * TD],
                                    pb[5][:, :], ALU.add)
                e._e.engine_nop().then_inc(S["sDXL"], 1)

            def round_d(tt, j):
                n = tt * NJ + j
                e.wait_ge(S["sDL"][n % 2], 16 * (n // 2 + 1))
                if n == 0:
                    e.wait_ge(S["bPRE3"], 6)
                if tt >= 2 and j == 0:
                    e.wait_ge(S["sDMM"], 8 * (tt - 1))
                e.tensor_scalar(t5b[:, :], p2ld[n % 2][:, :], scl[:, 8:9],
                                C_ap, ALU.mult, ALU.add).then_inc(S["sDS2"], 1)
                e.tensor_scalar(otile[tt % 2][:, 2048 * j:2048 * (j + 1)],
                                t5b[:, :], C_ap, 1.0,
                                ALU.subtract, ALU.mult).then_inc(S["sDV2"], 1)

            if eng == "sync":
                e.wait_ge(S["sCSP"][0], 16 * (NB // 2))
                e.wait_ge(S["sCSP"][1], 16 * (NB // 2))
                e.wait_ge(S["sAWW"][0], 16 * (NAPPLY // 2))
                e.wait_ge(S["sAWW"][1], 16 * (NAPPLY // 2))
                flatd = sp_wd.rearrange("c p n -> (c p) n")
                for ci in range(ICH):
                    e.dma_start(out=wd_sb[:, ci * D:(ci + 1) * D],
                                in_=flatd[:, 128 * ci:128 * (ci + 1)],
                                transpose=True).then_inc(S["sWDL"], 16)
            for t in range(NT):
                _sc = nc.named_scope(f"pd{t}")
                _sc.__enter__()
                if eng == "sync":
                    if t == 0:
                        for j in range(NJ):
                            if j >= 2:
                                e.wait_ge(S["sDS2"], j - 1)
                            e.dma_start(
                                out=p2ld[j % 2].ap().rearrange("p (c x) -> p c x", c=4),
                                in_=sp_p2.rearrange("k p n -> p k n")[:, 4 * j:4 * j + 4,
                                                                      0:TD]
                            ).then_inc(S["sDL"][j % 2], 16)
                    if t >= 1:
                        e.wait_ge(S["sDEV"], 8 * t)
                    e.dma_start(
                        out=x1d.ap().rearrange("p (k n) -> p k n", k=DC),
                        in_=sp_x1.rearrange("k p n -> p k n")[:, :, t * TD:(t + 1) * TD]
                    ).then_inc(S["sDX1"], 16)
                    if t + 1 < NT:
                        for j in range(NJ):
                            n = (t + 1) * NJ + j
                            e.wait_ge(S["sDS2"], n - 1)
                            e.dma_start(
                                out=p2ld[n % 2].ap().rearrange("p (c x) -> p c x", c=4),
                                in_=sp_p2.rearrange("k p n -> p k n")[:, 4 * j:4 * j + 4,
                                                                      (t + 1) * TD:(t + 2) * TD]
                            ).then_inc(S["sDL"][n % 2], 16)
                    if t >= 1:
                        e.wait_ge(S["sDXL"], t)
                        e.dma_start(
                            out=out_fm.rearrange("k p n -> p k n")[:, :,
                                                                   (t - 1) * TD:t * TD],
                            in_=sqt28.ap().rearrange("p (k n) -> p k n", k=DC)
                        ).then_inc(S["sOUT"], 16)
                if eng == "tensor":
                    if t == 0:
                        e.wait_ge(S["sWDL"], 16 * ICH)

                    def stats2(ts):
                        if ts >= 2:
                            e.wait_ge(S["sDVC"], ts)
                        for ei in range(DC):
                            e.wait_ge(S["sDCB"], 8 * ts + ei + 1)
                            if ei < DC - 1:
                                e.matmul(pb[3][0:1, :], ones_c[:, :],
                                         r2bs[ts % 2][:, ei * TD:(ei + 1) * TD],
                                         start=(ei == 0), stop=False)
                                e.matmul(pb[4][0:1, :], ones_c[:, :],
                                         sqt28[:, ei * TD:(ei + 1) * TD],
                                         start=(ei == 0), stop=False
                                         ).then_inc(S["sDSTe"], 1)
                            else:
                                e.matmul(pb[4][0:1, :], ones_c[:, :],
                                         sqt28[:, ei * TD:(ei + 1) * TD],
                                         start=False, stop=True).then_inc(S["sDSTe"], 1)
                                e.matmul(pb[3][0:1, :], ones_c[:, :],
                                         r2bs[ts % 2][:, ei * TD:(ei + 1) * TD],
                                         start=False, stop=True).then_inc(S["sDST"], 1)

                    def bcast2(ts):
                        e.wait_ge(S["sDAB"], ts + 1)
                        if ts >= 1:
                            e.wait_ge(S["sDXL"], ts)
                        e.matmul(pb[4][:, :], ones_r[:, :], abv[0:1, 512:1024],
                                 start=True, stop=True)
                        e.matmul(pb[5][:, :], ones_r[:, :], stx[0:1, 0:512],
                                 start=True, stop=True).then_inc(S["sDBC"], 1)

                    if t >= 1:
                        stats2(t - 1)
                    e.wait_ge(S["sDV2"], NJ * (t + 1))
                    for h in range(DC):
                        if 8 * t + h >= 2:
                            e.wait_ge(S["sDEV"], 8 * t + h - 1)
                        m = None
                        for i in range(ICH):
                            m = e.matmul(pb[h % 2][:, :],
                                         wd_sb[:, i * D + 128 * h: i * D + 128 * h + 128],
                                         otile[t % 2][:, i * TD:(i + 1) * TD],
                                         start=(i == 0), stop=(i == ICH - 1))
                        m.then_inc(S["sDMM"], 1)
                        if h == 0 and t >= 1:
                            bcast2(t - 1)
                    if t == NT - 1:
                        stats2(t)
                        bcast2(t)
                elif eng == "vector":
                    if t == 0:
                        for j in range(NJ):
                            round_d(0, j)
                    e.wait_ge(S["sDX1"], 16 * (t + 1))
                    if t >= 2:
                        e.wait_ge(S["sDXL"], t - 1)
                    if t >= 1:
                        ln_post2(t - 1)
                        ab_post2(t - 1)
                    for h in range(DC):
                        e.wait_ge(S["sDMM"], 8 * t + h + 1)
                        e.scalar_tensor_tensor(r2bs[t % 2][:, h * TD:(h + 1) * TD],
                                               pb[h % 2][:, :], scl[:, 9:10],
                                               x1d[:, h * TD:(h + 1) * TD],
                                               ALU.mult, ALU.add).then_inc(S["sDEV"], 1)
                        if h == 0 and t >= 1:
                            affine2(t - 1)
                        if h < NJ and t + 1 < NT:
                            round_d(t + 1, h)
                    if t == NT - 1:
                        ln_post2(t)
                        ab_post2(t)
                        affine2(t)
                elif eng == "scalar":
                    if t >= 1:
                        e.wait_ge(S["sDVC"], t)
                        e.activation(stx[0:1, 0:512], stx[0:1, 512:1024], AF.Sqrt,
                                     bias=EPS_ap).then_inc(S["sDSQ"], 1)
                    for h in range(DC):
                        e.wait_ge(S["sDEV"], 8 * t + h + 1)
                        if t >= 1:
                            e.wait_ge(S["sOUT"], 16 * t)
                        e.activation(sqt28[:, h * TD:(h + 1) * TD],
                                     r2bs[t % 2][:, h * TD:(h + 1) * TD],
                                     AF.Square).then_inc(S["sDCB"], 1)
                    if t == NT - 1:
                        e.wait_ge(S["sDVC"], t + 1)
                        e.activation(stx[0:1, 0:512], stx[0:1, 512:1024], AF.Sqrt,
                                     bias=EPS_ap).then_inc(S["sDSQ"], 1)
                _sc.__exit__(None, None, None)
            if eng == "sync":
                e.wait_ge(S["sDXL"], NT)
                e.dma_start(
                    out=out_fm.rearrange("k p n -> p k n")[:, :, (NT - 1) * TD:NT * TD],
                    in_=sqt28.ap().rearrange("p (k n) -> p k n", k=DC)
                ).then_inc(S["sOUT"], 16)
                e.wait_ge(S["sOUT"], 16 * NT)
            if eng == "gpsimd":
                e.wait_ge(S["sOUT"], 16 * NT)

        def stream(eng, e):
            with nc.named_scope("setup"):
                ph_setup(eng, e)
            with nc.named_scope("mixer"):
                for t in range(NT):
                    with nc.named_scope(f"mx{t}"):
                        ph_mixer_tile(eng, e, t)
                ph_mixer_tail(eng, e)
            with nc.named_scope("bar1"):
                ph_bar1_fill(eng, e)
                ph_gmax_barrier(eng, e, 0, 1)
                ph_bar1_post(eng, e)
            ph_A_wu_loads(eng, e)
            with nc.named_scope("xq"):
                ph_xq(eng, e)
            if stop_after == "mixer":
                if eng == "sync":
                    e.wait_ge(S["sXQS"][0], 16 * (NT // 2))
                    e.wait_ge(S["sXQS"][1], 16 * (NT // 2))
                    e.wait_ge(S["sAWW"][0], 16 * 24)
                    e.wait_ge(S["sAWW"][1], 16 * 24)
                    e.dma_start(out=gdbg[:, 0:8], in_=mxc[:, :]).then_inc(S["sOUT"], 16)
                    e.dma_start(out=gdbg[:, 8:10], in_=scl[:, 0:2]).then_inc(S["sOUT"], 16)
                    e.wait_ge(S["sOUT"], 32)
                elif eng == "gpsimd":
                    e.wait_ge(S["sOUT"], 32)
                return
            with nc.named_scope("phA"):
                ph_A(eng, e)
            with nc.named_scope("bar2"):
                ph_bar2_fill(eng, e)
                ph_gmax_barrier(eng, e, 1, 2)
                ph_bar2_post(eng, e)
            if stop_after == "A":
                if eng == "sync":
                    e.wait_ge(S["sASP"][0], 16 * (NA // 8))
                    e.wait_ge(S["sASP"][1], 16 * (NA // 8))
                    e.wait_ge(S["bPRE1"], 6)
                    e.dma_start(out=gdbg[:, 0:6], in_=scl[:, 0:6]).then_inc(S["sOUT"], 16)
                    e.wait_ge(S["sOUT"], 16)
                elif eng == "gpsimd":
                    e.wait_ge(S["sOUT"], 16)
                return
            pbar(e, 1)
            with nc.named_scope("phB"):
                ph_B(eng, e)
            with nc.named_scope("bar3"):
                ph_bar3_fill(eng, e)
                ph_gmax_barrier(eng, e, 2, 1)
                ph_bar3_post(eng, e)
            if stop_after == "B":
                if eng == "sync":
                    e.wait_ge(S["sBSP"][0], 16 * (NB // 2))
                    e.wait_ge(S["sBSP"][1], 16 * (NB // 2))
                    e.wait_ge(S["bPRE2"], 6)
                    e.dma_start(out=gdbg[:, 0:8], in_=scl[:, 0:8]).then_inc(S["sOUT"], 16)
                    e.wait_ge(S["sOUT"], 16)
                elif eng == "gpsimd":
                    e.wait_ge(S["sOUT"], 16)
                return
            with nc.named_scope("phC"):
                ph_C(eng, e)
            with nc.named_scope("bar4"):
                ph_bar4_fill(eng, e)
                ph_gmax_barrier(eng, e, 3, 1)
                ph_bar4_post(eng, e)
            if stop_after == "C":
                if eng == "sync":
                    e.wait_ge(S["sCSP"][0], 16 * (NB // 2))
                    e.wait_ge(S["sCSP"][1], 16 * (NB // 2))
                    e.wait_ge(S["bPRE3"], 6)
                    e.dma_start(out=gdbg[:, 0:10], in_=scl[:, 0:10]).then_inc(S["sOUT"], 16)
                    e.wait_ge(S["sOUT"], 16)
                elif eng == "gpsimd":
                    e.wait_ge(S["sOUT"], 16)
                return
            pbar(e, 2)
            with nc.named_scope("phD"):
                ph_D(eng, e)

        with nc.Block() as bk:
            @bk.sync
            def _(e):
                stream("sync", FE(e, fences["sync"]))

            @bk.gpsimd
            def _(e):
                stream("gpsimd", FE(e, fences["gpsimd"]))

            @bk.tensor
            def _(e):
                stream("tensor", FE(e, fences["tensor"]))

            @bk.scalar
            def _(e):
                stream("scalar", FE(e, fences["scalar"]))

            @bk.vector
            def _(e):
                stream("vector", FE(e, fences["vector"]))
    return nc


def _prep_inputs(L0_attn_x, L0_attn_Wg, L0_attn_bg, L0_attn_Wv, L0_attn_bv,
                 L0_attn_Wd, L0_attn_bd, L0_ffn_w_gate, L0_ffn_w_up, L0_ffn_w_down):
    f = np.ascontiguousarray
    shared = {
        "wg_t": f(L0_attn_Wg.T).reshape(DC, 128, D),
        "wv_t": f(L0_attn_Wv.T).reshape(DC, 128, D),
        "wd_t": f(L0_attn_Wd.T).reshape(DC, 128, D),
        "bgh": f(L0_attn_bg.reshape(DC, 128).T),
        "bvh": f(L0_attn_bv.reshape(DC, 128).T),
        "bdh": f(L0_attn_bd.reshape(DC, 128).T),
        "wu_n": f(L0_ffn_w_up).reshape(ICH, 128, D),
        "wq_n": f(L0_ffn_w_gate).reshape(ICH, 128, D),
        "wdn_n": f(L0_ffn_w_down).reshape(DC, 128, I),
    }
    in_maps = []
    for c in range(NCORES):
        xc = f(L0_attn_x[2 * c:2 * c + 2].reshape(TOK, D).T).reshape(DC, 128, TOK)
        m = {"x_fm": xc}
        m.update(shared)
        in_maps.append(m)
    return in_maps


def _run(inputs, stop_after="full", debug=False):
    nc = build(stop_after=stop_after, debug=debug)
    in_maps = _prep_inputs(**{k: np.asarray(v) for k, v in inputs.items()})
    res = run_bass_kernel_spmd(nc, in_maps, core_ids=list(range(NCORES)))
    if debug:
        return res
    out = np.empty((B, S, D), dtype=np.float32)
    for c in range(NCORES):
        o = res.results[c]["out_fm"].reshape(D, TOK)
        out[2 * c:2 * c + 2] = o.T.reshape(2, S, D)
    return out


def kernel(**inputs):
    return _run(inputs)


# version 08
# speedup vs baseline: 1.0225x; 1.0225x over previous
import contextlib

import numpy as np
import concourse.bass as bass
import concourse.mybir as mybir
from concourse.bass_utils import run_bass_kernel_spmd

F32 = mybir.dt.float32
BF16 = mybir.dt.bfloat16
ALU = mybir.AluOpType
AF = mybir.ActivationFunctionType
AX = mybir.AxisListType

NCORES = 8
B, S, D, I = 16, 2048, 1024, 3072
TOK = (B // NCORES) * S
TD = 512
NT = TOK // TD
DC = D // 128
ICH = I // 128
QMAX = 32767.0
C_LO = 1.5 * 2.0 ** 23
C_DELTA = 1.5 * 2.0 ** 52 - 1.5 * 2.0 ** 23

ENGS = ("sync", "gpsimd", "tensor", "scalar", "vector")

NAPPLY = 72
NA = NT * 2 * ICH
NB = NT * (ICH // 4)
NJ = ICH // 4


def _ap_span(ap):
    try:
        dims = list(ap.ap)
        off = ap.offset
        span = 1
        for st, cnt in dims:
            span += abs(st) * (cnt - 1)
        refined = None
        if dims:
            pstr, pcnt = dims[0]
            rest = 1
            for st, cnt in dims[1:]:
                rest += abs(st) * (cnt - 1)
            if pstr >= rest and pstr > 0:
                p_lo = off // pstr
                c_lo = off % pstr
                refined = (pstr, p_lo, p_lo + pcnt, c_lo, c_lo + rest)
        return (ap.tensor.name, off, off + span, refined)
    except AttributeError:
        return None


def _overlap(k, lst):
    for (nm, lo, hi, ref) in lst:
        if nm != k[0]:
            continue
        if k[2] <= lo or hi <= k[1]:
            continue
        if ref is not None and k[3] is not None and ref[0] == k[3][0]:
            if (k[3][2] <= ref[1] or ref[2] <= k[3][1]
                    or k[3][4] <= ref[3] or ref[4] <= k[3][3]):
                continue
        return True
    return False


class FE:

    def __init__(self, eng, state):
        self._e = eng
        self._w = state["w"]
        self._r = state["r"]

    def __getattr__(self, name):
        return getattr(self._e, name)

    def drain(self, *a, **k):
        self._w.clear()
        self._r.clear()
        return self._e.drain(*a, **k)

    def _pre(self, outs, ins):
        need = False
        for ap in list(outs) + list(ins):
            if ap is None or isinstance(ap, (int, float)):
                continue
            k = _ap_span(ap)
            if k is not None and _overlap(k, self._w):
                need = True
                break
        if not need:
            for ap in outs:
                if ap is None or isinstance(ap, (int, float)):
                    continue
                k = _ap_span(ap)
                if k is not None and _overlap(k, self._r):
                    need = True
                    break
        if need:
            self._e.drain()
            self._w.clear()
            self._r.clear()

    def _post(self, outs, ins):
        for ap in outs:
            if ap is None or isinstance(ap, (int, float)):
                continue
            k = _ap_span(ap)
            if k is not None:
                self._w.append(k)
        for ap in ins:
            if ap is None or isinstance(ap, (int, float)):
                continue
            k = _ap_span(ap)
            if k is not None:
                self._r.append(k)

    def _op(self, outs, ins, fn):
        self._pre(outs, ins)
        r = fn()
        self._post(outs, ins)
        return r

    def memset(self, out, v):
        return self._op([out], [], lambda: self._e.memset(out, v))

    def tensor_scalar(self, out, in0, s1, s2, op0, op1=None):
        if op1 is not None:
            return self._op([out], [in0, s1, s2],
                            lambda: self._e.tensor_scalar(out, in0, s1, s2, op0, op1))
        return self._op([out], [in0, s1],
                        lambda: self._e.tensor_scalar(out, in0, s1, s2, op0))

    def tensor_tensor(self, out, a, b, op):
        return self._op([out], [a, b], lambda: self._e.tensor_tensor(out, a, b, op))

    def scalar_tensor_tensor(self, out, in0, sc, in1, op0, op1):
        return self._op([out], [in0, sc, in1],
                        lambda: self._e.scalar_tensor_tensor(out, in0, sc, in1, op0, op1))

    def tensor_tensor_scan(self, out, d0, d1, init, op0, op1):
        return self._op([out], [d0, d1, init],
                        lambda: self._e.tensor_tensor_scan(out, d0, d1, init, op0, op1))

    def reduce_max(self, out, in_, axis, apply_absolute_value=False):
        return self._op([out], [in_],
                        lambda: self._e.reduce_max(out, in_, axis=axis,
                                                   apply_absolute_value=apply_absolute_value))

    def reciprocal(self, out, in_):
        return self._op([out], [in_], lambda: self._e.reciprocal(out, in_))

    def activation(self, out, in_, func, bias=None, scale=None):
        kw = {}
        if bias is not None:
            kw["bias"] = bias
        if scale is not None:
            kw["scale"] = scale
        return self._op([out], [in_, bias, scale],
                        lambda: self._e.activation(out, in_, func, **kw))

    def affine_select(self, out, in_, compare_op, fill, base, pattern,
                      channel_multiplier):
        return self._op([out], [in_],
                        lambda: self._e.affine_select(
                            out=out, in_=in_, compare_op=compare_op, fill=fill,
                            base=base, pattern=pattern,
                            channel_multiplier=channel_multiplier))

    def tensor_copy(self, out, in_):
        return self._op([out], [in_], lambda: self._e.tensor_copy(out, in_))

    def dma_start(self, out, in_, **kw):
        self._pre([], [in_])
        return self._e.dma_start(out=out, in_=in_, **kw)


def build(stop_after="full", debug=False):
    nc = bass.Bass("TRN2", target_bir_lowering=False)
    dbg = "ExternalOutput" if debug else "Internal"

    x_fm = nc.dram_tensor("x_fm", [DC, 128, TOK], F32, kind="ExternalInput").ap()
    wg_t = nc.dram_tensor("wg_t", [DC, 128, D], F32, kind="ExternalInput").ap()
    wv_t = nc.dram_tensor("wv_t", [DC, 128, D], F32, kind="ExternalInput").ap()
    wd_t = nc.dram_tensor("wd_t", [DC, 128, D], F32, kind="ExternalInput").ap()
    bgh = nc.dram_tensor("bgh", [128, DC], F32, kind="ExternalInput").ap()
    bvh = nc.dram_tensor("bvh", [128, DC], F32, kind="ExternalInput").ap()
    bdh = nc.dram_tensor("bdh", [128, DC], F32, kind="ExternalInput").ap()
    wu_n = nc.dram_tensor("wu_n", [ICH, 128, D], F32, kind="ExternalInput").ap()
    wq_n = nc.dram_tensor("wq_n", [ICH, 128, D], F32, kind="ExternalInput").ap()
    wdn_n = nc.dram_tensor("wdn_n", [DC, 128, I], F32, kind="ExternalInput").ap()

    out_fm = nc.dram_tensor("out_fm", [DC, 128, TOK], F32, kind="ExternalOutput").ap()

    sp_x1 = nc.dram_tensor("sp_x1", [DC, 128, TOK], F32, kind=dbg).ap()
    sp_xq = nc.dram_tensor("sp_xq", [DC, 128, TOK], BF16, kind=dbg).ap()
    sp_up = nc.dram_tensor("sp_up", [ICH, 128, TOK], F32, kind=dbg).ap()
    sp_gt = nc.dram_tensor("sp_gt", [ICH, 128, TOK], F32, kind=dbg).ap()
    sp_p1 = nc.dram_tensor("sp_p1", [ICH, 128, TOK], F32, kind=dbg).ap()
    sp_p2 = nc.dram_tensor("sp_p2", [ICH, 128, TOK], F32, kind=dbg).ap()
    sp_wu = nc.dram_tensor("sp_wu", [ICH, 128, D], BF16, kind=dbg).ap()
    sp_wq = nc.dram_tensor("sp_wq", [ICH, 128, D], BF16, kind=dbg).ap()
    sp_wd = nc.dram_tensor("sp_wd", [DC, 128, I], BF16, kind=dbg).ap()
    ccb = []
    for bi in range(4):
        ci = nc.dram_tensor(f"cc_in{bi}", [128, 2], F32).ap()
        co = nc.dram_tensor(f"cc_out{bi}", [128, 2], F32).ap()
        ccb.append((ci, co))
    gdbg = nc.dram_tensor("gdbg", [128, 16], F32, kind=dbg).ap()

    fences = {k: {"w": [], "r": []} for k in ENGS}
    es = contextlib.ExitStack()
    with es:
        sbuf = lambda name, shape, dt=F32: es.enter_context(nc.sbuf_tensor(name, shape, dt))
        sem = lambda name: es.enter_context(nc.semaphore(name))

        cst = sbuf("cst", [128, 16])
        ident = sbuf("ident", [128, 128])
        ones_bf = sbuf("ones_bf", [128, 1], BF16)
        ones_r = sbuf("ones_r", [1, 128])
        ones_c = sbuf("ones_c", [128, 1])
        selm = sbuf("selm", [2, 256])
        biases = sbuf("biases", [128, 3 * DC])
        scl = sbuf("scl", [128, 16])
        gmaxs = sbuf("gmaxs", [128, 8])
        mx2 = sbuf("mx2", [128, 2])
        gsc = sbuf("gsc", [2, 1])
        stx = sbuf("stx", [1, 1024])
        abv = sbuf("abv", [1, 1024])
        carry = sbuf("carry", [128, DC])

        pb = [es.enter_context(nc.psum_tensor(f"pb{i}", [128, 512], F32))
              for i in range(8)]

        A0 = 40928
        LIM = 229344

        def at(name, shape, dt, off):
            end = off + shape[1] * mybir.dt.size(dt)
            assert end <= LIM, (name, end)
            return nc.alloc_sbuf_tensor_at(name, shape, dt, offset=off)

        wgm = at("wgm", [128, DC * D], BF16, A0)
        wvm = at("wvm", [128, DC * D], BF16, A0 + 16384)
        wdm = at("wdm", [128, DC * D], BF16, A0 + 32768)
        xf = [at(f"xf{i}", [128, DC * TD], F32, A0 + 49152 + 16384 * i) for i in range(2)]
        xbm = [at(f"xbm{i}", [128, DC * TD], BF16, A0 + 81920 + 8192 * i) for i in range(2)]
        wk_xg = [at(f"wkxg{i}", [128, TD], F32, A0 + 98304 + 2048 * i) for i in range(2)]
        wk_tv = [at(f"wktv{i}", [128, TD], F32, A0 + 102400 + 2048 * i) for i in range(2)]
        wk_sa = [at(f"wksa{i}", [128, TD], F32, A0 + 106496 + 2048 * i) for i in range(2)]
        wk_xs1 = at("wkxs", [128, TD], F32, A0 + 110592)
        wk_av1 = at("wkav", [128, TD], F32, A0 + 112640)
        hbuf = at("hbuf", [128, 2 * TD], F32, A0 + 114688)
        rbuf = at("rbuf", [128, DC * TD], F32, A0 + 118784)
        sqt8 = at("sqt8", [128, DC * TD], F32, A0 + 135168)
        rbuf2 = at("rbuf2", [128, DC * TD], F32, A0 + 151552)
        rbufs = [rbuf, rbuf2]
        mxc = at("mxc", [128, 8], F32, A0 + 167936)
        DQ0 = A0 + 168000
        dq_nat = [at(f"dqn{i}", [128, D], F32, DQ0 + 4096 * i) for i in range(2)]
        dq_wb = [at(f"dqwb{i}", [128, D], BF16, DQ0 + 8192 + 2048 * i) for i in range(2)]
        dq_sc = at("dqsc", [128, 128], F32, DQ0 + 12288)
        wk_xs2 = at("wkxs2", [128, TD], F32, DQ0 + 12800)
        wk_av2 = at("wkav2", [128, TD], F32, DQ0 + 14848)
        wk_xs = [wk_xs1, wk_xs2]
        wk_av = [wk_av1, wk_av2]

        wu_sb = at("wu_sb", [128, DC * I], BF16, A0)
        wq_sb = at("wq_sb", [128, DC * I], BF16, A0 + 49152)
        xqa = [at(f"xqa{i}", [128, DC * TD], BF16, A0 + 98304 + 8192 * i) for i in range(2)]
        gbuf = [at(f"gbuf{i}", [128, 2048], F32, A0 + 114688 + 8192 * i) for i in range(2)]
        astat = at("astat", [128, 96], F32, A0 + 131072)

        ld1 = [at(f"ld1_{i}", [128, 2048], F32, A0 + 8192 * i) for i in range(2)]
        ld2 = [at(f"ld2_{i}", [128, 2048], F32, A0 + 16384 + 8192 * i) for i in range(2)]
        tb1 = [at(f"tb1_{i}", [128, 2048], F32, A0 + 32768 + 8192 * i) for i in range(2)]
        tb2 = [at(f"tb2_{i}", [128, 2048], F32, A0 + 49152 + 8192 * i) for i in range(2)]
        tb3 = [at(f"tb3_{i}", [128, 2048], F32, A0 + 65536 + 8192 * i) for i in range(2)]
        tb4 = [at(f"tb4_{i}", [128, 2048], F32, A0 + 81920 + 8192 * i) for i in range(2)]
        tb5 = [at(f"tb5_{i}", [128, 2048], F32, A0 + 98304 + 8192 * i) for i in range(2)]
        bstat = at("bstat", [128, 48], F32, A0 + 114688)

        wd_sb = at("wd_sb", [128, ICH * D], BF16, A0)
        otile = [at(f"otile{i}", [128, ICH * TD], BF16, A0 + 49152 + 24576 * i) for i in range(2)]
        p2ld = [at(f"p2ld{i}", [128, 2048], F32, A0 + 98304 + 8192 * i) for i in range(2)]
        t5b = at("t5b", [128, 2048], F32, A0 + 114688)
        x1d = at("x1d", [128, DC * TD], F32, A0 + 122880)
        r2b = at("r2b", [128, DC * TD], F32, A0 + 139264)
        r2b2 = at("r2b2", [128, DC * TD], F32, A0 + 155648)
        r2bs = [r2b, r2b2]
        sqt28 = at("sqt28", [128, DC * TD], F32, A0 + 172032)

        S = {n: sem(n) for n in (
            "sWM sWC sMM sEP sRS sCB sSTe sST sSE sVC sSQ sAB sBC sXL sLNE sDLNE sVX "
            "sSPX sPB sXQV sANL sAPC "
            "sWL2 sWQL sWDL sAMM sAEV sARD sBS sBR sBV sCS sCV "
            "sDS2 sDV2 sDX1 sDMM sDEV sDCB sDST sDSTe sDSE2 sDVC sDSQ "
            "sDAB sDBC sDXL sOUT sSU".split())}
        for pn in ("sLX sLB sXQL sXQA sAWW sXQS sASP sBL sBSP "
                   "sCL sCSP sDL".split()):
            S[pn] = [sem(pn + "0"), sem(pn + "1")]
        for bi in range(4):
            S[f"bPRE{bi}"] = sem(f"bPRE{bi}")
            S[f"bDMA{bi}"] = sem(f"bDMA{bi}")
            S[f"bCC{bi}"] = sem(f"bCC{bi}")

        C_ap = cst[:, 1:2]
        negC_ap = cst[:, 2:3]
        EPS_ap = cst[0:1, 5:6]

        apply_steps = []
        for c in range(ICH):
            apply_steps.append((wu_n[c][:, :], sp_wu[c][:, :]))
        for c in range(ICH):
            apply_steps.append((wq_n[c][:, :], sp_wq[c][:, :]))
        for hc in range(DC):
            for j in range(3):
                apply_steps.append((wdn_n[hc][:, j * D:(j + 1) * D],
                                    sp_wd[hc][:, j * D:(j + 1) * D]))
        assert len(apply_steps) == NAPPLY
        APT = 6

        def applies_for_tile(t):
            if t == NT - 1:
                return range(APT * t, APT * t + 2)
            return range(APT * t, APT * (t + 1))

        def x_src(t):
            return x_fm.rearrange("k p n -> p k n")[:, :, t * TD:(t + 1) * TD]

        def pbar(e, k):
            e.drain()
            if hasattr(e._e, "engine_nop"):
                e._e.engine_nop().then_inc(S["sPB"], 1)
            else:
                e._e.nop().then_inc(S["sPB"], 1)
            e._e.wait_ge(S["sPB"], 5 * k)
            e._w.clear()
            e._r.clear()


        def ph_setup(eng, e):
            if eng == "gpsimd":
                e.memset(ident[:, :], 0.0)
                e.affine_select(out=ident[:, :], in_=ident[:, :],
                                compare_op=ALU.not_equal, fill=1.0, base=0,
                                pattern=[[-1, 128]], channel_multiplier=1)
                e.memset(ones_bf[:, :], 1.0)
                e.memset(ones_c[:, :], 1.0)
                e.memset(ones_r[:, :], 1.0)
                e.memset(selm[0:2, :], 1.0)
                e.affine_select(out=selm[0:2, :], in_=selm[0:2, :],
                                compare_op=ALU.is_ge, fill=0.0, base=0,
                                pattern=[[1, 256]], channel_multiplier=-128)
                e.affine_select(out=selm[0:2, :], in_=selm[0:2, :],
                                compare_op=ALU.is_ge, fill=0.0, base=127,
                                pattern=[[-1, 256]], channel_multiplier=128)
                e._e.engine_nop().then_inc(S["sSU"], 1)
                for dst, srcw in ((wgm, wg_t), (wvm, wv_t), (wdm, wd_t)):
                    e.dma_start(out=dst.ap().rearrange("p (k n) -> p k n", k=DC),
                                in_=srcw.rearrange("k p n -> p k n")).then_inc(S["sWC"], 16)
            elif eng == "vector":
                e.memset(cst[:, 3:4], 1.0)
                e.memset(cst[:, 5:6], 1e-5)
                e.tensor_scalar(cst[:, 4:5], cst[:, 3:4], 2.0 ** -30, 1.0,
                                ALU.add, ALU.subtract)
                e.tensor_scalar(cst[:, 0:1], cst[:, 4:5], 2.0 ** 30, 1.0,
                                ALU.mult, ALU.mult)
                e.tensor_scalar(cst[:, 1:2], cst[:, 0:1], C_DELTA, C_LO,
                                ALU.mult, ALU.add)
                e.tensor_scalar(cst[:, 2:3], cst[:, 1:2], -1.0, 0.0, ALU.mult, ALU.add)
            elif eng == "sync":
                e.dma_start(out=biases[:, 0:DC], in_=bgh[:, :]).then_inc(S["sWM"], 16)
                e.dma_start(out=biases[:, DC:2 * DC], in_=bvh[:, :]).then_inc(S["sWM"], 16)
                e.dma_start(out=biases[:, 2 * DC:3 * DC], in_=bdh[:, :]).then_inc(S["sWM"], 16)

        def ph_gapply_store(e, s):
            e.wait_ge(S["sAPC"], s + 1)
            _, pdst = apply_steps[s]
            e.dma_start(out=pdst,
                        in_=dq_wb[s % 2][:, :]).then_inc(S["sAWW"][s % 2], 16)

        def ph_gapply(e, s, do_store=True):
            src, dst = apply_steps[s]
            if s >= 1:
                e.wait_ge(S["sAPC"], s - 1)
            e.dma_start(out=dq_nat[s % 2][:, :], in_=src).then_inc(S["sANL"], 16)
            if s >= 1 and do_store:
                ph_gapply_store(e, s - 1)

        def ph_vapply2(e, s):
            na, nb_ = dq_nat[s % 2], dq_nat[(s + 1) % 2]
            base = dq_sc[:, :]
            bc = lambda c0: bass.AP(base.tensor, base.offset + c0,
                                    [[128, 128], [1, 32], [0, 32]])
            e.wait_ge(S["sANL"], 16 * (s + 2))
            if s >= 2:
                e.wait_ge(S["sAWW"][0], 16 * (s // 2))
                e.wait_ge(S["sAWW"][1], 16 * (s // 2))
            na3 = na.ap().rearrange("p (b x) -> p b x", x=32)
            nb3 = nb_.ap().rearrange("p (b x) -> p b x", x=32)
            e.reduce_max(dq_sc[:, 0:32], na3, axis=AX.X, apply_absolute_value=True)
            e.reduce_max(dq_sc[:, 32:64], nb3, axis=AX.X, apply_absolute_value=True)
            e.tensor_scalar(dq_sc[:, 0:64], dq_sc[:, 0:64], 1.0 / 7.0, 1e-12,
                            ALU.mult, ALU.max)
            e.reciprocal(dq_sc[:, 64:128], dq_sc[:, 0:64])
            e.tensor_tensor(na3, na3, bc(64), ALU.mult)
            e.tensor_scalar(na[:, :], na[:, :], C_ap, C_ap, ALU.add, ALU.subtract)
            e.tensor_tensor(dq_wb[s % 2].ap().rearrange("p (b x) -> p b x", x=32),
                            na3, bc(0), ALU.mult).then_inc(S["sAPC"], 1)
            e.tensor_tensor(nb3, nb3, bc(96), ALU.mult)
            e.tensor_scalar(nb_[:, :], nb_[:, :], C_ap, C_ap, ALU.add, ALU.subtract)
            e.tensor_tensor(dq_wb[(s + 1) % 2].ap().rearrange("p (b x) -> p b x", x=32),
                            nb3, bc(32), ALU.mult).then_inc(S["sAPC"], 1)
        def ph_mixer_tile(eng, e, t):
            if eng == "sync":
                if t == 0:
                    for tt in (0, 1):
                        e.dma_start(out=xf[tt].ap().rearrange("p (k n) -> p k n", k=DC),
                                    in_=x_src(tt)).then_inc(S["sLX"][tt % 2], 16)
                if t >= 1:
                    e.wait_ge(S["sXL"], t)
                    e.dma_start(
                        out=sp_x1.rearrange("k p n -> p k n")[:, :,
                                                              (t - 1) * TD:t * TD],
                        in_=xf[(t - 1) % 2].ap().rearrange("p (k n) -> p k n", k=DC)
                    ).then_inc(S["sSPX"], 16)
                    if t + 1 < NT:
                        e.wait_ge(S["sSPX"], 16 * t)
                        e.dma_start(out=xf[(t + 1) % 2].ap().rearrange("p (k n) -> p k n", k=DC),
                                    in_=x_src(t + 1)).then_inc(S["sLX"][(t + 1) % 2], 16)
            elif eng == "gpsimd":
                if t == 0:
                    for tt in (0, 1):
                        e.dma_start(out=xbm[tt].ap().rearrange("p (k n) -> p k n", k=DC),
                                    in_=x_src(tt)).then_inc(S["sLB"][tt % 2], 16)
                for s in applies_for_tile(t):
                    ph_gapply(e, s)
                if t == NT - 1:
                    ph_gapply(e, APT * NT - 4, do_store=False)
                    ph_gapply(e, APT * NT - 3, do_store=False)
                if t + 2 < NT:
                    e.wait_ge(S["sMM"], 8 * (t + 1))
                    e.dma_start(out=xbm[t % 2].ap().rearrange("p (k n) -> p k n", k=DC),
                                in_=x_src(t + 2)).then_inc(S["sLB"][t % 2], 16)
            elif eng == "tensor":
                if t == 0:
                    e.wait_ge(S["sWM"], 48)
                    e.wait_ge(S["sWC"], 48)
                e.wait_ge(S["sLB"][t % 2], 16 * (t // 2 + 1))
                e.wait_ge(S["sLX"][t % 2], 16 * (t // 2 + 1))

                def stats(ts):
                    if ts >= 2:
                        e.wait_ge(S["sVC"], ts)
                    for ei in range(DC):
                        e.wait_ge(S["sCB"], 8 * ts + ei + 1)
                        if ei < DC - 1:
                            e.matmul(pb[3][0:1, :], ones_c[:, :],
                                     rbufs[ts % 2][:, ei * TD:(ei + 1) * TD],
                                     start=(ei == 0), stop=False)
                            e.matmul(pb[4][0:1, :], ones_c[:, :], sqt8[:, ei * TD:(ei + 1) * TD],
                                     start=(ei == 0), stop=False).then_inc(S["sSTe"], 1)
                        else:
                            e.matmul(pb[4][0:1, :], ones_c[:, :], sqt8[:, ei * TD:(ei + 1) * TD],
                                     start=False, stop=True).then_inc(S["sSTe"], 1)
                            e.matmul(pb[3][0:1, :], ones_c[:, :],
                                     rbufs[ts % 2][:, ei * TD:(ei + 1) * TD],
                                     start=False, stop=True).then_inc(S["sST"], 1)

                def bcast(ts):
                    e.wait_ge(S["sAB"], ts + 1)
                    if ts >= 1:
                        e.wait_ge(S["sXL"], ts)
                    e.matmul(pb[4][:, :], ones_r[:, :], abv[0:1, 512:1024],
                             start=True, stop=True)
                    e.matmul(pb[5][:, :], ones_r[:, :], stx[0:1, 0:512],
                             start=True, stop=True).then_inc(S["sBC"], 1)

                if t >= 1:
                    stats(t - 1)
                for ei in range(DC):
                    if 8 * t + ei >= 1:
                        e.wait_ge(S["sEP"], 8 * t + ei)
                    last = None
                    for wi, wsb in enumerate((wgm, wvm, wdm)):
                        for k in range(DC):
                            last = e.matmul(
                                pb[wi][:, :],
                                wsb[:, k * D + 128 * ei: k * D + 128 * ei + 128],
                                xbm[t % 2][:, k * TD:(k + 1) * TD],
                                start=(k == 0), stop=(k == DC - 1))
                    last.then_inc(S["sMM"], 1)
                    if ei == 3 and t >= 1:
                        bcast(t - 1)
                if t == NT - 1:
                    stats(t)
                    bcast(t)
            elif eng == "scalar":
                def sq_m(k):
                    e.wait_ge(S["sRS"], 8 * t + 2 * (k // 2 + 1))
                    if t >= 1:
                        e.wait_ge(S["sSTe"], 8 * (t - 1) + k + 1)
                    e.activation(sqt8[:, k * TD:(k + 1) * TD],
                                 rbufs[t % 2][:, k * TD:(k + 1) * TD],
                                 AF.Square).then_inc(S["sCB"], 1)

                def ln_sc(ts):
                    e.wait_ge(S["sST"], ts + 1)
                    e.activation(stx[0:1, 0:512], pb[3][0:1, :], AF.Identity)
                    e.activation(stx[0:1, 512:1024], pb[4][0:1, :], AF.Identity)
                    e.activation(abv[0:1, 0:512], stx[0:1, 0:512], AF.Identity,
                                 scale=1.0 / 1024.0)
                    e.activation(abv[0:1, 512:1024], stx[0:1, 512:1024], AF.Identity,
                                 scale=1.0 / 1024.0)
                    e.activation(stx[0:1, 0:512], abv[0:1, 0:512],
                                 AF.Square).then_inc(S["sLNE"], 1)

                if t >= 1:
                    ln_sc(t - 1)
                    e.wait_ge(S["sVC"], t)
                    e.activation(stx[0:1, 0:512], stx[0:1, 512:1024], AF.Sqrt,
                                 bias=EPS_ap).then_inc(S["sSQ"], 1)
                for ei in range(DC):
                    e.wait_ge(S["sMM"], 8 * t + ei + 1)
                    if 8 * t + ei >= 2:
                        e.wait_ge(S["sVX"], 8 * t + ei - 1)
                    e.activation(wk_xg[ei % 2][:, :], pb[0][:, :], AF.Sigmoid,
                                 bias=biases[:, ei:ei + 1])
                    e.activation(wk_tv[ei % 2][:, :], pb[1][:, :], AF.Tanh,
                                 bias=biases[:, DC + ei:DC + ei + 1])
                    e.activation(wk_sa[ei % 2][:, :], pb[2][:, :], AF.Sigmoid,
                                 bias=biases[:, 2 * DC + ei:2 * DC + ei + 1]
                                 ).then_inc(S["sEP"], 1)
                    if ei >= 2:
                        sq_m(ei - 2)
                sq_m(DC - 2)
                sq_m(DC - 1)
                if t == NT - 1:
                    ln_sc(t)
                    e.wait_ge(S["sVC"], t + 1)
                    e.activation(stx[0:1, 0:512], stx[0:1, 512:1024], AF.Sqrt,
                                 bias=EPS_ap).then_inc(S["sSQ"], 1)
            elif eng == "vector":
                def ln_post(ts):
                    e.wait_ge(S["sLNE"], ts + 1)
                    e.tensor_tensor(stx[0:1, 512:1024], abv[0:1, 512:1024],
                                    stx[0:1, 0:512], ALU.subtract).then_inc(S["sVC"], 1)
                    e._w.clear()
                    e._r.clear()
                    e.drain()

                def ab_post(ts):
                    e.wait_ge(S["sSQ"], ts + 1)
                    e.reciprocal(abv[0:1, 512:1024], stx[0:1, 0:512])
                    e.scalar_tensor_tensor(stx[0:1, 0:512], abv[0:1, 0:512], -1.0,
                                           abv[0:1, 512:1024], ALU.mult,
                                           ALU.mult).then_inc(S["sAB"], 1)

                def affine(ts):
                    e.wait_ge(S["sBC"], ts + 1)
                    if ts >= 1:
                        e.wait_ge(S["sSPX"], 16 * ts)
                    p4 = pb[4][:, :]
                    p5 = pb[5][:, :]
                    b4 = bass.AP(p4.tensor, p4.offset, [[p4.ap[0][0], 128], [0, DC], [1, TD]])
                    b5 = bass.AP(p5.tensor, p5.offset, [[p5.ap[0][0], 128], [0, DC], [1, TD]])
                    xv = xf[ts % 2].ap().rearrange("p (b x) -> p b x", x=TD)
                    rv = rbufs[ts % 2].ap().rearrange("p (b x) -> p b x", x=TD)
                    e.tensor_tensor(xv, rv, b4, ALU.mult)
                    e.tensor_tensor(xv, xv, b5, ALU.add)
                    e.reduce_max(mxc[:, ts:ts + 1], xf[ts % 2][:, :], axis=AX.X,
                                 apply_absolute_value=True).then_inc(S["sXL"], 1)

                if t >= 1:
                    ln_post(t - 1)
                rb = rbufs[t % 2]
                for ei in range(DC):
                    e.wait_ge(S["sEP"], 8 * t + ei + 1)
                    sl = (ei % 2) * TD
                    e.tensor_tensor(wk_xs[ei % 2][:, :], wk_xg[ei % 2][:, :],
                                    wk_tv[ei % 2][:, :], ALU.mult)
                    e.tensor_scalar(wk_av[ei % 2][:, :], wk_sa[ei % 2][:, :],
                                    0.998, 0.001, ALU.mult,
                                    ALU.add).then_inc(S["sVX"], 1)
                    if ei >= 2 and ei % 2 == 0:
                        e.tensor_tensor(rb[:, (ei - 2) * TD:ei * TD],
                                        xf[t % 2][:, (ei - 2) * TD:ei * TD],
                                        hbuf[:, 0:2 * TD], ALU.add).then_inc(S["sRS"], 2)
                    if ei >= 1:
                        psl = ((ei - 1) % 2) * TD
                        e.tensor_scalar(carry[:, ei - 1:ei],
                                        hbuf[:, psl + TD - 1:psl + TD],
                                        1.0, 0.0, ALU.mult, ALU.add)
                    init = 0.0 if t % 4 == 0 else carry[:, ei:ei + 1]
                    e.tensor_tensor_scan(hbuf[:, sl:sl + TD], wk_av[ei % 2][:, :],
                                         wk_xs[ei % 2][:, :], init, ALU.mult, ALU.add)
                    if ei == 1 and t >= 1:
                        ab_post(t - 1)
                    if ei == 3 and t >= 1:
                        affine(t - 1)
                    if ei in (1, 3, 5) and (t < NT - 1 or ei == 1):
                        ph_vapply2(e, APT * t + ei - 1)
                e.tensor_scalar(carry[:, DC - 1:DC],
                                hbuf[:, ((DC - 1) % 2) * TD + TD - 1:
                                      ((DC - 1) % 2) * TD + TD],
                                1.0, 0.0, ALU.mult, ALU.add)
                e.tensor_tensor(rb[:, (DC - 2) * TD:DC * TD],
                                xf[t % 2][:, (DC - 2) * TD:DC * TD],
                                hbuf[:, 0:2 * TD], ALU.add).then_inc(S["sRS"], 2)
                if t == NT - 1:
                    ln_post(t)
                    ab_post(t)
                    affine(t)

        def ph_mixer_tail(eng, e):
            if eng == "sync":
                e.wait_ge(S["sXL"], NT)
                e.dma_start(
                    out=sp_x1.rearrange("k p n -> p k n")[:, :, (NT - 1) * TD:NT * TD],
                    in_=xf[(NT - 1) % 2].ap().rearrange("p (k n) -> p k n", k=DC)
                ).then_inc(S["sSPX"], 16)
        def ph_gmax_barrier(eng, e, bi, k):
            cin, cout = ccb[bi]
            bP, bD, bC = S[f"bPRE{bi}"], S[f"bDMA{bi}"], S[f"bCC{bi}"]
            if eng == "sync":
                e.wait_ge(bP, 1)
                e.dma_start(out=cin[:, :], in_=mx2[:, :]).then_inc(bD, 16)
                e.wait_ge(bC, 1)
                e.dma_start(out=mx2[:, :], in_=cout[:, :]).then_inc(bD, 16)
            elif eng == "gpsimd":
                e.wait_ge(bD, 16)
                e.collective_compute(
                    "AllReduce", ALU.max, replica_groups=[list(range(NCORES))],
                    ins=[cin[:, :]], outs=[cout[:, :]]).then_inc(bC, 1)
            elif eng == "tensor":
                e.wait_ge(bD, 32)
                e.transpose(pb[7][0:2, 0:128], mx2[:, :], ident[:, :]).then_inc(bP, 1)
                e.wait_ge(bP, 3)
                m = None
                for j in range(k):
                    m = e.matmul(pb[6][:, j:j + 1], selm[0:2, 128 * j:128 * j + 128],
                                 gsc[0:2, 0:1], start=True, stop=True)
                m.then_inc(bP, 1)
            elif eng == "scalar":
                e.wait_ge(bP, 4)
                e.activation(gmaxs[:, 2 * bi:2 * bi + k], pb[6][:, 0:k],
                             AF.Copy).then_inc(bP, 1)
            elif eng == "vector":
                e.wait_ge(bP, 2)
                e.reduce_max(gsc[0:2, 0:1], pb[7][0:2, 0:128], axis=AX.X).then_inc(bP, 1)
                e.wait_ge(bP, 5)

        def ph_bar1_fill(eng, e):
            if eng == "vector":
                e.wait_ge(S["sXL"], NT)
                e.reduce_max(mx2[:, 0:1], mxc[:, 0:NT], axis=AX.X)
                e.reduce_max(mx2[:, 1:2], mxc[:, 0:NT], axis=AX.X).then_inc(S["bPRE0"], 1)
                ph_vapply2(e, APT * NT - 4)
                ph_vapply2(e, APT * NT - 2)

        def ph_bar1_post(eng, e):
            if eng == "vector":
                e.tensor_scalar(scl[:, 0:1], gmaxs[:, 0:1], 1.0 / QMAX, 1e-12,
                                ALU.mult, ALU.max)
                e.reciprocal(scl[:, 1:2], scl[:, 0:1]).then_inc(S["bPRE0"], 1)

        def ph_xq(eng, e):
            for t in range(NT):
                if eng == "sync":
                    if t == 0:
                        e.wait_ge(S["sSPX"], 16 * NT)
                    if t >= 2:
                        e.wait_ge(S["sXQV"], t - 1)
                    e.dma_start(
                        out=xf[t % 2].ap().rearrange("p (k n) -> p k n", k=DC),
                        in_=sp_x1.rearrange("k p n -> p k n")[:, :, t * TD:(t + 1) * TD]
                    ).then_inc(S["sXQL"][t % 2], 16)
                elif eng == "vector":
                    e.wait_ge(S["sXQL"][t % 2], 16 * (t // 2 + 1))
                    if t == 0:
                        e.wait_ge(S["bPRE0"], 6)
                    if t >= 2:
                        e.wait_ge(S["sXQS"][t % 2], 16 * ((t - 2) // 2 + 1))
                    if t in (2, 3):
                        e.wait_ge(S["sXQA"][t % 2], 16)
                    e.tensor_scalar(sqt8[:, :], xf[t % 2][:, :], scl[:, 1:2], C_ap,
                                    ALU.mult, ALU.add)
                    e.tensor_scalar(xbm[t % 2][:, :], sqt8[:, :], C_ap, scl[:, 0:1],
                                    ALU.subtract, ALU.mult).then_inc(S["sXQV"], 1)
                if eng == "sync":
                    e.wait_ge(S["sXQV"], t + 1)
                    e.dma_start(
                        out=sp_xq.rearrange("k p n -> p k n")[:, :, t * TD:(t + 1) * TD],
                        in_=xbm[t % 2].ap().rearrange("p (k n) -> p k n", k=DC)
                    ).then_inc(S["sXQS"][t % 2], 16)
                    if t <= 1:
                        e.dma_start(out=xqa[t][:, :],
                                    in_=xbm[t][:, :]).then_inc(S["sXQA"][t % 2], 16)

        aorder = [(0, 0), (1, 0), (0, 1), (1, 1)]
        for t in range(2, NT):
            aorder.append((t, 0))
            aorder.append((t, 1))

        def ph_A_wu_loads(eng, e):
            if eng == "sync":
                e.wait_ge(S["sMM"], 8 * NT)
                e.wait_ge(S["sAWW"][0], 16 * 12)
                e.wait_ge(S["sAWW"][1], 16 * 12)
                flat = sp_wu.rearrange("c p n -> (c p) n")
                for k in range(DC):
                    e.dma_start(out=wu_sb[:, k * I:(k + 1) * I],
                                in_=flat[:, 128 * k:128 * (k + 1)],
                                transpose=True).then_inc(S["sWL2"], 16)

        def ph_A(eng, e):
            if eng == "sync":
                e.wait_ge(S["sXQS"][0], 16 * (NT // 2))
                e.wait_ge(S["sXQS"][1], 16 * (NT // 2))
                e.wait_ge(S["sAWW"][0], 16 * 24)
                e.wait_ge(S["sAWW"][1], 16 * 24)
                flat = sp_wq.rearrange("c p n -> (c p) n")
                for k in range(DC):
                    e.dma_start(out=wq_sb[:, k * I:(k + 1) * I],
                                in_=flat[:, 128 * k:128 * (k + 1)],
                                transpose=True).then_inc(S["sWQL"], 16)
            for oi, (t, w) in enumerate(aorder):
                _sc = nc.named_scope(f"pa{oi}")
                _sc.__enter__()
                wsb = wu_sb if w == 0 else wq_sb
                spw = sp_up if w == 0 else sp_gt
                if eng == "tensor":
                    if oi == 0:
                        e.wait_ge(S["sWL2"], 16 * 8)
                        e.wait_ge(S["sEP"], 8 * NT)
                        e.wait_ge(S["sVC"], NT)
                    if oi == 2:
                        e.wait_ge(S["sWQL"], 16 * 8)
                    if w == 0:
                        e.wait_ge(S["sXQA"][t % 2], 16 * (t // 2 + 1))
                    for i in range(ICH):
                        n = oi * ICH + i
                        if n >= 4:
                            e.wait_ge(S["sAEV"], n - 3)
                        m = None
                        for k in range(DC):
                            m = e.matmul(
                                pb[n % 4][:, :],
                                wsb[:, k * I + 128 * i: k * I + 128 * i + 128],
                                xqa[t % 2][:, k * TD:(k + 1) * TD],
                                start=(k == 0), stop=(k == DC - 1))
                        m.then_inc(S["sAMM"], 1)
                elif eng == "scalar":
                    for i in range(ICH):
                        n = oi * ICH + i
                        g = n // 4
                        if n == 0:
                            e.wait_ge(S["sST"], NT)
                            e.wait_ge(S["sXL"], NT)
                        if g >= 2:
                            e.wait_ge(S["sASP"][g % 2], 16 * ((g - 2) // 2 + 1))
                            e.wait_ge(S["sARD"], g - 1)
                        e.wait_ge(S["sAMM"], n + 1)
                        e.activation(gbuf[g % 2][:, 512 * (i % 4):512 * (i % 4) + 512],
                                     pb[n % 4][:, :], AF.Copy).then_inc(S["sAEV"], 1)
                elif eng == "vector":
                    for i in range(3, ICH, 4):
                        n = oi * ICH + i
                        g = n // 4
                        col = w * 48 + t * 6 + i // 4
                        e.wait_ge(S["sAEV"], n + 1)
                        e.reduce_max(astat[:, col:col + 1], gbuf[g % 2][:, :],
                                     axis=AX.X, apply_absolute_value=True
                                     ).then_inc(S["sARD"], 1)
                elif eng == "sync":
                    for i in range(3, ICH, 4):
                        n = oi * ICH + i
                        g = n // 4
                        e.wait_ge(S["sAEV"], n + 1)
                        e.dma_start(
                            out=spw.rearrange("k p n -> p k n")[:, i - 3:i + 1,
                                                                t * TD:(t + 1) * TD],
                            in_=gbuf[g % 2].ap().rearrange("p (c n) -> p c n", c=4)
                        ).then_inc(S["sASP"][g % 2], 16)
                    if t + 2 < NT and (w == 1 if t >= 2 else oi == 3):
                        for tl in ((t + 2,) if t >= 2 else (2, 3)):
                            e.wait_ge(S["sAMM"], (aorder.index((tl - 2, 1)) + 1) * ICH)
                            e.dma_start(
                                out=xqa[tl % 2].ap().rearrange("p (k n) -> p k n", k=DC),
                                in_=sp_xq.rearrange("k p n -> p k n")[:, :,
                                                                      tl * TD:(tl + 1) * TD]
                            ).then_inc(S["sXQA"][tl % 2], 16)
                _sc.__exit__(None, None, None)

        def ph_bar2_fill(eng, e):
            if eng == "vector":
                e.wait_ge(S["sARD"], NA // 4)
                e.reduce_max(mx2[:, 0:1], astat[:, 0:48], axis=AX.X)
                e.reduce_max(mx2[:, 1:2], astat[:, 48:96], axis=AX.X).then_inc(S["bPRE1"], 1)

        def ph_bar2_post(eng, e):
            if eng == "vector":
                e.tensor_scalar(scl[:, 4:5], gmaxs[:, 2:3], 1.0 / QMAX, 1e-12,
                                ALU.mult, ALU.max)
                e.tensor_scalar(scl[:, 2:3], gmaxs[:, 3:4], 1.0 / QMAX, 1e-12,
                                ALU.mult, ALU.max)
                e.reciprocal(scl[:, 5:6], scl[:, 4:5])
                e.reciprocal(scl[:, 3:4], scl[:, 2:3]).then_inc(S["bPRE1"], 1)

        def ph_B(eng, e):
            def vtail(n):
                e.wait_ge(S["sBS"], 2 * (n + 1))
                if n >= 2:
                    e.wait_ge(S["sBSP"][n % 2], 16 * ((n - 2) // 2 + 1))
                e.tensor_scalar(tb5[n % 2][:, :], tb4[n % 2][:, :], C_ap, 65535.0,
                                ALU.subtract, ALU.min)
                e.tensor_tensor(tb5[n % 2][:, :], tb2[n % 2][:, :],
                                tb5[n % 2][:, :], ALU.mult)
                e.reduce_max(bstat[:, n:n + 1], tb5[n % 2][:, :], axis=AX.X,
                             apply_absolute_value=True).then_inc(S["sBV"], 3)

            for n in range(NB):
                t, j = divmod(n, ICH // 4)
                if eng == "sync":
                    if n == 0:
                        e.wait_ge(S["sASP"][0], 16 * (NA // 8))
                        e.wait_ge(S["sASP"][1], 16 * (NA // 8))
                    if n >= 2:
                        e.wait_ge(S["sBR"], n - 1)
                    e.dma_start(
                        out=ld1[n % 2].ap().rearrange("p (c x) -> p c x", c=4),
                        in_=sp_gt.rearrange("k p n -> p k n")[:, 4 * j:4 * j + 4,
                                                              t * TD:(t + 1) * TD]
                    ).then_inc(S["sBL"][n % 2], 16)
                elif eng == "vector":
                    e.wait_ge(S["sBL"][n % 2], 16 * (n // 2 + 1))
                    if n == 0:
                        e.wait_ge(S["bPRE1"], 6)
                    if n >= 2:
                        e.wait_ge(S["sBS"], 2 * (n - 1))
                    e.tensor_scalar(tb1[n % 2][:, :], ld1[n % 2][:, :],
                                    scl[:, 3:4], C_ap, ALU.mult, ALU.add)
                    e.tensor_scalar(tb2[n % 2][:, :], tb1[n % 2][:, :], C_ap, 1.0,
                                    ALU.subtract, ALU.mult).then_inc(S["sBR"], 1)
                    if n >= 1:
                        vtail(n - 1)
                    if n == NB - 1:
                        vtail(n)
                elif eng == "scalar":
                    e.wait_ge(S["sBR"], n + 1)
                    if n >= 2:
                        e.wait_ge(S["sBV"], 3 * (n - 1))
                    e.activation(tb3[n % 2][:, :], tb2[n % 2][:, :], AF.Sigmoid,
                                 scale=scl[:, 2:3])
                    e.activation(tb4[n % 2][:, :], tb3[n % 2][:, :], AF.Identity,
                                 bias=C_ap, scale=65536.0).then_inc(S["sBS"], 2)
                elif eng == "gpsimd":
                    e.wait_ge(S["sBV"], 3 * (n + 1))
                    e.dma_start(
                        out=sp_p1.rearrange("k p n -> p k n")[:, 4 * j:4 * j + 4,
                                                              t * TD:(t + 1) * TD],
                        in_=tb5[n % 2].ap().rearrange("p (c x) -> p c x", c=4)
                    ).then_inc(S["sBSP"][n % 2], 16)

        def ph_bar3_fill(eng, e):
            if eng == "vector":
                e.wait_ge(S["sBV"], 3 * NB)
                e.reduce_max(mx2[:, 0:1], bstat[:, 0:NB], axis=AX.X)
                e.reduce_max(mx2[:, 1:2], bstat[:, 0:NB], axis=AX.X).then_inc(S["bPRE2"], 1)

        def ph_bar3_post(eng, e):
            if eng == "vector":
                e.reciprocal(scl[:, 7:8], gmaxs[:, 4:5])
                e.tensor_scalar(scl[:, 6:7], scl[:, 7:8], QMAX, 0.0,
                                ALU.mult, ALU.add).then_inc(S["bPRE2"], 1)

        def ph_C(eng, e):
            if eng == "gpsimd":
                ph_gapply(e, APT * NT, do_store=False)
                ph_gapply(e, APT * NT + 1, do_store=False)
            for n in range(NB):
                t, j = divmod(n, ICH // 4)
                if eng == "sync":
                    if n == 0:
                        e.wait_ge(S["sBSP"][0], 16 * (NB // 2))
                        e.wait_ge(S["sBSP"][1], 16 * (NB // 2))
                    if n >= 2:
                        e.wait_ge(S["sCS"], 2 * (n - 1))
                    e.dma_start(
                        out=ld1[n % 2].ap().rearrange("p (c x) -> p c x", c=4),
                        in_=sp_p1.rearrange("k p n -> p k n")[:, 4 * j:4 * j + 4,
                                                              t * TD:(t + 1) * TD]
                    ).then_inc(S["sCL"][n % 2], 16)
                    e.dma_start(
                        out=ld2[n % 2].ap().rearrange("p (c x) -> p c x", c=4),
                        in_=sp_up.rearrange("k p n -> p k n")[:, 4 * j:4 * j + 4,
                                                              t * TD:(t + 1) * TD]
                    ).then_inc(S["sCL"][n % 2], 16)
                elif eng == "scalar":
                    e.wait_ge(S["sCL"][n % 2], 32 * (n // 2 + 1))
                    if n == 0:
                        e.wait_ge(S["bPRE2"], 6)
                    if n >= 2:
                        e.wait_ge(S["sCV"], 2 * (n - 1))
                    e.activation(tb1[n % 2][:, :], ld1[n % 2][:, :], AF.Identity,
                                 bias=C_ap, scale=scl[:, 6:7])
                    e.activation(tb2[n % 2][:, :], tb1[n % 2][:, :], AF.Identity,
                                 bias=negC_ap)
                    e.activation(tb3[n % 2][:, :], ld2[n % 2][:, :], AF.Identity,
                                 bias=C_ap, scale=scl[:, 5:6]).then_inc(S["sCS"], 2)
                elif eng == "vector":
                    e.wait_ge(S["sCS"], 2 * (n + 1))
                    if n >= 2:
                        e.wait_ge(S["sCSP"][n % 2], 16 * ((n - 2) // 2 + 1))
                    e.scalar_tensor_tensor(tb4[n % 2][:, :], tb3[n % 2][:, :], negC_ap,
                                           tb2[n % 2][:, :], ALU.add, ALU.mult)
                    e.reduce_max(bstat[:, n:n + 1], tb4[n % 2][:, :], axis=AX.X,
                                 apply_absolute_value=True).then_inc(S["sCV"], 2)
                    if n % 4 == 0:
                        ph_vapply2(e, APT * NT + n // 2)
                elif eng == "gpsimd":
                    e.wait_ge(S["sCV"], 2 * (n + 1))
                    e.dma_start(
                        out=sp_p2.rearrange("k p n -> p k n")[:, 4 * j:4 * j + 4,
                                                              t * TD:(t + 1) * TD],
                        in_=tb4[n % 2].ap().rearrange("p (c x) -> p c x", c=4)
                    ).then_inc(S["sCSP"][n % 2], 16)
                    if n % 2 == 1:
                        jj = (n - 1) // 2
                        if 50 + jj <= NAPPLY - 1:
                            ph_gapply(e, 50 + jj, do_store=False)
                        if APT * NT + jj <= NAPPLY - 2:
                            ph_gapply_store(e, APT * NT + jj)
            if eng == "gpsimd":
                ph_gapply_store(e, NAPPLY - 1)

        def ph_bar4_fill(eng, e):
            if eng == "vector":
                e.wait_ge(S["sCV"], 2 * NB)
                e.reduce_max(mx2[:, 0:1], bstat[:, 0:NB], axis=AX.X)
                e.reduce_max(mx2[:, 1:2], bstat[:, 0:NB], axis=AX.X).then_inc(S["bPRE3"], 1)

        def ph_bar4_post(eng, e):
            if eng == "vector":
                e.reciprocal(scl[:, 10:11], gmaxs[:, 6:7])
                e.tensor_scalar(scl[:, 11:12], gmaxs[:, 4:5], 1.0 / (QMAX * 65536.0),
                                0.0, ALU.mult, ALU.add)
                e.tensor_scalar(scl[:, 8:9], scl[:, 10:11], QMAX, 0.0, ALU.mult, ALU.add)
                e.tensor_tensor(scl[:, 12:13], scl[:, 11:12], scl[:, 2:3], ALU.mult)
                e.tensor_tensor(scl[:, 13:14], scl[:, 12:13], scl[:, 4:5], ALU.mult)
                e.tensor_tensor(scl[:, 14:15], scl[:, 13:14], gmaxs[:, 6:7], ALU.mult)
                e.tensor_scalar(scl[:, 9:10], scl[:, 14:15], 1.0 / QMAX, 0.0,
                                ALU.mult, ALU.add).then_inc(S["bPRE3"], 1)

        def ph_D(eng, e):
            def ln_post2(ts):
                e.wait_ge(S["sDLNE"], ts + 1)
                e.tensor_tensor(stx[0:1, 512:1024], abv[0:1, 512:1024],
                                stx[0:1, 0:512], ALU.subtract).then_inc(S["sDVC"], 1)
                e.drain()

            def ln_sc2(ts):
                e.wait_ge(S["sDST"], ts + 1)
                e.activation(stx[0:1, 0:512], pb[3][0:1, :], AF.Identity)
                e.activation(stx[0:1, 512:1024], pb[4][0:1, :], AF.Identity)
                e.activation(abv[0:1, 0:512], stx[0:1, 0:512], AF.Identity,
                             scale=1.0 / 1024.0)
                e.activation(abv[0:1, 512:1024], stx[0:1, 512:1024], AF.Identity,
                             scale=1.0 / 1024.0)
                e.activation(stx[0:1, 0:512], abv[0:1, 0:512],
                             AF.Square).then_inc(S["sDLNE"], 1)

            def ab_post2(ts):
                e.wait_ge(S["sDSQ"], ts + 1)
                e.reciprocal(abv[0:1, 512:1024], stx[0:1, 0:512])
                e.scalar_tensor_tensor(stx[0:1, 0:512], abv[0:1, 0:512], -1.0,
                                       abv[0:1, 512:1024], ALU.mult,
                                       ALU.mult).then_inc(S["sDAB"], 1)

            def affine2(ts):
                e.wait_ge(S["sDBC"], ts + 1)
                e.wait_ge(S["sOUT"], 16 * ts)
                p4 = pb[4][:, :]
                p5 = pb[5][:, :]
                b4 = bass.AP(p4.tensor, p4.offset, [[p4.ap[0][0], 128], [0, DC], [1, TD]])
                b5 = bass.AP(p5.tensor, p5.offset, [[p5.ap[0][0], 128], [0, DC], [1, TD]])
                sv = sqt28.ap().rearrange("p (b x) -> p b x", x=TD)
                rv = r2bs[ts % 2].ap().rearrange("p (b x) -> p b x", x=TD)
                e.tensor_tensor(sv, rv, b4, ALU.mult)
                e.tensor_tensor(sv, sv, b5, ALU.add)
                e._e.engine_nop().then_inc(S["sDXL"], 1)

            def round_d(tt, j):
                n = tt * NJ + j
                e.wait_ge(S["sDL"][n % 2], 16 * (n // 2 + 1))
                if n == 0:
                    e.wait_ge(S["bPRE3"], 6)
                if tt >= 2 and j == 0:
                    e.wait_ge(S["sDMM"], 8 * (tt - 1))
                e.tensor_scalar(t5b[:, :], p2ld[n % 2][:, :], scl[:, 8:9],
                                C_ap, ALU.mult, ALU.add).then_inc(S["sDS2"], 1)
                e.tensor_scalar(otile[tt % 2][:, 2048 * j:2048 * (j + 1)],
                                t5b[:, :], C_ap, 1.0,
                                ALU.subtract, ALU.mult).then_inc(S["sDV2"], 1)

            if eng == "sync":
                e.wait_ge(S["sCSP"][0], 16 * (NB // 2))
                e.wait_ge(S["sCSP"][1], 16 * (NB // 2))
                e.wait_ge(S["sAWW"][0], 16 * (NAPPLY // 2))
                e.wait_ge(S["sAWW"][1], 16 * (NAPPLY // 2))
                flatd = sp_wd.rearrange("c p n -> (c p) n")
                for ci in range(ICH):
                    e.dma_start(out=wd_sb[:, ci * D:(ci + 1) * D],
                                in_=flatd[:, 128 * ci:128 * (ci + 1)],
                                transpose=True).then_inc(S["sWDL"], 16)
            for t in range(NT):
                _sc = nc.named_scope(f"pd{t}")
                _sc.__enter__()
                if eng == "sync":
                    if t == 0:
                        for j in range(NJ):
                            if j >= 2:
                                e.wait_ge(S["sDS2"], j - 1)
                            e.dma_start(
                                out=p2ld[j % 2].ap().rearrange("p (c x) -> p c x", c=4),
                                in_=sp_p2.rearrange("k p n -> p k n")[:, 4 * j:4 * j + 4,
                                                                      0:TD]
                            ).then_inc(S["sDL"][j % 2], 16)
                    if t >= 1:
                        e.wait_ge(S["sDEV"], 8 * t)
                    e.dma_start(
                        out=x1d.ap().rearrange("p (k n) -> p k n", k=DC),
                        in_=sp_x1.rearrange("k p n -> p k n")[:, :, t * TD:(t + 1) * TD]
                    ).then_inc(S["sDX1"], 16)
                    if t + 1 < NT:
                        for j in range(NJ):
                            n = (t + 1) * NJ + j
                            e.wait_ge(S["sDS2"], n - 1)
                            e.dma_start(
                                out=p2ld[n % 2].ap().rearrange("p (c x) -> p c x", c=4),
                                in_=sp_p2.rearrange("k p n -> p k n")[:, 4 * j:4 * j + 4,
                                                                      (t + 1) * TD:(t + 2) * TD]
                            ).then_inc(S["sDL"][n % 2], 16)
                    if t >= 1:
                        e.wait_ge(S["sDXL"], t)
                        e.dma_start(
                            out=out_fm.rearrange("k p n -> p k n")[:, :,
                                                                   (t - 1) * TD:t * TD],
                            in_=sqt28.ap().rearrange("p (k n) -> p k n", k=DC)
                        ).then_inc(S["sOUT"], 16)
                if eng == "tensor":
                    if t == 0:
                        e.wait_ge(S["sWDL"], 16 * ICH)

                    def stats2(ts):
                        if ts >= 2:
                            e.wait_ge(S["sDVC"], ts)
                        for ei in range(DC):
                            e.wait_ge(S["sDCB"], 8 * ts + ei + 1)
                            if ei < DC - 1:
                                e.matmul(pb[3][0:1, :], ones_c[:, :],
                                         r2bs[ts % 2][:, ei * TD:(ei + 1) * TD],
                                         start=(ei == 0), stop=False)
                                e.matmul(pb[4][0:1, :], ones_c[:, :],
                                         sqt28[:, ei * TD:(ei + 1) * TD],
                                         start=(ei == 0), stop=False
                                         ).then_inc(S["sDSTe"], 1)
                            else:
                                e.matmul(pb[4][0:1, :], ones_c[:, :],
                                         sqt28[:, ei * TD:(ei + 1) * TD],
                                         start=False, stop=True).then_inc(S["sDSTe"], 1)
                                e.matmul(pb[3][0:1, :], ones_c[:, :],
                                         r2bs[ts % 2][:, ei * TD:(ei + 1) * TD],
                                         start=False, stop=True).then_inc(S["sDST"], 1)

                    def bcast2(ts):
                        e.wait_ge(S["sDAB"], ts + 1)
                        if ts >= 1:
                            e.wait_ge(S["sDXL"], ts)
                        e.matmul(pb[4][:, :], ones_r[:, :], abv[0:1, 512:1024],
                                 start=True, stop=True)
                        e.matmul(pb[5][:, :], ones_r[:, :], stx[0:1, 0:512],
                                 start=True, stop=True).then_inc(S["sDBC"], 1)

                    if t >= 1:
                        stats2(t - 1)
                    e.wait_ge(S["sDV2"], NJ * (t + 1))
                    for h in range(DC):
                        if 8 * t + h >= 2:
                            e.wait_ge(S["sDEV"], 8 * t + h - 1)
                        m = None
                        for i in range(ICH):
                            m = e.matmul(pb[h % 2][:, :],
                                         wd_sb[:, i * D + 128 * h: i * D + 128 * h + 128],
                                         otile[t % 2][:, i * TD:(i + 1) * TD],
                                         start=(i == 0), stop=(i == ICH - 1))
                        m.then_inc(S["sDMM"], 1)
                        if h == 0 and t >= 1:
                            bcast2(t - 1)
                    if t == NT - 1:
                        stats2(t)
                        bcast2(t)
                elif eng == "vector":
                    if t == 0:
                        for j in range(NJ):
                            round_d(0, j)
                    e.wait_ge(S["sDX1"], 16 * (t + 1))
                    if t >= 2:
                        e.wait_ge(S["sDXL"], t - 1)
                    if t >= 1:
                        ln_post2(t - 1)
                        ab_post2(t - 1)
                    for h in range(DC):
                        e.wait_ge(S["sDMM"], 8 * t + h + 1)
                        e.scalar_tensor_tensor(r2bs[t % 2][:, h * TD:(h + 1) * TD],
                                               pb[h % 2][:, :], scl[:, 9:10],
                                               x1d[:, h * TD:(h + 1) * TD],
                                               ALU.mult, ALU.add).then_inc(S["sDEV"], 1)
                        if h == 0 and t >= 1:
                            affine2(t - 1)
                        if h < NJ and t + 1 < NT:
                            round_d(t + 1, h)
                    if t == NT - 1:
                        ln_post2(t)
                        ab_post2(t)
                        affine2(t)
                elif eng == "scalar":
                    if t >= 1:
                        ln_sc2(t - 1)
                        e.wait_ge(S["sDVC"], t)
                        e.activation(stx[0:1, 0:512], stx[0:1, 512:1024], AF.Sqrt,
                                     bias=EPS_ap).then_inc(S["sDSQ"], 1)
                    for h in range(DC):
                        e.wait_ge(S["sDEV"], 8 * t + h + 1)
                        if t >= 1:
                            e.wait_ge(S["sOUT"], 16 * t)
                        e.activation(sqt28[:, h * TD:(h + 1) * TD],
                                     r2bs[t % 2][:, h * TD:(h + 1) * TD],
                                     AF.Square).then_inc(S["sDCB"], 1)
                    if t == NT - 1:
                        ln_sc2(t)
                        e.wait_ge(S["sDVC"], t + 1)
                        e.activation(stx[0:1, 0:512], stx[0:1, 512:1024], AF.Sqrt,
                                     bias=EPS_ap).then_inc(S["sDSQ"], 1)
                _sc.__exit__(None, None, None)
            if eng == "sync":
                e.wait_ge(S["sDXL"], NT)
                e.dma_start(
                    out=out_fm.rearrange("k p n -> p k n")[:, :, (NT - 1) * TD:NT * TD],
                    in_=sqt28.ap().rearrange("p (k n) -> p k n", k=DC)
                ).then_inc(S["sOUT"], 16)
                e.wait_ge(S["sOUT"], 16 * NT)
            if eng == "gpsimd":
                e.wait_ge(S["sOUT"], 16 * NT)

        def stream(eng, e):
            with nc.named_scope("setup"):
                ph_setup(eng, e)
            with nc.named_scope("mixer"):
                for t in range(NT):
                    with nc.named_scope(f"mx{t}"):
                        ph_mixer_tile(eng, e, t)
                ph_mixer_tail(eng, e)
            with nc.named_scope("bar1"):
                ph_bar1_fill(eng, e)
                ph_gmax_barrier(eng, e, 0, 1)
                ph_bar1_post(eng, e)
            if eng == "gpsimd":
                ph_gapply_store(e, APT * NT - 5)
                ph_gapply_store(e, APT * NT - 4)
                ph_gapply(e, APT * NT - 2, do_store=False)
                ph_gapply(e, APT * NT - 1, do_store=False)
                ph_gapply_store(e, APT * NT - 3)
                ph_gapply_store(e, APT * NT - 2)
                ph_gapply_store(e, APT * NT - 1)
            ph_A_wu_loads(eng, e)
            with nc.named_scope("xq"):
                ph_xq(eng, e)
            if stop_after == "mixer":
                if eng == "sync":
                    e.wait_ge(S["sXQS"][0], 16 * (NT // 2))
                    e.wait_ge(S["sXQS"][1], 16 * (NT // 2))
                    e.wait_ge(S["sAWW"][0], 16 * 24)
                    e.wait_ge(S["sAWW"][1], 16 * 24)
                    e.dma_start(out=gdbg[:, 0:8], in_=mxc[:, :]).then_inc(S["sOUT"], 16)
                    e.dma_start(out=gdbg[:, 8:10], in_=scl[:, 0:2]).then_inc(S["sOUT"], 16)
                    e.wait_ge(S["sOUT"], 32)
                elif eng == "gpsimd":
                    e.wait_ge(S["sOUT"], 32)
                return
            with nc.named_scope("phA"):
                ph_A(eng, e)
            with nc.named_scope("bar2"):
                ph_bar2_fill(eng, e)
                ph_gmax_barrier(eng, e, 1, 2)
                ph_bar2_post(eng, e)
            if stop_after == "A":
                if eng == "sync":
                    e.wait_ge(S["sASP"][0], 16 * (NA // 8))
                    e.wait_ge(S["sASP"][1], 16 * (NA // 8))
                    e.wait_ge(S["bPRE1"], 6)
                    e.dma_start(out=gdbg[:, 0:6], in_=scl[:, 0:6]).then_inc(S["sOUT"], 16)
                    e.wait_ge(S["sOUT"], 16)
                elif eng == "gpsimd":
                    e.wait_ge(S["sOUT"], 16)
                return
            pbar(e, 1)
            with nc.named_scope("phB"):
                ph_B(eng, e)
            with nc.named_scope("bar3"):
                ph_bar3_fill(eng, e)
                ph_gmax_barrier(eng, e, 2, 1)
                ph_bar3_post(eng, e)
            if stop_after == "B":
                if eng == "sync":
                    e.wait_ge(S["sBSP"][0], 16 * (NB // 2))
                    e.wait_ge(S["sBSP"][1], 16 * (NB // 2))
                    e.wait_ge(S["bPRE2"], 6)
                    e.dma_start(out=gdbg[:, 0:8], in_=scl[:, 0:8]).then_inc(S["sOUT"], 16)
                    e.wait_ge(S["sOUT"], 16)
                elif eng == "gpsimd":
                    e.wait_ge(S["sOUT"], 16)
                return
            with nc.named_scope("phC"):
                ph_C(eng, e)
            with nc.named_scope("bar4"):
                ph_bar4_fill(eng, e)
                ph_gmax_barrier(eng, e, 3, 1)
                ph_bar4_post(eng, e)
            if stop_after == "C":
                if eng == "sync":
                    e.wait_ge(S["sCSP"][0], 16 * (NB // 2))
                    e.wait_ge(S["sCSP"][1], 16 * (NB // 2))
                    e.wait_ge(S["bPRE3"], 6)
                    e.dma_start(out=gdbg[:, 0:10], in_=scl[:, 0:10]).then_inc(S["sOUT"], 16)
                    e.wait_ge(S["sOUT"], 16)
                elif eng == "gpsimd":
                    e.wait_ge(S["sOUT"], 16)
                return
            pbar(e, 2)
            with nc.named_scope("phD"):
                ph_D(eng, e)

        with nc.Block() as bk:
            @bk.sync
            def _(e):
                stream("sync", FE(e, fences["sync"]))

            @bk.gpsimd
            def _(e):
                stream("gpsimd", FE(e, fences["gpsimd"]))

            @bk.tensor
            def _(e):
                stream("tensor", FE(e, fences["tensor"]))

            @bk.scalar
            def _(e):
                stream("scalar", FE(e, fences["scalar"]))

            @bk.vector
            def _(e):
                stream("vector", FE(e, fences["vector"]))
    return nc


def _prep_inputs(L0_attn_x, L0_attn_Wg, L0_attn_bg, L0_attn_Wv, L0_attn_bv,
                 L0_attn_Wd, L0_attn_bd, L0_ffn_w_gate, L0_ffn_w_up, L0_ffn_w_down):
    f = np.ascontiguousarray
    shared = {
        "wg_t": f(L0_attn_Wg.T).reshape(DC, 128, D),
        "wv_t": f(L0_attn_Wv.T).reshape(DC, 128, D),
        "wd_t": f(L0_attn_Wd.T).reshape(DC, 128, D),
        "bgh": f(L0_attn_bg.reshape(DC, 128).T),
        "bvh": f(L0_attn_bv.reshape(DC, 128).T),
        "bdh": f(L0_attn_bd.reshape(DC, 128).T),
        "wu_n": f(L0_ffn_w_up).reshape(ICH, 128, D),
        "wq_n": f(L0_ffn_w_gate).reshape(ICH, 128, D),
        "wdn_n": f(L0_ffn_w_down).reshape(DC, 128, I),
    }
    in_maps = []
    for c in range(NCORES):
        xc = f(L0_attn_x[2 * c:2 * c + 2].reshape(TOK, D).T).reshape(DC, 128, TOK)
        m = {"x_fm": xc}
        m.update(shared)
        in_maps.append(m)
    return in_maps


def _run(inputs, stop_after="full", debug=False):
    nc = build(stop_after=stop_after, debug=debug)
    in_maps = _prep_inputs(**{k: np.asarray(v) for k, v in inputs.items()})
    res = run_bass_kernel_spmd(nc, in_maps, core_ids=list(range(NCORES)))
    if debug:
        return res
    out = np.empty((B, S, D), dtype=np.float32)
    for c in range(NCORES):
        o = res.results[c]["out_fm"].reshape(D, TOK)
        out[2 * c:2 * c + 2] = o.T.reshape(2, S, D)
    return out


def kernel(**inputs):
    return _run(inputs)


# version 09
# speedup vs baseline: 1.0411x; 1.0182x over previous
import contextlib

import numpy as np
import concourse.bass as bass
import concourse.mybir as mybir
from concourse.bass_utils import run_bass_kernel_spmd

F32 = mybir.dt.float32
BF16 = mybir.dt.bfloat16
ALU = mybir.AluOpType
AF = mybir.ActivationFunctionType
AX = mybir.AxisListType

NCORES = 8
B, S, D, I = 16, 2048, 1024, 3072
TOK = (B // NCORES) * S
TD = 512
NT = TOK // TD
DC = D // 128
ICH = I // 128
QMAX = 32767.0
C_LO = 1.5 * 2.0 ** 23
C_DELTA = 1.5 * 2.0 ** 52 - 1.5 * 2.0 ** 23

ENGS = ("sync", "gpsimd", "tensor", "scalar", "vector")

NAPPLY = 72
NA = NT * 2 * ICH
NB = NT * (ICH // 4)
NJ = ICH // 4


def _ap_span(ap):
    try:
        dims = list(ap.ap)
        off = ap.offset
        span = 1
        for st, cnt in dims:
            span += abs(st) * (cnt - 1)
        refined = None
        if dims:
            pstr, pcnt = dims[0]
            rest = 1
            for st, cnt in dims[1:]:
                rest += abs(st) * (cnt - 1)
            if pstr >= rest and pstr > 0:
                p_lo = off // pstr
                c_lo = off % pstr
                refined = (pstr, p_lo, p_lo + pcnt, c_lo, c_lo + rest)
        return (ap.tensor.name, off, off + span, refined)
    except AttributeError:
        return None


def _overlap(k, lst):
    for (nm, lo, hi, ref) in lst:
        if nm != k[0]:
            continue
        if k[2] <= lo or hi <= k[1]:
            continue
        if ref is not None and k[3] is not None and ref[0] == k[3][0]:
            if (k[3][2] <= ref[1] or ref[2] <= k[3][1]
                    or k[3][4] <= ref[3] or ref[4] <= k[3][3]):
                continue
        return True
    return False


class FE:

    def __init__(self, eng, state):
        self._e = eng
        self._w = state["w"]
        self._r = state["r"]

    def __getattr__(self, name):
        return getattr(self._e, name)

    def drain(self, *a, **k):
        self._w.clear()
        self._r.clear()
        return self._e.drain(*a, **k)

    def _pre(self, outs, ins):
        need = False
        for ap in list(outs) + list(ins):
            if ap is None or isinstance(ap, (int, float)):
                continue
            k = _ap_span(ap)
            if k is not None and _overlap(k, self._w):
                need = True
                break
        if not need:
            for ap in outs:
                if ap is None or isinstance(ap, (int, float)):
                    continue
                k = _ap_span(ap)
                if k is not None and _overlap(k, self._r):
                    need = True
                    break
        if need:
            self._e.drain()
            self._w.clear()
            self._r.clear()

    def _post(self, outs, ins):
        for ap in outs:
            if ap is None or isinstance(ap, (int, float)):
                continue
            k = _ap_span(ap)
            if k is not None:
                self._w.append(k)
        for ap in ins:
            if ap is None or isinstance(ap, (int, float)):
                continue
            k = _ap_span(ap)
            if k is not None:
                self._r.append(k)

    def _op(self, outs, ins, fn):
        self._pre(outs, ins)
        r = fn()
        self._post(outs, ins)
        return r

    def memset(self, out, v):
        return self._op([out], [], lambda: self._e.memset(out, v))

    def tensor_scalar(self, out, in0, s1, s2, op0, op1=None):
        if op1 is not None:
            return self._op([out], [in0, s1, s2],
                            lambda: self._e.tensor_scalar(out, in0, s1, s2, op0, op1))
        return self._op([out], [in0, s1],
                        lambda: self._e.tensor_scalar(out, in0, s1, s2, op0))

    def tensor_tensor(self, out, a, b, op):
        return self._op([out], [a, b], lambda: self._e.tensor_tensor(out, a, b, op))

    def scalar_tensor_tensor(self, out, in0, sc, in1, op0, op1):
        return self._op([out], [in0, sc, in1],
                        lambda: self._e.scalar_tensor_tensor(out, in0, sc, in1, op0, op1))

    def tensor_tensor_scan(self, out, d0, d1, init, op0, op1):
        return self._op([out], [d0, d1, init],
                        lambda: self._e.tensor_tensor_scan(out, d0, d1, init, op0, op1))

    def reduce_max(self, out, in_, axis, apply_absolute_value=False):
        return self._op([out], [in_],
                        lambda: self._e.reduce_max(out, in_, axis=axis,
                                                   apply_absolute_value=apply_absolute_value))

    def reciprocal(self, out, in_):
        return self._op([out], [in_], lambda: self._e.reciprocal(out, in_))

    def activation(self, out, in_, func, bias=None, scale=None):
        kw = {}
        if bias is not None:
            kw["bias"] = bias
        if scale is not None:
            kw["scale"] = scale
        return self._op([out], [in_, bias, scale],
                        lambda: self._e.activation(out, in_, func, **kw))

    def affine_select(self, out, in_, compare_op, fill, base, pattern,
                      channel_multiplier):
        return self._op([out], [in_],
                        lambda: self._e.affine_select(
                            out=out, in_=in_, compare_op=compare_op, fill=fill,
                            base=base, pattern=pattern,
                            channel_multiplier=channel_multiplier))

    def tensor_copy(self, out, in_):
        return self._op([out], [in_], lambda: self._e.tensor_copy(out, in_))

    def dma_start(self, out, in_, **kw):
        self._pre([], [in_])
        return self._e.dma_start(out=out, in_=in_, **kw)


def build(stop_after="full", debug=False):
    nc = bass.Bass("TRN2", target_bir_lowering=False)
    dbg = "ExternalOutput" if debug else "Internal"

    x_fm = nc.dram_tensor("x_fm", [DC, 128, TOK], F32, kind="ExternalInput").ap()
    wg_t = nc.dram_tensor("wg_t", [DC, 128, D], F32, kind="ExternalInput").ap()
    wv_t = nc.dram_tensor("wv_t", [DC, 128, D], F32, kind="ExternalInput").ap()
    wd_t = nc.dram_tensor("wd_t", [DC, 128, D], F32, kind="ExternalInput").ap()
    bgh = nc.dram_tensor("bgh", [128, DC], F32, kind="ExternalInput").ap()
    bvh = nc.dram_tensor("bvh", [128, DC], F32, kind="ExternalInput").ap()
    bdh = nc.dram_tensor("bdh", [128, DC], F32, kind="ExternalInput").ap()
    wu_n = nc.dram_tensor("wu_n", [ICH, 128, D], F32, kind="ExternalInput").ap()
    wq_n = nc.dram_tensor("wq_n", [ICH, 128, D], F32, kind="ExternalInput").ap()
    wdn_n = nc.dram_tensor("wdn_n", [DC, 128, I], F32, kind="ExternalInput").ap()

    out_fm = nc.dram_tensor("out_fm", [DC, 128, TOK], F32, kind="ExternalOutput").ap()

    sp_x1 = nc.dram_tensor("sp_x1", [DC, 128, TOK], F32, kind=dbg).ap()
    sp_xq = nc.dram_tensor("sp_xq", [DC, 128, TOK], BF16, kind=dbg).ap()
    sp_up = nc.dram_tensor("sp_up", [ICH, 128, TOK], F32, kind=dbg).ap()
    sp_gt = nc.dram_tensor("sp_gt", [ICH, 128, TOK], F32, kind=dbg).ap()
    sp_p1 = nc.dram_tensor("sp_p1", [ICH, 128, TOK], F32, kind=dbg).ap()
    sp_p2 = nc.dram_tensor("sp_p2", [ICH, 128, TOK], F32, kind=dbg).ap()
    sp_wu = nc.dram_tensor("sp_wu", [ICH, 128, D], BF16, kind=dbg).ap()
    sp_wq = nc.dram_tensor("sp_wq", [ICH, 128, D], BF16, kind=dbg).ap()
    sp_wd = nc.dram_tensor("sp_wd", [DC, 128, I], BF16, kind=dbg).ap()
    ccb = []
    for bi in range(4):
        ci = nc.dram_tensor(f"cc_in{bi}", [128, 2], F32).ap()
        co = nc.dram_tensor(f"cc_out{bi}", [128, 2], F32).ap()
        ccb.append((ci, co))
    gdbg = nc.dram_tensor("gdbg", [128, 16], F32, kind=dbg).ap()

    fences = {k: {"w": [], "r": []} for k in ENGS}
    es = contextlib.ExitStack()
    with es:
        sbuf = lambda name, shape, dt=F32: es.enter_context(nc.sbuf_tensor(name, shape, dt))
        sem = lambda name: es.enter_context(nc.semaphore(name))

        cst = sbuf("cst", [128, 16])
        ident = sbuf("ident", [128, 128])
        ones_bf = sbuf("ones_bf", [128, 1], BF16)
        ones_r = sbuf("ones_r", [1, 128])
        ones_c = sbuf("ones_c", [128, 1])
        selm = sbuf("selm", [2, 256])
        biases = sbuf("biases", [128, 3 * DC])
        scl = sbuf("scl", [128, 16])
        gmaxs = sbuf("gmaxs", [128, 8])
        mx2 = sbuf("mx2", [128, 2])
        gsc = sbuf("gsc", [2, 1])
        stx = sbuf("stx", [1, 1024])
        abv = sbuf("abv", [1, 1024])
        carry = sbuf("carry", [128, DC])

        pb = [es.enter_context(nc.psum_tensor(f"pb{i}", [128, 512], F32))
              for i in range(8)]

        A0 = 40928
        LIM = 229344

        def at(name, shape, dt, off):
            end = off + shape[1] * mybir.dt.size(dt)
            assert end <= LIM, (name, end)
            return nc.alloc_sbuf_tensor_at(name, shape, dt, offset=off)

        wgm = at("wgm", [128, DC * D], BF16, A0)
        wvm = at("wvm", [128, DC * D], BF16, A0 + 16384)
        wdm = at("wdm", [128, DC * D], BF16, A0 + 32768)
        xf = [at(f"xf{i}", [128, DC * TD], F32, A0 + 49152 + 16384 * i) for i in range(2)]
        xbm = [at(f"xbm{i}", [128, DC * TD], BF16, A0 + 81920 + 8192 * i) for i in range(2)]
        wk_xg = [at(f"wkxg{i}", [128, TD], F32, A0 + 98304 + 2048 * i) for i in range(2)]
        wk_tv = [at(f"wktv{i}", [128, TD], F32, A0 + 102400 + 2048 * i) for i in range(2)]
        wk_sa = [at(f"wksa{i}", [128, TD], F32, A0 + 106496 + 2048 * i) for i in range(2)]
        wk_xs1 = at("wkxs", [128, TD], F32, A0 + 110592)
        wk_av1 = at("wkav", [128, TD], F32, A0 + 112640)
        hbuf = at("hbuf", [128, 2 * TD], F32, A0 + 114688)
        rbuf = at("rbuf", [128, DC * TD], F32, A0 + 118784)
        sqt8 = at("sqt8", [128, DC * TD], F32, A0 + 135168)
        rbuf2 = at("rbuf2", [128, DC * TD], F32, A0 + 151552)
        rbufs = [rbuf, rbuf2]
        mxc = at("mxc", [128, 8], F32, A0 + 167936)
        DQ0 = A0 + 168000
        dq_nat = [at(f"dqn{i}", [128, D], F32, DQ0 + 4096 * i) for i in range(2)]
        dq_wb = [at(f"dqwb{i}", [128, D], BF16, DQ0 + 8192 + 2048 * i) for i in range(2)]
        dq_sc = at("dqsc", [128, 128], F32, DQ0 + 12288)
        wk_xs2 = at("wkxs2", [128, TD], F32, DQ0 + 12800)
        wk_av2 = at("wkav2", [128, TD], F32, DQ0 + 14848)
        wk_xs = [wk_xs1, wk_xs2]
        wk_av = [wk_av1, wk_av2]

        wu_sb = at("wu_sb", [128, DC * I], BF16, A0)
        wq_sb = at("wq_sb", [128, DC * I], BF16, A0 + 49152)
        xqa = [at(f"xqa{i}", [128, DC * TD], BF16, A0 + 98304 + 8192 * i) for i in range(2)]
        gbuf = [at(f"gbuf{i}", [128, 2048], F32, A0 + 114688 + 8192 * i) for i in range(2)]
        astat = at("astat", [128, 96], F32, A0 + 131072)

        ld1 = [at(f"ld1_{i}", [128, 2048], F32, A0 + 8192 * i) for i in range(2)]
        ld2 = [at(f"ld2_{i}", [128, 2048], F32, A0 + 16384 + 8192 * i) for i in range(2)]
        tb1 = [at(f"tb1_{i}", [128, 2048], F32, A0 + 32768 + 8192 * i) for i in range(2)]
        tb2 = [at(f"tb2_{i}", [128, 2048], F32, A0 + 49152 + 8192 * i) for i in range(2)]
        tb3 = [at(f"tb3_{i}", [128, 2048], F32, A0 + 65536 + 8192 * i) for i in range(2)]
        tb4 = [at(f"tb4_{i}", [128, 2048], F32, A0 + 81920 + 8192 * i) for i in range(2)]
        tb5 = [at(f"tb5_{i}", [128, 2048], F32, A0 + 98304 + 8192 * i) for i in range(2)]
        bstat = at("bstat", [128, 48], F32, A0 + 114688)

        wd_sb = at("wd_sb", [128, ICH * D], BF16, A0)
        otile = [at(f"otile{i}", [128, ICH * TD], BF16, A0 + 49152 + 24576 * i) for i in range(2)]
        p2ld = [at(f"p2ld{i}", [128, 2048], F32, A0 + 98304 + 8192 * i) for i in range(2)]
        t5b = at("t5b", [128, 2048], F32, A0 + 114688)
        x1d = at("x1d", [128, DC * TD], F32, A0 + 122880)
        r2b = at("r2b", [128, DC * TD], F32, A0 + 139264)
        r2b2 = at("r2b2", [128, DC * TD], F32, A0 + 155648)
        r2bs = [r2b, r2b2]
        sqt28 = at("sqt28", [128, DC * TD], F32, A0 + 172032)

        S = {n: sem(n) for n in (
            "sWM sWC sMM sEP sRS sCB sSTe sST sSE sVC sSQ sAB sBC sXL sLNE sDLNE sVX "
            "sSPX sPB sXQV sANL sAPC "
            "sWL2 sWQL sWDL sAMM sAEV sARD sBS sBR sBV sCS sCV "
            "sDS2 sDV2 sDX1 sDMM sDEV sDCB sDST sDSTe sDSE2 sDVC sDSQ "
            "sDAB sDBC sDXL sOUT sSU".split())}
        for pn in ("sLX sLB sXQL sXQA sAWW sXQS sASP sBL sBSP "
                   "sCL sCSP sDL".split()):
            S[pn] = [sem(pn + "0"), sem(pn + "1")]
        for bi in range(4):
            S[f"bPRE{bi}"] = sem(f"bPRE{bi}")
            S[f"bDMA{bi}"] = sem(f"bDMA{bi}")
            S[f"bCC{bi}"] = sem(f"bCC{bi}")

        C_ap = cst[:, 1:2]
        negC_ap = cst[:, 2:3]
        EPS_ap = cst[0:1, 5:6]

        apply_steps = []
        for c in range(ICH):
            apply_steps.append((wu_n[c][:, :], sp_wu[c][:, :]))
        for c in range(ICH):
            apply_steps.append((wq_n[c][:, :], sp_wq[c][:, :]))
        for hc in range(DC):
            for j in range(3):
                apply_steps.append((wdn_n[hc][:, j * D:(j + 1) * D],
                                    sp_wd[hc][:, j * D:(j + 1) * D]))
        assert len(apply_steps) == NAPPLY
        APT = 6

        def applies_for_tile(t):
            if t == NT - 1:
                return range(APT * t, APT * t + 2)
            return range(APT * t, APT * (t + 1))

        def x_src(t):
            return x_fm.rearrange("k p n -> p k n")[:, :, t * TD:(t + 1) * TD]

        def pbar(e, k):
            e.drain()
            if hasattr(e._e, "engine_nop"):
                e._e.engine_nop().then_inc(S["sPB"], 1)
            else:
                e._e.nop().then_inc(S["sPB"], 1)
            e._e.wait_ge(S["sPB"], 5 * k)
            e._w.clear()
            e._r.clear()


        def ph_setup(eng, e):
            if eng == "gpsimd":
                e.memset(ident[:, :], 0.0)
                e.affine_select(out=ident[:, :], in_=ident[:, :],
                                compare_op=ALU.not_equal, fill=1.0, base=0,
                                pattern=[[-1, 128]], channel_multiplier=1)
                e.memset(ones_bf[:, :], 1.0)
                e.memset(ones_c[:, :], 1.0)
                e.memset(ones_r[:, :], 1.0)
                e.memset(selm[0:2, :], 1.0)
                e.affine_select(out=selm[0:2, :], in_=selm[0:2, :],
                                compare_op=ALU.is_ge, fill=0.0, base=0,
                                pattern=[[1, 256]], channel_multiplier=-128)
                e.affine_select(out=selm[0:2, :], in_=selm[0:2, :],
                                compare_op=ALU.is_ge, fill=0.0, base=127,
                                pattern=[[-1, 256]], channel_multiplier=128)
                e._e.engine_nop().then_inc(S["sSU"], 1)
                for dst, srcw in ((wgm, wg_t), (wvm, wv_t), (wdm, wd_t)):
                    e.dma_start(out=dst.ap().rearrange("p (k n) -> p k n", k=DC),
                                in_=srcw.rearrange("k p n -> p k n")).then_inc(S["sWC"], 16)
            elif eng == "vector":
                e.memset(cst[:, 3:4], 1.0)
                e.memset(cst[:, 5:6], 1e-5)
                e.tensor_scalar(cst[:, 4:5], cst[:, 3:4], 2.0 ** -30, 1.0,
                                ALU.add, ALU.subtract)
                e.tensor_scalar(cst[:, 0:1], cst[:, 4:5], 2.0 ** 30, 1.0,
                                ALU.mult, ALU.mult)
                e.tensor_scalar(cst[:, 1:2], cst[:, 0:1], C_DELTA, C_LO,
                                ALU.mult, ALU.add)
                e.tensor_scalar(cst[:, 2:3], cst[:, 1:2], -1.0, 0.0, ALU.mult, ALU.add)
            elif eng == "sync":
                e.dma_start(out=biases[:, 0:DC], in_=bgh[:, :]).then_inc(S["sWM"], 16)
                e.dma_start(out=biases[:, DC:2 * DC], in_=bvh[:, :]).then_inc(S["sWM"], 16)
                e.dma_start(out=biases[:, 2 * DC:3 * DC], in_=bdh[:, :]).then_inc(S["sWM"], 16)

        def ph_gapply_store(e, s):
            e.wait_ge(S["sAPC"], s + 1)
            _, pdst = apply_steps[s]
            e.dma_start(out=pdst,
                        in_=dq_wb[s % 2][:, :]).then_inc(S["sAWW"][s % 2], 16)

        def ph_gapply(e, s, do_store=True):
            src, dst = apply_steps[s]
            if s >= 1:
                e.wait_ge(S["sAPC"], s - 1)
            e.dma_start(out=dq_nat[s % 2][:, :], in_=src).then_inc(S["sANL"], 16)
            if s >= 1 and do_store:
                ph_gapply_store(e, s - 1)

        def ph_vapply2(e, s):
            na, nb_ = dq_nat[s % 2], dq_nat[(s + 1) % 2]
            base = dq_sc[:, :]
            bc = lambda c0: bass.AP(base.tensor, base.offset + c0,
                                    [[128, 128], [1, 32], [0, 32]])
            e.wait_ge(S["sANL"], 16 * (s + 2))
            if s >= 2:
                e.wait_ge(S["sAWW"][0], 16 * (s // 2))
                e.wait_ge(S["sAWW"][1], 16 * (s // 2))
            na3 = na.ap().rearrange("p (b x) -> p b x", x=32)
            nb3 = nb_.ap().rearrange("p (b x) -> p b x", x=32)
            e.reduce_max(dq_sc[:, 0:32], na3, axis=AX.X, apply_absolute_value=True)
            e.reduce_max(dq_sc[:, 32:64], nb3, axis=AX.X, apply_absolute_value=True)
            e.tensor_scalar(dq_sc[:, 0:64], dq_sc[:, 0:64], 1.0 / 7.0, 1e-12,
                            ALU.mult, ALU.max)
            e.reciprocal(dq_sc[:, 64:128], dq_sc[:, 0:64])
            e.tensor_tensor(na3, na3, bc(64), ALU.mult)
            e.tensor_scalar(na[:, :], na[:, :], C_ap, C_ap, ALU.add, ALU.subtract)
            e.tensor_tensor(dq_wb[s % 2].ap().rearrange("p (b x) -> p b x", x=32),
                            na3, bc(0), ALU.mult).then_inc(S["sAPC"], 1)
            e.tensor_tensor(nb3, nb3, bc(96), ALU.mult)
            e.tensor_scalar(nb_[:, :], nb_[:, :], C_ap, C_ap, ALU.add, ALU.subtract)
            e.tensor_tensor(dq_wb[(s + 1) % 2].ap().rearrange("p (b x) -> p b x", x=32),
                            nb3, bc(32), ALU.mult).then_inc(S["sAPC"], 1)
        def ph_mixer_tile(eng, e, t):
            if eng == "sync":
                if t == 0:
                    for tt in (0, 1):
                        e.dma_start(out=xf[tt].ap().rearrange("p (k n) -> p k n", k=DC),
                                    in_=x_src(tt)).then_inc(S["sLX"][tt % 2], 16)
                if t >= 1:
                    e.wait_ge(S["sXL"], t)
                    e.dma_start(
                        out=sp_x1.rearrange("k p n -> p k n")[:, :,
                                                              (t - 1) * TD:t * TD],
                        in_=xf[(t - 1) % 2].ap().rearrange("p (k n) -> p k n", k=DC)
                    ).then_inc(S["sSPX"], 16)
                    if t + 1 < NT:
                        e.wait_ge(S["sSPX"], 16 * t)
                        e.dma_start(out=xf[(t + 1) % 2].ap().rearrange("p (k n) -> p k n", k=DC),
                                    in_=x_src(t + 1)).then_inc(S["sLX"][(t + 1) % 2], 16)
            elif eng == "gpsimd":
                if t == 0:
                    for tt in (0, 1):
                        e.dma_start(out=xbm[tt].ap().rearrange("p (k n) -> p k n", k=DC),
                                    in_=x_src(tt)).then_inc(S["sLB"][tt % 2], 16)
                for s in applies_for_tile(t):
                    ph_gapply(e, s)
                if t == NT - 1:
                    ph_gapply(e, APT * NT - 4, do_store=False)
                    ph_gapply(e, APT * NT - 3, do_store=False)
                if t + 2 < NT:
                    e.wait_ge(S["sMM"], 8 * (t + 1))
                    e.dma_start(out=xbm[t % 2].ap().rearrange("p (k n) -> p k n", k=DC),
                                in_=x_src(t + 2)).then_inc(S["sLB"][t % 2], 16)
            elif eng == "tensor":
                if t == 0:
                    e.wait_ge(S["sWM"], 48)
                    e.wait_ge(S["sWC"], 48)
                e.wait_ge(S["sLB"][t % 2], 16 * (t // 2 + 1))
                e.wait_ge(S["sLX"][t % 2], 16 * (t // 2 + 1))

                def stats(ts):
                    if ts >= 2:
                        e.wait_ge(S["sVC"], ts)
                    for ei in range(DC):
                        e.wait_ge(S["sCB"], 8 * ts + ei + 1)
                        if ei < DC - 1:
                            e.matmul(pb[3][0:1, :], ones_c[:, :],
                                     rbufs[ts % 2][:, ei * TD:(ei + 1) * TD],
                                     start=(ei == 0), stop=False)
                            e.matmul(pb[4][0:1, :], ones_c[:, :], sqt8[:, ei * TD:(ei + 1) * TD],
                                     start=(ei == 0), stop=False).then_inc(S["sSTe"], 1)
                        else:
                            e.matmul(pb[4][0:1, :], ones_c[:, :], sqt8[:, ei * TD:(ei + 1) * TD],
                                     start=False, stop=True).then_inc(S["sSTe"], 1)
                            e.matmul(pb[3][0:1, :], ones_c[:, :],
                                     rbufs[ts % 2][:, ei * TD:(ei + 1) * TD],
                                     start=False, stop=True).then_inc(S["sST"], 1)

                def bcast(ts):
                    e.wait_ge(S["sAB"], ts + 1)
                    if ts >= 1:
                        e.wait_ge(S["sXL"], ts)
                    e.matmul(pb[4][:, :], ones_r[:, :], abv[0:1, 512:1024],
                             start=True, stop=True)
                    e.matmul(pb[5][:, :], ones_r[:, :], stx[0:1, 0:512],
                             start=True, stop=True).then_inc(S["sBC"], 1)

                if t >= 1:
                    stats(t - 1)
                for ei in range(DC):
                    if 8 * t + ei >= 1:
                        e.wait_ge(S["sEP"], 8 * t + ei)
                    last = None
                    for wi, wsb in enumerate((wgm, wvm, wdm)):
                        for k in range(DC):
                            last = e.matmul(
                                pb[wi][:, :],
                                wsb[:, k * D + 128 * ei: k * D + 128 * ei + 128],
                                xbm[t % 2][:, k * TD:(k + 1) * TD],
                                start=(k == 0), stop=(k == DC - 1))
                    last.then_inc(S["sMM"], 1)
                    if ei == 3 and t >= 1:
                        bcast(t - 1)
                if t == NT - 1:
                    stats(t)
                    bcast(t)
            elif eng == "scalar":
                def sq_m(k):
                    e.wait_ge(S["sRS"], 8 * t + 2 * (k // 2 + 1))
                    if t >= 1:
                        e.wait_ge(S["sSTe"], 8 * (t - 1) + k + 1)
                    e.activation(sqt8[:, k * TD:(k + 1) * TD],
                                 rbufs[t % 2][:, k * TD:(k + 1) * TD],
                                 AF.Square).then_inc(S["sCB"], 1)

                def ln_sc(ts):
                    e.wait_ge(S["sST"], ts + 1)
                    e.activation(stx[0:1, 0:512], pb[3][0:1, :], AF.Identity)
                    e.activation(stx[0:1, 512:1024], pb[4][0:1, :], AF.Identity)
                    e.activation(abv[0:1, 0:512], stx[0:1, 0:512], AF.Identity,
                                 scale=1.0 / 1024.0)
                    e.activation(abv[0:1, 512:1024], stx[0:1, 512:1024], AF.Identity,
                                 scale=1.0 / 1024.0)
                    e.activation(stx[0:1, 0:512], abv[0:1, 0:512],
                                 AF.Square).then_inc(S["sLNE"], 1)

                if t >= 1:
                    ln_sc(t - 1)
                    e.wait_ge(S["sVC"], t)
                    e.activation(stx[0:1, 0:512], stx[0:1, 512:1024], AF.Sqrt,
                                 bias=EPS_ap).then_inc(S["sSQ"], 1)
                for ei in range(DC):
                    e.wait_ge(S["sMM"], 8 * t + ei + 1)
                    if 8 * t + ei >= 2:
                        e.wait_ge(S["sVX"], 8 * t + ei - 1)
                    e.activation(wk_xg[ei % 2][:, :], pb[0][:, :], AF.Sigmoid,
                                 bias=biases[:, ei:ei + 1])
                    e.activation(wk_tv[ei % 2][:, :], pb[1][:, :], AF.Tanh,
                                 bias=biases[:, DC + ei:DC + ei + 1])
                    e.activation(wk_sa[ei % 2][:, :], pb[2][:, :], AF.Sigmoid,
                                 bias=biases[:, 2 * DC + ei:2 * DC + ei + 1]
                                 ).then_inc(S["sEP"], 1)
                    if ei >= 2:
                        sq_m(ei - 2)
                sq_m(DC - 2)
                sq_m(DC - 1)
                if t == NT - 1:
                    ln_sc(t)
                    e.wait_ge(S["sVC"], t + 1)
                    e.activation(stx[0:1, 0:512], stx[0:1, 512:1024], AF.Sqrt,
                                 bias=EPS_ap).then_inc(S["sSQ"], 1)
            elif eng == "vector":
                def ln_post(ts):
                    e.wait_ge(S["sLNE"], ts + 1)
                    e.tensor_tensor(stx[0:1, 512:1024], abv[0:1, 512:1024],
                                    stx[0:1, 0:512], ALU.subtract).then_inc(S["sVC"], 1)
                    e._w.clear()
                    e._r.clear()
                    e.drain()

                def ab_post(ts):
                    e.wait_ge(S["sSQ"], ts + 1)
                    e.reciprocal(abv[0:1, 512:1024], stx[0:1, 0:512])
                    e.scalar_tensor_tensor(stx[0:1, 0:512], abv[0:1, 0:512], -1.0,
                                           abv[0:1, 512:1024], ALU.mult,
                                           ALU.mult).then_inc(S["sAB"], 1)

                def affine(ts):
                    e.wait_ge(S["sBC"], ts + 1)
                    if ts >= 1:
                        e.wait_ge(S["sSPX"], 16 * ts)
                    p4 = pb[4][:, :]
                    p5 = pb[5][:, :]
                    b4 = bass.AP(p4.tensor, p4.offset, [[p4.ap[0][0], 128], [0, DC], [1, TD]])
                    b5 = bass.AP(p5.tensor, p5.offset, [[p5.ap[0][0], 128], [0, DC], [1, TD]])
                    xv = xf[ts % 2].ap().rearrange("p (b x) -> p b x", x=TD)
                    rv = rbufs[ts % 2].ap().rearrange("p (b x) -> p b x", x=TD)
                    e.tensor_tensor(xv, rv, b4, ALU.mult)
                    e.tensor_tensor(xv, xv, b5, ALU.add)
                    e.reduce_max(mxc[:, ts:ts + 1], xf[ts % 2][:, :], axis=AX.X,
                                 apply_absolute_value=True).then_inc(S["sXL"], 1)

                if t >= 1:
                    ln_post(t - 1)
                rb = rbufs[t % 2]
                for ei in range(DC):
                    e.wait_ge(S["sEP"], 8 * t + ei + 1)
                    sl = (ei % 2) * TD
                    e.tensor_tensor(wk_xs[ei % 2][:, :], wk_xg[ei % 2][:, :],
                                    wk_tv[ei % 2][:, :], ALU.mult)
                    e.tensor_scalar(wk_av[ei % 2][:, :], wk_sa[ei % 2][:, :],
                                    0.998, 0.001, ALU.mult,
                                    ALU.add).then_inc(S["sVX"], 1)
                    if ei >= 2 and ei % 2 == 0:
                        e.tensor_tensor(rb[:, (ei - 2) * TD:ei * TD],
                                        xf[t % 2][:, (ei - 2) * TD:ei * TD],
                                        hbuf[:, 0:2 * TD], ALU.add).then_inc(S["sRS"], 2)
                    if ei >= 1:
                        psl = ((ei - 1) % 2) * TD
                        e.tensor_scalar(carry[:, ei - 1:ei],
                                        hbuf[:, psl + TD - 1:psl + TD],
                                        1.0, 0.0, ALU.mult, ALU.add)
                    init = 0.0 if t % 4 == 0 else carry[:, ei:ei + 1]
                    e.tensor_tensor_scan(hbuf[:, sl:sl + TD], wk_av[ei % 2][:, :],
                                         wk_xs[ei % 2][:, :], init, ALU.mult, ALU.add)
                    if ei == 1 and t >= 1:
                        ab_post(t - 1)
                    if ei == 3 and t >= 1:
                        affine(t - 1)
                    if ei in (1, 3, 5) and (t < NT - 1 or ei == 1):
                        ph_vapply2(e, APT * t + ei - 1)
                e.tensor_scalar(carry[:, DC - 1:DC],
                                hbuf[:, ((DC - 1) % 2) * TD + TD - 1:
                                      ((DC - 1) % 2) * TD + TD],
                                1.0, 0.0, ALU.mult, ALU.add)
                e.tensor_tensor(rb[:, (DC - 2) * TD:DC * TD],
                                xf[t % 2][:, (DC - 2) * TD:DC * TD],
                                hbuf[:, 0:2 * TD], ALU.add).then_inc(S["sRS"], 2)
                if t == NT - 1:
                    ln_post(t)
                    ab_post(t)
                    affine(t)

        def ph_mixer_tail(eng, e):
            if eng == "sync":
                e.wait_ge(S["sXL"], NT)
                e.dma_start(
                    out=sp_x1.rearrange("k p n -> p k n")[:, :, (NT - 1) * TD:NT * TD],
                    in_=xf[(NT - 1) % 2].ap().rearrange("p (k n) -> p k n", k=DC)
                ).then_inc(S["sSPX"], 16)
        def ph_gmax_barrier(eng, e, bi, k):
            cin, cout = ccb[bi]
            bP, bD, bC = S[f"bPRE{bi}"], S[f"bDMA{bi}"], S[f"bCC{bi}"]
            if eng == "sync":
                e.wait_ge(bP, 1)
                e.dma_start(out=cin[:, :], in_=mx2[:, :]).then_inc(bD, 16)
                e.wait_ge(bC, 1)
                e.dma_start(out=mx2[:, :], in_=cout[:, :]).then_inc(bD, 16)
            elif eng == "gpsimd":
                e.wait_ge(bD, 16)
                e.collective_compute(
                    "AllReduce", ALU.max, replica_groups=[list(range(NCORES))],
                    ins=[cin[:, :]], outs=[cout[:, :]]).then_inc(bC, 1)
            elif eng == "tensor":
                e.wait_ge(bD, 32)
                e.transpose(pb[7][0:2, 0:128], mx2[:, :], ident[:, :]).then_inc(bP, 1)
                e.wait_ge(bP, 3)
                m = None
                for j in range(k):
                    m = e.matmul(pb[6][:, j:j + 1], selm[0:2, 128 * j:128 * j + 128],
                                 gsc[0:2, 0:1], start=True, stop=True)
                m.then_inc(bP, 1)
            elif eng == "scalar":
                e.wait_ge(bP, 4)
                e.activation(gmaxs[:, 2 * bi:2 * bi + k], pb[6][:, 0:k],
                             AF.Copy).then_inc(bP, 1)
            elif eng == "vector":
                e.wait_ge(bP, 2)
                e.reduce_max(gsc[0:2, 0:1], pb[7][0:2, 0:128], axis=AX.X).then_inc(bP, 1)
                e.wait_ge(bP, 5)

        def ph_bar1_fill(eng, e):
            if eng == "vector":
                e.wait_ge(S["sXL"], NT)
                e.reduce_max(mx2[:, 0:1], mxc[:, 0:NT], axis=AX.X)
                e.reduce_max(mx2[:, 1:2], mxc[:, 0:NT], axis=AX.X).then_inc(S["bPRE0"], 1)
                ph_vapply2(e, APT * NT - 4)
                ph_vapply2(e, APT * NT - 2)

        def ph_bar1_post(eng, e):
            if eng == "vector":
                e.tensor_scalar(scl[:, 0:1], gmaxs[:, 0:1], 1.0 / QMAX, 1e-12,
                                ALU.mult, ALU.max)
                e.reciprocal(scl[:, 1:2], scl[:, 0:1]).then_inc(S["bPRE0"], 1)

        def ph_xq(eng, e):
            for t in range(NT):
                if eng == "sync":
                    if t == 0:
                        e.wait_ge(S["sSPX"], 16 * NT)
                    if t >= 2:
                        e.wait_ge(S["sXQV"], t - 1)
                    e.dma_start(
                        out=xf[t % 2].ap().rearrange("p (k n) -> p k n", k=DC),
                        in_=sp_x1.rearrange("k p n -> p k n")[:, :, t * TD:(t + 1) * TD]
                    ).then_inc(S["sXQL"][t % 2], 16)
                elif eng == "vector":
                    e.wait_ge(S["sXQL"][t % 2], 16 * (t // 2 + 1))
                    if t == 0:
                        e.wait_ge(S["bPRE0"], 6)
                    if t >= 2:
                        e.wait_ge(S["sXQS"][t % 2], 16 * ((t - 2) // 2 + 1))
                    if t in (2, 3):
                        e.wait_ge(S["sXQA"][t % 2], 16)
                    e.tensor_scalar(sqt8[:, :], xf[t % 2][:, :], scl[:, 1:2], C_ap,
                                    ALU.mult, ALU.add)
                    e.tensor_scalar(xbm[t % 2][:, :], sqt8[:, :], C_ap, scl[:, 0:1],
                                    ALU.subtract, ALU.mult).then_inc(S["sXQV"], 1)
                    for g in xq_reds.get(t, ()):
                        red_a(e, g)
                if eng == "sync":
                    e.wait_ge(S["sXQV"], t + 1)
                    e.dma_start(
                        out=sp_xq.rearrange("k p n -> p k n")[:, :, t * TD:(t + 1) * TD],
                        in_=xbm[t % 2].ap().rearrange("p (k n) -> p k n", k=DC)
                    ).then_inc(S["sXQS"][t % 2], 16)
                    if t <= 1:
                        e.dma_start(out=xqa[t][:, :],
                                    in_=xbm[t][:, :]).then_inc(S["sXQA"][t % 2], 16)

        aorder = [(0, 0), (1, 0), (0, 1), (1, 1)]
        for t in range(2, NT):
            aorder.append((t, 0))
            aorder.append((t, 1))

        def red_a(e, g):
            oi = g // (ICH // 4)
            gg = g % (ICH // 4)
            t, w = aorder[oi]
            n = oi * ICH + gg * 4 + 3
            col = w * 48 + t * 6 + gg
            e.wait_ge(S["sAEV"], n + 1)
            e.reduce_max(astat[:, col:col + 1], gbuf[g % 2][:, :], axis=AX.X,
                         apply_absolute_value=True).then_inc(S["sARD"], 1)

        xq_reds = {2: [0], 3: [1, 2], 4: [3, 4], 5: [5, 6], 6: [7, 8],
                   7: [9, 10, 11]}

        def ph_A_wu_loads(eng, e):
            if eng == "sync":
                e.wait_ge(S["sMM"], 8 * NT)
                e.wait_ge(S["sAWW"][0], 16 * 12)
                e.wait_ge(S["sAWW"][1], 16 * 12)
                flat = sp_wu.rearrange("c p n -> (c p) n")
                for k in range(DC):
                    e.dma_start(out=wu_sb[:, k * I:(k + 1) * I],
                                in_=flat[:, 128 * k:128 * (k + 1)],
                                transpose=True).then_inc(S["sWL2"], 16)

        def ph_A(eng, e):
            if eng == "sync":
                e.wait_ge(S["sXQS"][0], 16 * (NT // 2))
                e.wait_ge(S["sXQS"][1], 16 * (NT // 2))
                e.wait_ge(S["sAWW"][0], 16 * 24)
                e.wait_ge(S["sAWW"][1], 16 * 24)
                flat = sp_wq.rearrange("c p n -> (c p) n")
                for k in range(DC):
                    e.dma_start(out=wq_sb[:, k * I:(k + 1) * I],
                                in_=flat[:, 128 * k:128 * (k + 1)],
                                transpose=True).then_inc(S["sWQL"], 16)
            for oi, (t, w) in enumerate(aorder):
                _sc = nc.named_scope(f"pa{oi}")
                _sc.__enter__()
                wsb = wu_sb if w == 0 else wq_sb
                spw = sp_up if w == 0 else sp_gt
                if eng == "tensor":
                    if oi == 0:
                        e.wait_ge(S["sWL2"], 16 * 8)
                        e.wait_ge(S["sEP"], 8 * NT)
                        e.wait_ge(S["sVC"], NT)
                    if oi == 2:
                        e.wait_ge(S["sWQL"], 16 * 8)
                    if w == 0:
                        e.wait_ge(S["sXQA"][t % 2], 16 * (t // 2 + 1))
                    for i in range(ICH):
                        n = oi * ICH + i
                        if n >= 4:
                            e.wait_ge(S["sAEV"], n - 3)
                        m = None
                        for k in range(DC):
                            m = e.matmul(
                                pb[n % 4][:, :],
                                wsb[:, k * I + 128 * i: k * I + 128 * i + 128],
                                xqa[t % 2][:, k * TD:(k + 1) * TD],
                                start=(k == 0), stop=(k == DC - 1))
                        m.then_inc(S["sAMM"], 1)
                elif eng == "scalar":
                    for i in range(ICH):
                        n = oi * ICH + i
                        g = n // 4
                        if n == 0:
                            e.wait_ge(S["sST"], NT)
                            e.wait_ge(S["sXL"], NT)
                        if g >= 2:
                            e.wait_ge(S["sASP"][g % 2], 16 * ((g - 2) // 2 + 1))
                            e.wait_ge(S["sARD"], g - 1)
                        e.wait_ge(S["sAMM"], n + 1)
                        e.activation(gbuf[g % 2][:, 512 * (i % 4):512 * (i % 4) + 512],
                                     pb[n % 4][:, :], AF.Copy).then_inc(S["sAEV"], 1)
                        if i % 4 == 3:
                            e.dma_start(
                                out=spw.rearrange("k p n -> p k n")[:, i - 3:i + 1,
                                                                    t * TD:(t + 1) * TD],
                                in_=gbuf[g % 2].ap().rearrange("p (c n) -> p c n", c=4)
                            ).then_inc(S["sASP"][g % 2], 16)
                elif eng == "vector":
                    if oi >= 2:
                        for i in range(3, ICH, 4):
                            red_a(e, oi * (ICH // 4) + i // 4)
                elif eng == "sync":
                    if t + 2 < NT and (w == 1 if t >= 2 else oi == 3):
                        for tl in ((t + 2,) if t >= 2 else (2, 3)):
                            e.wait_ge(S["sAMM"], (aorder.index((tl - 2, 1)) + 1) * ICH)
                            e.dma_start(
                                out=xqa[tl % 2].ap().rearrange("p (k n) -> p k n", k=DC),
                                in_=sp_xq.rearrange("k p n -> p k n")[:, :,
                                                                      tl * TD:(tl + 1) * TD]
                            ).then_inc(S["sXQA"][tl % 2], 16)
                _sc.__exit__(None, None, None)

        def ph_bar2_fill(eng, e):
            if eng == "vector":
                e.wait_ge(S["sARD"], NA // 4)
                e.reduce_max(mx2[:, 0:1], astat[:, 0:48], axis=AX.X)
                e.reduce_max(mx2[:, 1:2], astat[:, 48:96], axis=AX.X).then_inc(S["bPRE1"], 1)

        def ph_bar2_post(eng, e):
            if eng == "vector":
                e.tensor_scalar(scl[:, 4:5], gmaxs[:, 2:3], 1.0 / QMAX, 1e-12,
                                ALU.mult, ALU.max)
                e.tensor_scalar(scl[:, 2:3], gmaxs[:, 3:4], 1.0 / QMAX, 1e-12,
                                ALU.mult, ALU.max)
                e.reciprocal(scl[:, 5:6], scl[:, 4:5])
                e.reciprocal(scl[:, 3:4], scl[:, 2:3]).then_inc(S["bPRE1"], 1)

        def ph_B(eng, e):
            def vtail(n):
                e.wait_ge(S["sBS"], 2 * (n + 1))
                if n >= 2:
                    e.wait_ge(S["sBSP"][n % 2], 16 * ((n - 2) // 2 + 1))
                e.tensor_scalar(tb5[n % 2][:, :], tb4[n % 2][:, :], C_ap, 65535.0,
                                ALU.subtract, ALU.min)
                e.tensor_tensor(tb5[n % 2][:, :], tb2[n % 2][:, :],
                                tb5[n % 2][:, :], ALU.mult)
                e.reduce_max(bstat[:, n:n + 1], tb5[n % 2][:, :], axis=AX.X,
                             apply_absolute_value=True).then_inc(S["sBV"], 3)

            for n in range(NB):
                t, j = divmod(n, ICH // 4)
                if eng == "sync":
                    if n == 0:
                        e.wait_ge(S["sASP"][0], 16 * (NA // 8))
                        e.wait_ge(S["sASP"][1], 16 * (NA // 8))
                    if n >= 2:
                        e.wait_ge(S["sBR"], n - 1)
                    e.dma_start(
                        out=ld1[n % 2].ap().rearrange("p (c x) -> p c x", c=4),
                        in_=sp_gt.rearrange("k p n -> p k n")[:, 4 * j:4 * j + 4,
                                                              t * TD:(t + 1) * TD]
                    ).then_inc(S["sBL"][n % 2], 16)
                elif eng == "vector":
                    e.wait_ge(S["sBL"][n % 2], 16 * (n // 2 + 1))
                    if n == 0:
                        e.wait_ge(S["bPRE1"], 6)
                    if n >= 2:
                        e.wait_ge(S["sBS"], 2 * (n - 1))
                    e.tensor_scalar(tb1[n % 2][:, :], ld1[n % 2][:, :],
                                    scl[:, 3:4], C_ap, ALU.mult, ALU.add)
                    e.tensor_scalar(tb2[n % 2][:, :], tb1[n % 2][:, :], C_ap, 1.0,
                                    ALU.subtract, ALU.mult).then_inc(S["sBR"], 1)
                    if n >= 1:
                        vtail(n - 1)
                    if n == NB - 1:
                        vtail(n)
                elif eng == "scalar":
                    e.wait_ge(S["sBR"], n + 1)
                    if n >= 2:
                        e.wait_ge(S["sBV"], 3 * (n - 1))
                    e.activation(tb3[n % 2][:, :], tb2[n % 2][:, :], AF.Sigmoid,
                                 scale=scl[:, 2:3])
                    e.activation(tb4[n % 2][:, :], tb3[n % 2][:, :], AF.Identity,
                                 bias=C_ap, scale=65536.0).then_inc(S["sBS"], 2)
                elif eng == "gpsimd":
                    e.wait_ge(S["sBV"], 3 * (n + 1))
                    e.dma_start(
                        out=sp_p1.rearrange("k p n -> p k n")[:, 4 * j:4 * j + 4,
                                                              t * TD:(t + 1) * TD],
                        in_=tb5[n % 2].ap().rearrange("p (c x) -> p c x", c=4)
                    ).then_inc(S["sBSP"][n % 2], 16)

        def ph_bar3_fill(eng, e):
            if eng == "vector":
                e.wait_ge(S["sBV"], 3 * NB)
                e.reduce_max(mx2[:, 0:1], bstat[:, 0:NB], axis=AX.X)
                e.reduce_max(mx2[:, 1:2], bstat[:, 0:NB], axis=AX.X).then_inc(S["bPRE2"], 1)

        def ph_bar3_post(eng, e):
            if eng == "vector":
                e.reciprocal(scl[:, 7:8], gmaxs[:, 4:5])
                e.tensor_scalar(scl[:, 6:7], scl[:, 7:8], QMAX, 0.0,
                                ALU.mult, ALU.add).then_inc(S["bPRE2"], 1)

        def ph_C(eng, e):
            if eng == "gpsimd":
                ph_gapply(e, APT * NT, do_store=False)
                ph_gapply(e, APT * NT + 1, do_store=False)
            for n in range(NB):
                t, j = divmod(n, ICH // 4)
                if eng == "sync":
                    if n == 0:
                        e.wait_ge(S["sBSP"][0], 16 * (NB // 2))
                        e.wait_ge(S["sBSP"][1], 16 * (NB // 2))
                    if n >= 2:
                        e.wait_ge(S["sCS"], 2 * (n - 1))
                    e.dma_start(
                        out=ld1[n % 2].ap().rearrange("p (c x) -> p c x", c=4),
                        in_=sp_p1.rearrange("k p n -> p k n")[:, 4 * j:4 * j + 4,
                                                              t * TD:(t + 1) * TD]
                    ).then_inc(S["sCL"][n % 2], 16)
                    e.dma_start(
                        out=ld2[n % 2].ap().rearrange("p (c x) -> p c x", c=4),
                        in_=sp_up.rearrange("k p n -> p k n")[:, 4 * j:4 * j + 4,
                                                              t * TD:(t + 1) * TD]
                    ).then_inc(S["sCL"][n % 2], 16)
                elif eng == "scalar":
                    e.wait_ge(S["sCL"][n % 2], 32 * (n // 2 + 1))
                    if n == 0:
                        e.wait_ge(S["bPRE2"], 6)
                    if n >= 2:
                        e.wait_ge(S["sCV"], 2 * (n - 1))
                    e.activation(tb1[n % 2][:, :], ld1[n % 2][:, :], AF.Identity,
                                 bias=C_ap, scale=scl[:, 6:7])
                    e.activation(tb2[n % 2][:, :], tb1[n % 2][:, :], AF.Identity,
                                 bias=negC_ap)
                    e.activation(tb3[n % 2][:, :], ld2[n % 2][:, :], AF.Identity,
                                 bias=C_ap, scale=scl[:, 5:6]).then_inc(S["sCS"], 2)
                elif eng == "vector":
                    e.wait_ge(S["sCS"], 2 * (n + 1))
                    if n >= 2:
                        e.wait_ge(S["sCSP"][n % 2], 16 * ((n - 2) // 2 + 1))
                    e.scalar_tensor_tensor(tb4[n % 2][:, :], tb3[n % 2][:, :], negC_ap,
                                           tb2[n % 2][:, :], ALU.add, ALU.mult)
                    e.reduce_max(bstat[:, n:n + 1], tb4[n % 2][:, :], axis=AX.X,
                                 apply_absolute_value=True).then_inc(S["sCV"], 2)
                    if n % 4 == 0:
                        ph_vapply2(e, APT * NT + n // 2)
                elif eng == "gpsimd":
                    e.wait_ge(S["sCV"], 2 * (n + 1))
                    e.dma_start(
                        out=sp_p2.rearrange("k p n -> p k n")[:, 4 * j:4 * j + 4,
                                                              t * TD:(t + 1) * TD],
                        in_=tb4[n % 2].ap().rearrange("p (c x) -> p c x", c=4)
                    ).then_inc(S["sCSP"][n % 2], 16)
                    if n % 2 == 1:
                        jj = (n - 1) // 2
                        if 50 + jj <= NAPPLY - 1:
                            ph_gapply(e, 50 + jj, do_store=False)
                        if APT * NT + jj <= NAPPLY - 2:
                            ph_gapply_store(e, APT * NT + jj)
            if eng == "gpsimd":
                ph_gapply_store(e, NAPPLY - 1)

        def ph_bar4_fill(eng, e):
            if eng == "vector":
                e.wait_ge(S["sCV"], 2 * NB)
                e.reduce_max(mx2[:, 0:1], bstat[:, 0:NB], axis=AX.X)
                e.reduce_max(mx2[:, 1:2], bstat[:, 0:NB], axis=AX.X).then_inc(S["bPRE3"], 1)

        def ph_bar4_post(eng, e):
            if eng == "vector":
                e.reciprocal(scl[:, 10:11], gmaxs[:, 6:7])
                e.tensor_scalar(scl[:, 11:12], gmaxs[:, 4:5], 1.0 / (QMAX * 65536.0),
                                0.0, ALU.mult, ALU.add)
                e.tensor_scalar(scl[:, 8:9], scl[:, 10:11], QMAX, 0.0, ALU.mult, ALU.add)
                e.tensor_tensor(scl[:, 12:13], scl[:, 11:12], scl[:, 2:3], ALU.mult)
                e.tensor_tensor(scl[:, 13:14], scl[:, 12:13], scl[:, 4:5], ALU.mult)
                e.tensor_tensor(scl[:, 14:15], scl[:, 13:14], gmaxs[:, 6:7], ALU.mult)
                e.tensor_scalar(scl[:, 9:10], scl[:, 14:15], 1.0 / QMAX, 0.0,
                                ALU.mult, ALU.add).then_inc(S["bPRE3"], 1)

        def ph_D(eng, e):
            def ln_post2(ts):
                e.wait_ge(S["sDLNE"], ts + 1)
                e.tensor_tensor(stx[0:1, 512:1024], abv[0:1, 512:1024],
                                stx[0:1, 0:512], ALU.subtract).then_inc(S["sDVC"], 1)
                e.drain()

            def ln_sc2(ts):
                e.wait_ge(S["sDST"], ts + 1)
                e.activation(stx[0:1, 0:512], pb[3][0:1, :], AF.Identity)
                e.activation(stx[0:1, 512:1024], pb[4][0:1, :], AF.Identity)
                e.activation(abv[0:1, 0:512], stx[0:1, 0:512], AF.Identity,
                             scale=1.0 / 1024.0)
                e.activation(abv[0:1, 512:1024], stx[0:1, 512:1024], AF.Identity,
                             scale=1.0 / 1024.0)
                e.activation(stx[0:1, 0:512], abv[0:1, 0:512],
                             AF.Square).then_inc(S["sDLNE"], 1)

            def ab_post2(ts):
                e.wait_ge(S["sDSQ"], ts + 1)
                e.reciprocal(abv[0:1, 512:1024], stx[0:1, 0:512])
                e.scalar_tensor_tensor(stx[0:1, 0:512], abv[0:1, 0:512], -1.0,
                                       abv[0:1, 512:1024], ALU.mult,
                                       ALU.mult).then_inc(S["sDAB"], 1)

            def affine2(ts):
                e.wait_ge(S["sDBC"], ts + 1)
                e.wait_ge(S["sOUT"], 16 * ts)
                p4 = pb[4][:, :]
                p5 = pb[5][:, :]
                b4 = bass.AP(p4.tensor, p4.offset, [[p4.ap[0][0], 128], [0, DC], [1, TD]])
                b5 = bass.AP(p5.tensor, p5.offset, [[p5.ap[0][0], 128], [0, DC], [1, TD]])
                sv = sqt28.ap().rearrange("p (b x) -> p b x", x=TD)
                rv = r2bs[ts % 2].ap().rearrange("p (b x) -> p b x", x=TD)
                e.tensor_tensor(sv, rv, b4, ALU.mult)
                e.tensor_tensor(sv, sv, b5, ALU.add)
                e._e.engine_nop().then_inc(S["sDXL"], 1)

            def round_d(tt, j):
                n = tt * NJ + j
                e.wait_ge(S["sDL"][n % 2], 16 * (n // 2 + 1))
                if n == 0:
                    e.wait_ge(S["bPRE3"], 6)
                if tt >= 2 and j == 0:
                    e.wait_ge(S["sDMM"], 8 * (tt - 1))
                e.tensor_scalar(t5b[:, :], p2ld[n % 2][:, :], scl[:, 8:9],
                                C_ap, ALU.mult, ALU.add).then_inc(S["sDS2"], 1)
                e.tensor_scalar(otile[tt % 2][:, 2048 * j:2048 * (j + 1)],
                                t5b[:, :], C_ap, 1.0,
                                ALU.subtract, ALU.mult).then_inc(S["sDV2"], 1)

            if eng == "sync":
                e.wait_ge(S["sCSP"][0], 16 * (NB // 2))
                e.wait_ge(S["sCSP"][1], 16 * (NB // 2))
                e.wait_ge(S["sAWW"][0], 16 * (NAPPLY // 2))
                e.wait_ge(S["sAWW"][1], 16 * (NAPPLY // 2))
                flatd = sp_wd.rearrange("c p n -> (c p) n")
                for ci in range(ICH):
                    e.dma_start(out=wd_sb[:, ci * D:(ci + 1) * D],
                                in_=flatd[:, 128 * ci:128 * (ci + 1)],
                                transpose=True).then_inc(S["sWDL"], 16)
            for t in range(NT):
                _sc = nc.named_scope(f"pd{t}")
                _sc.__enter__()
                if eng == "sync":
                    if t == 0:
                        for j in range(NJ):
                            if j >= 2:
                                e.wait_ge(S["sDS2"], j - 1)
                            e.dma_start(
                                out=p2ld[j % 2].ap().rearrange("p (c x) -> p c x", c=4),
                                in_=sp_p2.rearrange("k p n -> p k n")[:, 4 * j:4 * j + 4,
                                                                      0:TD]
                            ).then_inc(S["sDL"][j % 2], 16)
                    if t >= 1:
                        e.wait_ge(S["sDEV"], 8 * t)
                    e.dma_start(
                        out=x1d.ap().rearrange("p (k n) -> p k n", k=DC),
                        in_=sp_x1.rearrange("k p n -> p k n")[:, :, t * TD:(t + 1) * TD]
                    ).then_inc(S["sDX1"], 16)
                    if t + 1 < NT:
                        for j in range(NJ):
                            n = (t + 1) * NJ + j
                            e.wait_ge(S["sDS2"], n - 1)
                            e.dma_start(
                                out=p2ld[n % 2].ap().rearrange("p (c x) -> p c x", c=4),
                                in_=sp_p2.rearrange("k p n -> p k n")[:, 4 * j:4 * j + 4,
                                                                      (t + 1) * TD:(t + 2) * TD]
                            ).then_inc(S["sDL"][n % 2], 16)
                    if t >= 1:
                        e.wait_ge(S["sDXL"], t)
                        e.dma_start(
                            out=out_fm.rearrange("k p n -> p k n")[:, :,
                                                                   (t - 1) * TD:t * TD],
                            in_=sqt28.ap().rearrange("p (k n) -> p k n", k=DC)
                        ).then_inc(S["sOUT"], 16)
                if eng == "tensor":
                    if t == 0:
                        e.wait_ge(S["sWDL"], 16 * ICH)

                    def stats2(ts):
                        if ts >= 2:
                            e.wait_ge(S["sDVC"], ts)
                        for ei in range(DC):
                            e.wait_ge(S["sDCB"], 8 * ts + ei + 1)
                            if ei < DC - 1:
                                e.matmul(pb[3][0:1, :], ones_c[:, :],
                                         r2bs[ts % 2][:, ei * TD:(ei + 1) * TD],
                                         start=(ei == 0), stop=False)
                                e.matmul(pb[4][0:1, :], ones_c[:, :],
                                         sqt28[:, ei * TD:(ei + 1) * TD],
                                         start=(ei == 0), stop=False
                                         ).then_inc(S["sDSTe"], 1)
                            else:
                                e.matmul(pb[4][0:1, :], ones_c[:, :],
                                         sqt28[:, ei * TD:(ei + 1) * TD],
                                         start=False, stop=True).then_inc(S["sDSTe"], 1)
                                e.matmul(pb[3][0:1, :], ones_c[:, :],
                                         r2bs[ts % 2][:, ei * TD:(ei + 1) * TD],
                                         start=False, stop=True).then_inc(S["sDST"], 1)

                    def bcast2(ts):
                        e.wait_ge(S["sDAB"], ts + 1)
                        if ts >= 1:
                            e.wait_ge(S["sDXL"], ts)
                        e.matmul(pb[4][:, :], ones_r[:, :], abv[0:1, 512:1024],
                                 start=True, stop=True)
                        e.matmul(pb[5][:, :], ones_r[:, :], stx[0:1, 0:512],
                                 start=True, stop=True).then_inc(S["sDBC"], 1)

                    if t >= 1:
                        stats2(t - 1)
                    e.wait_ge(S["sDV2"], NJ * (t + 1))
                    for h in range(DC):
                        if 8 * t + h >= 2:
                            e.wait_ge(S["sDEV"], 8 * t + h - 1)
                        m = None
                        for i in range(ICH):
                            m = e.matmul(pb[h % 2][:, :],
                                         wd_sb[:, i * D + 128 * h: i * D + 128 * h + 128],
                                         otile[t % 2][:, i * TD:(i + 1) * TD],
                                         start=(i == 0), stop=(i == ICH - 1))
                        m.then_inc(S["sDMM"], 1)
                        if h == 0 and t >= 1:
                            bcast2(t - 1)
                    if t == NT - 1:
                        stats2(t)
                        bcast2(t)
                elif eng == "vector":
                    if t == 0:
                        for j in range(NJ):
                            round_d(0, j)
                    e.wait_ge(S["sDX1"], 16 * (t + 1))
                    if t >= 2:
                        e.wait_ge(S["sDXL"], t - 1)
                    if t >= 1:
                        ln_post2(t - 1)
                        ab_post2(t - 1)
                    for h in range(DC):
                        e.wait_ge(S["sDMM"], 8 * t + h + 1)
                        e.scalar_tensor_tensor(r2bs[t % 2][:, h * TD:(h + 1) * TD],
                                               pb[h % 2][:, :], scl[:, 9:10],
                                               x1d[:, h * TD:(h + 1) * TD],
                                               ALU.mult, ALU.add).then_inc(S["sDEV"], 1)
                        if h == 0 and t >= 1:
                            affine2(t - 1)
                        if h < NJ and t + 1 < NT:
                            round_d(t + 1, h)
                    if t == NT - 1:
                        ln_post2(t)
                        ab_post2(t)
                        affine2(t)
                elif eng == "scalar":
                    if t >= 1:
                        ln_sc2(t - 1)
                        e.wait_ge(S["sDVC"], t)
                        e.activation(stx[0:1, 0:512], stx[0:1, 512:1024], AF.Sqrt,
                                     bias=EPS_ap).then_inc(S["sDSQ"], 1)
                    for h in range(DC):
                        e.wait_ge(S["sDEV"], 8 * t + h + 1)
                        if t >= 1:
                            e.wait_ge(S["sOUT"], 16 * t)
                        e.activation(sqt28[:, h * TD:(h + 1) * TD],
                                     r2bs[t % 2][:, h * TD:(h + 1) * TD],
                                     AF.Square).then_inc(S["sDCB"], 1)
                    if t == NT - 1:
                        ln_sc2(t)
                        e.wait_ge(S["sDVC"], t + 1)
                        e.activation(stx[0:1, 0:512], stx[0:1, 512:1024], AF.Sqrt,
                                     bias=EPS_ap).then_inc(S["sDSQ"], 1)
                _sc.__exit__(None, None, None)
            if eng == "sync":
                e.wait_ge(S["sDXL"], NT)
                e.dma_start(
                    out=out_fm.rearrange("k p n -> p k n")[:, :, (NT - 1) * TD:NT * TD],
                    in_=sqt28.ap().rearrange("p (k n) -> p k n", k=DC)
                ).then_inc(S["sOUT"], 16)
                e.wait_ge(S["sOUT"], 16 * NT)
            if eng == "gpsimd":
                e.wait_ge(S["sOUT"], 16 * NT)

        def stream(eng, e):
            with nc.named_scope("setup"):
                ph_setup(eng, e)
            with nc.named_scope("mixer"):
                for t in range(NT):
                    with nc.named_scope(f"mx{t}"):
                        ph_mixer_tile(eng, e, t)
                ph_mixer_tail(eng, e)
            with nc.named_scope("bar1"):
                ph_bar1_fill(eng, e)
                ph_gmax_barrier(eng, e, 0, 1)
                ph_bar1_post(eng, e)
            if eng == "gpsimd":
                ph_gapply_store(e, APT * NT - 5)
                ph_gapply_store(e, APT * NT - 4)
                ph_gapply(e, APT * NT - 2, do_store=False)
                ph_gapply(e, APT * NT - 1, do_store=False)
                ph_gapply_store(e, APT * NT - 3)
                ph_gapply_store(e, APT * NT - 2)
                ph_gapply_store(e, APT * NT - 1)
            ph_A_wu_loads(eng, e)
            with nc.named_scope("xq"):
                ph_xq(eng, e)
            if stop_after == "mixer":
                if eng == "sync":
                    e.wait_ge(S["sXQS"][0], 16 * (NT // 2))
                    e.wait_ge(S["sXQS"][1], 16 * (NT // 2))
                    e.wait_ge(S["sAWW"][0], 16 * 24)
                    e.wait_ge(S["sAWW"][1], 16 * 24)
                    e.dma_start(out=gdbg[:, 0:8], in_=mxc[:, :]).then_inc(S["sOUT"], 16)
                    e.dma_start(out=gdbg[:, 8:10], in_=scl[:, 0:2]).then_inc(S["sOUT"], 16)
                    e.wait_ge(S["sOUT"], 32)
                elif eng == "gpsimd":
                    e.wait_ge(S["sOUT"], 32)
                return
            with nc.named_scope("phA"):
                ph_A(eng, e)
            with nc.named_scope("bar2"):
                ph_bar2_fill(eng, e)
                ph_gmax_barrier(eng, e, 1, 2)
                ph_bar2_post(eng, e)
            if stop_after == "A":
                if eng == "sync":
                    e.wait_ge(S["sASP"][0], 16 * (NA // 8))
                    e.wait_ge(S["sASP"][1], 16 * (NA // 8))
                    e.wait_ge(S["bPRE1"], 6)
                    e.dma_start(out=gdbg[:, 0:6], in_=scl[:, 0:6]).then_inc(S["sOUT"], 16)
                    e.wait_ge(S["sOUT"], 16)
                elif eng == "gpsimd":
                    e.wait_ge(S["sOUT"], 16)
                return
            pbar(e, 1)
            with nc.named_scope("phB"):
                ph_B(eng, e)
            with nc.named_scope("bar3"):
                ph_bar3_fill(eng, e)
                ph_gmax_barrier(eng, e, 2, 1)
                ph_bar3_post(eng, e)
            if stop_after == "B":
                if eng == "sync":
                    e.wait_ge(S["sBSP"][0], 16 * (NB // 2))
                    e.wait_ge(S["sBSP"][1], 16 * (NB // 2))
                    e.wait_ge(S["bPRE2"], 6)
                    e.dma_start(out=gdbg[:, 0:8], in_=scl[:, 0:8]).then_inc(S["sOUT"], 16)
                    e.wait_ge(S["sOUT"], 16)
                elif eng == "gpsimd":
                    e.wait_ge(S["sOUT"], 16)
                return
            with nc.named_scope("phC"):
                ph_C(eng, e)
            with nc.named_scope("bar4"):
                ph_bar4_fill(eng, e)
                ph_gmax_barrier(eng, e, 3, 1)
                ph_bar4_post(eng, e)
            if stop_after == "C":
                if eng == "sync":
                    e.wait_ge(S["sCSP"][0], 16 * (NB // 2))
                    e.wait_ge(S["sCSP"][1], 16 * (NB // 2))
                    e.wait_ge(S["bPRE3"], 6)
                    e.dma_start(out=gdbg[:, 0:10], in_=scl[:, 0:10]).then_inc(S["sOUT"], 16)
                    e.wait_ge(S["sOUT"], 16)
                elif eng == "gpsimd":
                    e.wait_ge(S["sOUT"], 16)
                return
            pbar(e, 2)
            with nc.named_scope("phD"):
                ph_D(eng, e)

        with nc.Block() as bk:
            @bk.sync
            def _(e):
                stream("sync", FE(e, fences["sync"]))

            @bk.gpsimd
            def _(e):
                stream("gpsimd", FE(e, fences["gpsimd"]))

            @bk.tensor
            def _(e):
                stream("tensor", FE(e, fences["tensor"]))

            @bk.scalar
            def _(e):
                stream("scalar", FE(e, fences["scalar"]))

            @bk.vector
            def _(e):
                stream("vector", FE(e, fences["vector"]))
    return nc


def _prep_inputs(L0_attn_x, L0_attn_Wg, L0_attn_bg, L0_attn_Wv, L0_attn_bv,
                 L0_attn_Wd, L0_attn_bd, L0_ffn_w_gate, L0_ffn_w_up, L0_ffn_w_down):
    f = np.ascontiguousarray
    shared = {
        "wg_t": f(L0_attn_Wg.T).reshape(DC, 128, D),
        "wv_t": f(L0_attn_Wv.T).reshape(DC, 128, D),
        "wd_t": f(L0_attn_Wd.T).reshape(DC, 128, D),
        "bgh": f(L0_attn_bg.reshape(DC, 128).T),
        "bvh": f(L0_attn_bv.reshape(DC, 128).T),
        "bdh": f(L0_attn_bd.reshape(DC, 128).T),
        "wu_n": f(L0_ffn_w_up).reshape(ICH, 128, D),
        "wq_n": f(L0_ffn_w_gate).reshape(ICH, 128, D),
        "wdn_n": f(L0_ffn_w_down).reshape(DC, 128, I),
    }
    in_maps = []
    for c in range(NCORES):
        xc = f(L0_attn_x[2 * c:2 * c + 2].reshape(TOK, D).T).reshape(DC, 128, TOK)
        m = {"x_fm": xc}
        m.update(shared)
        in_maps.append(m)
    return in_maps


def _run(inputs, stop_after="full", debug=False):
    nc = build(stop_after=stop_after, debug=debug)
    in_maps = _prep_inputs(**{k: np.asarray(v) for k, v in inputs.items()})
    res = run_bass_kernel_spmd(nc, in_maps, core_ids=list(range(NCORES)))
    if debug:
        return res
    out = np.empty((B, S, D), dtype=np.float32)
    for c in range(NCORES):
        o = res.results[c]["out_fm"].reshape(D, TOK)
        out[2 * c:2 * c + 2] = o.T.reshape(2, S, D)
    return out


def kernel(**inputs):
    return _run(inputs)


# version 10
# speedup vs baseline: 1.0540x; 1.0124x over previous
import contextlib

import numpy as np
import concourse.bass as bass
import concourse.mybir as mybir
from concourse.bass_utils import run_bass_kernel_spmd

F32 = mybir.dt.float32
BF16 = mybir.dt.bfloat16
ALU = mybir.AluOpType
AF = mybir.ActivationFunctionType
AX = mybir.AxisListType

NCORES = 8
B, S, D, I = 16, 2048, 1024, 3072
TOK = (B // NCORES) * S
TD = 512
NT = TOK // TD
DC = D // 128
ICH = I // 128
QMAX = 32767.0
C_LO = 1.5 * 2.0 ** 23
C_DELTA = 1.5 * 2.0 ** 52 - 1.5 * 2.0 ** 23

ENGS = ("sync", "gpsimd", "tensor", "scalar", "vector")

NAPPLY = 72
NA = NT * 2 * ICH
NB = NT * (ICH // 4)
NJ = ICH // 4


def _ap_span(ap):
    try:
        dims = list(ap.ap)
        off = ap.offset
        span = 1
        for st, cnt in dims:
            span += abs(st) * (cnt - 1)
        refined = None
        if dims:
            pstr, pcnt = dims[0]
            rest = 1
            for st, cnt in dims[1:]:
                rest += abs(st) * (cnt - 1)
            if pstr >= rest and pstr > 0:
                p_lo = off // pstr
                c_lo = off % pstr
                refined = (pstr, p_lo, p_lo + pcnt, c_lo, c_lo + rest)
        return (ap.tensor.name, off, off + span, refined)
    except AttributeError:
        return None


def _overlap(k, lst):
    for (nm, lo, hi, ref) in lst:
        if nm != k[0]:
            continue
        if k[2] <= lo or hi <= k[1]:
            continue
        if ref is not None and k[3] is not None and ref[0] == k[3][0]:
            if (k[3][2] <= ref[1] or ref[2] <= k[3][1]
                    or k[3][4] <= ref[3] or ref[4] <= k[3][3]):
                continue
        return True
    return False


class FE:

    def __init__(self, eng, state):
        self._e = eng
        self._w = state["w"]
        self._r = state["r"]

    def __getattr__(self, name):
        return getattr(self._e, name)

    def drain(self, *a, **k):
        self._w.clear()
        self._r.clear()
        return self._e.drain(*a, **k)

    def _pre(self, outs, ins):
        need = False
        for ap in list(outs) + list(ins):
            if ap is None or isinstance(ap, (int, float)):
                continue
            k = _ap_span(ap)
            if k is not None and _overlap(k, self._w):
                need = True
                break
        if not need:
            for ap in outs:
                if ap is None or isinstance(ap, (int, float)):
                    continue
                k = _ap_span(ap)
                if k is not None and _overlap(k, self._r):
                    need = True
                    break
        if need:
            self._e.drain()
            self._w.clear()
            self._r.clear()

    def _post(self, outs, ins):
        for ap in outs:
            if ap is None or isinstance(ap, (int, float)):
                continue
            k = _ap_span(ap)
            if k is not None:
                self._w.append(k)
        for ap in ins:
            if ap is None or isinstance(ap, (int, float)):
                continue
            k = _ap_span(ap)
            if k is not None:
                self._r.append(k)

    def _op(self, outs, ins, fn):
        self._pre(outs, ins)
        r = fn()
        self._post(outs, ins)
        return r

    def memset(self, out, v):
        return self._op([out], [], lambda: self._e.memset(out, v))

    def tensor_scalar(self, out, in0, s1, s2, op0, op1=None):
        if op1 is not None:
            return self._op([out], [in0, s1, s2],
                            lambda: self._e.tensor_scalar(out, in0, s1, s2, op0, op1))
        return self._op([out], [in0, s1],
                        lambda: self._e.tensor_scalar(out, in0, s1, s2, op0))

    def tensor_tensor(self, out, a, b, op):
        return self._op([out], [a, b], lambda: self._e.tensor_tensor(out, a, b, op))

    def scalar_tensor_tensor(self, out, in0, sc, in1, op0, op1):
        return self._op([out], [in0, sc, in1],
                        lambda: self._e.scalar_tensor_tensor(out, in0, sc, in1, op0, op1))

    def tensor_tensor_scan(self, out, d0, d1, init, op0, op1):
        return self._op([out], [d0, d1, init],
                        lambda: self._e.tensor_tensor_scan(out, d0, d1, init, op0, op1))

    def reduce_max(self, out, in_, axis, apply_absolute_value=False):
        return self._op([out], [in_],
                        lambda: self._e.reduce_max(out, in_, axis=axis,
                                                   apply_absolute_value=apply_absolute_value))

    def reciprocal(self, out, in_):
        return self._op([out], [in_], lambda: self._e.reciprocal(out, in_))

    def activation(self, out, in_, func, bias=None, scale=None):
        kw = {}
        if bias is not None:
            kw["bias"] = bias
        if scale is not None:
            kw["scale"] = scale
        return self._op([out], [in_, bias, scale],
                        lambda: self._e.activation(out, in_, func, **kw))

    def affine_select(self, out, in_, compare_op, fill, base, pattern,
                      channel_multiplier):
        return self._op([out], [in_],
                        lambda: self._e.affine_select(
                            out=out, in_=in_, compare_op=compare_op, fill=fill,
                            base=base, pattern=pattern,
                            channel_multiplier=channel_multiplier))

    def tensor_copy(self, out, in_):
        return self._op([out], [in_], lambda: self._e.tensor_copy(out, in_))

    def dma_start(self, out, in_, **kw):
        self._pre([], [in_])
        return self._e.dma_start(out=out, in_=in_, **kw)


def build(stop_after="full", debug=False):
    nc = bass.Bass("TRN2", target_bir_lowering=False)
    dbg = "ExternalOutput" if debug else "Internal"

    x_fm = nc.dram_tensor("x_fm", [DC, 128, TOK], F32, kind="ExternalInput").ap()
    wg_t = nc.dram_tensor("wg_t", [DC, 128, D], F32, kind="ExternalInput").ap()
    wv_t = nc.dram_tensor("wv_t", [DC, 128, D], F32, kind="ExternalInput").ap()
    wd_t = nc.dram_tensor("wd_t", [DC, 128, D], F32, kind="ExternalInput").ap()
    bgh = nc.dram_tensor("bgh", [128, DC], F32, kind="ExternalInput").ap()
    bvh = nc.dram_tensor("bvh", [128, DC], F32, kind="ExternalInput").ap()
    bdh = nc.dram_tensor("bdh", [128, DC], F32, kind="ExternalInput").ap()
    wu_n = nc.dram_tensor("wu_n", [ICH, 128, D], F32, kind="ExternalInput").ap()
    wq_n = nc.dram_tensor("wq_n", [ICH, 128, D], F32, kind="ExternalInput").ap()
    wdn_n = nc.dram_tensor("wdn_n", [DC, 128, I], F32, kind="ExternalInput").ap()

    out_fm = nc.dram_tensor("out_fm", [DC, 128, TOK], F32, kind="ExternalOutput").ap()

    sp_x1 = nc.dram_tensor("sp_x1", [DC, 128, TOK], F32, kind=dbg).ap()
    sp_xq = nc.dram_tensor("sp_xq", [DC, 128, TOK], BF16, kind=dbg).ap()
    sp_up = nc.dram_tensor("sp_up", [ICH, 128, TOK], F32, kind=dbg).ap()
    sp_gt = nc.dram_tensor("sp_gt", [ICH, 128, TOK], F32, kind=dbg).ap()
    sp_p1 = nc.dram_tensor("sp_p1", [ICH, 128, TOK], F32, kind=dbg).ap()
    sp_p2 = nc.dram_tensor("sp_p2", [ICH, 128, TOK], F32, kind=dbg).ap()
    sp_wu = nc.dram_tensor("sp_wu", [ICH, 128, D], BF16, kind=dbg).ap()
    sp_wq = nc.dram_tensor("sp_wq", [ICH, 128, D], BF16, kind=dbg).ap()
    sp_wd = nc.dram_tensor("sp_wd", [DC, 128, I], BF16, kind=dbg).ap()
    ccb = []
    for bi in range(4):
        ci = nc.dram_tensor(f"cc_in{bi}", [128, 2], F32).ap()
        co = nc.dram_tensor(f"cc_out{bi}", [128, 2], F32).ap()
        ccb.append((ci, co))
    gdbg = nc.dram_tensor("gdbg", [128, 16], F32, kind=dbg).ap()

    fences = {k: {"w": [], "r": []} for k in ENGS}
    es = contextlib.ExitStack()
    with es:
        sbuf = lambda name, shape, dt=F32: es.enter_context(nc.sbuf_tensor(name, shape, dt))
        sem = lambda name: es.enter_context(nc.semaphore(name))

        cst = sbuf("cst", [128, 16])
        ident = sbuf("ident", [128, 128])
        ones_bf = sbuf("ones_bf", [128, 1], BF16)
        ones_r = sbuf("ones_r", [1, 128])
        ones_c = sbuf("ones_c", [128, 1])
        selm = sbuf("selm", [2, 256])
        biases = sbuf("biases", [128, 3 * DC])
        scl = sbuf("scl", [128, 16])
        gmaxs = sbuf("gmaxs", [128, 8])
        mx2 = sbuf("mx2", [128, 2])
        gsc = sbuf("gsc", [2, 1])
        stx = sbuf("stx", [1, 1024])
        abv = sbuf("abv", [1, 1024])
        carry = sbuf("carry", [128, DC])

        pb = [es.enter_context(nc.psum_tensor(f"pb{i}", [128, 512], F32))
              for i in range(8)]

        A0 = 40928
        LIM = 229344

        def at(name, shape, dt, off):
            end = off + shape[1] * mybir.dt.size(dt)
            assert end <= LIM, (name, end)
            return nc.alloc_sbuf_tensor_at(name, shape, dt, offset=off)

        wgm = at("wgm", [128, DC * D], BF16, A0)
        wvm = at("wvm", [128, DC * D], BF16, A0 + 16384)
        wdm = at("wdm", [128, DC * D], BF16, A0 + 32768)
        xf = [at(f"xf{i}", [128, DC * TD], F32, A0 + 49152 + 16384 * i) for i in range(2)]
        xbm = [at(f"xbm{i}", [128, DC * TD], BF16, A0 + 81920 + 8192 * i) for i in range(2)]
        wk_xg = [at(f"wkxg{i}", [128, TD], F32, A0 + 98304 + 2048 * i) for i in range(2)]
        wk_tv = [at(f"wktv{i}", [128, TD], F32, A0 + 102400 + 2048 * i) for i in range(2)]
        wk_sa = [at(f"wksa{i}", [128, TD], F32, A0 + 106496 + 2048 * i) for i in range(2)]
        wk_xs1 = at("wkxs", [128, TD], F32, A0 + 110592)
        wk_av1 = at("wkav", [128, TD], F32, A0 + 112640)
        hbuf = at("hbuf", [128, 2 * TD], F32, A0 + 114688)
        rbuf = at("rbuf", [128, DC * TD], F32, A0 + 118784)
        sqt8 = at("sqt8", [128, DC * TD], F32, A0 + 135168)
        rbuf2 = at("rbuf2", [128, DC * TD], F32, A0 + 151552)
        rbufs = [rbuf, rbuf2]
        mxc = at("mxc", [128, 8], F32, A0 + 167936)
        DQ0 = A0 + 168000
        dq_nat = [at(f"dqn{i}", [128, D], F32, DQ0 + 4096 * i) for i in range(2)]
        dq_wb = [at(f"dqwb{i}", [128, D], BF16, DQ0 + 8192 + 2048 * i) for i in range(2)]
        dq_sc = at("dqsc", [128, 128], F32, DQ0 + 12288)
        wk_xs2 = at("wkxs2", [128, TD], F32, DQ0 + 12800)
        wk_av2 = at("wkav2", [128, TD], F32, DQ0 + 14848)
        wk_xs = [wk_xs1, wk_xs2]
        wk_av = [wk_av1, wk_av2]

        wu_sb = at("wu_sb", [128, DC * I], BF16, A0)
        wq_sb = at("wq_sb", [128, DC * I], BF16, A0 + 49152)
        xqa = [at(f"xqa{i}", [128, DC * TD], BF16, A0 + 98304 + 8192 * i) for i in range(2)]
        gbuf = [at(f"gbuf{i}", [128, 2048], F32, A0 + 114688 + 8192 * i) for i in range(2)]
        astat = at("astat", [128, 96], F32, A0 + 131072)

        ld1 = [at(f"ld1_{i}", [128, 2048], F32, A0 + 8192 * i) for i in range(2)]
        ld2 = [at(f"ld2_{i}", [128, 2048], F32, A0 + 16384 + 8192 * i) for i in range(2)]
        tb1 = [at(f"tb1_{i}", [128, 2048], F32, A0 + 32768 + 8192 * i) for i in range(2)]
        tb2 = [at(f"tb2_{i}", [128, 2048], F32, A0 + 49152 + 8192 * i) for i in range(2)]
        tb3 = [at(f"tb3_{i}", [128, 2048], F32, A0 + 65536 + 8192 * i) for i in range(2)]
        tb4 = [at(f"tb4_{i}", [128, 2048], F32, A0 + 81920 + 8192 * i) for i in range(2)]
        tb5 = [at(f"tb5_{i}", [128, 2048], F32, A0 + 98304 + 8192 * i) for i in range(2)]
        bstat = at("bstat", [128, 48], F32, A0 + 114688)

        wd_sb = at("wd_sb", [128, ICH * D], BF16, A0)
        otile = [at(f"otile{i}", [128, ICH * TD], BF16, A0 + 49152 + 24576 * i) for i in range(2)]
        p2ld = [at(f"p2ld{i}", [128, 2048], F32, A0 + 98304 + 8192 * i) for i in range(2)]
        t5b = at("t5b", [128, 2048], F32, A0 + 114688)
        x1d = at("x1d", [128, DC * TD], F32, A0 + 122880)
        r2b = at("r2b", [128, DC * TD], F32, A0 + 139264)
        r2b2 = at("r2b2", [128, DC * TD], F32, A0 + 155648)
        r2bs = [r2b, r2b2]
        sqt28 = at("sqt28", [128, DC * TD], F32, A0 + 172032)

        S = {n: sem(n) for n in (
            "sWM sWC sMM sEP sRS sCB sSTe sST sSE sVC sSQ sAB sBC sXL sLNE sDLNE sVX "
            "sSPX sPB sXQV sANL sAPC "
            "sWL2 sWQL sWDL sAMM sAEV sARD sBS sBR sBV sCS sCV "
            "sDS2 sDV2 sDX1 sDMM sDEV sDCB sDST sDSTe sDSE2 sDVC sDSQ "
            "sDAB sDBC sDXL sOUT sSU".split())}
        for pn in ("sLX sLB sXQL sXQA sAWW sXQS sASP sBL sBSP "
                   "sCL sCSP sDL".split()):
            S[pn] = [sem(pn + "0"), sem(pn + "1")]
        for bi in range(4):
            S[f"bPRE{bi}"] = sem(f"bPRE{bi}")
            S[f"bDMA{bi}"] = sem(f"bDMA{bi}")
            S[f"bCC{bi}"] = sem(f"bCC{bi}")

        C_ap = cst[:, 1:2]
        negC_ap = cst[:, 2:3]
        EPS_ap = cst[0:1, 5:6]

        apply_steps = []
        for c in range(ICH):
            apply_steps.append((wu_n[c][:, :], sp_wu[c][:, :]))
        for c in range(ICH):
            apply_steps.append((wq_n[c][:, :], sp_wq[c][:, :]))
        for hc in range(DC):
            for j in range(3):
                apply_steps.append((wdn_n[hc][:, j * D:(j + 1) * D],
                                    sp_wd[hc][:, j * D:(j + 1) * D]))
        assert len(apply_steps) == NAPPLY
        APT = 6

        def applies_for_tile(t):
            if t == NT - 1:
                return range(APT * t, APT * t + 2)
            return range(APT * t, APT * (t + 1))

        def x_src(t):
            return x_fm.rearrange("k p n -> p k n")[:, :, t * TD:(t + 1) * TD]

        def pbar(e, k):
            e.drain()
            if hasattr(e._e, "engine_nop"):
                e._e.engine_nop().then_inc(S["sPB"], 1)
            else:
                e._e.nop().then_inc(S["sPB"], 1)
            e._e.wait_ge(S["sPB"], 5 * k)
            e._w.clear()
            e._r.clear()


        def ph_setup(eng, e):
            if eng == "gpsimd":
                e.memset(ident[:, :], 0.0)
                e.affine_select(out=ident[:, :], in_=ident[:, :],
                                compare_op=ALU.not_equal, fill=1.0, base=0,
                                pattern=[[-1, 128]], channel_multiplier=1)
                e.memset(ones_bf[:, :], 1.0)
                e.memset(ones_c[:, :], 1.0)
                e.memset(ones_r[:, :], 1.0)
                e.memset(selm[0:2, :], 1.0)
                e.affine_select(out=selm[0:2, :], in_=selm[0:2, :],
                                compare_op=ALU.is_ge, fill=0.0, base=0,
                                pattern=[[1, 256]], channel_multiplier=-128)
                e.affine_select(out=selm[0:2, :], in_=selm[0:2, :],
                                compare_op=ALU.is_ge, fill=0.0, base=127,
                                pattern=[[-1, 256]], channel_multiplier=128)
                e._e.engine_nop().then_inc(S["sSU"], 1)
                for dst, srcw in ((wgm, wg_t), (wvm, wv_t), (wdm, wd_t)):
                    e.dma_start(out=dst.ap().rearrange("p (k n) -> p k n", k=DC),
                                in_=srcw.rearrange("k p n -> p k n")).then_inc(S["sWC"], 16)
            elif eng == "vector":
                e.memset(cst[:, 3:4], 1.0)
                e.memset(cst[:, 5:6], 1e-5)
                e.tensor_scalar(cst[:, 4:5], cst[:, 3:4], 2.0 ** -30, 1.0,
                                ALU.add, ALU.subtract)
                e.tensor_scalar(cst[:, 0:1], cst[:, 4:5], 2.0 ** 30, 1.0,
                                ALU.mult, ALU.mult)
                e.tensor_scalar(cst[:, 1:2], cst[:, 0:1], C_DELTA, C_LO,
                                ALU.mult, ALU.add)
                e.tensor_scalar(cst[:, 2:3], cst[:, 1:2], -1.0, 0.0, ALU.mult, ALU.add)
            elif eng == "sync":
                e.dma_start(out=biases[:, 0:DC], in_=bgh[:, :]).then_inc(S["sWM"], 16)
                e.dma_start(out=biases[:, DC:2 * DC], in_=bvh[:, :]).then_inc(S["sWM"], 16)
                e.dma_start(out=biases[:, 2 * DC:3 * DC], in_=bdh[:, :]).then_inc(S["sWM"], 16)

        def ph_gapply_store(e, s):
            e.wait_ge(S["sAPC"], s + 1)
            _, pdst = apply_steps[s]
            e.dma_start(out=pdst,
                        in_=dq_wb[s % 2][:, :]).then_inc(S["sAWW"][s % 2], 16)

        def ph_gapply(e, s, do_store=True):
            src, dst = apply_steps[s]
            if s >= 1:
                e.wait_ge(S["sAPC"], s - 1)
            e.dma_start(out=dq_nat[s % 2][:, :], in_=src).then_inc(S["sANL"], 16)
            if s >= 1 and do_store:
                ph_gapply_store(e, s - 1)

        def ph_vapply2(e, s):
            na, nb_ = dq_nat[s % 2], dq_nat[(s + 1) % 2]
            base = dq_sc[:, :]
            bc = lambda c0: bass.AP(base.tensor, base.offset + c0,
                                    [[128, 128], [1, 32], [0, 32]])
            e.wait_ge(S["sANL"], 16 * (s + 2))
            if s >= 2:
                e.wait_ge(S["sAWW"][0], 16 * (s // 2))
                e.wait_ge(S["sAWW"][1], 16 * (s // 2))
            na3 = na.ap().rearrange("p (b x) -> p b x", x=32)
            nb3 = nb_.ap().rearrange("p (b x) -> p b x", x=32)
            e.reduce_max(dq_sc[:, 0:32], na3, axis=AX.X, apply_absolute_value=True)
            e.reduce_max(dq_sc[:, 32:64], nb3, axis=AX.X, apply_absolute_value=True)
            e.tensor_scalar(dq_sc[:, 0:64], dq_sc[:, 0:64], 1.0 / 7.0, 1e-12,
                            ALU.mult, ALU.max)
            e.reciprocal(dq_sc[:, 64:128], dq_sc[:, 0:64])
            e.tensor_tensor(na3, na3, bc(64), ALU.mult)
            e.tensor_scalar(na[:, :], na[:, :], C_ap, C_ap, ALU.add, ALU.subtract)
            e.tensor_tensor(dq_wb[s % 2].ap().rearrange("p (b x) -> p b x", x=32),
                            na3, bc(0), ALU.mult).then_inc(S["sAPC"], 1)
            e.tensor_tensor(nb3, nb3, bc(96), ALU.mult)
            e.tensor_scalar(nb_[:, :], nb_[:, :], C_ap, C_ap, ALU.add, ALU.subtract)
            e.tensor_tensor(dq_wb[(s + 1) % 2].ap().rearrange("p (b x) -> p b x", x=32),
                            nb3, bc(32), ALU.mult).then_inc(S["sAPC"], 1)
        def ph_mixer_tile(eng, e, t):
            if eng == "sync":
                if t == 0:
                    for tt in (0, 1):
                        e.dma_start(out=xf[tt].ap().rearrange("p (k n) -> p k n", k=DC),
                                    in_=x_src(tt)).then_inc(S["sLX"][tt % 2], 16)
                if t >= 1:
                    e.wait_ge(S["sXL"], t)
                    e.dma_start(
                        out=sp_x1.rearrange("k p n -> p k n")[:, :,
                                                              (t - 1) * TD:t * TD],
                        in_=xf[(t - 1) % 2].ap().rearrange("p (k n) -> p k n", k=DC)
                    ).then_inc(S["sSPX"], 16)
                    if t + 1 < NT:
                        e.wait_ge(S["sSPX"], 16 * t)
                        e.dma_start(out=xf[(t + 1) % 2].ap().rearrange("p (k n) -> p k n", k=DC),
                                    in_=x_src(t + 1)).then_inc(S["sLX"][(t + 1) % 2], 16)
            elif eng == "gpsimd":
                if t == 0:
                    for tt in (0, 1):
                        e.dma_start(out=xbm[tt].ap().rearrange("p (k n) -> p k n", k=DC),
                                    in_=x_src(tt)).then_inc(S["sLB"][tt % 2], 16)
                for s in applies_for_tile(t):
                    ph_gapply(e, s)
                if t == NT - 1:
                    ph_gapply(e, APT * NT - 4, do_store=False)
                    ph_gapply(e, APT * NT - 3, do_store=False)
                if t + 2 < NT:
                    e.wait_ge(S["sMM"], 8 * (t + 1))
                    e.dma_start(out=xbm[t % 2].ap().rearrange("p (k n) -> p k n", k=DC),
                                in_=x_src(t + 2)).then_inc(S["sLB"][t % 2], 16)
            elif eng == "tensor":
                if t == 0:
                    e.wait_ge(S["sWM"], 48)
                    e.wait_ge(S["sWC"], 48)
                e.wait_ge(S["sLB"][t % 2], 16 * (t // 2 + 1))
                e.wait_ge(S["sLX"][t % 2], 16 * (t // 2 + 1))

                def stats(ts):
                    if ts >= 2:
                        e.wait_ge(S["sVC"], ts)
                    if ts >= 1:
                        e.wait_ge(S["sXL"], ts)
                    for ei in range(DC):
                        e.wait_ge(S["sCB"], 8 * ts + ei + 1)
                        if ei < DC - 1:
                            e.matmul(pb[3][0:1, :], ones_c[:, :],
                                     rbufs[ts % 2][:, ei * TD:(ei + 1) * TD],
                                     start=(ei == 0), stop=False)
                            e.matmul(pb[4][0:1, :], ones_c[:, :], sqt8[:, ei * TD:(ei + 1) * TD],
                                     start=(ei == 0), stop=False).then_inc(S["sSTe"], 1)
                        else:
                            e.matmul(pb[4][0:1, :], ones_c[:, :], sqt8[:, ei * TD:(ei + 1) * TD],
                                     start=False, stop=True).then_inc(S["sSTe"], 1)
                            e.matmul(pb[3][0:1, :], ones_c[:, :],
                                     rbufs[ts % 2][:, ei * TD:(ei + 1) * TD],
                                     start=False, stop=True).then_inc(S["sST"], 1)

                def bcast(ts):
                    e.wait_ge(S["sAB"], ts + 1)
                    if ts >= 1:
                        e.wait_ge(S["sXL"], ts)
                    e.matmul(pb[4][:, :], ones_r[:, :], abv[0:1, 512:1024],
                             start=True, stop=True)
                    e.matmul(pb[5][:, :], ones_r[:, :], stx[0:1, 0:512],
                             start=True, stop=True).then_inc(S["sBC"], 1)

                if t >= 1:
                    stats(t - 1)
                for ei in range(DC):
                    if 8 * t + ei >= 1:
                        e.wait_ge(S["sEP"], 8 * t + ei)
                    last = None
                    for wi, wsb in enumerate((wgm, wvm, wdm)):
                        for k in range(DC):
                            last = e.matmul(
                                pb[wi][:, :],
                                wsb[:, k * D + 128 * ei: k * D + 128 * ei + 128],
                                xbm[t % 2][:, k * TD:(k + 1) * TD],
                                start=(k == 0), stop=(k == DC - 1))
                    last.then_inc(S["sMM"], 1)
                    if ei == 4 and t >= 1:
                        bcast(t - 1)
                if t == NT - 1:
                    stats(t)
                    bcast(t)
            elif eng == "scalar":
                def sq_m(k):
                    e.wait_ge(S["sRS"], 8 * t + 2 * (k // 2 + 1))
                    if t >= 1:
                        e.wait_ge(S["sSTe"], 8 * (t - 1) + k + 1)
                    e.activation(sqt8[:, k * TD:(k + 1) * TD],
                                 rbufs[t % 2][:, k * TD:(k + 1) * TD],
                                 AF.Square).then_inc(S["sCB"], 1)

                def ln_sc(ts):
                    e.wait_ge(S["sST"], ts + 1)
                    e.activation(stx[0:1, 0:512], pb[3][0:1, :], AF.Identity)
                    e.activation(stx[0:1, 512:1024], pb[4][0:1, :], AF.Identity)
                    e.activation(abv[0:1, 0:512], stx[0:1, 0:512], AF.Identity,
                                 scale=1.0 / 1024.0)
                    e.activation(abv[0:1, 512:1024], stx[0:1, 512:1024], AF.Identity,
                                 scale=1.0 / 1024.0)
                    e.activation(stx[0:1, 0:512], abv[0:1, 0:512],
                                 AF.Square).then_inc(S["sLNE"], 1)

                if t >= 1:
                    ln_sc(t - 1)
                for ei in range(DC):
                    e.wait_ge(S["sMM"], 8 * t + ei + 1)
                    if 8 * t + ei >= 2:
                        e.wait_ge(S["sVX"], 8 * t + ei - 1)
                    e.activation(wk_xg[ei % 2][:, :], pb[0][:, :], AF.Sigmoid,
                                 bias=biases[:, ei:ei + 1])
                    e.activation(wk_tv[ei % 2][:, :], pb[1][:, :], AF.Tanh,
                                 bias=biases[:, DC + ei:DC + ei + 1])
                    e.activation(wk_sa[ei % 2][:, :], pb[2][:, :], AF.Sigmoid,
                                 bias=biases[:, 2 * DC + ei:2 * DC + ei + 1]
                                 ).then_inc(S["sEP"], 1)
                    if ei == 1 and t >= 1:
                        e.wait_ge(S["sVC"], t)
                        e.activation(stx[0:1, 0:512], stx[0:1, 512:1024], AF.Sqrt,
                                     bias=EPS_ap).then_inc(S["sSQ"], 1)
                    if ei >= 2:
                        sq_m(ei - 2)
                sq_m(DC - 2)
                sq_m(DC - 1)
                if t == NT - 1:
                    ln_sc(t)
                    e.wait_ge(S["sVC"], t + 1)
                    e.activation(stx[0:1, 0:512], stx[0:1, 512:1024], AF.Sqrt,
                                 bias=EPS_ap).then_inc(S["sSQ"], 1)
            elif eng == "vector":
                def ln_post(ts):
                    e.wait_ge(S["sLNE"], ts + 1)
                    e.tensor_tensor(stx[0:1, 512:1024], abv[0:1, 512:1024],
                                    stx[0:1, 0:512], ALU.subtract).then_inc(S["sVC"], 1)
                    e._w.clear()
                    e._r.clear()
                    e.drain()

                def ab_post(ts):
                    e.wait_ge(S["sSQ"], ts + 1)
                    e.reciprocal(abv[0:1, 512:1024], stx[0:1, 0:512])
                    e.scalar_tensor_tensor(stx[0:1, 0:512], abv[0:1, 0:512], -1.0,
                                           abv[0:1, 512:1024], ALU.mult,
                                           ALU.mult).then_inc(S["sAB"], 1)

                def affine(ts):
                    e.wait_ge(S["sBC"], ts + 1)
                    if ts >= 1:
                        e.wait_ge(S["sSPX"], 16 * ts)
                    p4 = pb[4][:, :]
                    p5 = pb[5][:, :]
                    b4 = bass.AP(p4.tensor, p4.offset, [[p4.ap[0][0], 128], [0, DC], [1, TD]])
                    b5 = bass.AP(p5.tensor, p5.offset, [[p5.ap[0][0], 128], [0, DC], [1, TD]])
                    xv = xf[ts % 2].ap().rearrange("p (b x) -> p b x", x=TD)
                    rv = rbufs[ts % 2].ap().rearrange("p (b x) -> p b x", x=TD)
                    e.tensor_tensor(xv, rv, b4, ALU.mult)
                    e.tensor_tensor(xv, xv, b5, ALU.add)
                    e.reduce_max(mxc[:, ts:ts + 1], xf[ts % 2][:, :], axis=AX.X,
                                 apply_absolute_value=True).then_inc(S["sXL"], 1)

                rb = rbufs[t % 2]
                for ei in range(DC):
                    e.wait_ge(S["sEP"], 8 * t + ei + 1)
                    sl = (ei % 2) * TD
                    e.tensor_tensor(wk_xs[ei % 2][:, :], wk_xg[ei % 2][:, :],
                                    wk_tv[ei % 2][:, :], ALU.mult)
                    e.tensor_scalar(wk_av[ei % 2][:, :], wk_sa[ei % 2][:, :],
                                    0.998, 0.001, ALU.mult,
                                    ALU.add).then_inc(S["sVX"], 1)
                    if ei >= 2 and ei % 2 == 0:
                        e.tensor_tensor(rb[:, (ei - 2) * TD:ei * TD],
                                        xf[t % 2][:, (ei - 2) * TD:ei * TD],
                                        hbuf[:, 0:2 * TD], ALU.add).then_inc(S["sRS"], 2)
                    if ei >= 1:
                        psl = ((ei - 1) % 2) * TD
                        e.tensor_scalar(carry[:, ei - 1:ei],
                                        hbuf[:, psl + TD - 1:psl + TD],
                                        1.0, 0.0, ALU.mult, ALU.add)
                    init = 0.0 if t % 4 == 0 else carry[:, ei:ei + 1]
                    e.tensor_tensor_scan(hbuf[:, sl:sl + TD], wk_av[ei % 2][:, :],
                                         wk_xs[ei % 2][:, :], init, ALU.mult, ALU.add)
                    if ei == 0 and t >= 1:
                        ln_post(t - 1)
                    if ei == 2 and t >= 1:
                        ab_post(t - 1)
                    if ei == 4 and t >= 1:
                        affine(t - 1)
                    if ei in (1, 3, 5) and (t < NT - 1 or ei == 1):
                        ph_vapply2(e, APT * t + ei - 1)
                e.tensor_scalar(carry[:, DC - 1:DC],
                                hbuf[:, ((DC - 1) % 2) * TD + TD - 1:
                                      ((DC - 1) % 2) * TD + TD],
                                1.0, 0.0, ALU.mult, ALU.add)
                e.tensor_tensor(rb[:, (DC - 2) * TD:DC * TD],
                                xf[t % 2][:, (DC - 2) * TD:DC * TD],
                                hbuf[:, 0:2 * TD], ALU.add).then_inc(S["sRS"], 2)
                if t == NT - 1:
                    ln_post(t)
                    ab_post(t)
                    affine(t)

        def ph_mixer_tail(eng, e):
            if eng == "sync":
                e.wait_ge(S["sXL"], NT)
                e.dma_start(
                    out=sp_x1.rearrange("k p n -> p k n")[:, :, (NT - 1) * TD:NT * TD],
                    in_=xf[(NT - 1) % 2].ap().rearrange("p (k n) -> p k n", k=DC)
                ).then_inc(S["sSPX"], 16)
        def ph_gmax_barrier(eng, e, bi, k):
            cin, cout = ccb[bi]
            bP, bD, bC = S[f"bPRE{bi}"], S[f"bDMA{bi}"], S[f"bCC{bi}"]
            if eng == "sync":
                e.wait_ge(bP, 1)
                e.dma_start(out=cin[:, :], in_=mx2[:, :]).then_inc(bD, 16)
                e.wait_ge(bC, 1)
                e.dma_start(out=mx2[:, :], in_=cout[:, :]).then_inc(bD, 16)
            elif eng == "gpsimd":
                e.wait_ge(bD, 16)
                e.collective_compute(
                    "AllReduce", ALU.max, replica_groups=[list(range(NCORES))],
                    ins=[cin[:, :]], outs=[cout[:, :]]).then_inc(bC, 1)
            elif eng == "tensor":
                e.wait_ge(bD, 32)
                e.transpose(pb[7][0:2, 0:128], mx2[:, :], ident[:, :]).then_inc(bP, 1)
                e.wait_ge(bP, 3)
                m = None
                for j in range(k):
                    m = e.matmul(pb[6][:, j:j + 1], selm[0:2, 128 * j:128 * j + 128],
                                 gsc[0:2, 0:1], start=True, stop=True)
                m.then_inc(bP, 1)
            elif eng == "scalar":
                e.wait_ge(bP, 4)
                e.activation(gmaxs[:, 2 * bi:2 * bi + k], pb[6][:, 0:k],
                             AF.Copy).then_inc(bP, 1)
            elif eng == "vector":
                e.wait_ge(bP, 2)
                e.reduce_max(gsc[0:2, 0:1], pb[7][0:2, 0:128], axis=AX.X).then_inc(bP, 1)
                e.wait_ge(bP, 5)

        def ph_bar1_fill(eng, e):
            if eng == "vector":
                e.wait_ge(S["sXL"], NT)
                e.reduce_max(mx2[:, 0:1], mxc[:, 0:NT], axis=AX.X)
                e.reduce_max(mx2[:, 1:2], mxc[:, 0:NT], axis=AX.X).then_inc(S["bPRE0"], 1)
                ph_vapply2(e, APT * NT - 4)
                ph_vapply2(e, APT * NT - 2)

        def ph_bar1_post(eng, e):
            if eng == "vector":
                e.tensor_scalar(scl[:, 0:1], gmaxs[:, 0:1], 1.0 / QMAX, 1e-12,
                                ALU.mult, ALU.max)
                e.reciprocal(scl[:, 1:2], scl[:, 0:1]).then_inc(S["bPRE0"], 1)

        def ph_xq(eng, e):
            for t in range(NT):
                if eng == "sync":
                    if t == 0:
                        e.wait_ge(S["sSPX"], 16 * NT)
                    if t >= 2:
                        e.wait_ge(S["sXQV"], t - 1)
                    e.dma_start(
                        out=xf[t % 2].ap().rearrange("p (k n) -> p k n", k=DC),
                        in_=sp_x1.rearrange("k p n -> p k n")[:, :, t * TD:(t + 1) * TD]
                    ).then_inc(S["sXQL"][t % 2], 16)
                elif eng == "vector":
                    e.wait_ge(S["sXQL"][t % 2], 16 * (t // 2 + 1))
                    if t == 0:
                        e.wait_ge(S["bPRE0"], 6)
                    if t >= 2:
                        e.wait_ge(S["sXQS"][t % 2], 16 * ((t - 2) // 2 + 1))
                    if t in (2, 3):
                        e.wait_ge(S["sXQA"][t % 2], 16)
                    e.tensor_scalar(sqt8[:, :], xf[t % 2][:, :], scl[:, 1:2], C_ap,
                                    ALU.mult, ALU.add)
                    e.tensor_scalar(xbm[t % 2][:, :], sqt8[:, :], C_ap, scl[:, 0:1],
                                    ALU.subtract, ALU.mult).then_inc(S["sXQV"], 1)
                    if stop_after != "mixer":
                        for g in xq_reds.get(t, ()):
                            red_a(e, g)
                if eng == "sync":
                    e.wait_ge(S["sXQV"], t + 1)
                    e.dma_start(
                        out=sp_xq.rearrange("k p n -> p k n")[:, :, t * TD:(t + 1) * TD],
                        in_=xbm[t % 2].ap().rearrange("p (k n) -> p k n", k=DC)
                    ).then_inc(S["sXQS"][t % 2], 16)
                    if t <= 1:
                        e.dma_start(out=xqa[t][:, :],
                                    in_=xbm[t][:, :]).then_inc(S["sXQA"][t % 2], 16)

        aorder = [(0, 0), (1, 0), (0, 1), (1, 1)]
        for t in range(2, NT):
            aorder.append((t, 0))
            aorder.append((t, 1))

        def red_a(e, g):
            oi = g // (ICH // 4)
            gg = g % (ICH // 4)
            t, w = aorder[oi]
            n = oi * ICH + gg * 4 + 3
            col = w * 48 + t * 6 + gg
            e.wait_ge(S["sAEV"], n + 1)
            e.reduce_max(astat[:, col:col + 1], gbuf[g % 2][:, :], axis=AX.X,
                         apply_absolute_value=True).then_inc(S["sARD"], 1)

        xq_reds = {2: [0], 3: [1, 2], 4: [3, 4], 5: [5, 6], 6: [7, 8],
                   7: [9, 10, 11]}

        def ph_A_wu_loads(eng, e):
            if eng == "sync":
                e.wait_ge(S["sMM"], 8 * NT)
                e.wait_ge(S["sAWW"][0], 16 * 12)
                e.wait_ge(S["sAWW"][1], 16 * 12)
                flat = sp_wu.rearrange("c p n -> (c p) n")
                for k in range(DC):
                    e.dma_start(out=wu_sb[:, k * I:(k + 1) * I],
                                in_=flat[:, 128 * k:128 * (k + 1)],
                                transpose=True).then_inc(S["sWL2"], 16)

        def ph_A(eng, e):
            if eng == "gpsimd":
                for sd in range(APT * NT, NAPPLY):
                    ph_gapply(e, sd, do_store=(sd > APT * NT))
                ph_gapply_store(e, NAPPLY - 1)
            if eng == "sync":
                e.wait_ge(S["sXQS"][0], 16 * (NT // 2))
                e.wait_ge(S["sXQS"][1], 16 * (NT // 2))
                e.wait_ge(S["sAWW"][0], 16 * 24)
                e.wait_ge(S["sAWW"][1], 16 * 24)
                flat = sp_wq.rearrange("c p n -> (c p) n")
                for k in range(DC):
                    e.dma_start(out=wq_sb[:, k * I:(k + 1) * I],
                                in_=flat[:, 128 * k:128 * (k + 1)],
                                transpose=True).then_inc(S["sWQL"], 16)
            for oi, (t, w) in enumerate(aorder):
                _sc = nc.named_scope(f"pa{oi}")
                _sc.__enter__()
                wsb = wu_sb if w == 0 else wq_sb
                spw = sp_up if w == 0 else sp_gt
                if eng == "tensor":
                    if oi == 0:
                        e.wait_ge(S["sWL2"], 16 * 8)
                        e.wait_ge(S["sEP"], 8 * NT)
                        e.wait_ge(S["sVC"], NT)
                    if oi == 2:
                        e.wait_ge(S["sWQL"], 16 * 8)
                    if w == 0:
                        e.wait_ge(S["sXQA"][t % 2], 16 * (t // 2 + 1))
                    for i in range(ICH):
                        n = oi * ICH + i
                        if n >= 4:
                            e.wait_ge(S["sAEV"], n - 3)
                        m = None
                        for k in range(DC):
                            m = e.matmul(
                                pb[n % 4][:, :],
                                wsb[:, k * I + 128 * i: k * I + 128 * i + 128],
                                xqa[t % 2][:, k * TD:(k + 1) * TD],
                                start=(k == 0), stop=(k == DC - 1))
                        m.then_inc(S["sAMM"], 1)
                elif eng == "scalar":
                    for i in range(ICH):
                        n = oi * ICH + i
                        g = n // 4
                        if n == 0:
                            e.wait_ge(S["sST"], NT)
                            e.wait_ge(S["sXL"], NT)
                        if g >= 2:
                            e.wait_ge(S["sASP"][g % 2], 16 * ((g - 2) // 2 + 1))
                            e.wait_ge(S["sARD"], g - 1)
                        e.wait_ge(S["sAMM"], n + 1)
                        e.activation(gbuf[g % 2][:, 512 * (i % 4):512 * (i % 4) + 512],
                                     pb[n % 4][:, :], AF.Copy).then_inc(S["sAEV"], 1)
                        if i % 4 == 3:
                            e.dma_start(
                                out=spw.rearrange("k p n -> p k n")[:, i - 3:i + 1,
                                                                    t * TD:(t + 1) * TD],
                                in_=gbuf[g % 2].ap().rearrange("p (c n) -> p c n", c=4)
                            ).then_inc(S["sASP"][g % 2], 16)
                elif eng == "vector":
                    if oi >= 2:
                        for i in range(3, ICH, 4):
                            red_a(e, oi * (ICH // 4) + i // 4)
                        if oi <= 13:
                            ph_vapply2(e, APT * NT + 2 * (oi - 2))
                elif eng == "sync":
                    if t + 2 < NT and (w == 1 if t >= 2 else oi == 3):
                        for tl in ((t + 2,) if t >= 2 else (2, 3)):
                            e.wait_ge(S["sAMM"], (aorder.index((tl - 2, 1)) + 1) * ICH)
                            e.dma_start(
                                out=xqa[tl % 2].ap().rearrange("p (k n) -> p k n", k=DC),
                                in_=sp_xq.rearrange("k p n -> p k n")[:, :,
                                                                      tl * TD:(tl + 1) * TD]
                            ).then_inc(S["sXQA"][tl % 2], 16)
                _sc.__exit__(None, None, None)

        def ph_bar2_fill(eng, e):
            if eng == "vector":
                e.wait_ge(S["sARD"], NA // 4)
                e.reduce_max(mx2[:, 0:1], astat[:, 0:48], axis=AX.X)
                e.reduce_max(mx2[:, 1:2], astat[:, 48:96], axis=AX.X).then_inc(S["bPRE1"], 1)

        def ph_bar2_post(eng, e):
            if eng == "vector":
                e.tensor_scalar(scl[:, 4:5], gmaxs[:, 2:3], 1.0 / QMAX, 1e-12,
                                ALU.mult, ALU.max)
                e.tensor_scalar(scl[:, 2:3], gmaxs[:, 3:4], 1.0 / QMAX, 1e-12,
                                ALU.mult, ALU.max)
                e.reciprocal(scl[:, 5:6], scl[:, 4:5])
                e.reciprocal(scl[:, 3:4], scl[:, 2:3]).then_inc(S["bPRE1"], 1)

        def ph_B(eng, e):
            def vtail(n):
                e.wait_ge(S["sBS"], 2 * (n + 1))
                if n >= 2:
                    e.wait_ge(S["sBSP"][n % 2], 16 * ((n - 2) // 2 + 1))
                e.tensor_scalar(tb5[n % 2][:, :], tb4[n % 2][:, :], C_ap, 65535.0,
                                ALU.subtract, ALU.min)
                e.tensor_tensor(tb5[n % 2][:, :], tb2[n % 2][:, :],
                                tb5[n % 2][:, :], ALU.mult)
                e.reduce_max(bstat[:, n:n + 1], tb5[n % 2][:, :], axis=AX.X,
                             apply_absolute_value=True).then_inc(S["sBV"], 3)

            for n in range(NB):
                t, j = divmod(n, ICH // 4)
                if eng == "sync":
                    if n == 0:
                        e.wait_ge(S["sASP"][0], 16 * (NA // 8))
                        e.wait_ge(S["sASP"][1], 16 * (NA // 8))
                    if n >= 2:
                        e.wait_ge(S["sBR"], n - 1)
                    e.dma_start(
                        out=ld1[n % 2].ap().rearrange("p (c x) -> p c x", c=4),
                        in_=sp_gt.rearrange("k p n -> p k n")[:, 4 * j:4 * j + 4,
                                                              t * TD:(t + 1) * TD]
                    ).then_inc(S["sBL"][n % 2], 16)
                elif eng == "vector":
                    e.wait_ge(S["sBL"][n % 2], 16 * (n // 2 + 1))
                    if n == 0:
                        e.wait_ge(S["bPRE1"], 6)
                    if n >= 2:
                        e.wait_ge(S["sBS"], 2 * (n - 1))
                    e.tensor_scalar(tb1[n % 2][:, :], ld1[n % 2][:, :],
                                    scl[:, 3:4], C_ap, ALU.mult, ALU.add)
                    e.tensor_scalar(tb2[n % 2][:, :], tb1[n % 2][:, :], C_ap, 1.0,
                                    ALU.subtract, ALU.mult).then_inc(S["sBR"], 1)
                    if n >= 1:
                        vtail(n - 1)
                    if n == NB - 1:
                        vtail(n)
                elif eng == "scalar":
                    e.wait_ge(S["sBR"], n + 1)
                    if n >= 2:
                        e.wait_ge(S["sBV"], 3 * (n - 1))
                    e.activation(tb3[n % 2][:, :], tb2[n % 2][:, :], AF.Sigmoid,
                                 scale=scl[:, 2:3])
                    e.activation(tb4[n % 2][:, :], tb3[n % 2][:, :], AF.Identity,
                                 bias=C_ap, scale=65536.0).then_inc(S["sBS"], 2)
                elif eng == "gpsimd":
                    e.wait_ge(S["sBV"], 3 * (n + 1))
                    e.dma_start(
                        out=sp_p1.rearrange("k p n -> p k n")[:, 4 * j:4 * j + 4,
                                                              t * TD:(t + 1) * TD],
                        in_=tb5[n % 2].ap().rearrange("p (c x) -> p c x", c=4)
                    ).then_inc(S["sBSP"][n % 2], 16)

        def ph_bar3_fill(eng, e):
            if eng == "vector":
                e.wait_ge(S["sBV"], 3 * NB)
                e.reduce_max(mx2[:, 0:1], bstat[:, 0:NB], axis=AX.X)
                e.reduce_max(mx2[:, 1:2], bstat[:, 0:NB], axis=AX.X).then_inc(S["bPRE2"], 1)

        def ph_bar3_post(eng, e):
            if eng == "vector":
                e.reciprocal(scl[:, 7:8], gmaxs[:, 4:5])
                e.tensor_scalar(scl[:, 6:7], scl[:, 7:8], QMAX, 0.0,
                                ALU.mult, ALU.add).then_inc(S["bPRE2"], 1)

        def ph_C(eng, e):
            for n in range(NB):
                t, j = divmod(n, ICH // 4)
                if eng == "sync":
                    if n == 0:
                        e.wait_ge(S["sBSP"][0], 16 * (NB // 2))
                        e.wait_ge(S["sBSP"][1], 16 * (NB // 2))
                    if n >= 2:
                        e.wait_ge(S["sCS"], 2 * (n - 1))
                    e.dma_start(
                        out=ld1[n % 2].ap().rearrange("p (c x) -> p c x", c=4),
                        in_=sp_p1.rearrange("k p n -> p k n")[:, 4 * j:4 * j + 4,
                                                              t * TD:(t + 1) * TD]
                    ).then_inc(S["sCL"][n % 2], 16)
                    e.dma_start(
                        out=ld2[n % 2].ap().rearrange("p (c x) -> p c x", c=4),
                        in_=sp_up.rearrange("k p n -> p k n")[:, 4 * j:4 * j + 4,
                                                              t * TD:(t + 1) * TD]
                    ).then_inc(S["sCL"][n % 2], 16)
                elif eng == "scalar":
                    e.wait_ge(S["sCL"][n % 2], 32 * (n // 2 + 1))
                    if n == 0:
                        e.wait_ge(S["bPRE2"], 6)
                    if n >= 2:
                        e.wait_ge(S["sCV"], 2 * (n - 1))
                    e.activation(tb1[n % 2][:, :], ld1[n % 2][:, :], AF.Identity,
                                 bias=C_ap, scale=scl[:, 6:7])
                    e.activation(tb2[n % 2][:, :], tb1[n % 2][:, :], AF.Identity,
                                 bias=negC_ap)
                    e.activation(tb3[n % 2][:, :], ld2[n % 2][:, :], AF.Identity,
                                 bias=C_ap, scale=scl[:, 5:6]).then_inc(S["sCS"], 2)
                elif eng == "vector":
                    e.wait_ge(S["sCS"], 2 * (n + 1))
                    if n >= 2:
                        e.wait_ge(S["sCSP"][n % 2], 16 * ((n - 2) // 2 + 1))
                    e.scalar_tensor_tensor(tb4[n % 2][:, :], tb3[n % 2][:, :], negC_ap,
                                           tb2[n % 2][:, :], ALU.add, ALU.mult)
                    e.reduce_max(bstat[:, n:n + 1], tb4[n % 2][:, :], axis=AX.X,
                                 apply_absolute_value=True).then_inc(S["sCV"], 2)
                elif eng == "gpsimd":
                    e.wait_ge(S["sCV"], 2 * (n + 1))
                    e.dma_start(
                        out=sp_p2.rearrange("k p n -> p k n")[:, 4 * j:4 * j + 4,
                                                              t * TD:(t + 1) * TD],
                        in_=tb4[n % 2].ap().rearrange("p (c x) -> p c x", c=4)
                    ).then_inc(S["sCSP"][n % 2], 16)

        def ph_bar4_fill(eng, e):
            if eng == "vector":
                e.wait_ge(S["sCV"], 2 * NB)
                e.reduce_max(mx2[:, 0:1], bstat[:, 0:NB], axis=AX.X)
                e.reduce_max(mx2[:, 1:2], bstat[:, 0:NB], axis=AX.X).then_inc(S["bPRE3"], 1)

        def ph_bar4_post(eng, e):
            if eng == "vector":
                e.reciprocal(scl[:, 10:11], gmaxs[:, 6:7])
                e.tensor_scalar(scl[:, 11:12], gmaxs[:, 4:5], 1.0 / (QMAX * 65536.0),
                                0.0, ALU.mult, ALU.add)
                e.tensor_scalar(scl[:, 8:9], scl[:, 10:11], QMAX, 0.0, ALU.mult, ALU.add)
                e.tensor_tensor(scl[:, 12:13], scl[:, 11:12], scl[:, 2:3], ALU.mult)
                e.tensor_tensor(scl[:, 13:14], scl[:, 12:13], scl[:, 4:5], ALU.mult)
                e.tensor_tensor(scl[:, 14:15], scl[:, 13:14], gmaxs[:, 6:7], ALU.mult)
                e.tensor_scalar(scl[:, 9:10], scl[:, 14:15], 1.0 / QMAX, 0.0,
                                ALU.mult, ALU.add).then_inc(S["bPRE3"], 1)

        def ph_D_loads(eng, e):
            if eng == "sync":
                e.wait_ge(S["sCSP"][0], 16 * (NB // 2))
                e.wait_ge(S["sCSP"][1], 16 * (NB // 2))
                e.wait_ge(S["sAWW"][0], 16 * (NAPPLY // 2))
                e.wait_ge(S["sAWW"][1], 16 * (NAPPLY // 2))
                flatd = sp_wd.rearrange("c p n -> (c p) n")
                for ci in range(ICH):
                    e.dma_start(out=wd_sb[:, ci * D:(ci + 1) * D],
                                in_=flatd[:, 128 * ci:128 * (ci + 1)],
                                transpose=True).then_inc(S["sWDL"], 16)
                for j in range(2):
                    e.dma_start(
                        out=p2ld[j % 2].ap().rearrange("p (c x) -> p c x", c=4),
                        in_=sp_p2.rearrange("k p n -> p k n")[:, 4 * j:4 * j + 4,
                                                              0:TD]
                    ).then_inc(S["sDL"][j % 2], 16)
                e.dma_start(
                    out=x1d.ap().rearrange("p (k n) -> p k n", k=DC),
                    in_=sp_x1.rearrange("k p n -> p k n")[:, :, 0:TD]
                ).then_inc(S["sDX1"], 16)

        def ph_D(eng, e):
            def ln_post2(ts):
                e.wait_ge(S["sDLNE"], ts + 1)
                e.tensor_tensor(stx[0:1, 512:1024], abv[0:1, 512:1024],
                                stx[0:1, 0:512], ALU.subtract).then_inc(S["sDVC"], 1)
                e.drain()

            def ln_sc2(ts):
                e.wait_ge(S["sDST"], ts + 1)
                e.activation(stx[0:1, 0:512], pb[3][0:1, :], AF.Identity)
                e.activation(stx[0:1, 512:1024], pb[4][0:1, :], AF.Identity)
                e.activation(abv[0:1, 0:512], stx[0:1, 0:512], AF.Identity,
                             scale=1.0 / 1024.0)
                e.activation(abv[0:1, 512:1024], stx[0:1, 512:1024], AF.Identity,
                             scale=1.0 / 1024.0)
                e.activation(stx[0:1, 0:512], abv[0:1, 0:512],
                             AF.Square).then_inc(S["sDLNE"], 1)

            def ab_post2(ts):
                e.wait_ge(S["sDSQ"], ts + 1)
                e.reciprocal(abv[0:1, 512:1024], stx[0:1, 0:512])
                e.scalar_tensor_tensor(stx[0:1, 0:512], abv[0:1, 0:512], -1.0,
                                       abv[0:1, 512:1024], ALU.mult,
                                       ALU.mult).then_inc(S["sDAB"], 1)

            def affine2(ts):
                e.wait_ge(S["sDBC"], ts + 1)
                e.wait_ge(S["sOUT"], 16 * ts)
                p4 = pb[4][:, :]
                p5 = pb[5][:, :]
                b4 = bass.AP(p4.tensor, p4.offset, [[p4.ap[0][0], 128], [0, DC], [1, TD]])
                b5 = bass.AP(p5.tensor, p5.offset, [[p5.ap[0][0], 128], [0, DC], [1, TD]])
                sv = sqt28.ap().rearrange("p (b x) -> p b x", x=TD)
                rv = r2bs[ts % 2].ap().rearrange("p (b x) -> p b x", x=TD)
                e.tensor_tensor(sv, rv, b4, ALU.mult)
                e.tensor_tensor(sv, sv, b5, ALU.add)
                e._e.engine_nop().then_inc(S["sDXL"], 1)

            def round_d(tt, j):
                n = tt * NJ + j
                e.wait_ge(S["sDL"][n % 2], 16 * (n // 2 + 1))
                if n == 0:
                    e.wait_ge(S["bPRE3"], 6)
                if tt >= 2 and j == 0:
                    e.wait_ge(S["sDMM"], 8 * (tt - 1))
                e.tensor_scalar(t5b[:, :], p2ld[n % 2][:, :], scl[:, 8:9],
                                C_ap, ALU.mult, ALU.add).then_inc(S["sDS2"], 1)
                e.tensor_scalar(otile[tt % 2][:, 2048 * j:2048 * (j + 1)],
                                t5b[:, :], C_ap, 1.0,
                                ALU.subtract, ALU.mult).then_inc(S["sDV2"], 1)

            for t in range(NT):
                _sc = nc.named_scope(f"pd{t}")
                _sc.__enter__()
                if eng == "sync":
                    if t == 0:
                        for j in range(2, NJ):
                            e.wait_ge(S["sDS2"], j - 1)
                            e.dma_start(
                                out=p2ld[j % 2].ap().rearrange("p (c x) -> p c x", c=4),
                                in_=sp_p2.rearrange("k p n -> p k n")[:, 4 * j:4 * j + 4,
                                                                      0:TD]
                            ).then_inc(S["sDL"][j % 2], 16)
                    if t >= 1:
                        e.wait_ge(S["sDEV"], 8 * t)
                        e.dma_start(
                            out=x1d.ap().rearrange("p (k n) -> p k n", k=DC),
                            in_=sp_x1.rearrange("k p n -> p k n")[:, :,
                                                                  t * TD:(t + 1) * TD]
                        ).then_inc(S["sDX1"], 16)
                    if t + 1 < NT:
                        for j in range(NJ):
                            n = (t + 1) * NJ + j
                            e.wait_ge(S["sDS2"], n - 1)
                            e.dma_start(
                                out=p2ld[n % 2].ap().rearrange("p (c x) -> p c x", c=4),
                                in_=sp_p2.rearrange("k p n -> p k n")[:, 4 * j:4 * j + 4,
                                                                      (t + 1) * TD:(t + 2) * TD]
                            ).then_inc(S["sDL"][n % 2], 16)
                    if t >= 1:
                        e.wait_ge(S["sDXL"], t)
                        e.dma_start(
                            out=out_fm.rearrange("k p n -> p k n")[:, :,
                                                                   (t - 1) * TD:t * TD],
                            in_=sqt28.ap().rearrange("p (k n) -> p k n", k=DC)
                        ).then_inc(S["sOUT"], 16)
                if eng == "tensor":
                    if t == 0:
                        e.wait_ge(S["sWDL"], 16 * ICH)

                    def stats2(ts):
                        if ts >= 2:
                            e.wait_ge(S["sDVC"], ts)
                        for ei in range(DC):
                            e.wait_ge(S["sDCB"], 8 * ts + ei + 1)
                            if ei < DC - 1:
                                e.matmul(pb[3][0:1, :], ones_c[:, :],
                                         r2bs[ts % 2][:, ei * TD:(ei + 1) * TD],
                                         start=(ei == 0), stop=False)
                                e.matmul(pb[4][0:1, :], ones_c[:, :],
                                         sqt28[:, ei * TD:(ei + 1) * TD],
                                         start=(ei == 0), stop=False
                                         ).then_inc(S["sDSTe"], 1)
                            else:
                                e.matmul(pb[4][0:1, :], ones_c[:, :],
                                         sqt28[:, ei * TD:(ei + 1) * TD],
                                         start=False, stop=True).then_inc(S["sDSTe"], 1)
                                e.matmul(pb[3][0:1, :], ones_c[:, :],
                                         r2bs[ts % 2][:, ei * TD:(ei + 1) * TD],
                                         start=False, stop=True).then_inc(S["sDST"], 1)

                    def bcast2(ts):
                        e.wait_ge(S["sDAB"], ts + 1)
                        if ts >= 1:
                            e.wait_ge(S["sDXL"], ts)
                        e.matmul(pb[4][:, :], ones_r[:, :], abv[0:1, 512:1024],
                                 start=True, stop=True)
                        e.matmul(pb[5][:, :], ones_r[:, :], stx[0:1, 0:512],
                                 start=True, stop=True).then_inc(S["sDBC"], 1)

                    if t >= 1:
                        stats2(t - 1)
                    e.wait_ge(S["sDV2"], NJ * (t + 1))
                    for h in range(DC):
                        if 8 * t + h >= 2:
                            e.wait_ge(S["sDEV"], 8 * t + h - 1)
                        m = None
                        for i in range(ICH):
                            m = e.matmul(pb[h % 2][:, :],
                                         wd_sb[:, i * D + 128 * h: i * D + 128 * h + 128],
                                         otile[t % 2][:, i * TD:(i + 1) * TD],
                                         start=(i == 0), stop=(i == ICH - 1))
                        m.then_inc(S["sDMM"], 1)
                        if h == 0 and t >= 1:
                            bcast2(t - 1)
                    if t == NT - 1:
                        stats2(t)
                        bcast2(t)
                elif eng == "vector":
                    if t == 0:
                        for j in range(NJ):
                            round_d(0, j)
                    e.wait_ge(S["sDX1"], 16 * (t + 1))
                    if t >= 2:
                        e.wait_ge(S["sDXL"], t - 1)
                    if t >= 1:
                        ln_post2(t - 1)
                        ab_post2(t - 1)
                    for h in range(DC):
                        e.wait_ge(S["sDMM"], 8 * t + h + 1)
                        e.scalar_tensor_tensor(r2bs[t % 2][:, h * TD:(h + 1) * TD],
                                               pb[h % 2][:, :], scl[:, 9:10],
                                               x1d[:, h * TD:(h + 1) * TD],
                                               ALU.mult, ALU.add).then_inc(S["sDEV"], 1)
                        if h == 0 and t >= 1:
                            affine2(t - 1)
                        if h < NJ and t + 1 < NT:
                            round_d(t + 1, h)
                    if t == NT - 1:
                        ln_post2(t)
                        ab_post2(t)
                        affine2(t)
                elif eng == "scalar":
                    if t >= 1:
                        ln_sc2(t - 1)
                        e.wait_ge(S["sDVC"], t)
                        e.activation(stx[0:1, 0:512], stx[0:1, 512:1024], AF.Sqrt,
                                     bias=EPS_ap).then_inc(S["sDSQ"], 1)
                    for h in range(DC):
                        e.wait_ge(S["sDEV"], 8 * t + h + 1)
                        if t >= 1:
                            e.wait_ge(S["sOUT"], 16 * t)
                        e.activation(sqt28[:, h * TD:(h + 1) * TD],
                                     r2bs[t % 2][:, h * TD:(h + 1) * TD],
                                     AF.Square).then_inc(S["sDCB"], 1)
                    if t == NT - 1:
                        ln_sc2(t)
                        e.wait_ge(S["sDVC"], t + 1)
                        e.activation(stx[0:1, 0:512], stx[0:1, 512:1024], AF.Sqrt,
                                     bias=EPS_ap).then_inc(S["sDSQ"], 1)
                _sc.__exit__(None, None, None)
            if eng == "sync":
                e.wait_ge(S["sDXL"], NT)
                e.dma_start(
                    out=out_fm.rearrange("k p n -> p k n")[:, :, (NT - 1) * TD:NT * TD],
                    in_=sqt28.ap().rearrange("p (k n) -> p k n", k=DC)
                ).then_inc(S["sOUT"], 16)
                e.wait_ge(S["sOUT"], 16 * NT)
            if eng == "gpsimd":
                e.wait_ge(S["sOUT"], 16 * NT)

        def stream(eng, e):
            with nc.named_scope("setup"):
                ph_setup(eng, e)
            with nc.named_scope("mixer"):
                for t in range(NT):
                    with nc.named_scope(f"mx{t}"):
                        ph_mixer_tile(eng, e, t)
                ph_mixer_tail(eng, e)
            with nc.named_scope("bar1"):
                ph_bar1_fill(eng, e)
                ph_gmax_barrier(eng, e, 0, 1)
                ph_bar1_post(eng, e)
            if eng == "gpsimd":
                ph_gapply_store(e, APT * NT - 5)
                ph_gapply_store(e, APT * NT - 4)
                ph_gapply(e, APT * NT - 2, do_store=False)
                ph_gapply(e, APT * NT - 1, do_store=False)
                ph_gapply_store(e, APT * NT - 3)
                ph_gapply_store(e, APT * NT - 2)
                ph_gapply_store(e, APT * NT - 1)
            ph_A_wu_loads(eng, e)
            with nc.named_scope("xq"):
                ph_xq(eng, e)
            if stop_after == "mixer":
                if eng == "sync":
                    e.wait_ge(S["sXQS"][0], 16 * (NT // 2))
                    e.wait_ge(S["sXQS"][1], 16 * (NT // 2))
                    e.wait_ge(S["sAWW"][0], 16 * 24)
                    e.wait_ge(S["sAWW"][1], 16 * 24)
                    e.dma_start(out=gdbg[:, 0:8], in_=mxc[:, :]).then_inc(S["sOUT"], 16)
                    e.dma_start(out=gdbg[:, 8:10], in_=scl[:, 0:2]).then_inc(S["sOUT"], 16)
                    e.wait_ge(S["sOUT"], 32)
                elif eng == "gpsimd":
                    e.wait_ge(S["sOUT"], 32)
                return
            with nc.named_scope("phA"):
                ph_A(eng, e)
            with nc.named_scope("bar2"):
                ph_bar2_fill(eng, e)
                ph_gmax_barrier(eng, e, 1, 2)
                ph_bar2_post(eng, e)
            if stop_after == "A":
                if eng == "sync":
                    e.wait_ge(S["sASP"][0], 16 * (NA // 8))
                    e.wait_ge(S["sASP"][1], 16 * (NA // 8))
                    e.wait_ge(S["bPRE1"], 6)
                    e.dma_start(out=gdbg[:, 0:6], in_=scl[:, 0:6]).then_inc(S["sOUT"], 16)
                    e.wait_ge(S["sOUT"], 16)
                elif eng == "gpsimd":
                    e.wait_ge(S["sOUT"], 16)
                return
            pbar(e, 1)
            with nc.named_scope("phB"):
                ph_B(eng, e)
            with nc.named_scope("bar3"):
                ph_bar3_fill(eng, e)
                ph_gmax_barrier(eng, e, 2, 1)
                ph_bar3_post(eng, e)
            if stop_after == "B":
                if eng == "sync":
                    e.wait_ge(S["sBSP"][0], 16 * (NB // 2))
                    e.wait_ge(S["sBSP"][1], 16 * (NB // 2))
                    e.wait_ge(S["bPRE2"], 6)
                    e.dma_start(out=gdbg[:, 0:8], in_=scl[:, 0:8]).then_inc(S["sOUT"], 16)
                    e.wait_ge(S["sOUT"], 16)
                elif eng == "gpsimd":
                    e.wait_ge(S["sOUT"], 16)
                return
            with nc.named_scope("phC"):
                ph_C(eng, e)
            ph_D_loads(eng, e)
            with nc.named_scope("bar4"):
                ph_bar4_fill(eng, e)
                ph_gmax_barrier(eng, e, 3, 1)
                ph_bar4_post(eng, e)
            if stop_after == "C":
                if eng == "sync":
                    e.wait_ge(S["sCSP"][0], 16 * (NB // 2))
                    e.wait_ge(S["sCSP"][1], 16 * (NB // 2))
                    e.wait_ge(S["bPRE3"], 6)
                    e.dma_start(out=gdbg[:, 0:10], in_=scl[:, 0:10]).then_inc(S["sOUT"], 16)
                    e.wait_ge(S["sOUT"], 16)
                elif eng == "gpsimd":
                    e.wait_ge(S["sOUT"], 16)
                return
            pbar(e, 2)
            with nc.named_scope("phD"):
                ph_D(eng, e)

        with nc.Block() as bk:
            @bk.sync
            def _(e):
                stream("sync", FE(e, fences["sync"]))

            @bk.gpsimd
            def _(e):
                stream("gpsimd", FE(e, fences["gpsimd"]))

            @bk.tensor
            def _(e):
                stream("tensor", FE(e, fences["tensor"]))

            @bk.scalar
            def _(e):
                stream("scalar", FE(e, fences["scalar"]))

            @bk.vector
            def _(e):
                stream("vector", FE(e, fences["vector"]))
    return nc


def _prep_inputs(L0_attn_x, L0_attn_Wg, L0_attn_bg, L0_attn_Wv, L0_attn_bv,
                 L0_attn_Wd, L0_attn_bd, L0_ffn_w_gate, L0_ffn_w_up, L0_ffn_w_down):
    f = np.ascontiguousarray
    shared = {
        "wg_t": f(L0_attn_Wg.T).reshape(DC, 128, D),
        "wv_t": f(L0_attn_Wv.T).reshape(DC, 128, D),
        "wd_t": f(L0_attn_Wd.T).reshape(DC, 128, D),
        "bgh": f(L0_attn_bg.reshape(DC, 128).T),
        "bvh": f(L0_attn_bv.reshape(DC, 128).T),
        "bdh": f(L0_attn_bd.reshape(DC, 128).T),
        "wu_n": f(L0_ffn_w_up).reshape(ICH, 128, D),
        "wq_n": f(L0_ffn_w_gate).reshape(ICH, 128, D),
        "wdn_n": f(L0_ffn_w_down).reshape(DC, 128, I),
    }
    in_maps = []
    for c in range(NCORES):
        xc = f(L0_attn_x[2 * c:2 * c + 2].reshape(TOK, D).T).reshape(DC, 128, TOK)
        m = {"x_fm": xc}
        m.update(shared)
        in_maps.append(m)
    return in_maps


def _run(inputs, stop_after="full", debug=False):
    nc = build(stop_after=stop_after, debug=debug)
    in_maps = _prep_inputs(**{k: np.asarray(v) for k, v in inputs.items()})
    res = run_bass_kernel_spmd(nc, in_maps, core_ids=list(range(NCORES)))
    if debug:
        return res
    out = np.empty((B, S, D), dtype=np.float32)
    for c in range(NCORES):
        o = res.results[c]["out_fm"].reshape(D, TOK)
        out[2 * c:2 * c + 2] = o.T.reshape(2, S, D)
    return out


def kernel(**inputs):
    return _run(inputs)


# version 11
# speedup vs baseline: 1.0706x; 1.0158x over previous
import contextlib

import numpy as np
import concourse.bass as bass
import concourse.mybir as mybir
from concourse.bass_utils import run_bass_kernel_spmd

F32 = mybir.dt.float32
BF16 = mybir.dt.bfloat16
ALU = mybir.AluOpType
AF = mybir.ActivationFunctionType
AX = mybir.AxisListType

NCORES = 8
B, S, D, I = 16, 2048, 1024, 3072
TOK = (B // NCORES) * S
TD = 512
NT = TOK // TD
DC = D // 128
ICH = I // 128
QMAX = 32767.0
C_LO = 1.5 * 2.0 ** 23
C_DELTA = 1.5 * 2.0 ** 52 - 1.5 * 2.0 ** 23

ENGS = ("sync", "gpsimd", "tensor", "scalar", "vector")

NAPPLY = 72
NA = NT * 2 * ICH
NB = NT * (ICH // 4)
NJ = ICH // 4


def _ap_span(ap):
    try:
        dims = list(ap.ap)
        off = ap.offset
        span = 1
        for st, cnt in dims:
            span += abs(st) * (cnt - 1)
        refined = None
        if dims:
            pstr, pcnt = dims[0]
            rest = 1
            for st, cnt in dims[1:]:
                rest += abs(st) * (cnt - 1)
            if pstr >= rest and pstr > 0:
                p_lo = off // pstr
                c_lo = off % pstr
                refined = (pstr, p_lo, p_lo + pcnt, c_lo, c_lo + rest)
        return (ap.tensor.name, off, off + span, refined)
    except AttributeError:
        return None


def _overlap(k, lst):
    for (nm, lo, hi, ref) in lst:
        if nm != k[0]:
            continue
        if k[2] <= lo or hi <= k[1]:
            continue
        if ref is not None and k[3] is not None and ref[0] == k[3][0]:
            if (k[3][2] <= ref[1] or ref[2] <= k[3][1]
                    or k[3][4] <= ref[3] or ref[4] <= k[3][3]):
                continue
        return True
    return False


class FE:

    def __init__(self, eng, state):
        self._e = eng
        self._w = state["w"]
        self._r = state["r"]

    def __getattr__(self, name):
        return getattr(self._e, name)

    def drain(self, *a, **k):
        self._w.clear()
        self._r.clear()
        return self._e.drain(*a, **k)

    def _pre(self, outs, ins):
        need = False
        for ap in list(outs) + list(ins):
            if ap is None or isinstance(ap, (int, float)):
                continue
            k = _ap_span(ap)
            if k is not None and _overlap(k, self._w):
                need = True
                break
        if not need:
            for ap in outs:
                if ap is None or isinstance(ap, (int, float)):
                    continue
                k = _ap_span(ap)
                if k is not None and _overlap(k, self._r):
                    need = True
                    break
        if need:
            self._e.drain()
            self._w.clear()
            self._r.clear()

    def _post(self, outs, ins):
        for ap in outs:
            if ap is None or isinstance(ap, (int, float)):
                continue
            k = _ap_span(ap)
            if k is not None:
                self._w.append(k)
        for ap in ins:
            if ap is None or isinstance(ap, (int, float)):
                continue
            k = _ap_span(ap)
            if k is not None:
                self._r.append(k)

    def _op(self, outs, ins, fn):
        self._pre(outs, ins)
        r = fn()
        self._post(outs, ins)
        return r

    def memset(self, out, v):
        return self._op([out], [], lambda: self._e.memset(out, v))

    def tensor_scalar(self, out, in0, s1, s2, op0, op1=None):
        if op1 is not None:
            return self._op([out], [in0, s1, s2],
                            lambda: self._e.tensor_scalar(out, in0, s1, s2, op0, op1))
        return self._op([out], [in0, s1],
                        lambda: self._e.tensor_scalar(out, in0, s1, s2, op0))

    def tensor_tensor(self, out, a, b, op):
        return self._op([out], [a, b], lambda: self._e.tensor_tensor(out, a, b, op))

    def scalar_tensor_tensor(self, out, in0, sc, in1, op0, op1):
        return self._op([out], [in0, sc, in1],
                        lambda: self._e.scalar_tensor_tensor(out, in0, sc, in1, op0, op1))

    def tensor_tensor_scan(self, out, d0, d1, init, op0, op1):
        return self._op([out], [d0, d1, init],
                        lambda: self._e.tensor_tensor_scan(out, d0, d1, init, op0, op1))

    def reduce_max(self, out, in_, axis, apply_absolute_value=False):
        return self._op([out], [in_],
                        lambda: self._e.reduce_max(out, in_, axis=axis,
                                                   apply_absolute_value=apply_absolute_value))

    def reciprocal(self, out, in_):
        return self._op([out], [in_], lambda: self._e.reciprocal(out, in_))

    def activation(self, out, in_, func, bias=None, scale=None):
        kw = {}
        if bias is not None:
            kw["bias"] = bias
        if scale is not None:
            kw["scale"] = scale
        return self._op([out], [in_, bias, scale],
                        lambda: self._e.activation(out, in_, func, **kw))

    def affine_select(self, out, in_, compare_op, fill, base, pattern,
                      channel_multiplier):
        return self._op([out], [in_],
                        lambda: self._e.affine_select(
                            out=out, in_=in_, compare_op=compare_op, fill=fill,
                            base=base, pattern=pattern,
                            channel_multiplier=channel_multiplier))

    def tensor_copy(self, out, in_):
        return self._op([out], [in_], lambda: self._e.tensor_copy(out, in_))

    def dma_start(self, out, in_, **kw):
        self._pre([], [in_])
        return self._e.dma_start(out=out, in_=in_, **kw)


def build(stop_after="full", debug=False):
    nc = bass.Bass("TRN2", target_bir_lowering=False)
    dbg = "ExternalOutput" if debug else "Internal"

    x_fm = nc.dram_tensor("x_fm", [DC, 128, TOK], F32, kind="ExternalInput").ap()
    wg_t = nc.dram_tensor("wg_t", [DC, 128, D], F32, kind="ExternalInput").ap()
    wv_t = nc.dram_tensor("wv_t", [DC, 128, D], F32, kind="ExternalInput").ap()
    wd_t = nc.dram_tensor("wd_t", [DC, 128, D], F32, kind="ExternalInput").ap()
    bgh = nc.dram_tensor("bgh", [128, DC], F32, kind="ExternalInput").ap()
    bvh = nc.dram_tensor("bvh", [128, DC], F32, kind="ExternalInput").ap()
    bdh = nc.dram_tensor("bdh", [128, DC], F32, kind="ExternalInput").ap()
    wu_n = nc.dram_tensor("wu_n", [ICH, 128, D], F32, kind="ExternalInput").ap()
    wq_n = nc.dram_tensor("wq_n", [ICH, 128, D], F32, kind="ExternalInput").ap()
    wdn_n = nc.dram_tensor("wdn_n", [DC, 128, I], F32, kind="ExternalInput").ap()

    out_fm = nc.dram_tensor("out_fm", [DC, 128, TOK], F32, kind="ExternalOutput").ap()

    sp_x1 = nc.dram_tensor("sp_x1", [DC, 128, TOK], F32, kind=dbg).ap()
    sp_xq = nc.dram_tensor("sp_xq", [DC, 128, TOK], BF16, kind=dbg).ap()
    sp_up = nc.dram_tensor("sp_up", [ICH, 128, TOK], F32, kind=dbg).ap()
    sp_gt = nc.dram_tensor("sp_gt", [ICH, 128, TOK], F32, kind=dbg).ap()
    sp_p1 = nc.dram_tensor("sp_p1", [ICH, 128, TOK], F32, kind=dbg).ap()
    sp_p2 = nc.dram_tensor("sp_p2", [ICH, 128, TOK], F32, kind=dbg).ap()
    sp_wu = nc.dram_tensor("sp_wu", [ICH, 128, D], BF16, kind=dbg).ap()
    sp_wq = nc.dram_tensor("sp_wq", [ICH, 128, D], BF16, kind=dbg).ap()
    sp_wd = nc.dram_tensor("sp_wd", [DC, 128, I], BF16, kind=dbg).ap()
    ccb = []
    for bi in range(4):
        ci = nc.dram_tensor(f"cc_in{bi}", [128, 2], F32).ap()
        co = nc.dram_tensor(f"cc_out{bi}", [128, 2], F32).ap()
        ccb.append((ci, co))
    gdbg = nc.dram_tensor("gdbg", [128, 16], F32, kind=dbg).ap()

    fences = {k: {"w": [], "r": []} for k in ENGS}
    es = contextlib.ExitStack()
    with es:
        sbuf = lambda name, shape, dt=F32: es.enter_context(nc.sbuf_tensor(name, shape, dt))
        sem = lambda name: es.enter_context(nc.semaphore(name))

        cst = sbuf("cst", [128, 16])
        ident = sbuf("ident", [128, 128])
        ones_bf = sbuf("ones_bf", [128, 1], BF16)
        ones_r = sbuf("ones_r", [1, 128])
        ones_c = sbuf("ones_c", [128, 1])
        selm = sbuf("selm", [2, 256])
        biases = sbuf("biases", [128, 3 * DC])
        scl = sbuf("scl", [128, 16])
        gmaxs = sbuf("gmaxs", [128, 8])
        mx2 = sbuf("mx2", [128, 2])
        gsc = sbuf("gsc", [2, 1])
        stx = sbuf("stx", [1, 1024])
        abv = sbuf("abv", [1, 1024])
        carry = sbuf("carry", [128, DC])

        pb = [es.enter_context(nc.psum_tensor(f"pb{i}", [128, 512], F32))
              for i in range(8)]

        A0 = 40928
        LIM = 229344

        def at(name, shape, dt, off):
            end = off + shape[1] * mybir.dt.size(dt)
            assert end <= LIM, (name, end)
            return nc.alloc_sbuf_tensor_at(name, shape, dt, offset=off)

        wgm = at("wgm", [128, DC * D], BF16, A0)
        wvm = at("wvm", [128, DC * D], BF16, A0 + 16384)
        wdm = at("wdm", [128, DC * D], BF16, A0 + 32768)
        xf = [at(f"xf{i}", [128, DC * TD], F32, A0 + 49152 + 16384 * i) for i in range(2)]
        xbm = [at(f"xbm{i}", [128, DC * TD], BF16, A0 + 81920 + 8192 * i) for i in range(2)]
        wk_xg = [at(f"wkxg{i}", [128, TD], F32, A0 + 98304 + 2048 * i) for i in range(2)]
        wk_tv = [at(f"wktv{i}", [128, TD], F32, A0 + 102400 + 2048 * i) for i in range(2)]
        wk_sa = [at(f"wksa{i}", [128, TD], F32, A0 + 106496 + 2048 * i) for i in range(2)]
        wk_xs1 = at("wkxs", [128, TD], F32, A0 + 110592)
        wk_av1 = at("wkav", [128, TD], F32, A0 + 112640)
        hbuf = at("hbuf", [128, 2 * TD], F32, A0 + 114688)
        rbuf = at("rbuf", [128, DC * TD], F32, A0 + 118784)
        sqt8 = at("sqt8", [128, DC * TD], F32, A0 + 135168)
        rbuf2 = at("rbuf2", [128, DC * TD], F32, A0 + 151552)
        rbufs = [rbuf, rbuf2]
        mxc = at("mxc", [128, 8], F32, A0 + 167936)
        DQ0 = A0 + 168000
        dq_nat = [at(f"dqn{i}", [128, D], F32, DQ0 + 4096 * i) for i in range(2)]
        dq_wb = [at(f"dqwb{i}", [128, D], BF16, DQ0 + 8192 + 2048 * i) for i in range(2)]
        dq_sc = at("dqsc", [128, 128], F32, DQ0 + 12288)
        wk_xs2 = at("wkxs2", [128, TD], F32, DQ0 + 12800)
        wk_av2 = at("wkav2", [128, TD], F32, DQ0 + 14848)
        wk_xs = [wk_xs1, wk_xs2]
        wk_av = [wk_av1, wk_av2]

        wu_sb = at("wu_sb", [128, DC * I], BF16, A0)
        wq_sb = at("wq_sb", [128, DC * I], BF16, A0 + 49152)
        xqa = [at(f"xqa{i}", [128, DC * TD], BF16, A0 + 98304 + 8192 * i) for i in range(4)]
        gbuf = [at(f"gbuf{i}", [128, 2048], F32, A0 + 151552 + 8192 * i) for i in range(2)]
        astat = at("astat", [128, 96], F32, A0 + 131072)

        ld1 = [at(f"ld1_{i}", [128, 2048], F32, A0 + 8192 * i) for i in range(2)]
        ld2 = [at(f"ld2_{i}", [128, 2048], F32, A0 + 16384 + 8192 * i) for i in range(2)]
        tb1 = [at(f"tb1_{i}", [128, 2048], F32, A0 + 32768 + 8192 * i) for i in range(2)]
        tb2 = [at(f"tb2_{i}", [128, 2048], F32, A0 + 49152 + 8192 * i) for i in range(2)]
        tb3 = [at(f"tb3_{i}", [128, 2048], F32, A0 + 65536 + 8192 * i) for i in range(2)]
        tb4 = [at(f"tb4_{i}", [128, 2048], F32, A0 + 81920 + 8192 * i) for i in range(2)]
        tb5 = [at(f"tb5_{i}", [128, 2048], F32, A0 + 98304 + 8192 * i) for i in range(2)]
        bstat = at("bstat", [128, 48], F32, A0 + 114688)

        wd_sb = at("wd_sb", [128, ICH * D], BF16, A0)
        otile = [at(f"otile{i}", [128, ICH * TD], BF16, A0 + 49152 + 24576 * i) for i in range(2)]
        p2ld = [at(f"p2ld{i}", [128, 2048], F32, A0 + 98304 + 8192 * i) for i in range(2)]
        t5b = at("t5b", [128, 2048], F32, A0 + 114688)
        x1d = at("x1d", [128, DC * TD], F32, A0 + 122880)
        r2b = at("r2b", [128, DC * TD], F32, A0 + 139264)
        r2b2 = at("r2b2", [128, DC * TD], F32, A0 + 155648)
        r2bs = [r2b, r2b2]
        sqt28 = at("sqt28", [128, DC * TD], F32, A0 + 172032)

        S = {n: sem(n) for n in (
            "sWM sWC sMM sEP sRS sCB sSTe sST sSE sVC sSQ sAB sBC sXL sLNE sDLNE sVX "
            "sSPX sPB sXQV sANL sAPC "
            "sWL2 sWQL sWDL sAMM sAEV sARD sBS sBR sBV sCS sCV "
            "sDS2 sDV2 sDX1 sDMM sDEV sDCB sDST sDSTe sDSE2 sDVC sDSQ "
            "sDAB sDBC sDXL sOUT sSU".split())}
        for pn in ("sLX sLB sXQL sAWW sXQS sASP sBL sBSP "
                   "sCL sCSP sDL".split()):
            S[pn] = [sem(pn + "0"), sem(pn + "1")]
        S["sXQA"] = [sem(f"sXQA{i}") for i in range(4)]
        for bi in range(4):
            S[f"bPRE{bi}"] = sem(f"bPRE{bi}")
            S[f"bDMA{bi}"] = sem(f"bDMA{bi}")
            S[f"bCC{bi}"] = sem(f"bCC{bi}")

        C_ap = cst[:, 1:2]
        negC_ap = cst[:, 2:3]
        EPS_ap = cst[0:1, 5:6]

        apply_steps = []
        for c in range(ICH):
            apply_steps.append((wu_n[c][:, :], sp_wu[c][:, :]))
        for c in range(ICH):
            apply_steps.append((wq_n[c][:, :], sp_wq[c][:, :]))
        for hc in range(DC):
            for j in range(3):
                apply_steps.append((wdn_n[hc][:, j * D:(j + 1) * D],
                                    sp_wd[hc][:, j * D:(j + 1) * D]))
        assert len(apply_steps) == NAPPLY
        APT = 6

        def applies_for_tile(t):
            if t == NT - 1:
                return range(APT * t, APT * t + 2)
            return range(APT * t, APT * (t + 1))

        def x_src(t):
            return x_fm.rearrange("k p n -> p k n")[:, :, t * TD:(t + 1) * TD]

        def pbar(e, k):
            e.drain()
            if hasattr(e._e, "engine_nop"):
                e._e.engine_nop().then_inc(S["sPB"], 1)
            else:
                e._e.nop().then_inc(S["sPB"], 1)
            e._e.wait_ge(S["sPB"], 5 * k)
            e._w.clear()
            e._r.clear()


        def ph_setup(eng, e):
            if eng == "gpsimd":
                e.memset(ident[:, :], 0.0)
                e.affine_select(out=ident[:, :], in_=ident[:, :],
                                compare_op=ALU.not_equal, fill=1.0, base=0,
                                pattern=[[-1, 128]], channel_multiplier=1)
                e.memset(ones_bf[:, :], 1.0)
                e.memset(ones_c[:, :], 1.0)
                e.memset(ones_r[:, :], 1.0)
                e.memset(selm[0:2, :], 1.0)
                e.affine_select(out=selm[0:2, :], in_=selm[0:2, :],
                                compare_op=ALU.is_ge, fill=0.0, base=0,
                                pattern=[[1, 256]], channel_multiplier=-128)
                e.affine_select(out=selm[0:2, :], in_=selm[0:2, :],
                                compare_op=ALU.is_ge, fill=0.0, base=127,
                                pattern=[[-1, 256]], channel_multiplier=128)
                e._e.engine_nop().then_inc(S["sSU"], 1)
                for dst, srcw in ((wgm, wg_t), (wvm, wv_t), (wdm, wd_t)):
                    e.dma_start(out=dst.ap().rearrange("p (k n) -> p k n", k=DC),
                                in_=srcw.rearrange("k p n -> p k n")).then_inc(S["sWC"], 16)
            elif eng == "vector":
                e.memset(cst[:, 3:4], 1.0)
                e.memset(cst[:, 5:6], 1e-5)
                e.tensor_scalar(cst[:, 4:5], cst[:, 3:4], 2.0 ** -30, 1.0,
                                ALU.add, ALU.subtract)
                e.tensor_scalar(cst[:, 0:1], cst[:, 4:5], 2.0 ** 30, 1.0,
                                ALU.mult, ALU.mult)
                e.tensor_scalar(cst[:, 1:2], cst[:, 0:1], C_DELTA, C_LO,
                                ALU.mult, ALU.add)
                e.tensor_scalar(cst[:, 2:3], cst[:, 1:2], -1.0, 0.0, ALU.mult, ALU.add)
            elif eng == "sync":
                e.dma_start(out=biases[:, 0:DC], in_=bgh[:, :]).then_inc(S["sWM"], 16)
                e.dma_start(out=biases[:, DC:2 * DC], in_=bvh[:, :]).then_inc(S["sWM"], 16)
                e.dma_start(out=biases[:, 2 * DC:3 * DC], in_=bdh[:, :]).then_inc(S["sWM"], 16)

        def ph_gapply_store(e, s):
            e.wait_ge(S["sAPC"], s + 1)
            _, pdst = apply_steps[s]
            e.dma_start(out=pdst,
                        in_=dq_wb[s % 2][:, :]).then_inc(S["sAWW"][s % 2], 16)

        def ph_gapply(e, s, do_store=True):
            src, dst = apply_steps[s]
            if s >= 1:
                e.wait_ge(S["sAPC"], s - 1)
            e.dma_start(out=dq_nat[s % 2][:, :], in_=src).then_inc(S["sANL"], 16)
            if s >= 1 and do_store:
                ph_gapply_store(e, s - 1)

        def ph_vapply2(e, s):
            na, nb_ = dq_nat[s % 2], dq_nat[(s + 1) % 2]
            base = dq_sc[:, :]
            bc = lambda c0: bass.AP(base.tensor, base.offset + c0,
                                    [[128, 128], [1, 32], [0, 32]])
            e.wait_ge(S["sANL"], 16 * (s + 2))
            if s >= 2:
                e.wait_ge(S["sAWW"][0], 16 * (s // 2))
                e.wait_ge(S["sAWW"][1], 16 * (s // 2))
            na3 = na.ap().rearrange("p (b x) -> p b x", x=32)
            nb3 = nb_.ap().rearrange("p (b x) -> p b x", x=32)
            e.reduce_max(dq_sc[:, 0:32], na3, axis=AX.X, apply_absolute_value=True)
            e.reduce_max(dq_sc[:, 32:64], nb3, axis=AX.X, apply_absolute_value=True)
            e.tensor_scalar(dq_sc[:, 0:64], dq_sc[:, 0:64], 1.0 / 7.0, 1e-12,
                            ALU.mult, ALU.max)
            e.reciprocal(dq_sc[:, 64:128], dq_sc[:, 0:64])
            e.tensor_tensor(na3, na3, bc(64), ALU.mult)
            e.tensor_scalar(na[:, :], na[:, :], C_ap, C_ap, ALU.add, ALU.subtract)
            e.tensor_tensor(dq_wb[s % 2].ap().rearrange("p (b x) -> p b x", x=32),
                            na3, bc(0), ALU.mult).then_inc(S["sAPC"], 1)
            e.tensor_tensor(nb3, nb3, bc(96), ALU.mult)
            e.tensor_scalar(nb_[:, :], nb_[:, :], C_ap, C_ap, ALU.add, ALU.subtract)
            e.tensor_tensor(dq_wb[(s + 1) % 2].ap().rearrange("p (b x) -> p b x", x=32),
                            nb3, bc(32), ALU.mult).then_inc(S["sAPC"], 1)
        def ph_mixer_tile(eng, e, t):
            if eng == "sync":
                if t == 0:
                    for tt in (0, 1):
                        e.dma_start(out=xf[tt].ap().rearrange("p (k n) -> p k n", k=DC),
                                    in_=x_src(tt)).then_inc(S["sLX"][tt % 2], 16)
                if t >= 1:
                    e.wait_ge(S["sXL"], t)
                    e.dma_start(
                        out=sp_x1.rearrange("k p n -> p k n")[:, :,
                                                              (t - 1) * TD:t * TD],
                        in_=xf[(t - 1) % 2].ap().rearrange("p (k n) -> p k n", k=DC)
                    ).then_inc(S["sSPX"], 16)
                    if t + 1 < NT:
                        e.wait_ge(S["sSPX"], 16 * t)
                        e.dma_start(out=xf[(t + 1) % 2].ap().rearrange("p (k n) -> p k n", k=DC),
                                    in_=x_src(t + 1)).then_inc(S["sLX"][(t + 1) % 2], 16)
            elif eng == "gpsimd":
                if t == 0:
                    for tt in (0, 1):
                        e.dma_start(out=xbm[tt].ap().rearrange("p (k n) -> p k n", k=DC),
                                    in_=x_src(tt)).then_inc(S["sLB"][tt % 2], 16)
                for s in applies_for_tile(t):
                    ph_gapply(e, s)
                if t == NT - 1:
                    ph_gapply(e, APT * NT - 4, do_store=False)
                    ph_gapply(e, APT * NT - 3, do_store=False)
                if t + 2 < NT:
                    e.wait_ge(S["sMM"], 8 * (t + 1))
                    e.dma_start(out=xbm[t % 2].ap().rearrange("p (k n) -> p k n", k=DC),
                                in_=x_src(t + 2)).then_inc(S["sLB"][t % 2], 16)
            elif eng == "tensor":
                if t == 0:
                    e.wait_ge(S["sWM"], 48)
                    e.wait_ge(S["sWC"], 48)
                e.wait_ge(S["sLB"][t % 2], 16 * (t // 2 + 1))
                e.wait_ge(S["sLX"][t % 2], 16 * (t // 2 + 1))

                def stats(ts):
                    if ts >= 2:
                        e.wait_ge(S["sVC"], ts)
                    if ts >= 1:
                        e.wait_ge(S["sXL"], ts)
                    for ei in range(DC):
                        e.wait_ge(S["sCB"], 8 * ts + ei + 1)
                        if ei < DC - 1:
                            e.matmul(pb[3][0:1, :], ones_c[:, :],
                                     rbufs[ts % 2][:, ei * TD:(ei + 1) * TD],
                                     start=(ei == 0), stop=False)
                            e.matmul(pb[4][0:1, :], ones_c[:, :], sqt8[:, ei * TD:(ei + 1) * TD],
                                     start=(ei == 0), stop=False).then_inc(S["sSTe"], 1)
                        else:
                            e.matmul(pb[4][0:1, :], ones_c[:, :], sqt8[:, ei * TD:(ei + 1) * TD],
                                     start=False, stop=True).then_inc(S["sSTe"], 1)
                            e.matmul(pb[3][0:1, :], ones_c[:, :],
                                     rbufs[ts % 2][:, ei * TD:(ei + 1) * TD],
                                     start=False, stop=True).then_inc(S["sST"], 1)

                def bcast(ts):
                    e.wait_ge(S["sAB"], ts + 1)
                    if ts >= 1:
                        e.wait_ge(S["sXL"], ts)
                    e.matmul(pb[4][:, :], ones_r[:, :], abv[0:1, 512:1024],
                             start=True, stop=True)
                    e.matmul(pb[5][:, :], ones_r[:, :], stx[0:1, 0:512],
                             start=True, stop=True).then_inc(S["sBC"], 1)

                if t >= 1:
                    stats(t - 1)
                for ei in range(DC):
                    if 8 * t + ei >= 1:
                        e.wait_ge(S["sEP"], 8 * t + ei)
                    last = None
                    for wi, wsb in enumerate((wgm, wvm, wdm)):
                        for k in range(DC):
                            last = e.matmul(
                                pb[wi][:, :],
                                wsb[:, k * D + 128 * ei: k * D + 128 * ei + 128],
                                xbm[t % 2][:, k * TD:(k + 1) * TD],
                                start=(k == 0), stop=(k == DC - 1))
                    last.then_inc(S["sMM"], 1)
                    if ei == 4 and t >= 1:
                        bcast(t - 1)
                if t == NT - 1:
                    stats(t)
                    bcast(t)
            elif eng == "scalar":
                def sq_m(k):
                    e.wait_ge(S["sRS"], 8 * t + 2 * (k // 2 + 1))
                    if t >= 1:
                        e.wait_ge(S["sSTe"], 8 * (t - 1) + k + 1)
                    e.activation(sqt8[:, k * TD:(k + 1) * TD],
                                 rbufs[t % 2][:, k * TD:(k + 1) * TD],
                                 AF.Square).then_inc(S["sCB"], 1)

                def ln_sc(ts):
                    e.wait_ge(S["sST"], ts + 1)
                    e.activation(stx[0:1, 0:512], pb[3][0:1, :], AF.Identity)
                    e.activation(stx[0:1, 512:1024], pb[4][0:1, :], AF.Identity)
                    e.activation(abv[0:1, 0:512], stx[0:1, 0:512], AF.Identity,
                                 scale=1.0 / 1024.0)
                    e.activation(abv[0:1, 512:1024], stx[0:1, 512:1024], AF.Identity,
                                 scale=1.0 / 1024.0)
                    e.activation(stx[0:1, 0:512], abv[0:1, 0:512],
                                 AF.Square).then_inc(S["sLNE"], 1)

                if t >= 1:
                    ln_sc(t - 1)
                for ei in range(DC):
                    e.wait_ge(S["sMM"], 8 * t + ei + 1)
                    if 8 * t + ei >= 2:
                        e.wait_ge(S["sVX"], 8 * t + ei - 1)
                    e.activation(wk_xg[ei % 2][:, :], pb[0][:, :], AF.Sigmoid,
                                 bias=biases[:, ei:ei + 1])
                    e.activation(wk_tv[ei % 2][:, :], pb[1][:, :], AF.Tanh,
                                 bias=biases[:, DC + ei:DC + ei + 1])
                    e.activation(wk_sa[ei % 2][:, :], pb[2][:, :], AF.Sigmoid,
                                 bias=biases[:, 2 * DC + ei:2 * DC + ei + 1]
                                 ).then_inc(S["sEP"], 1)
                    if ei == 1 and t >= 1:
                        e.wait_ge(S["sVC"], t)
                        e.activation(stx[0:1, 0:512], stx[0:1, 512:1024], AF.Sqrt,
                                     bias=EPS_ap).then_inc(S["sSQ"], 1)
                    if ei >= 2:
                        sq_m(ei - 2)
                sq_m(DC - 2)
                sq_m(DC - 1)
                if t == NT - 1:
                    ln_sc(t)
                    e.wait_ge(S["sVC"], t + 1)
                    e.activation(stx[0:1, 0:512], stx[0:1, 512:1024], AF.Sqrt,
                                 bias=EPS_ap).then_inc(S["sSQ"], 1)
            elif eng == "vector":
                def ln_post(ts):
                    e.wait_ge(S["sLNE"], ts + 1)
                    e.tensor_tensor(stx[0:1, 512:1024], abv[0:1, 512:1024],
                                    stx[0:1, 0:512], ALU.subtract).then_inc(S["sVC"], 1)
                    e._w.clear()
                    e._r.clear()
                    e.drain()

                def ab_post(ts):
                    e.wait_ge(S["sSQ"], ts + 1)
                    e.reciprocal(abv[0:1, 512:1024], stx[0:1, 0:512])
                    e.scalar_tensor_tensor(stx[0:1, 0:512], abv[0:1, 0:512], -1.0,
                                           abv[0:1, 512:1024], ALU.mult,
                                           ALU.mult).then_inc(S["sAB"], 1)

                def affine(ts):
                    e.wait_ge(S["sBC"], ts + 1)
                    if ts >= 1:
                        e.wait_ge(S["sSPX"], 16 * ts)
                    p4 = pb[4][:, :]
                    p5 = pb[5][:, :]
                    b4 = bass.AP(p4.tensor, p4.offset, [[p4.ap[0][0], 128], [0, DC], [1, TD]])
                    b5 = bass.AP(p5.tensor, p5.offset, [[p5.ap[0][0], 128], [0, DC], [1, TD]])
                    xv = xf[ts % 2].ap().rearrange("p (b x) -> p b x", x=TD)
                    rv = rbufs[ts % 2].ap().rearrange("p (b x) -> p b x", x=TD)
                    e.tensor_tensor(xv, rv, b4, ALU.mult)
                    e.tensor_tensor(xv, xv, b5, ALU.add)
                    e.reduce_max(mxc[:, ts:ts + 1], xf[ts % 2][:, :], axis=AX.X,
                                 apply_absolute_value=True).then_inc(S["sXL"], 1)

                rb = rbufs[t % 2]
                for ei in range(DC):
                    e.wait_ge(S["sEP"], 8 * t + ei + 1)
                    sl = (ei % 2) * TD
                    e.tensor_tensor(wk_xs[ei % 2][:, :], wk_xg[ei % 2][:, :],
                                    wk_tv[ei % 2][:, :], ALU.mult)
                    e.tensor_scalar(wk_av[ei % 2][:, :], wk_sa[ei % 2][:, :],
                                    0.998, 0.001, ALU.mult,
                                    ALU.add).then_inc(S["sVX"], 1)
                    if ei >= 2 and ei % 2 == 0:
                        e.tensor_tensor(rb[:, (ei - 2) * TD:ei * TD],
                                        xf[t % 2][:, (ei - 2) * TD:ei * TD],
                                        hbuf[:, 0:2 * TD], ALU.add).then_inc(S["sRS"], 2)
                    if ei >= 1:
                        psl = ((ei - 1) % 2) * TD
                        e.tensor_scalar(carry[:, ei - 1:ei],
                                        hbuf[:, psl + TD - 1:psl + TD],
                                        1.0, 0.0, ALU.mult, ALU.add)
                    init = 0.0 if t % 4 == 0 else carry[:, ei:ei + 1]
                    e.tensor_tensor_scan(hbuf[:, sl:sl + TD], wk_av[ei % 2][:, :],
                                         wk_xs[ei % 2][:, :], init, ALU.mult, ALU.add)
                    if ei == 0 and t >= 1:
                        ln_post(t - 1)
                    if ei == 2 and t >= 1:
                        ab_post(t - 1)
                    if ei == 4 and t >= 1:
                        affine(t - 1)
                    if ei in (1, 3, 5) and (t < NT - 1 or ei == 1):
                        ph_vapply2(e, APT * t + ei - 1)
                e.tensor_scalar(carry[:, DC - 1:DC],
                                hbuf[:, ((DC - 1) % 2) * TD + TD - 1:
                                      ((DC - 1) % 2) * TD + TD],
                                1.0, 0.0, ALU.mult, ALU.add)
                e.tensor_tensor(rb[:, (DC - 2) * TD:DC * TD],
                                xf[t % 2][:, (DC - 2) * TD:DC * TD],
                                hbuf[:, 0:2 * TD], ALU.add).then_inc(S["sRS"], 2)
                if t == NT - 1:
                    ln_post(t)
                    ab_post(t)
                    affine(t)

        def ph_mixer_tail(eng, e):
            if eng == "sync":
                e.wait_ge(S["sXL"], NT)
                e.dma_start(
                    out=sp_x1.rearrange("k p n -> p k n")[:, :, (NT - 1) * TD:NT * TD],
                    in_=xf[(NT - 1) % 2].ap().rearrange("p (k n) -> p k n", k=DC)
                ).then_inc(S["sSPX"], 16)
        def ph_gmax_barrier(eng, e, bi, k):
            cin, cout = ccb[bi]
            bP, bD, bC = S[f"bPRE{bi}"], S[f"bDMA{bi}"], S[f"bCC{bi}"]
            if eng == "sync":
                e.wait_ge(bP, 1)
                e.dma_start(out=cin[:, :], in_=mx2[:, :]).then_inc(bD, 16)
                e.wait_ge(bC, 1)
                e.dma_start(out=mx2[:, :], in_=cout[:, :]).then_inc(bD, 16)
            elif eng == "gpsimd":
                e.wait_ge(bD, 16)
                e.collective_compute(
                    "AllReduce", ALU.max, replica_groups=[list(range(NCORES))],
                    ins=[cin[:, :]], outs=[cout[:, :]]).then_inc(bC, 1)
            elif eng == "tensor":
                e.wait_ge(bD, 32)
                e.transpose(pb[7][0:2, 0:128], mx2[:, :], ident[:, :]).then_inc(bP, 1)
                e.wait_ge(bP, 3)
                m = None
                for j in range(k):
                    m = e.matmul(pb[6][:, j:j + 1], selm[0:2, 128 * j:128 * j + 128],
                                 gsc[0:2, 0:1], start=True, stop=True)
                m.then_inc(bP, 1)
            elif eng == "scalar":
                e.wait_ge(bP, 4)
                e.activation(gmaxs[:, 2 * bi:2 * bi + k], pb[6][:, 0:k],
                             AF.Copy).then_inc(bP, 1)
            elif eng == "vector":
                e.wait_ge(bP, 2)
                e.reduce_max(gsc[0:2, 0:1], pb[7][0:2, 0:128], axis=AX.X).then_inc(bP, 1)
                e.wait_ge(bP, 5)

        def ph_bar1_fill(eng, e):
            if eng == "vector":
                e.wait_ge(S["sXL"], NT)
                e.reduce_max(mx2[:, 0:1], mxc[:, 0:NT], axis=AX.X)
                e.reduce_max(mx2[:, 1:2], mxc[:, 0:NT], axis=AX.X).then_inc(S["bPRE0"], 1)
                ph_vapply2(e, APT * NT - 4)
                ph_vapply2(e, APT * NT - 2)

        def ph_bar1_post(eng, e):
            if eng == "vector":
                e.tensor_scalar(scl[:, 0:1], gmaxs[:, 0:1], 1.0 / QMAX, 1e-12,
                                ALU.mult, ALU.max)
                e.reciprocal(scl[:, 1:2], scl[:, 0:1]).then_inc(S["bPRE0"], 1)

        def ph_xq(eng, e):
            for t in range(NT):
                if eng == "sync":
                    if t == 0:
                        e.wait_ge(S["sSPX"], 16 * NT)
                    if t >= 2:
                        e.wait_ge(S["sXQV"], t - 1)
                    e.dma_start(
                        out=xf[t % 2].ap().rearrange("p (k n) -> p k n", k=DC),
                        in_=sp_x1.rearrange("k p n -> p k n")[:, :, t * TD:(t + 1) * TD]
                    ).then_inc(S["sXQL"][t % 2], 16)
                elif eng == "vector":
                    e.wait_ge(S["sXQL"][t % 2], 16 * (t // 2 + 1))
                    if t == 0:
                        e.wait_ge(S["bPRE0"], 6)
                    if t >= 2:
                        e.wait_ge(S["sXQS"][t % 2], 16 * ((t - 2) // 2 + 1))
                    if t in (2, 3, 4, 5):
                        e.wait_ge(S["sXQA"][t - 2], 16)
                    e.tensor_scalar(sqt8[:, :], xf[t % 2][:, :], scl[:, 1:2], C_ap,
                                    ALU.mult, ALU.add)
                    e.tensor_scalar(xbm[t % 2][:, :], sqt8[:, :], C_ap, scl[:, 0:1],
                                    ALU.subtract, ALU.mult).then_inc(S["sXQV"], 1)
                    if stop_after != "mixer":
                        for g in xq_reds.get(t, ()):
                            red_a(e, g)
                if eng == "sync":
                    e.wait_ge(S["sXQV"], t + 1)
                    e.dma_start(
                        out=sp_xq.rearrange("k p n -> p k n")[:, :, t * TD:(t + 1) * TD],
                        in_=xbm[t % 2].ap().rearrange("p (k n) -> p k n", k=DC)
                    ).then_inc(S["sXQS"][t % 2], 16)
                    if t <= 3:
                        e.dma_start(out=xqa[t][:, :],
                                    in_=xbm[t % 2][:, :]).then_inc(S["sXQA"][t], 16)

        aorder = [(0, 0), (1, 0), (2, 0), (3, 0), (0, 1), (1, 1), (2, 1), (3, 1),
                  (4, 0), (5, 0), (4, 1), (5, 1), (6, 0), (7, 0), (6, 1), (7, 1)]

        def red_a(e, g):
            oi = g // (ICH // 4)
            gg = g % (ICH // 4)
            t, w = aorder[oi]
            n = oi * ICH + gg * 4 + 3
            col = w * 48 + t * 6 + gg
            e.wait_ge(S["sAEV"], n + 1)
            e.reduce_max(astat[:, col:col + 1], gbuf[g % 2][:, :], axis=AX.X,
                         apply_absolute_value=True).then_inc(S["sARD"], 1)

        xq_reds = {2: [0], 3: [1, 2], 4: [3, 4], 5: [5, 6], 6: [7, 8],
                   7: [9, 10, 11]}

        def ph_A_wu_loads(eng, e):
            if eng == "sync":
                e.wait_ge(S["sMM"], 8 * NT)
                e.wait_ge(S["sAWW"][0], 16 * 12)
                e.wait_ge(S["sAWW"][1], 16 * 12)
                flat = sp_wu.rearrange("c p n -> (c p) n")
                for k in range(DC):
                    e.dma_start(out=wu_sb[:, k * I:(k + 1) * I],
                                in_=flat[:, 128 * k:128 * (k + 1)],
                                transpose=True).then_inc(S["sWL2"], 16)

        def ph_A(eng, e):
            if eng == "gpsimd":
                for sd in range(APT * NT, NAPPLY):
                    ph_gapply(e, sd, do_store=(sd > APT * NT))
                ph_gapply_store(e, NAPPLY - 1)
            if eng == "sync":
                e.wait_ge(S["sXQS"][0], 16 * (NT // 2))
                e.wait_ge(S["sXQS"][1], 16 * (NT // 2))
                e.wait_ge(S["sAWW"][0], 16 * 24)
                e.wait_ge(S["sAWW"][1], 16 * 24)
                flat = sp_wq.rearrange("c p n -> (c p) n")
                for k in range(DC):
                    e.dma_start(out=wq_sb[:, k * I:(k + 1) * I],
                                in_=flat[:, 128 * k:128 * (k + 1)],
                                transpose=True).then_inc(S["sWQL"], 16)
                for tl in range(4, NT):
                    e.wait_ge(S["sAMM"],
                              (aorder.index((tl - 4, 1)) + 1) * ICH)
                    e.dma_start(
                        out=xqa[tl % 4].ap().rearrange("p (k n) -> p k n", k=DC),
                        in_=sp_xq.rearrange("k p n -> p k n")[:, :,
                                                              tl * TD:(tl + 1) * TD]
                    ).then_inc(S["sXQA"][tl % 4], 16)
            for oi, (t, w) in enumerate(aorder):
                _sc = nc.named_scope(f"pa{oi}")
                _sc.__enter__()
                wsb = wu_sb if w == 0 else wq_sb
                spw = sp_up if w == 0 else sp_gt
                if eng == "tensor":
                    if oi == 0:
                        e.wait_ge(S["sWL2"], 16 * 8)
                        e.wait_ge(S["sEP"], 8 * NT)
                        e.wait_ge(S["sVC"], NT)
                    if oi == 4:
                        e.wait_ge(S["sWQL"], 16 * 8)
                    if w == 0:
                        e.wait_ge(S["sXQA"][t % 4], 16 * (1 if t < 4 else 2))
                    for i in range(ICH):
                        n = oi * ICH + i
                        if n >= 4:
                            e.wait_ge(S["sAEV"], n - 3)
                        m = None
                        for k in range(DC):
                            m = e.matmul(
                                pb[n % 4][:, :],
                                wsb[:, k * I + 128 * i: k * I + 128 * i + 128],
                                xqa[t % 4][:, k * TD:(k + 1) * TD],
                                start=(k == 0), stop=(k == DC - 1))
                        m.then_inc(S["sAMM"], 1)
                elif eng == "scalar":
                    for i in range(ICH):
                        n = oi * ICH + i
                        g = n // 4
                        if n == 0:
                            e.wait_ge(S["sST"], NT)
                            e.wait_ge(S["sXL"], NT)
                        if g >= 2:
                            e.wait_ge(S["sASP"][g % 2], 16 * ((g - 2) // 2 + 1))
                            e.wait_ge(S["sARD"], g - 1)
                        e.wait_ge(S["sAMM"], n + 1)
                        e.activation(gbuf[g % 2][:, 512 * (i % 4):512 * (i % 4) + 512],
                                     pb[n % 4][:, :], AF.Copy).then_inc(S["sAEV"], 1)
                        if i % 4 == 3:
                            e.dma_start(
                                out=spw.rearrange("k p n -> p k n")[:, i - 3:i + 1,
                                                                    t * TD:(t + 1) * TD],
                                in_=gbuf[g % 2].ap().rearrange("p (c n) -> p c n", c=4)
                            ).then_inc(S["sASP"][g % 2], 16)
                elif eng == "vector":
                    if oi >= 2:
                        for i in range(3, ICH, 4):
                            red_a(e, oi * (ICH // 4) + i // 4)
                        if oi <= 13:
                            ph_vapply2(e, APT * NT + 2 * (oi - 2))
                _sc.__exit__(None, None, None)

        def ph_bar2_fill(eng, e):
            if eng == "vector":
                e.wait_ge(S["sARD"], NA // 4)
                e.reduce_max(mx2[:, 0:1], astat[:, 0:48], axis=AX.X)
                e.reduce_max(mx2[:, 1:2], astat[:, 48:96], axis=AX.X).then_inc(S["bPRE1"], 1)

        def ph_bar2_post(eng, e):
            if eng == "vector":
                e.tensor_scalar(scl[:, 4:5], gmaxs[:, 2:3], 1.0 / QMAX, 1e-12,
                                ALU.mult, ALU.max)
                e.tensor_scalar(scl[:, 2:3], gmaxs[:, 3:4], 1.0 / QMAX, 1e-12,
                                ALU.mult, ALU.max)
                e.reciprocal(scl[:, 5:6], scl[:, 4:5])
                e.reciprocal(scl[:, 3:4], scl[:, 2:3]).then_inc(S["bPRE1"], 1)

        def ph_B(eng, e):
            def vtail(n):
                e.wait_ge(S["sBS"], 2 * (n + 1))
                if n >= 2:
                    e.wait_ge(S["sBSP"][n % 2], 16 * ((n - 2) // 2 + 1))
                e.tensor_scalar(tb5[n % 2][:, :], tb4[n % 2][:, :], C_ap, 65535.0,
                                ALU.subtract, ALU.min)
                e.tensor_tensor(tb5[n % 2][:, :], tb2[n % 2][:, :],
                                tb5[n % 2][:, :], ALU.mult)
                e.reduce_max(bstat[:, n:n + 1], tb5[n % 2][:, :], axis=AX.X,
                             apply_absolute_value=True).then_inc(S["sBV"], 3)

            for n in range(NB):
                t, j = divmod(n, ICH // 4)
                if eng == "sync":
                    if n == 0:
                        e.wait_ge(S["sASP"][0], 16 * (NA // 8))
                        e.wait_ge(S["sASP"][1], 16 * (NA // 8))
                    if n >= 2:
                        e.wait_ge(S["sBR"], n - 1)
                    e.dma_start(
                        out=ld1[n % 2].ap().rearrange("p (c x) -> p c x", c=4),
                        in_=sp_gt.rearrange("k p n -> p k n")[:, 4 * j:4 * j + 4,
                                                              t * TD:(t + 1) * TD]
                    ).then_inc(S["sBL"][n % 2], 16)
                elif eng == "vector":
                    e.wait_ge(S["sBL"][n % 2], 16 * (n // 2 + 1))
                    if n == 0:
                        e.wait_ge(S["bPRE1"], 6)
                    if n >= 2:
                        e.wait_ge(S["sBS"], 2 * (n - 1))
                    e.tensor_scalar(tb1[n % 2][:, :], ld1[n % 2][:, :],
                                    scl[:, 3:4], C_ap, ALU.mult, ALU.add)
                    e.tensor_scalar(tb2[n % 2][:, :], tb1[n % 2][:, :], C_ap, 1.0,
                                    ALU.subtract, ALU.mult).then_inc(S["sBR"], 1)
                    if n >= 1:
                        vtail(n - 1)
                    if n == NB - 1:
                        vtail(n)
                elif eng == "scalar":
                    e.wait_ge(S["sBR"], n + 1)
                    if n >= 2:
                        e.wait_ge(S["sBV"], 3 * (n - 1))
                    e.activation(tb3[n % 2][:, :], tb2[n % 2][:, :], AF.Sigmoid,
                                 scale=scl[:, 2:3])
                    e.activation(tb4[n % 2][:, :], tb3[n % 2][:, :], AF.Identity,
                                 bias=C_ap, scale=65536.0).then_inc(S["sBS"], 2)
                elif eng == "gpsimd":
                    e.wait_ge(S["sBV"], 3 * (n + 1))
                    e.dma_start(
                        out=sp_p1.rearrange("k p n -> p k n")[:, 4 * j:4 * j + 4,
                                                              t * TD:(t + 1) * TD],
                        in_=tb5[n % 2].ap().rearrange("p (c x) -> p c x", c=4)
                    ).then_inc(S["sBSP"][n % 2], 16)

        def ph_bar3_fill(eng, e):
            if eng == "vector":
                e.wait_ge(S["sBV"], 3 * NB)
                e.reduce_max(mx2[:, 0:1], bstat[:, 0:NB], axis=AX.X)
                e.reduce_max(mx2[:, 1:2], bstat[:, 0:NB], axis=AX.X).then_inc(S["bPRE2"], 1)

        def ph_bar3_post(eng, e):
            if eng == "vector":
                e.reciprocal(scl[:, 7:8], gmaxs[:, 4:5])
                e.tensor_scalar(scl[:, 6:7], scl[:, 7:8], QMAX, 0.0,
                                ALU.mult, ALU.add).then_inc(S["bPRE2"], 1)

        def ph_C(eng, e):
            for n in range(NB):
                t, j = divmod(n, ICH // 4)
                if eng == "sync":
                    if n == 0:
                        e.wait_ge(S["sBSP"][0], 16 * (NB // 2))
                        e.wait_ge(S["sBSP"][1], 16 * (NB // 2))
                    if n >= 2:
                        e.wait_ge(S["sCS"], 2 * (n - 1))
                    e.dma_start(
                        out=ld1[n % 2].ap().rearrange("p (c x) -> p c x", c=4),
                        in_=sp_p1.rearrange("k p n -> p k n")[:, 4 * j:4 * j + 4,
                                                              t * TD:(t + 1) * TD]
                    ).then_inc(S["sCL"][n % 2], 16)
                    e.dma_start(
                        out=ld2[n % 2].ap().rearrange("p (c x) -> p c x", c=4),
                        in_=sp_up.rearrange("k p n -> p k n")[:, 4 * j:4 * j + 4,
                                                              t * TD:(t + 1) * TD]
                    ).then_inc(S["sCL"][n % 2], 16)
                elif eng == "scalar":
                    e.wait_ge(S["sCL"][n % 2], 32 * (n // 2 + 1))
                    if n == 0:
                        e.wait_ge(S["bPRE2"], 6)
                    if n >= 2:
                        e.wait_ge(S["sCV"], 2 * (n - 1))
                    e.activation(tb1[n % 2][:, :], ld1[n % 2][:, :], AF.Identity,
                                 bias=C_ap, scale=scl[:, 6:7])
                    e.activation(tb3[n % 2][:, :], ld2[n % 2][:, :], AF.Identity,
                                 bias=C_ap, scale=scl[:, 5:6]).then_inc(S["sCS"], 2)
                elif eng == "vector":
                    e.wait_ge(S["sCS"], 2 * (n + 1))
                    if n >= 2:
                        e.wait_ge(S["sCSP"][n % 2], 16 * ((n - 2) // 2 + 1))
                    e.tensor_scalar(tb2[n % 2][:, :], tb1[n % 2][:, :], C_ap, 1.0,
                                    ALU.subtract, ALU.mult)
                    e.scalar_tensor_tensor(tb4[n % 2][:, :], tb3[n % 2][:, :], negC_ap,
                                           tb2[n % 2][:, :], ALU.add, ALU.mult)
                    e.reduce_max(bstat[:, n:n + 1], tb4[n % 2][:, :], axis=AX.X,
                                 apply_absolute_value=True).then_inc(S["sCV"], 2)
                elif eng == "gpsimd":
                    e.wait_ge(S["sCV"], 2 * (n + 1))
                    e.dma_start(
                        out=sp_p2.rearrange("k p n -> p k n")[:, 4 * j:4 * j + 4,
                                                              t * TD:(t + 1) * TD],
                        in_=tb4[n % 2].ap().rearrange("p (c x) -> p c x", c=4)
                    ).then_inc(S["sCSP"][n % 2], 16)

        def ph_bar4_fill(eng, e):
            if eng == "vector":
                e.wait_ge(S["sCV"], 2 * NB)
                e.reduce_max(mx2[:, 0:1], bstat[:, 0:NB], axis=AX.X)
                e.reduce_max(mx2[:, 1:2], bstat[:, 0:NB], axis=AX.X).then_inc(S["bPRE3"], 1)

        def ph_bar4_post(eng, e):
            if eng == "vector":
                e.reciprocal(scl[:, 10:11], gmaxs[:, 6:7])
                e.tensor_scalar(scl[:, 11:12], gmaxs[:, 4:5], 1.0 / (QMAX * 65536.0),
                                0.0, ALU.mult, ALU.add)
                e.tensor_scalar(scl[:, 8:9], scl[:, 10:11], QMAX, 0.0, ALU.mult, ALU.add)
                e.tensor_tensor(scl[:, 12:13], scl[:, 11:12], scl[:, 2:3], ALU.mult)
                e.tensor_tensor(scl[:, 13:14], scl[:, 12:13], scl[:, 4:5], ALU.mult)
                e.tensor_tensor(scl[:, 14:15], scl[:, 13:14], gmaxs[:, 6:7], ALU.mult)
                e.tensor_scalar(scl[:, 9:10], scl[:, 14:15], 1.0 / QMAX, 0.0,
                                ALU.mult, ALU.add).then_inc(S["bPRE3"], 1)

        def ph_D_loads(eng, e):
            if eng == "sync":
                e.wait_ge(S["sCSP"][0], 16 * (NB // 2))
                e.wait_ge(S["sCSP"][1], 16 * (NB // 2))
                e.wait_ge(S["sAWW"][0], 16 * (NAPPLY // 2))
                e.wait_ge(S["sAWW"][1], 16 * (NAPPLY // 2))
                flatd = sp_wd.rearrange("c p n -> (c p) n")
                for ci in range(ICH):
                    e.dma_start(out=wd_sb[:, ci * D:(ci + 1) * D],
                                in_=flatd[:, 128 * ci:128 * (ci + 1)],
                                transpose=True).then_inc(S["sWDL"], 16)
                for j in range(2):
                    e.dma_start(
                        out=p2ld[j % 2].ap().rearrange("p (c x) -> p c x", c=4),
                        in_=sp_p2.rearrange("k p n -> p k n")[:, 4 * j:4 * j + 4,
                                                              0:TD]
                    ).then_inc(S["sDL"][j % 2], 16)
                e.dma_start(
                    out=x1d.ap().rearrange("p (k n) -> p k n", k=DC),
                    in_=sp_x1.rearrange("k p n -> p k n")[:, :, 0:TD]
                ).then_inc(S["sDX1"], 16)

        def ph_D(eng, e):
            def ln_post2(ts):
                e.wait_ge(S["sDLNE"], ts + 1)
                e.tensor_tensor(stx[0:1, 512:1024], abv[0:1, 512:1024],
                                stx[0:1, 0:512], ALU.subtract).then_inc(S["sDVC"], 1)
                e.drain()

            def ln_sc2(ts):
                e.wait_ge(S["sDST"], ts + 1)
                e.activation(stx[0:1, 0:512], pb[3][0:1, :], AF.Identity)
                e.activation(stx[0:1, 512:1024], pb[4][0:1, :], AF.Identity)
                e.activation(abv[0:1, 0:512], stx[0:1, 0:512], AF.Identity,
                             scale=1.0 / 1024.0)
                e.activation(abv[0:1, 512:1024], stx[0:1, 512:1024], AF.Identity,
                             scale=1.0 / 1024.0)
                e.activation(stx[0:1, 0:512], abv[0:1, 0:512],
                             AF.Square).then_inc(S["sDLNE"], 1)

            def ab_post2(ts):
                e.wait_ge(S["sDSQ"], ts + 1)
                e.reciprocal(abv[0:1, 512:1024], stx[0:1, 0:512])
                e.scalar_tensor_tensor(stx[0:1, 0:512], abv[0:1, 0:512], -1.0,
                                       abv[0:1, 512:1024], ALU.mult,
                                       ALU.mult).then_inc(S["sDAB"], 1)

            def affine2(ts):
                e.wait_ge(S["sDBC"], ts + 1)
                e.wait_ge(S["sOUT"], 16 * ts)
                p4 = pb[4][:, :]
                p5 = pb[5][:, :]
                b4 = bass.AP(p4.tensor, p4.offset, [[p4.ap[0][0], 128], [0, DC], [1, TD]])
                b5 = bass.AP(p5.tensor, p5.offset, [[p5.ap[0][0], 128], [0, DC], [1, TD]])
                sv = sqt28.ap().rearrange("p (b x) -> p b x", x=TD)
                rv = r2bs[ts % 2].ap().rearrange("p (b x) -> p b x", x=TD)
                e.tensor_tensor(sv, rv, b4, ALU.mult)
                e.tensor_tensor(sv, sv, b5, ALU.add)
                e._e.engine_nop().then_inc(S["sDXL"], 1)

            def round_d(tt, j):
                n = tt * NJ + j
                e.wait_ge(S["sDL"][n % 2], 16 * (n // 2 + 1))
                if n == 0:
                    e.wait_ge(S["bPRE3"], 6)
                if tt >= 2 and j == 0:
                    e.wait_ge(S["sDMM"], 8 * (tt - 1))
                e.tensor_scalar(t5b[:, :], p2ld[n % 2][:, :], scl[:, 8:9],
                                C_ap, ALU.mult, ALU.add).then_inc(S["sDS2"], 1)
                e.tensor_scalar(otile[tt % 2][:, 2048 * j:2048 * (j + 1)],
                                t5b[:, :], C_ap, 1.0,
                                ALU.subtract, ALU.mult).then_inc(S["sDV2"], 1)

            for t in range(NT):
                _sc = nc.named_scope(f"pd{t}")
                _sc.__enter__()
                if eng == "sync":
                    if t == 0:
                        for j in range(2, NJ):
                            e.wait_ge(S["sDS2"], j - 1)
                            e.dma_start(
                                out=p2ld[j % 2].ap().rearrange("p (c x) -> p c x", c=4),
                                in_=sp_p2.rearrange("k p n -> p k n")[:, 4 * j:4 * j + 4,
                                                                      0:TD]
                            ).then_inc(S["sDL"][j % 2], 16)
                    if t >= 1:
                        e.wait_ge(S["sDEV"], 8 * t)
                        e.dma_start(
                            out=x1d.ap().rearrange("p (k n) -> p k n", k=DC),
                            in_=sp_x1.rearrange("k p n -> p k n")[:, :,
                                                                  t * TD:(t + 1) * TD]
                        ).then_inc(S["sDX1"], 16)
                    if t + 1 < NT:
                        for j in range(NJ):
                            n = (t + 1) * NJ + j
                            e.wait_ge(S["sDS2"], n - 1)
                            e.dma_start(
                                out=p2ld[n % 2].ap().rearrange("p (c x) -> p c x", c=4),
                                in_=sp_p2.rearrange("k p n -> p k n")[:, 4 * j:4 * j + 4,
                                                                      (t + 1) * TD:(t + 2) * TD]
                            ).then_inc(S["sDL"][n % 2], 16)
                    if t >= 1:
                        e.wait_ge(S["sDXL"], t)
                        e.dma_start(
                            out=out_fm.rearrange("k p n -> p k n")[:, :,
                                                                   (t - 1) * TD:t * TD],
                            in_=sqt28.ap().rearrange("p (k n) -> p k n", k=DC)
                        ).then_inc(S["sOUT"], 16)
                if eng == "tensor":
                    if t == 0:
                        e.wait_ge(S["sWDL"], 16 * ICH)

                    def stats2(ts):
                        if ts >= 2:
                            e.wait_ge(S["sDVC"], ts)
                        for ei in range(DC):
                            e.wait_ge(S["sDCB"], 8 * ts + ei + 1)
                            if ei < DC - 1:
                                e.matmul(pb[3][0:1, :], ones_c[:, :],
                                         r2bs[ts % 2][:, ei * TD:(ei + 1) * TD],
                                         start=(ei == 0), stop=False)
                                e.matmul(pb[4][0:1, :], ones_c[:, :],
                                         sqt28[:, ei * TD:(ei + 1) * TD],
                                         start=(ei == 0), stop=False
                                         ).then_inc(S["sDSTe"], 1)
                            else:
                                e.matmul(pb[4][0:1, :], ones_c[:, :],
                                         sqt28[:, ei * TD:(ei + 1) * TD],
                                         start=False, stop=True).then_inc(S["sDSTe"], 1)
                                e.matmul(pb[3][0:1, :], ones_c[:, :],
                                         r2bs[ts % 2][:, ei * TD:(ei + 1) * TD],
                                         start=False, stop=True).then_inc(S["sDST"], 1)

                    def bcast2(ts):
                        e.wait_ge(S["sDAB"], ts + 1)
                        if ts >= 1:
                            e.wait_ge(S["sDXL"], ts)
                        e.matmul(pb[4][:, :], ones_r[:, :], abv[0:1, 512:1024],
                                 start=True, stop=True)
                        e.matmul(pb[5][:, :], ones_r[:, :], stx[0:1, 0:512],
                                 start=True, stop=True).then_inc(S["sDBC"], 1)

                    if t >= 1:
                        stats2(t - 1)
                    e.wait_ge(S["sDV2"], NJ * (t + 1))
                    for h in range(DC):
                        if 8 * t + h >= 2:
                            e.wait_ge(S["sDEV"], 8 * t + h - 1)
                        m = None
                        for i in range(ICH):
                            m = e.matmul(pb[h % 2][:, :],
                                         wd_sb[:, i * D + 128 * h: i * D + 128 * h + 128],
                                         otile[t % 2][:, i * TD:(i + 1) * TD],
                                         start=(i == 0), stop=(i == ICH - 1))
                        m.then_inc(S["sDMM"], 1)
                        if h == 0 and t >= 1:
                            bcast2(t - 1)
                    if t == NT - 1:
                        stats2(t)
                        bcast2(t)
                elif eng == "vector":
                    if t == 0:
                        for j in range(NJ):
                            round_d(0, j)
                    e.wait_ge(S["sDX1"], 16 * (t + 1))
                    if t >= 2:
                        e.wait_ge(S["sDXL"], t - 1)
                    if t >= 1:
                        ln_post2(t - 1)
                        ab_post2(t - 1)
                    for h in range(DC):
                        e.wait_ge(S["sDMM"], 8 * t + h + 1)
                        e.scalar_tensor_tensor(r2bs[t % 2][:, h * TD:(h + 1) * TD],
                                               pb[h % 2][:, :], scl[:, 9:10],
                                               x1d[:, h * TD:(h + 1) * TD],
                                               ALU.mult, ALU.add).then_inc(S["sDEV"], 1)
                        if h == 0 and t >= 1:
                            affine2(t - 1)
                        if h < NJ and t + 1 < NT:
                            round_d(t + 1, h)
                    if t == NT - 1:
                        ln_post2(t)
                        ab_post2(t)
                        affine2(t)
                elif eng == "scalar":
                    if t >= 1:
                        ln_sc2(t - 1)
                        e.wait_ge(S["sDVC"], t)
                        e.activation(stx[0:1, 0:512], stx[0:1, 512:1024], AF.Sqrt,
                                     bias=EPS_ap).then_inc(S["sDSQ"], 1)
                    for h in range(DC):
                        e.wait_ge(S["sDEV"], 8 * t + h + 1)
                        if t >= 1:
                            e.wait_ge(S["sOUT"], 16 * t)
                        e.activation(sqt28[:, h * TD:(h + 1) * TD],
                                     r2bs[t % 2][:, h * TD:(h + 1) * TD],
                                     AF.Square).then_inc(S["sDCB"], 1)
                    if t == NT - 1:
                        ln_sc2(t)
                        e.wait_ge(S["sDVC"], t + 1)
                        e.activation(stx[0:1, 0:512], stx[0:1, 512:1024], AF.Sqrt,
                                     bias=EPS_ap).then_inc(S["sDSQ"], 1)
                _sc.__exit__(None, None, None)
            if eng == "sync":
                e.wait_ge(S["sDXL"], NT)
                e.dma_start(
                    out=out_fm.rearrange("k p n -> p k n")[:, :, (NT - 1) * TD:NT * TD],
                    in_=sqt28.ap().rearrange("p (k n) -> p k n", k=DC)
                ).then_inc(S["sOUT"], 16)
                e.wait_ge(S["sOUT"], 16 * NT)
            if eng == "gpsimd":
                e.wait_ge(S["sOUT"], 16 * NT)

        def stream(eng, e):
            with nc.named_scope("setup"):
                ph_setup(eng, e)
            with nc.named_scope("mixer"):
                for t in range(NT):
                    with nc.named_scope(f"mx{t}"):
                        ph_mixer_tile(eng, e, t)
                ph_mixer_tail(eng, e)
            with nc.named_scope("bar1"):
                ph_bar1_fill(eng, e)
                ph_gmax_barrier(eng, e, 0, 1)
                ph_bar1_post(eng, e)
            if eng == "gpsimd":
                ph_gapply_store(e, APT * NT - 5)
                ph_gapply_store(e, APT * NT - 4)
                ph_gapply(e, APT * NT - 2, do_store=False)
                ph_gapply(e, APT * NT - 1, do_store=False)
                ph_gapply_store(e, APT * NT - 3)
                ph_gapply_store(e, APT * NT - 2)
                ph_gapply_store(e, APT * NT - 1)
            ph_A_wu_loads(eng, e)
            with nc.named_scope("xq"):
                ph_xq(eng, e)
            if stop_after == "mixer":
                if eng == "sync":
                    e.wait_ge(S["sXQS"][0], 16 * (NT // 2))
                    e.wait_ge(S["sXQS"][1], 16 * (NT // 2))
                    e.wait_ge(S["sAWW"][0], 16 * 24)
                    e.wait_ge(S["sAWW"][1], 16 * 24)
                    e.dma_start(out=gdbg[:, 0:8], in_=mxc[:, :]).then_inc(S["sOUT"], 16)
                    e.dma_start(out=gdbg[:, 8:10], in_=scl[:, 0:2]).then_inc(S["sOUT"], 16)
                    e.wait_ge(S["sOUT"], 32)
                elif eng == "gpsimd":
                    e.wait_ge(S["sOUT"], 32)
                return
            with nc.named_scope("phA"):
                ph_A(eng, e)
            with nc.named_scope("bar2"):
                ph_bar2_fill(eng, e)
                ph_gmax_barrier(eng, e, 1, 2)
                ph_bar2_post(eng, e)
            if stop_after == "A":
                if eng == "sync":
                    e.wait_ge(S["sASP"][0], 16 * (NA // 8))
                    e.wait_ge(S["sASP"][1], 16 * (NA // 8))
                    e.wait_ge(S["bPRE1"], 6)
                    e.dma_start(out=gdbg[:, 0:6], in_=scl[:, 0:6]).then_inc(S["sOUT"], 16)
                    e.wait_ge(S["sOUT"], 16)
                elif eng == "gpsimd":
                    e.wait_ge(S["sOUT"], 16)
                return
            pbar(e, 1)
            with nc.named_scope("phB"):
                ph_B(eng, e)
            with nc.named_scope("bar3"):
                ph_bar3_fill(eng, e)
                ph_gmax_barrier(eng, e, 2, 1)
                ph_bar3_post(eng, e)
            if stop_after == "B":
                if eng == "sync":
                    e.wait_ge(S["sBSP"][0], 16 * (NB // 2))
                    e.wait_ge(S["sBSP"][1], 16 * (NB // 2))
                    e.wait_ge(S["bPRE2"], 6)
                    e.dma_start(out=gdbg[:, 0:8], in_=scl[:, 0:8]).then_inc(S["sOUT"], 16)
                    e.wait_ge(S["sOUT"], 16)
                elif eng == "gpsimd":
                    e.wait_ge(S["sOUT"], 16)
                return
            with nc.named_scope("phC"):
                ph_C(eng, e)
            ph_D_loads(eng, e)
            with nc.named_scope("bar4"):
                ph_bar4_fill(eng, e)
                ph_gmax_barrier(eng, e, 3, 1)
                ph_bar4_post(eng, e)
            if stop_after == "C":
                if eng == "sync":
                    e.wait_ge(S["sCSP"][0], 16 * (NB // 2))
                    e.wait_ge(S["sCSP"][1], 16 * (NB // 2))
                    e.wait_ge(S["bPRE3"], 6)
                    e.dma_start(out=gdbg[:, 0:10], in_=scl[:, 0:10]).then_inc(S["sOUT"], 16)
                    e.wait_ge(S["sOUT"], 16)
                elif eng == "gpsimd":
                    e.wait_ge(S["sOUT"], 16)
                return
            pbar(e, 2)
            with nc.named_scope("phD"):
                ph_D(eng, e)

        with nc.Block() as bk:
            @bk.sync
            def _(e):
                stream("sync", FE(e, fences["sync"]))

            @bk.gpsimd
            def _(e):
                stream("gpsimd", FE(e, fences["gpsimd"]))

            @bk.tensor
            def _(e):
                stream("tensor", FE(e, fences["tensor"]))

            @bk.scalar
            def _(e):
                stream("scalar", FE(e, fences["scalar"]))

            @bk.vector
            def _(e):
                stream("vector", FE(e, fences["vector"]))
    return nc


def _prep_inputs(L0_attn_x, L0_attn_Wg, L0_attn_bg, L0_attn_Wv, L0_attn_bv,
                 L0_attn_Wd, L0_attn_bd, L0_ffn_w_gate, L0_ffn_w_up, L0_ffn_w_down):
    f = np.ascontiguousarray
    shared = {
        "wg_t": f(L0_attn_Wg.T).reshape(DC, 128, D),
        "wv_t": f(L0_attn_Wv.T).reshape(DC, 128, D),
        "wd_t": f(L0_attn_Wd.T).reshape(DC, 128, D),
        "bgh": f(L0_attn_bg.reshape(DC, 128).T),
        "bvh": f(L0_attn_bv.reshape(DC, 128).T),
        "bdh": f(L0_attn_bd.reshape(DC, 128).T),
        "wu_n": f(L0_ffn_w_up).reshape(ICH, 128, D),
        "wq_n": f(L0_ffn_w_gate).reshape(ICH, 128, D),
        "wdn_n": f(L0_ffn_w_down).reshape(DC, 128, I),
    }
    in_maps = []
    for c in range(NCORES):
        xc = f(L0_attn_x[2 * c:2 * c + 2].reshape(TOK, D).T).reshape(DC, 128, TOK)
        m = {"x_fm": xc}
        m.update(shared)
        in_maps.append(m)
    return in_maps


def _run(inputs, stop_after="full", debug=False):
    nc = build(stop_after=stop_after, debug=debug)
    in_maps = _prep_inputs(**{k: np.asarray(v) for k, v in inputs.items()})
    res = run_bass_kernel_spmd(nc, in_maps, core_ids=list(range(NCORES)))
    if debug:
        return res
    out = np.empty((B, S, D), dtype=np.float32)
    for c in range(NCORES):
        o = res.results[c]["out_fm"].reshape(D, TOK)
        out[2 * c:2 * c + 2] = o.T.reshape(2, S, D)
    return out


def kernel(**inputs):
    return _run(inputs)


# version 12
# speedup vs baseline: 1.0801x; 1.0088x over previous
import contextlib

import numpy as np
import concourse.bass as bass
import concourse.mybir as mybir
from concourse.bass_utils import run_bass_kernel_spmd

F32 = mybir.dt.float32
BF16 = mybir.dt.bfloat16
ALU = mybir.AluOpType
AF = mybir.ActivationFunctionType
AX = mybir.AxisListType

NCORES = 8
B, S, D, I = 16, 2048, 1024, 3072
TOK = (B // NCORES) * S
TD = 512
NT = TOK // TD
DC = D // 128
ICH = I // 128
QMAX = 32767.0
C_LO = 1.5 * 2.0 ** 23
C_DELTA = 1.5 * 2.0 ** 52 - 1.5 * 2.0 ** 23

ENGS = ("sync", "gpsimd", "tensor", "scalar", "vector")

NAPPLY = 72
NA = NT * 2 * ICH
NB = NT * (ICH // 4)
NJ = ICH // 4


def _ap_span(ap):
    try:
        dims = list(ap.ap)
        off = ap.offset
        span = 1
        for st, cnt in dims:
            span += abs(st) * (cnt - 1)
        refined = None
        if dims:
            pstr, pcnt = dims[0]
            rest = 1
            for st, cnt in dims[1:]:
                rest += abs(st) * (cnt - 1)
            if pstr >= rest and pstr > 0:
                p_lo = off // pstr
                c_lo = off % pstr
                refined = (pstr, p_lo, p_lo + pcnt, c_lo, c_lo + rest)
        return (ap.tensor.name, off, off + span, refined)
    except AttributeError:
        return None


def _overlap(k, lst):
    for (nm, lo, hi, ref) in lst:
        if nm != k[0]:
            continue
        if k[2] <= lo or hi <= k[1]:
            continue
        if ref is not None and k[3] is not None and ref[0] == k[3][0]:
            if (k[3][2] <= ref[1] or ref[2] <= k[3][1]
                    or k[3][4] <= ref[3] or ref[4] <= k[3][3]):
                continue
        return True
    return False


class FE:

    def __init__(self, eng, state):
        self._e = eng
        self._w = state["w"]
        self._r = state["r"]

    def __getattr__(self, name):
        return getattr(self._e, name)

    def drain(self, *a, **k):
        self._w.clear()
        self._r.clear()
        return self._e.drain(*a, **k)

    def _pre(self, outs, ins):
        need = False
        for ap in list(outs) + list(ins):
            if ap is None or isinstance(ap, (int, float)):
                continue
            k = _ap_span(ap)
            if k is not None and _overlap(k, self._w):
                need = True
                break
        if not need:
            for ap in outs:
                if ap is None or isinstance(ap, (int, float)):
                    continue
                k = _ap_span(ap)
                if k is not None and _overlap(k, self._r):
                    need = True
                    break
        if need:
            self._e.drain()
            self._w.clear()
            self._r.clear()

    def _post(self, outs, ins):
        for ap in outs:
            if ap is None or isinstance(ap, (int, float)):
                continue
            k = _ap_span(ap)
            if k is not None:
                self._w.append(k)
        for ap in ins:
            if ap is None or isinstance(ap, (int, float)):
                continue
            k = _ap_span(ap)
            if k is not None:
                self._r.append(k)

    def _op(self, outs, ins, fn):
        self._pre(outs, ins)
        r = fn()
        self._post(outs, ins)
        return r

    def memset(self, out, v):
        return self._op([out], [], lambda: self._e.memset(out, v))

    def tensor_scalar(self, out, in0, s1, s2, op0, op1=None):
        if op1 is not None:
            return self._op([out], [in0, s1, s2],
                            lambda: self._e.tensor_scalar(out, in0, s1, s2, op0, op1))
        return self._op([out], [in0, s1],
                        lambda: self._e.tensor_scalar(out, in0, s1, s2, op0))

    def tensor_tensor(self, out, a, b, op):
        return self._op([out], [a, b], lambda: self._e.tensor_tensor(out, a, b, op))

    def scalar_tensor_tensor(self, out, in0, sc, in1, op0, op1):
        return self._op([out], [in0, sc, in1],
                        lambda: self._e.scalar_tensor_tensor(out, in0, sc, in1, op0, op1))

    def tensor_tensor_scan(self, out, d0, d1, init, op0, op1):
        return self._op([out], [d0, d1, init],
                        lambda: self._e.tensor_tensor_scan(out, d0, d1, init, op0, op1))

    def reduce_max(self, out, in_, axis, apply_absolute_value=False):
        return self._op([out], [in_],
                        lambda: self._e.reduce_max(out, in_, axis=axis,
                                                   apply_absolute_value=apply_absolute_value))

    def reciprocal(self, out, in_):
        return self._op([out], [in_], lambda: self._e.reciprocal(out, in_))

    def activation(self, out, in_, func, bias=None, scale=None):
        kw = {}
        if bias is not None:
            kw["bias"] = bias
        if scale is not None:
            kw["scale"] = scale
        return self._op([out], [in_, bias, scale],
                        lambda: self._e.activation(out, in_, func, **kw))

    def affine_select(self, out, in_, compare_op, fill, base, pattern,
                      channel_multiplier):
        return self._op([out], [in_],
                        lambda: self._e.affine_select(
                            out=out, in_=in_, compare_op=compare_op, fill=fill,
                            base=base, pattern=pattern,
                            channel_multiplier=channel_multiplier))

    def tensor_copy(self, out, in_):
        return self._op([out], [in_], lambda: self._e.tensor_copy(out, in_))

    def dma_start(self, out, in_, **kw):
        self._pre([], [in_])
        return self._e.dma_start(out=out, in_=in_, **kw)


def build(stop_after="full", debug=False):
    nc = bass.Bass("TRN2", target_bir_lowering=False)
    dbg = "ExternalOutput" if debug else "Internal"

    x_fm = nc.dram_tensor("x_fm", [DC, 128, TOK], F32, kind="ExternalInput").ap()
    wg_t = nc.dram_tensor("wg_t", [DC, 128, D], F32, kind="ExternalInput").ap()
    wv_t = nc.dram_tensor("wv_t", [DC, 128, D], F32, kind="ExternalInput").ap()
    wd_t = nc.dram_tensor("wd_t", [DC, 128, D], F32, kind="ExternalInput").ap()
    bgh = nc.dram_tensor("bgh", [128, DC], F32, kind="ExternalInput").ap()
    bvh = nc.dram_tensor("bvh", [128, DC], F32, kind="ExternalInput").ap()
    bdh = nc.dram_tensor("bdh", [128, DC], F32, kind="ExternalInput").ap()
    wu_n = nc.dram_tensor("wu_n", [ICH, 128, D], F32, kind="ExternalInput").ap()
    wq_n = nc.dram_tensor("wq_n", [ICH, 128, D], F32, kind="ExternalInput").ap()
    wdn_n = nc.dram_tensor("wdn_n", [DC, 128, I], F32, kind="ExternalInput").ap()

    out_fm = nc.dram_tensor("out_fm", [DC, 128, TOK], F32, kind="ExternalOutput").ap()

    sp_x1 = nc.dram_tensor("sp_x1", [DC, 128, TOK], F32, kind=dbg).ap()
    sp_xq = nc.dram_tensor("sp_xq", [DC, 128, TOK], BF16, kind=dbg).ap()
    sp_up = nc.dram_tensor("sp_up", [ICH, 128, TOK], F32, kind=dbg).ap()
    sp_gt = nc.dram_tensor("sp_gt", [ICH, 128, TOK], F32, kind=dbg).ap()
    sp_p1 = nc.dram_tensor("sp_p1", [ICH, 128, TOK], F32, kind=dbg).ap()
    sp_p2 = nc.dram_tensor("sp_p2", [ICH, 128, TOK], F32, kind=dbg).ap()
    sp_wu = nc.dram_tensor("sp_wu", [ICH, 128, D], BF16, kind=dbg).ap()
    sp_wq = nc.dram_tensor("sp_wq", [ICH, 128, D], BF16, kind=dbg).ap()
    sp_wd = nc.dram_tensor("sp_wd", [DC, 128, I], BF16, kind=dbg).ap()
    ccb = []
    for bi in range(4):
        ci = nc.dram_tensor(f"cc_in{bi}", [128, 2], F32).ap()
        co = nc.dram_tensor(f"cc_out{bi}", [128, 2], F32).ap()
        ccb.append((ci, co))
    gdbg = nc.dram_tensor("gdbg", [128, 16], F32, kind=dbg).ap()

    fences = {k: {"w": [], "r": []} for k in ENGS}
    es = contextlib.ExitStack()
    with es:
        sbuf = lambda name, shape, dt=F32: es.enter_context(nc.sbuf_tensor(name, shape, dt))
        sem = lambda name: es.enter_context(nc.semaphore(name))

        cst = sbuf("cst", [128, 16])
        ident = sbuf("ident", [128, 128])
        ones_bf = sbuf("ones_bf", [128, 1], BF16)
        ones_r = sbuf("ones_r", [1, 128])
        ones_c = sbuf("ones_c", [128, 1])
        selm = sbuf("selm", [2, 256])
        biases = sbuf("biases", [128, 3 * DC])
        scl = sbuf("scl", [128, 16])
        gmaxs = sbuf("gmaxs", [128, 8])
        mx2 = sbuf("mx2", [128, 2])
        gsc = sbuf("gsc", [2, 1])
        stx = sbuf("stx", [1, 1024])
        abv = sbuf("abv", [1, 1024])
        carry = sbuf("carry", [128, DC])

        pb = [es.enter_context(nc.psum_tensor(f"pb{i}", [128, 512], F32))
              for i in range(8)]

        A0 = 40928
        LIM = 229344

        def at(name, shape, dt, off):
            end = off + shape[1] * mybir.dt.size(dt)
            assert end <= LIM, (name, end)
            return nc.alloc_sbuf_tensor_at(name, shape, dt, offset=off)

        wgm = at("wgm", [128, DC * D], BF16, A0)
        wvm = at("wvm", [128, DC * D], BF16, A0 + 16384)
        wdm = at("wdm", [128, DC * D], BF16, A0 + 32768)
        xf = [at(f"xf{i}", [128, DC * TD], F32, A0 + 49152 + 16384 * i) for i in range(2)]
        xbm = [at(f"xbm{i}", [128, DC * TD], BF16, A0 + 81920 + 8192 * i) for i in range(2)]
        wk_xg = [at(f"wkxg{i}", [128, TD], F32, A0 + 98304 + 2048 * i) for i in range(2)]
        wk_tv = [at(f"wktv{i}", [128, TD], F32, A0 + 102400 + 2048 * i) for i in range(2)]
        wk_sa = [at(f"wksa{i}", [128, TD], F32, A0 + 106496 + 2048 * i) for i in range(2)]
        wk_xs1 = at("wkxs", [128, TD], F32, A0 + 110592)
        wk_av1 = at("wkav", [128, TD], F32, A0 + 112640)
        hbuf = at("hbuf", [128, 2 * TD], F32, A0 + 114688)
        rbuf = at("rbuf", [128, DC * TD], F32, A0 + 118784)
        sqt8 = at("sqt8", [128, DC * TD], F32, A0 + 135168)
        rbuf2 = at("rbuf2", [128, DC * TD], F32, A0 + 151552)
        rbufs = [rbuf, rbuf2]
        mxc = at("mxc", [128, 8], F32, A0 + 167936)
        DQ0 = A0 + 168000
        dq_nat = [at(f"dqn{i}", [128, D], F32, DQ0 + 4096 * i) for i in range(2)]
        dq_wb = [at(f"dqwb{i}", [128, D], BF16, DQ0 + 8192 + 2048 * i) for i in range(2)]
        dq_sc = at("dqsc", [128, 128], F32, DQ0 + 12288)
        wk_xs2 = at("wkxs2", [128, TD], F32, DQ0 + 12800)
        wk_av2 = at("wkav2", [128, TD], F32, DQ0 + 14848)
        wk_xs = [wk_xs1, wk_xs2]
        wk_av = [wk_av1, wk_av2]

        wu_sb = at("wu_sb", [128, DC * I], BF16, A0)
        wq_sb = at("wq_sb", [128, DC * I], BF16, A0 + 49152)
        xqa = [at(f"xqa{i}", [128, DC * TD], BF16, A0 + 98304 + 8192 * i) for i in range(4)]
        gbuf = [at(f"gbuf{i}", [128, 2048], F32, A0 + 151552 + 8192 * i) for i in range(2)]
        astat = at("astat", [128, 96], F32, A0 + 131072)

        ld1 = [at(f"ld1_{i}", [128, 2048], F32, A0 + 8192 * i) for i in range(2)]
        ld2 = [at(f"ld2_{i}", [128, 2048], F32, A0 + 16384 + 8192 * i) for i in range(2)]
        tb1 = [at(f"tb1_{i}", [128, 2048], F32, A0 + 32768 + 8192 * i) for i in range(2)]
        tb2 = [at(f"tb2_{i}", [128, 2048], F32, A0 + 49152 + 8192 * i) for i in range(2)]
        tb3 = [at(f"tb3_{i}", [128, 2048], F32, A0 + 65536 + 8192 * i) for i in range(2)]
        tb4 = [at(f"tb4_{i}", [128, 2048], F32, A0 + 81920 + 8192 * i) for i in range(2)]
        tb5 = [at(f"tb5_{i}", [128, 2048], F32, A0 + 98304 + 8192 * i) for i in range(2)]
        bstat = at("bstat", [128, 48], F32, A0 + 114688)

        wd_sb = at("wd_sb", [128, ICH * D], BF16, A0)
        otile = [at(f"otile{i}", [128, ICH * TD], BF16, A0 + 49152 + 24576 * i) for i in range(2)]
        p2ld = [at(f"p2ld{i}", [128, 2048], F32, A0 + 98304 + 8192 * i) for i in range(2)]
        t5b = at("t5b", [128, 2048], F32, A0 + 114688)
        x1d = at("x1d", [128, DC * TD], F32, A0 + 122880)
        r2b = at("r2b", [128, DC * TD], F32, A0 + 139264)
        r2b2 = at("r2b2", [128, DC * TD], F32, A0 + 155648)
        r2bs = [r2b, r2b2]
        sqt28 = at("sqt28", [128, DC * TD], F32, A0 + 172032)

        S = {n: sem(n) for n in (
            "sWM sWC sMM sEP sRS sCB sSTe sST sSE sVC sSQ sAB sBC sXL sLNE sDLNE sVX "
            "sSPX sPB sXQV sANL sAPC "
            "sWL2 sWQL sWDL sAMM sAEV sARD sBS sBR sBV sCS sCV "
            "sDS2 sDV2 sDX1 sDMM sDEV sDCB sDST sDSTe sDSE2 sDVC sDSQ "
            "sDAB sDBC sDXL sOUT sSU".split())}
        for pn in ("sLX sLB sXQL sAWW sXQS sASP sBL sBSP "
                   "sCL sCSP sDL".split()):
            S[pn] = [sem(pn + "0"), sem(pn + "1")]
        S["sXQA"] = [sem(f"sXQA{i}") for i in range(4)]
        for bi in range(4):
            S[f"bPRE{bi}"] = sem(f"bPRE{bi}")
            S[f"bDMA{bi}"] = sem(f"bDMA{bi}")
            S[f"bCC{bi}"] = sem(f"bCC{bi}")

        C_ap = cst[:, 1:2]
        negC_ap = cst[:, 2:3]
        EPS_ap = cst[0:1, 5:6]

        apply_steps = []
        for c in range(ICH):
            apply_steps.append((wu_n[c][:, :], sp_wu[c][:, :]))
        for c in range(ICH):
            apply_steps.append((wq_n[c][:, :], sp_wq[c][:, :]))
        for hc in range(DC):
            for j in range(3):
                apply_steps.append((wdn_n[hc][:, j * D:(j + 1) * D],
                                    sp_wd[hc][:, j * D:(j + 1) * D]))
        assert len(apply_steps) == NAPPLY
        APT = 6

        def applies_for_tile(t):
            if t == NT - 1:
                return range(APT * t, APT * t + 2)
            return range(APT * t, APT * (t + 1))

        def x_src(t):
            return x_fm.rearrange("k p n -> p k n")[:, :, t * TD:(t + 1) * TD]

        def pbar(e, k):
            e.drain()
            if hasattr(e._e, "engine_nop"):
                e._e.engine_nop().then_inc(S["sPB"], 1)
            else:
                e._e.nop().then_inc(S["sPB"], 1)
            e._e.wait_ge(S["sPB"], 5 * k)
            e._w.clear()
            e._r.clear()


        def ph_setup(eng, e):
            if eng == "gpsimd":
                e.memset(ident[:, :], 0.0)
                e.affine_select(out=ident[:, :], in_=ident[:, :],
                                compare_op=ALU.not_equal, fill=1.0, base=0,
                                pattern=[[-1, 128]], channel_multiplier=1)
                e.memset(ones_bf[:, :], 1.0)
                e.memset(ones_c[:, :], 1.0)
                e.memset(ones_r[:, :], 1.0)
                e.memset(selm[0:2, :], 1.0)
                e.affine_select(out=selm[0:2, :], in_=selm[0:2, :],
                                compare_op=ALU.is_ge, fill=0.0, base=0,
                                pattern=[[1, 256]], channel_multiplier=-128)
                e.affine_select(out=selm[0:2, :], in_=selm[0:2, :],
                                compare_op=ALU.is_ge, fill=0.0, base=127,
                                pattern=[[-1, 256]], channel_multiplier=128)
                e._e.engine_nop().then_inc(S["sSU"], 1)
                for dst, srcw in ((wgm, wg_t), (wvm, wv_t), (wdm, wd_t)):
                    e.dma_start(out=dst.ap().rearrange("p (k n) -> p k n", k=DC),
                                in_=srcw.rearrange("k p n -> p k n")).then_inc(S["sWC"], 16)
            elif eng == "vector":
                e.memset(cst[:, 3:4], 1.0)
                e.memset(cst[:, 5:6], 1e-5)
                e.tensor_scalar(cst[:, 4:5], cst[:, 3:4], 2.0 ** -30, 1.0,
                                ALU.add, ALU.subtract)
                e.tensor_scalar(cst[:, 0:1], cst[:, 4:5], 2.0 ** 30, 1.0,
                                ALU.mult, ALU.mult)
                e.tensor_scalar(cst[:, 1:2], cst[:, 0:1], C_DELTA, C_LO,
                                ALU.mult, ALU.add)
                e.tensor_scalar(cst[:, 2:3], cst[:, 1:2], -1.0, 0.0, ALU.mult, ALU.add)
            elif eng == "sync":
                e.dma_start(out=biases[:, 0:DC], in_=bgh[:, :]).then_inc(S["sWM"], 16)
                e.dma_start(out=biases[:, DC:2 * DC], in_=bvh[:, :]).then_inc(S["sWM"], 16)
                e.dma_start(out=biases[:, 2 * DC:3 * DC], in_=bdh[:, :]).then_inc(S["sWM"], 16)

        def ph_gapply_store(e, s):
            e.wait_ge(S["sAPC"], s + 1)
            _, pdst = apply_steps[s]
            e.dma_start(out=pdst,
                        in_=dq_wb[s % 2][:, :]).then_inc(S["sAWW"][s % 2], 16)

        def ph_gapply(e, s, do_store=True):
            src, dst = apply_steps[s]
            if s >= 1:
                e.wait_ge(S["sAPC"], s - 1)
            e.dma_start(out=dq_nat[s % 2][:, :], in_=src).then_inc(S["sANL"], 16)
            if s >= 1 and do_store:
                ph_gapply_store(e, s - 1)

        def ph_vapply2(e, s):
            na, nb_ = dq_nat[s % 2], dq_nat[(s + 1) % 2]
            base = dq_sc[:, :]
            bc = lambda c0: bass.AP(base.tensor, base.offset + c0,
                                    [[128, 128], [1, 32], [0, 32]])
            e.wait_ge(S["sANL"], 16 * (s + 2))
            if s >= 2:
                e.wait_ge(S["sAWW"][0], 16 * (s // 2))
                e.wait_ge(S["sAWW"][1], 16 * (s // 2))
            na3 = na.ap().rearrange("p (b x) -> p b x", x=32)
            nb3 = nb_.ap().rearrange("p (b x) -> p b x", x=32)
            e.reduce_max(dq_sc[:, 0:32], na3, axis=AX.X, apply_absolute_value=True)
            e.reduce_max(dq_sc[:, 32:64], nb3, axis=AX.X, apply_absolute_value=True)
            e.tensor_scalar(dq_sc[:, 0:64], dq_sc[:, 0:64], 1.0 / 7.0, 1e-12,
                            ALU.mult, ALU.max)
            e.reciprocal(dq_sc[:, 64:128], dq_sc[:, 0:64])
            e.tensor_tensor(na3, na3, bc(64), ALU.mult)
            e.tensor_scalar(na[:, :], na[:, :], C_ap, C_ap, ALU.add, ALU.subtract)
            e.tensor_tensor(dq_wb[s % 2].ap().rearrange("p (b x) -> p b x", x=32),
                            na3, bc(0), ALU.mult).then_inc(S["sAPC"], 1)
            e.tensor_tensor(nb3, nb3, bc(96), ALU.mult)
            e.tensor_scalar(nb_[:, :], nb_[:, :], C_ap, C_ap, ALU.add, ALU.subtract)
            e.tensor_tensor(dq_wb[(s + 1) % 2].ap().rearrange("p (b x) -> p b x", x=32),
                            nb3, bc(32), ALU.mult).then_inc(S["sAPC"], 1)
        def ph_mixer_tile(eng, e, t):
            if eng == "sync":
                if t == 0:
                    for tt in (0, 1):
                        e.dma_start(out=xf[tt].ap().rearrange("p (k n) -> p k n", k=DC),
                                    in_=x_src(tt)).then_inc(S["sLX"][tt % 2], 16)
                if t >= 1:
                    e.wait_ge(S["sXL"], t)
                    e.dma_start(
                        out=sp_x1.rearrange("k p n -> p k n")[:, :,
                                                              (t - 1) * TD:t * TD],
                        in_=xf[(t - 1) % 2].ap().rearrange("p (k n) -> p k n", k=DC)
                    ).then_inc(S["sSPX"], 16)
                    if t + 1 < NT:
                        e.wait_ge(S["sSPX"], 16 * t)
                        e.dma_start(out=xf[(t + 1) % 2].ap().rearrange("p (k n) -> p k n", k=DC),
                                    in_=x_src(t + 1)).then_inc(S["sLX"][(t + 1) % 2], 16)
            elif eng == "gpsimd":
                if t == 0:
                    for tt in (0, 1):
                        e.dma_start(out=xbm[tt].ap().rearrange("p (k n) -> p k n", k=DC),
                                    in_=x_src(tt)).then_inc(S["sLB"][tt % 2], 16)
                for s in applies_for_tile(t):
                    ph_gapply(e, s)
                if t == NT - 1:
                    ph_gapply(e, APT * NT - 4, do_store=False)
                    ph_gapply(e, APT * NT - 3, do_store=False)
                if t + 2 < NT:
                    e.wait_ge(S["sMM"], 8 * (t + 1))
                    e.dma_start(out=xbm[t % 2].ap().rearrange("p (k n) -> p k n", k=DC),
                                in_=x_src(t + 2)).then_inc(S["sLB"][t % 2], 16)
            elif eng == "tensor":
                if t == 0:
                    e.wait_ge(S["sWM"], 48)
                    e.wait_ge(S["sWC"], 48)
                e.wait_ge(S["sLB"][t % 2], 16 * (t // 2 + 1))
                e.wait_ge(S["sLX"][t % 2], 16 * (t // 2 + 1))

                def stats(ts):
                    if ts >= 2:
                        e.wait_ge(S["sVC"], ts)
                    if ts >= 1:
                        e.wait_ge(S["sXL"], ts)
                    for ei in range(DC):
                        e.wait_ge(S["sCB"], 8 * ts + ei + 1)
                        if ei < DC - 1:
                            e.matmul(pb[3][0:1, :], ones_c[:, :],
                                     rbufs[ts % 2][:, ei * TD:(ei + 1) * TD],
                                     start=(ei == 0), stop=False)
                            e.matmul(pb[4][0:1, :], ones_c[:, :], sqt8[:, ei * TD:(ei + 1) * TD],
                                     start=(ei == 0), stop=False).then_inc(S["sSTe"], 1)
                        else:
                            e.matmul(pb[4][0:1, :], ones_c[:, :], sqt8[:, ei * TD:(ei + 1) * TD],
                                     start=False, stop=True).then_inc(S["sSTe"], 1)
                            e.matmul(pb[3][0:1, :], ones_c[:, :],
                                     rbufs[ts % 2][:, ei * TD:(ei + 1) * TD],
                                     start=False, stop=True).then_inc(S["sST"], 1)

                def bcast(ts):
                    e.wait_ge(S["sAB"], ts + 1)
                    if ts >= 1:
                        e.wait_ge(S["sXL"], ts)
                    e.matmul(pb[4][:, :], ones_r[:, :], abv[0:1, 512:1024],
                             start=True, stop=True)
                    e.matmul(pb[5][:, :], ones_r[:, :], stx[0:1, 0:512],
                             start=True, stop=True).then_inc(S["sBC"], 1)

                for ei in range(DC):
                    if 8 * t + ei >= 1:
                        e.wait_ge(S["sEP"], 8 * t + ei)
                    last = None
                    for wi, wsb in enumerate((wgm, wvm, wdm)):
                        for k in range(DC):
                            last = e.matmul(
                                pb[wi][:, :],
                                wsb[:, k * D + 128 * ei: k * D + 128 * ei + 128],
                                xbm[t % 2][:, k * TD:(k + 1) * TD],
                                start=(k == 0), stop=(k == DC - 1))
                    last.then_inc(S["sMM"], 1)
                    if ei == 0 and t >= 1:
                        stats(t - 1)
                    if ei == 4 and t >= 1:
                        bcast(t - 1)
                if t == NT - 1:
                    stats(t)
                    bcast(t)
            elif eng == "scalar":
                def sq_m(k):
                    e.wait_ge(S["sRS"], 8 * t + 2 * (k // 2 + 1))
                    if t >= 1:
                        e.wait_ge(S["sSTe"], 8 * (t - 1) + k + 1)
                    e.activation(sqt8[:, k * TD:(k + 1) * TD],
                                 rbufs[t % 2][:, k * TD:(k + 1) * TD],
                                 AF.Square).then_inc(S["sCB"], 1)

                def ln_sc(ts):
                    e.wait_ge(S["sST"], ts + 1)
                    e.activation(stx[0:1, 0:512], pb[3][0:1, :], AF.Identity)
                    e.activation(stx[0:1, 512:1024], pb[4][0:1, :], AF.Identity)
                    e.activation(abv[0:1, 0:512], stx[0:1, 0:512], AF.Identity,
                                 scale=1.0 / 1024.0)
                    e.activation(abv[0:1, 512:1024], stx[0:1, 512:1024], AF.Identity,
                                 scale=1.0 / 1024.0)
                    e.activation(stx[0:1, 0:512], abv[0:1, 0:512],
                                 AF.Square).then_inc(S["sLNE"], 1)

                for ei in range(DC):
                    e.wait_ge(S["sMM"], 8 * t + ei + 1)
                    if 8 * t + ei >= 2:
                        e.wait_ge(S["sVX"], 8 * t + ei - 1)
                    e.activation(wk_xg[ei % 2][:, :], pb[0][:, :], AF.Sigmoid,
                                 bias=biases[:, ei:ei + 1])
                    e.activation(wk_tv[ei % 2][:, :], pb[1][:, :], AF.Tanh,
                                 bias=biases[:, DC + ei:DC + ei + 1])
                    e.activation(wk_sa[ei % 2][:, :], pb[2][:, :], AF.Sigmoid,
                                 bias=biases[:, 2 * DC + ei:2 * DC + ei + 1]
                                 ).then_inc(S["sEP"], 1)
                    if ei == 0 and t >= 1:
                        ln_sc(t - 1)
                    if ei == 1 and t >= 1:
                        e.wait_ge(S["sVC"], t)
                        e.activation(stx[0:1, 0:512], stx[0:1, 512:1024], AF.Sqrt,
                                     bias=EPS_ap).then_inc(S["sSQ"], 1)
                    if ei >= 2:
                        sq_m(ei - 2)
                sq_m(DC - 2)
                sq_m(DC - 1)
                if t == NT - 1:
                    ln_sc(t)
                    e.wait_ge(S["sVC"], t + 1)
                    e.activation(stx[0:1, 0:512], stx[0:1, 512:1024], AF.Sqrt,
                                 bias=EPS_ap).then_inc(S["sSQ"], 1)
            elif eng == "vector":
                def ln_post(ts):
                    e.wait_ge(S["sLNE"], ts + 1)
                    e.tensor_tensor(stx[0:1, 512:1024], abv[0:1, 512:1024],
                                    stx[0:1, 0:512], ALU.subtract).then_inc(S["sVC"], 1)
                    e._w.clear()
                    e._r.clear()
                    e.drain()

                def ab_post(ts):
                    e.wait_ge(S["sSQ"], ts + 1)
                    e.reciprocal(abv[0:1, 512:1024], stx[0:1, 0:512])
                    e.scalar_tensor_tensor(stx[0:1, 0:512], abv[0:1, 0:512], -1.0,
                                           abv[0:1, 512:1024], ALU.mult,
                                           ALU.mult).then_inc(S["sAB"], 1)

                def affine(ts):
                    e.wait_ge(S["sBC"], ts + 1)
                    if ts >= 1:
                        e.wait_ge(S["sSPX"], 16 * ts)
                    p4 = pb[4][:, :]
                    p5 = pb[5][:, :]
                    b4 = bass.AP(p4.tensor, p4.offset, [[p4.ap[0][0], 128], [0, DC], [1, TD]])
                    b5 = bass.AP(p5.tensor, p5.offset, [[p5.ap[0][0], 128], [0, DC], [1, TD]])
                    xv = xf[ts % 2].ap().rearrange("p (b x) -> p b x", x=TD)
                    rv = rbufs[ts % 2].ap().rearrange("p (b x) -> p b x", x=TD)
                    e.tensor_tensor(xv, rv, b4, ALU.mult)
                    e.tensor_tensor(xv, xv, b5, ALU.add)
                    e.reduce_max(mxc[:, ts:ts + 1], xf[ts % 2][:, :], axis=AX.X,
                                 apply_absolute_value=True).then_inc(S["sXL"], 1)

                rb = rbufs[t % 2]
                for ei in range(DC):
                    e.wait_ge(S["sEP"], 8 * t + ei + 1)
                    sl = (ei % 2) * TD
                    e.tensor_tensor(wk_xs[ei % 2][:, :], wk_xg[ei % 2][:, :],
                                    wk_tv[ei % 2][:, :], ALU.mult)
                    e.tensor_scalar(wk_av[ei % 2][:, :], wk_sa[ei % 2][:, :],
                                    0.998, 0.001, ALU.mult,
                                    ALU.add).then_inc(S["sVX"], 1)
                    if ei >= 2 and ei % 2 == 0:
                        e.tensor_tensor(rb[:, (ei - 2) * TD:ei * TD],
                                        xf[t % 2][:, (ei - 2) * TD:ei * TD],
                                        hbuf[:, 0:2 * TD], ALU.add).then_inc(S["sRS"], 2)
                    if ei >= 1:
                        psl = ((ei - 1) % 2) * TD
                        e.tensor_scalar(carry[:, ei - 1:ei],
                                        hbuf[:, psl + TD - 1:psl + TD],
                                        1.0, 0.0, ALU.mult, ALU.add)
                    init = 0.0 if t % 4 == 0 else carry[:, ei:ei + 1]
                    e.tensor_tensor_scan(hbuf[:, sl:sl + TD], wk_av[ei % 2][:, :],
                                         wk_xs[ei % 2][:, :], init, ALU.mult, ALU.add)
                    if ei == 0 and t >= 1:
                        ln_post(t - 1)
                    if ei == 2 and t >= 1:
                        ab_post(t - 1)
                    if ei == 4 and t >= 1:
                        affine(t - 1)
                    if ei in (1, 3, 5) and (t < NT - 1 or ei == 1):
                        ph_vapply2(e, APT * t + ei - 1)
                e.tensor_scalar(carry[:, DC - 1:DC],
                                hbuf[:, ((DC - 1) % 2) * TD + TD - 1:
                                      ((DC - 1) % 2) * TD + TD],
                                1.0, 0.0, ALU.mult, ALU.add)
                e.tensor_tensor(rb[:, (DC - 2) * TD:DC * TD],
                                xf[t % 2][:, (DC - 2) * TD:DC * TD],
                                hbuf[:, 0:2 * TD], ALU.add).then_inc(S["sRS"], 2)
                if t == NT - 1:
                    ln_post(t)
                    ab_post(t)
                    affine(t)

        def ph_mixer_tail(eng, e):
            if eng == "sync":
                e.wait_ge(S["sXL"], NT)
                e.dma_start(
                    out=sp_x1.rearrange("k p n -> p k n")[:, :, (NT - 1) * TD:NT * TD],
                    in_=xf[(NT - 1) % 2].ap().rearrange("p (k n) -> p k n", k=DC)
                ).then_inc(S["sSPX"], 16)
        def ph_gmax_barrier(eng, e, bi, k):
            cin, cout = ccb[bi]
            bP, bD, bC = S[f"bPRE{bi}"], S[f"bDMA{bi}"], S[f"bCC{bi}"]
            if eng == "sync":
                e.wait_ge(bP, 1)
                e.dma_start(out=cin[:, :], in_=mx2[:, :]).then_inc(bD, 16)
                e.wait_ge(bC, 1)
                e.dma_start(out=mx2[:, :], in_=cout[:, :]).then_inc(bD, 16)
            elif eng == "gpsimd":
                e.wait_ge(bD, 16)
                e.collective_compute(
                    "AllReduce", ALU.max, replica_groups=[list(range(NCORES))],
                    ins=[cin[:, :]], outs=[cout[:, :]]).then_inc(bC, 1)
            elif eng == "tensor":
                e.wait_ge(bD, 32)
                e.transpose(pb[7][0:2, 0:128], mx2[:, :], ident[:, :]).then_inc(bP, 1)
                e.wait_ge(bP, 3)
                m = None
                for j in range(k):
                    m = e.matmul(pb[6][:, j:j + 1], selm[0:2, 128 * j:128 * j + 128],
                                 gsc[0:2, 0:1], start=True, stop=True)
                m.then_inc(bP, 1)
            elif eng == "scalar":
                e.wait_ge(bP, 4)
                e.activation(gmaxs[:, 2 * bi:2 * bi + k], pb[6][:, 0:k],
                             AF.Copy).then_inc(bP, 1)
            elif eng == "vector":
                e.wait_ge(bP, 2)
                e.reduce_max(gsc[0:2, 0:1], pb[7][0:2, 0:128], axis=AX.X).then_inc(bP, 1)
                e.wait_ge(bP, 5)

        def ph_bar1_fill(eng, e):
            if eng == "vector":
                e.wait_ge(S["sXL"], NT)
                e.reduce_max(mx2[:, 0:1], mxc[:, 0:NT], axis=AX.X)
                e.reduce_max(mx2[:, 1:2], mxc[:, 0:NT], axis=AX.X).then_inc(S["bPRE0"], 1)
                ph_vapply2(e, APT * NT - 4)
                ph_vapply2(e, APT * NT - 2)

        def ph_bar1_post(eng, e):
            if eng == "vector":
                e.tensor_scalar(scl[:, 0:1], gmaxs[:, 0:1], 1.0 / QMAX, 1e-12,
                                ALU.mult, ALU.max)
                e.reciprocal(scl[:, 1:2], scl[:, 0:1]).then_inc(S["bPRE0"], 1)

        def ph_xq(eng, e):
            for t in range(NT):
                if eng == "sync":
                    if t == 0:
                        e.wait_ge(S["sSPX"], 16 * NT)
                    if t >= 2:
                        e.wait_ge(S["sXQV"], t - 1)
                    e.dma_start(
                        out=xf[t % 2].ap().rearrange("p (k n) -> p k n", k=DC),
                        in_=sp_x1.rearrange("k p n -> p k n")[:, :, t * TD:(t + 1) * TD]
                    ).then_inc(S["sXQL"][t % 2], 16)
                elif eng == "vector":
                    e.wait_ge(S["sXQL"][t % 2], 16 * (t // 2 + 1))
                    if t == 0:
                        e.wait_ge(S["bPRE0"], 6)
                    if t >= 2:
                        e.wait_ge(S["sXQS"][t % 2], 16 * ((t - 2) // 2 + 1))
                    if t in (2, 3, 4, 5):
                        e.wait_ge(S["sXQA"][t - 2], 16)
                    e.tensor_scalar(sqt8[:, :], xf[t % 2][:, :], scl[:, 1:2], C_ap,
                                    ALU.mult, ALU.add)
                    e.tensor_scalar(xbm[t % 2][:, :], sqt8[:, :], C_ap, scl[:, 0:1],
                                    ALU.subtract, ALU.mult).then_inc(S["sXQV"], 1)
                    if stop_after != "mixer":
                        for g in xq_reds.get(t, ()):
                            red_a(e, g)
                if eng == "sync":
                    e.wait_ge(S["sXQV"], t + 1)
                    e.dma_start(
                        out=sp_xq.rearrange("k p n -> p k n")[:, :, t * TD:(t + 1) * TD],
                        in_=xbm[t % 2].ap().rearrange("p (k n) -> p k n", k=DC)
                    ).then_inc(S["sXQS"][t % 2], 16)
                    if t <= 3:
                        e.dma_start(out=xqa[t][:, :],
                                    in_=xbm[t % 2][:, :]).then_inc(S["sXQA"][t], 16)

        aorder = [(0, 0), (1, 0), (2, 0), (3, 0), (0, 1), (1, 1), (2, 1), (3, 1),
                  (4, 0), (5, 0), (4, 1), (5, 1), (6, 0), (7, 0), (6, 1), (7, 1)]

        def red_a(e, g):
            oi = g // (ICH // 4)
            gg = g % (ICH // 4)
            t, w = aorder[oi]
            n = oi * ICH + gg * 4 + 3
            col = w * 48 + t * 6 + gg
            e.wait_ge(S["sAEV"], n + 1)
            e.reduce_max(astat[:, col:col + 1], gbuf[g % 2][:, :], axis=AX.X,
                         apply_absolute_value=True).then_inc(S["sARD"], 1)

        xq_reds = {2: [0], 3: [1, 2], 4: [3, 4], 5: [5, 6], 6: [7, 8],
                   7: [9, 10, 11]}

        def ph_A_wu_loads(eng, e):
            if eng == "sync":
                e.wait_ge(S["sMM"], 8 * NT)
                e.wait_ge(S["sAWW"][0], 16 * 12)
                e.wait_ge(S["sAWW"][1], 16 * 12)
                flat = sp_wu.rearrange("c p n -> (c p) n")
                for k in range(DC):
                    e.dma_start(out=wu_sb[:, k * I:(k + 1) * I],
                                in_=flat[:, 128 * k:128 * (k + 1)],
                                transpose=True).then_inc(S["sWL2"], 16)

        def ph_A(eng, e):
            if eng == "gpsimd":
                for sd in range(APT * NT, NAPPLY):
                    ph_gapply(e, sd, do_store=(sd > APT * NT))
                ph_gapply_store(e, NAPPLY - 1)
            if eng == "sync":
                e.wait_ge(S["sXQS"][0], 16 * (NT // 2))
                e.wait_ge(S["sXQS"][1], 16 * (NT // 2))
                e.wait_ge(S["sAWW"][0], 16 * 24)
                e.wait_ge(S["sAWW"][1], 16 * 24)
                flat = sp_wq.rearrange("c p n -> (c p) n")
                for k in range(DC):
                    e.dma_start(out=wq_sb[:, k * I:(k + 1) * I],
                                in_=flat[:, 128 * k:128 * (k + 1)],
                                transpose=True).then_inc(S["sWQL"], 16)
                for tl in range(4, NT):
                    e.wait_ge(S["sAMM"],
                              (aorder.index((tl - 4, 1)) + 1) * ICH)
                    e.dma_start(
                        out=xqa[tl % 4].ap().rearrange("p (k n) -> p k n", k=DC),
                        in_=sp_xq.rearrange("k p n -> p k n")[:, :,
                                                              tl * TD:(tl + 1) * TD]
                    ).then_inc(S["sXQA"][tl % 4], 16)
            for oi, (t, w) in enumerate(aorder):
                _sc = nc.named_scope(f"pa{oi}")
                _sc.__enter__()
                wsb = wu_sb if w == 0 else wq_sb
                spw = sp_up if w == 0 else sp_gt
                if eng == "tensor":
                    if oi == 0:
                        e.wait_ge(S["sWL2"], 16 * 8)
                        e.wait_ge(S["sEP"], 8 * NT)
                        e.wait_ge(S["sVC"], NT)
                    if oi == 4:
                        e.wait_ge(S["sWQL"], 16 * 8)
                    if w == 0:
                        e.wait_ge(S["sXQA"][t % 4], 16 * (1 if t < 4 else 2))
                    for i in range(ICH):
                        n = oi * ICH + i
                        if n >= 4:
                            e.wait_ge(S["sAEV"], n - 3)
                        m = None
                        for k in range(DC):
                            m = e.matmul(
                                pb[n % 4][:, :],
                                wsb[:, k * I + 128 * i: k * I + 128 * i + 128],
                                xqa[t % 4][:, k * TD:(k + 1) * TD],
                                start=(k == 0), stop=(k == DC - 1))
                        m.then_inc(S["sAMM"], 1)
                elif eng == "scalar":
                    for i in range(ICH):
                        n = oi * ICH + i
                        g = n // 4
                        if n == 0:
                            e.wait_ge(S["sST"], NT)
                            e.wait_ge(S["sXL"], NT)
                        if g >= 2:
                            e.wait_ge(S["sASP"][g % 2], 16 * ((g - 2) // 2 + 1))
                            e.wait_ge(S["sARD"], g - 1)
                        e.wait_ge(S["sAMM"], n + 1)
                        e.activation(gbuf[g % 2][:, 512 * (i % 4):512 * (i % 4) + 512],
                                     pb[n % 4][:, :], AF.Copy).then_inc(S["sAEV"], 1)
                        if i % 4 == 3:
                            e.dma_start(
                                out=spw.rearrange("k p n -> p k n")[:, i - 3:i + 1,
                                                                    t * TD:(t + 1) * TD],
                                in_=gbuf[g % 2].ap().rearrange("p (c n) -> p c n", c=4)
                            ).then_inc(S["sASP"][g % 2], 16)
                elif eng == "vector":
                    if oi >= 2:
                        for i in range(3, ICH, 4):
                            red_a(e, oi * (ICH // 4) + i // 4)
                        if oi <= 13:
                            ph_vapply2(e, APT * NT + 2 * (oi - 2))
                _sc.__exit__(None, None, None)

        def ph_bar2_fill(eng, e):
            if eng == "vector":
                e.wait_ge(S["sARD"], NA // 4)
                e.reduce_max(mx2[:, 0:1], astat[:, 0:48], axis=AX.X)
                e.reduce_max(mx2[:, 1:2], astat[:, 48:96], axis=AX.X).then_inc(S["bPRE1"], 1)

        def ph_bar2_post(eng, e):
            if eng == "vector":
                e.tensor_scalar(scl[:, 4:5], gmaxs[:, 2:3], 1.0 / QMAX, 1e-12,
                                ALU.mult, ALU.max)
                e.tensor_scalar(scl[:, 2:3], gmaxs[:, 3:4], 1.0 / QMAX, 1e-12,
                                ALU.mult, ALU.max)
                e.reciprocal(scl[:, 5:6], scl[:, 4:5])
                e.reciprocal(scl[:, 3:4], scl[:, 2:3]).then_inc(S["bPRE1"], 1)

        def ph_B(eng, e):
            def vtail(n):
                e.wait_ge(S["sBS"], 2 * (n + 1))
                if n >= 2:
                    e.wait_ge(S["sBSP"][n % 2], 16 * ((n - 2) // 2 + 1))
                e.tensor_scalar(tb5[n % 2][:, :], tb4[n % 2][:, :], C_ap, 65535.0,
                                ALU.subtract, ALU.min)
                e.tensor_tensor(tb5[n % 2][:, :], tb2[n % 2][:, :],
                                tb5[n % 2][:, :], ALU.mult)
                e.reduce_max(bstat[:, n:n + 1], tb5[n % 2][:, :], axis=AX.X,
                             apply_absolute_value=True).then_inc(S["sBV"], 3)

            for n in range(NB):
                t, j = divmod(n, ICH // 4)
                if eng == "sync":
                    if n == 0:
                        e.wait_ge(S["sASP"][0], 16 * (NA // 8))
                        e.wait_ge(S["sASP"][1], 16 * (NA // 8))
                    if n >= 2:
                        e.wait_ge(S["sBR"], n - 1)
                    e.dma_start(
                        out=ld1[n % 2].ap().rearrange("p (c x) -> p c x", c=4),
                        in_=sp_gt.rearrange("k p n -> p k n")[:, 4 * j:4 * j + 4,
                                                              t * TD:(t + 1) * TD]
                    ).then_inc(S["sBL"][n % 2], 16)
                elif eng == "vector":
                    e.wait_ge(S["sBL"][n % 2], 16 * (n // 2 + 1))
                    if n == 0:
                        e.wait_ge(S["bPRE1"], 6)
                    if n >= 2:
                        e.wait_ge(S["sBS"], 2 * (n - 1))
                    e.tensor_scalar(tb1[n % 2][:, :], ld1[n % 2][:, :],
                                    scl[:, 3:4], C_ap, ALU.mult, ALU.add)
                    e.tensor_scalar(tb2[n % 2][:, :], tb1[n % 2][:, :], C_ap, 1.0,
                                    ALU.subtract, ALU.mult).then_inc(S["sBR"], 1)
                    if n >= 1:
                        vtail(n - 1)
                    if n == NB - 1:
                        vtail(n)
                elif eng == "scalar":
                    e.wait_ge(S["sBR"], n + 1)
                    if n >= 2:
                        e.wait_ge(S["sBV"], 3 * (n - 1))
                    e.activation(tb3[n % 2][:, :], tb2[n % 2][:, :], AF.Sigmoid,
                                 scale=scl[:, 2:3])
                    e.activation(tb4[n % 2][:, :], tb3[n % 2][:, :], AF.Identity,
                                 bias=C_ap, scale=65536.0).then_inc(S["sBS"], 2)
                elif eng == "gpsimd":
                    e.wait_ge(S["sBV"], 3 * (n + 1))
                    e.dma_start(
                        out=sp_p1.rearrange("k p n -> p k n")[:, 4 * j:4 * j + 4,
                                                              t * TD:(t + 1) * TD],
                        in_=tb5[n % 2].ap().rearrange("p (c x) -> p c x", c=4)
                    ).then_inc(S["sBSP"][n % 2], 16)

        def ph_bar3_fill(eng, e):
            if eng == "vector":
                e.wait_ge(S["sBV"], 3 * NB)
                e.reduce_max(mx2[:, 0:1], bstat[:, 0:NB], axis=AX.X)
                e.reduce_max(mx2[:, 1:2], bstat[:, 0:NB], axis=AX.X).then_inc(S["bPRE2"], 1)

        def ph_bar3_post(eng, e):
            if eng == "vector":
                e.reciprocal(scl[:, 7:8], gmaxs[:, 4:5])
                e.tensor_scalar(scl[:, 6:7], scl[:, 7:8], QMAX, 0.0,
                                ALU.mult, ALU.add).then_inc(S["bPRE2"], 1)

        def ph_C(eng, e):
            for n in range(NB):
                t, j = divmod(n, ICH // 4)
                if eng == "sync":
                    if n == 0:
                        e.wait_ge(S["sBSP"][0], 16 * (NB // 2))
                        e.wait_ge(S["sBSP"][1], 16 * (NB // 2))
                    if n >= 2:
                        e.wait_ge(S["sCS"], 2 * (n - 1))
                    e.dma_start(
                        out=ld1[n % 2].ap().rearrange("p (c x) -> p c x", c=4),
                        in_=sp_p1.rearrange("k p n -> p k n")[:, 4 * j:4 * j + 4,
                                                              t * TD:(t + 1) * TD]
                    ).then_inc(S["sCL"][n % 2], 16)
                    e.dma_start(
                        out=ld2[n % 2].ap().rearrange("p (c x) -> p c x", c=4),
                        in_=sp_up.rearrange("k p n -> p k n")[:, 4 * j:4 * j + 4,
                                                              t * TD:(t + 1) * TD]
                    ).then_inc(S["sCL"][n % 2], 16)
                elif eng == "scalar":
                    e.wait_ge(S["sCL"][n % 2], 32 * (n // 2 + 1))
                    if n == 0:
                        e.wait_ge(S["bPRE2"], 6)
                    if n >= 2:
                        e.wait_ge(S["sCV"], 2 * (n - 1))
                    e.activation(tb3[n % 2][:, :], ld2[n % 2][:, :], AF.Identity,
                                 bias=C_ap, scale=scl[:, 5:6])
                    e.activation(tb1[n % 2][:, :], ld1[n % 2][:, :], AF.Identity,
                                 bias=C_ap, scale=scl[:, 6:7])
                    e.activation(tb2[n % 2][:, :], tb1[n % 2][:, :], AF.Identity,
                                 bias=negC_ap).then_inc(S["sCS"], 2)
                elif eng == "vector":
                    e.wait_ge(S["sCS"], 2 * (n + 1))
                    if n >= 2:
                        e.wait_ge(S["sCSP"][n % 2], 16 * ((n - 2) // 2 + 1))
                    e.scalar_tensor_tensor(tb4[n % 2][:, :], tb3[n % 2][:, :], negC_ap,
                                           tb2[n % 2][:, :], ALU.add, ALU.mult)
                    e.reduce_max(bstat[:, n:n + 1], tb4[n % 2][:, :], axis=AX.X,
                                 apply_absolute_value=True).then_inc(S["sCV"], 2)
                elif eng == "gpsimd":
                    e.wait_ge(S["sCV"], 2 * (n + 1))
                    e.dma_start(
                        out=sp_p2.rearrange("k p n -> p k n")[:, 4 * j:4 * j + 4,
                                                              t * TD:(t + 1) * TD],
                        in_=tb4[n % 2].ap().rearrange("p (c x) -> p c x", c=4)
                    ).then_inc(S["sCSP"][n % 2], 16)

        def ph_bar4_fill(eng, e):
            if eng == "vector":
                e.wait_ge(S["sCV"], 2 * NB)
                e.reduce_max(mx2[:, 0:1], bstat[:, 0:NB], axis=AX.X)
                e.reduce_max(mx2[:, 1:2], bstat[:, 0:NB], axis=AX.X).then_inc(S["bPRE3"], 1)

        def ph_bar4_post(eng, e):
            if eng == "vector":
                e.reciprocal(scl[:, 10:11], gmaxs[:, 6:7])
                e.tensor_scalar(scl[:, 11:12], gmaxs[:, 4:5], 1.0 / (QMAX * 65536.0),
                                0.0, ALU.mult, ALU.add)
                e.tensor_scalar(scl[:, 8:9], scl[:, 10:11], QMAX, 0.0, ALU.mult, ALU.add)
                e.tensor_tensor(scl[:, 12:13], scl[:, 11:12], scl[:, 2:3], ALU.mult)
                e.tensor_tensor(scl[:, 13:14], scl[:, 12:13], scl[:, 4:5], ALU.mult)
                e.tensor_tensor(scl[:, 14:15], scl[:, 13:14], gmaxs[:, 6:7], ALU.mult)
                e.tensor_scalar(scl[:, 9:10], scl[:, 14:15], 1.0 / QMAX, 0.0,
                                ALU.mult, ALU.add).then_inc(S["bPRE3"], 1)

        def ph_D_loads(eng, e):
            if eng == "sync":
                e.wait_ge(S["sCSP"][0], 16 * (NB // 2))
                e.wait_ge(S["sCSP"][1], 16 * (NB // 2))
                e.wait_ge(S["sAWW"][0], 16 * (NAPPLY // 2))
                e.wait_ge(S["sAWW"][1], 16 * (NAPPLY // 2))
                flatd = sp_wd.rearrange("c p n -> (c p) n")
                for ci in range(ICH):
                    e.dma_start(out=wd_sb[:, ci * D:(ci + 1) * D],
                                in_=flatd[:, 128 * ci:128 * (ci + 1)],
                                transpose=True).then_inc(S["sWDL"], 16)
                for j in range(2):
                    e.dma_start(
                        out=p2ld[j % 2].ap().rearrange("p (c x) -> p c x", c=4),
                        in_=sp_p2.rearrange("k p n -> p k n")[:, 4 * j:4 * j + 4,
                                                              0:TD]
                    ).then_inc(S["sDL"][j % 2], 16)
                e.dma_start(
                    out=x1d.ap().rearrange("p (k n) -> p k n", k=DC),
                    in_=sp_x1.rearrange("k p n -> p k n")[:, :, 0:TD]
                ).then_inc(S["sDX1"], 16)

        def ph_D(eng, e):
            def ln_post2(ts):
                e.wait_ge(S["sDLNE"], ts + 1)
                e.tensor_tensor(stx[0:1, 512:1024], abv[0:1, 512:1024],
                                stx[0:1, 0:512], ALU.subtract).then_inc(S["sDVC"], 1)
                e.drain()

            def ln_sc2(ts):
                e.wait_ge(S["sDST"], ts + 1)
                e.activation(stx[0:1, 0:512], pb[3][0:1, :], AF.Identity)
                e.activation(stx[0:1, 512:1024], pb[4][0:1, :], AF.Identity)
                e.activation(abv[0:1, 0:512], stx[0:1, 0:512], AF.Identity,
                             scale=1.0 / 1024.0)
                e.activation(abv[0:1, 512:1024], stx[0:1, 512:1024], AF.Identity,
                             scale=1.0 / 1024.0)
                e.activation(stx[0:1, 0:512], abv[0:1, 0:512],
                             AF.Square).then_inc(S["sDLNE"], 1)

            def ab_post2(ts):
                e.wait_ge(S["sDSQ"], ts + 1)
                e.reciprocal(abv[0:1, 512:1024], stx[0:1, 0:512])
                e.scalar_tensor_tensor(stx[0:1, 0:512], abv[0:1, 0:512], -1.0,
                                       abv[0:1, 512:1024], ALU.mult,
                                       ALU.mult).then_inc(S["sDAB"], 1)

            def affine2(ts):
                e.wait_ge(S["sDBC"], ts + 1)
                e.wait_ge(S["sOUT"], 16 * ts)
                p4 = pb[4][:, :]
                p5 = pb[5][:, :]
                b4 = bass.AP(p4.tensor, p4.offset, [[p4.ap[0][0], 128], [0, DC], [1, TD]])
                b5 = bass.AP(p5.tensor, p5.offset, [[p5.ap[0][0], 128], [0, DC], [1, TD]])
                sv = sqt28.ap().rearrange("p (b x) -> p b x", x=TD)
                rv = r2bs[ts % 2].ap().rearrange("p (b x) -> p b x", x=TD)
                e.tensor_tensor(sv, rv, b4, ALU.mult)
                e.tensor_tensor(sv, sv, b5, ALU.add)
                e._e.engine_nop().then_inc(S["sDXL"], 1)

            def round_d(tt, j):
                n = tt * NJ + j
                e.wait_ge(S["sDL"][n % 2], 16 * (n // 2 + 1))
                if n == 0:
                    e.wait_ge(S["bPRE3"], 6)
                if tt >= 2 and j == 0:
                    e.wait_ge(S["sDMM"], 8 * (tt - 1))
                e.tensor_scalar(t5b[:, :], p2ld[n % 2][:, :], scl[:, 8:9],
                                C_ap, ALU.mult, ALU.add).then_inc(S["sDS2"], 1)
                e.tensor_scalar(otile[tt % 2][:, 2048 * j:2048 * (j + 1)],
                                t5b[:, :], C_ap, 1.0,
                                ALU.subtract, ALU.mult).then_inc(S["sDV2"], 1)

            for t in range(NT):
                _sc = nc.named_scope(f"pd{t}")
                _sc.__enter__()
                if eng == "sync":
                    if t == 0:
                        for j in range(2, NJ):
                            e.wait_ge(S["sDS2"], j - 1)
                            e.dma_start(
                                out=p2ld[j % 2].ap().rearrange("p (c x) -> p c x", c=4),
                                in_=sp_p2.rearrange("k p n -> p k n")[:, 4 * j:4 * j + 4,
                                                                      0:TD]
                            ).then_inc(S["sDL"][j % 2], 16)
                    if t >= 1:
                        e.wait_ge(S["sDEV"], 8 * t)
                        e.dma_start(
                            out=x1d.ap().rearrange("p (k n) -> p k n", k=DC),
                            in_=sp_x1.rearrange("k p n -> p k n")[:, :,
                                                                  t * TD:(t + 1) * TD]
                        ).then_inc(S["sDX1"], 16)
                    if t + 1 < NT:
                        for j in range(NJ):
                            n = (t + 1) * NJ + j
                            e.wait_ge(S["sDS2"], n - 1)
                            e.dma_start(
                                out=p2ld[n % 2].ap().rearrange("p (c x) -> p c x", c=4),
                                in_=sp_p2.rearrange("k p n -> p k n")[:, 4 * j:4 * j + 4,
                                                                      (t + 1) * TD:(t + 2) * TD]
                            ).then_inc(S["sDL"][n % 2], 16)
                    if t >= 1:
                        e.wait_ge(S["sDXL"], t)
                        e.dma_start(
                            out=out_fm.rearrange("k p n -> p k n")[:, :,
                                                                   (t - 1) * TD:t * TD],
                            in_=sqt28.ap().rearrange("p (k n) -> p k n", k=DC)
                        ).then_inc(S["sOUT"], 16)
                if eng == "tensor":
                    if t == 0:
                        e.wait_ge(S["sWDL"], 16 * ICH)

                    def stats2(ts):
                        if ts >= 2:
                            e.wait_ge(S["sDVC"], ts)
                        for ei in range(DC):
                            e.wait_ge(S["sDCB"], 8 * ts + ei + 1)
                            if ei < DC - 1:
                                e.matmul(pb[3][0:1, :], ones_c[:, :],
                                         r2bs[ts % 2][:, ei * TD:(ei + 1) * TD],
                                         start=(ei == 0), stop=False)
                                e.matmul(pb[4][0:1, :], ones_c[:, :],
                                         sqt28[:, ei * TD:(ei + 1) * TD],
                                         start=(ei == 0), stop=False
                                         ).then_inc(S["sDSTe"], 1)
                            else:
                                e.matmul(pb[4][0:1, :], ones_c[:, :],
                                         sqt28[:, ei * TD:(ei + 1) * TD],
                                         start=False, stop=True).then_inc(S["sDSTe"], 1)
                                e.matmul(pb[3][0:1, :], ones_c[:, :],
                                         r2bs[ts % 2][:, ei * TD:(ei + 1) * TD],
                                         start=False, stop=True).then_inc(S["sDST"], 1)

                    def bcast2(ts):
                        e.wait_ge(S["sDAB"], ts + 1)
                        if ts >= 1:
                            e.wait_ge(S["sDXL"], ts)
                        e.matmul(pb[4][:, :], ones_r[:, :], abv[0:1, 512:1024],
                                 start=True, stop=True)
                        e.matmul(pb[5][:, :], ones_r[:, :], stx[0:1, 0:512],
                                 start=True, stop=True).then_inc(S["sDBC"], 1)

                    if t >= 1:
                        stats2(t - 1)
                    e.wait_ge(S["sDV2"], NJ * (t + 1))
                    for h in range(DC):
                        if 8 * t + h >= 2:
                            e.wait_ge(S["sDEV"], 8 * t + h - 1)
                        m = None
                        for i in range(ICH):
                            m = e.matmul(pb[h % 2][:, :],
                                         wd_sb[:, i * D + 128 * h: i * D + 128 * h + 128],
                                         otile[t % 2][:, i * TD:(i + 1) * TD],
                                         start=(i == 0), stop=(i == ICH - 1))
                        m.then_inc(S["sDMM"], 1)
                        if h == 0 and t >= 1:
                            bcast2(t - 1)
                    if t == NT - 1:
                        stats2(t)
                        bcast2(t)
                elif eng == "vector":
                    if t == 0:
                        for j in range(NJ):
                            round_d(0, j)
                    e.wait_ge(S["sDX1"], 16 * (t + 1))
                    if t >= 2:
                        e.wait_ge(S["sDXL"], t - 1)
                    if t >= 1:
                        ln_post2(t - 1)
                        ab_post2(t - 1)
                    for h in range(DC):
                        e.wait_ge(S["sDMM"], 8 * t + h + 1)
                        e.scalar_tensor_tensor(r2bs[t % 2][:, h * TD:(h + 1) * TD],
                                               pb[h % 2][:, :], scl[:, 9:10],
                                               x1d[:, h * TD:(h + 1) * TD],
                                               ALU.mult, ALU.add).then_inc(S["sDEV"], 1)
                        if h == 0 and t >= 1:
                            affine2(t - 1)
                        if h < NJ and t + 1 < NT:
                            round_d(t + 1, h)
                    if t == NT - 1:
                        ln_post2(t)
                        ab_post2(t)
                        affine2(t)
                elif eng == "scalar":
                    if t >= 1:
                        ln_sc2(t - 1)
                        e.wait_ge(S["sDVC"], t)
                        e.activation(stx[0:1, 0:512], stx[0:1, 512:1024], AF.Sqrt,
                                     bias=EPS_ap).then_inc(S["sDSQ"], 1)
                    for h in range(DC):
                        e.wait_ge(S["sDEV"], 8 * t + h + 1)
                        if t >= 1:
                            e.wait_ge(S["sOUT"], 16 * t)
                        e.activation(sqt28[:, h * TD:(h + 1) * TD],
                                     r2bs[t % 2][:, h * TD:(h + 1) * TD],
                                     AF.Square).then_inc(S["sDCB"], 1)
                    if t == NT - 1:
                        ln_sc2(t)
                        e.wait_ge(S["sDVC"], t + 1)
                        e.activation(stx[0:1, 0:512], stx[0:1, 512:1024], AF.Sqrt,
                                     bias=EPS_ap).then_inc(S["sDSQ"], 1)
                _sc.__exit__(None, None, None)
            if eng == "sync":
                e.wait_ge(S["sDXL"], NT)
                e.dma_start(
                    out=out_fm.rearrange("k p n -> p k n")[:, :, (NT - 1) * TD:NT * TD],
                    in_=sqt28.ap().rearrange("p (k n) -> p k n", k=DC)
                ).then_inc(S["sOUT"], 16)
                e.wait_ge(S["sOUT"], 16 * NT)
            if eng == "gpsimd":
                e.wait_ge(S["sOUT"], 16 * NT)

        def stream(eng, e):
            with nc.named_scope("setup"):
                ph_setup(eng, e)
            with nc.named_scope("mixer"):
                for t in range(NT):
                    with nc.named_scope(f"mx{t}"):
                        ph_mixer_tile(eng, e, t)
                ph_mixer_tail(eng, e)
            with nc.named_scope("bar1"):
                ph_bar1_fill(eng, e)
                ph_gmax_barrier(eng, e, 0, 1)
                ph_bar1_post(eng, e)
            if eng == "gpsimd":
                ph_gapply_store(e, APT * NT - 5)
                ph_gapply_store(e, APT * NT - 4)
                ph_gapply(e, APT * NT - 2, do_store=False)
                ph_gapply(e, APT * NT - 1, do_store=False)
                ph_gapply_store(e, APT * NT - 3)
                ph_gapply_store(e, APT * NT - 2)
                ph_gapply_store(e, APT * NT - 1)
            ph_A_wu_loads(eng, e)
            with nc.named_scope("xq"):
                ph_xq(eng, e)
            if stop_after == "mixer":
                if eng == "sync":
                    e.wait_ge(S["sXQS"][0], 16 * (NT // 2))
                    e.wait_ge(S["sXQS"][1], 16 * (NT // 2))
                    e.wait_ge(S["sAWW"][0], 16 * 24)
                    e.wait_ge(S["sAWW"][1], 16 * 24)
                    e.dma_start(out=gdbg[:, 0:8], in_=mxc[:, :]).then_inc(S["sOUT"], 16)
                    e.dma_start(out=gdbg[:, 8:10], in_=scl[:, 0:2]).then_inc(S["sOUT"], 16)
                    e.wait_ge(S["sOUT"], 32)
                elif eng == "gpsimd":
                    e.wait_ge(S["sOUT"], 32)
                return
            with nc.named_scope("phA"):
                ph_A(eng, e)
            with nc.named_scope("bar2"):
                ph_bar2_fill(eng, e)
                ph_gmax_barrier(eng, e, 1, 2)
                ph_bar2_post(eng, e)
            if stop_after == "A":
                if eng == "sync":
                    e.wait_ge(S["sASP"][0], 16 * (NA // 8))
                    e.wait_ge(S["sASP"][1], 16 * (NA // 8))
                    e.wait_ge(S["bPRE1"], 6)
                    e.dma_start(out=gdbg[:, 0:6], in_=scl[:, 0:6]).then_inc(S["sOUT"], 16)
                    e.wait_ge(S["sOUT"], 16)
                elif eng == "gpsimd":
                    e.wait_ge(S["sOUT"], 16)
                return
            pbar(e, 1)
            with nc.named_scope("phB"):
                ph_B(eng, e)
            with nc.named_scope("bar3"):
                ph_bar3_fill(eng, e)
                ph_gmax_barrier(eng, e, 2, 1)
                ph_bar3_post(eng, e)
            if stop_after == "B":
                if eng == "sync":
                    e.wait_ge(S["sBSP"][0], 16 * (NB // 2))
                    e.wait_ge(S["sBSP"][1], 16 * (NB // 2))
                    e.wait_ge(S["bPRE2"], 6)
                    e.dma_start(out=gdbg[:, 0:8], in_=scl[:, 0:8]).then_inc(S["sOUT"], 16)
                    e.wait_ge(S["sOUT"], 16)
                elif eng == "gpsimd":
                    e.wait_ge(S["sOUT"], 16)
                return
            with nc.named_scope("phC"):
                ph_C(eng, e)
            ph_D_loads(eng, e)
            with nc.named_scope("bar4"):
                ph_bar4_fill(eng, e)
                ph_gmax_barrier(eng, e, 3, 1)
                ph_bar4_post(eng, e)
            if stop_after == "C":
                if eng == "sync":
                    e.wait_ge(S["sCSP"][0], 16 * (NB // 2))
                    e.wait_ge(S["sCSP"][1], 16 * (NB // 2))
                    e.wait_ge(S["bPRE3"], 6)
                    e.dma_start(out=gdbg[:, 0:10], in_=scl[:, 0:10]).then_inc(S["sOUT"], 16)
                    e.wait_ge(S["sOUT"], 16)
                elif eng == "gpsimd":
                    e.wait_ge(S["sOUT"], 16)
                return
            pbar(e, 2)
            with nc.named_scope("phD"):
                ph_D(eng, e)

        with nc.Block() as bk:
            @bk.sync
            def _(e):
                stream("sync", FE(e, fences["sync"]))

            @bk.gpsimd
            def _(e):
                stream("gpsimd", FE(e, fences["gpsimd"]))

            @bk.tensor
            def _(e):
                stream("tensor", FE(e, fences["tensor"]))

            @bk.scalar
            def _(e):
                stream("scalar", FE(e, fences["scalar"]))

            @bk.vector
            def _(e):
                stream("vector", FE(e, fences["vector"]))
    return nc


def _prep_inputs(L0_attn_x, L0_attn_Wg, L0_attn_bg, L0_attn_Wv, L0_attn_bv,
                 L0_attn_Wd, L0_attn_bd, L0_ffn_w_gate, L0_ffn_w_up, L0_ffn_w_down):
    f = np.ascontiguousarray
    shared = {
        "wg_t": f(L0_attn_Wg.T).reshape(DC, 128, D),
        "wv_t": f(L0_attn_Wv.T).reshape(DC, 128, D),
        "wd_t": f(L0_attn_Wd.T).reshape(DC, 128, D),
        "bgh": f(L0_attn_bg.reshape(DC, 128).T),
        "bvh": f(L0_attn_bv.reshape(DC, 128).T),
        "bdh": f(L0_attn_bd.reshape(DC, 128).T),
        "wu_n": f(L0_ffn_w_up).reshape(ICH, 128, D),
        "wq_n": f(L0_ffn_w_gate).reshape(ICH, 128, D),
        "wdn_n": f(L0_ffn_w_down).reshape(DC, 128, I),
    }
    in_maps = []
    for c in range(NCORES):
        xc = f(L0_attn_x[2 * c:2 * c + 2].reshape(TOK, D).T).reshape(DC, 128, TOK)
        m = {"x_fm": xc}
        m.update(shared)
        in_maps.append(m)
    return in_maps


def _run(inputs, stop_after="full", debug=False):
    nc = build(stop_after=stop_after, debug=debug)
    in_maps = _prep_inputs(**{k: np.asarray(v) for k, v in inputs.items()})
    res = run_bass_kernel_spmd(nc, in_maps, core_ids=list(range(NCORES)))
    if debug:
        return res
    out = np.empty((B, S, D), dtype=np.float32)
    for c in range(NCORES):
        o = res.results[c]["out_fm"].reshape(D, TOK)
        out[2 * c:2 * c + 2] = o.T.reshape(2, S, D)
    return out


def kernel(**inputs):
    return _run(inputs)
